# version 12
# baseline (speedup 1.0000x reference)
import math

import numpy as np
import ml_dtypes
import concourse.bass as bass
import concourse.mybir as mybir
from concourse.bass_utils import run_bass_kernel_spmd

F32 = mybir.dt.float32
F32R = mybir.dt.float32r
BF16 = mybir.dt.bfloat16
AF = mybir.ActivationFunctionType
ALU = mybir.AluOpType
AXX = mybir.AxisListType.X

D = 1024
T = 4096
H = 64
HID = 4096
P = 512
NCHUNK = T // P
C = 128
NSC = P // C
M_RND = float(2.0 ** 23 + 2.0 ** 22)
LN_EPS = 1e-5
SUR_THR = float(math.log(0.7 / 0.3) * (1.0 + 1e-6))

SUBLAYERS = 4
PIPELINE = True

_OFF = {}
_top = 0
def _cols(name, w):
    global _top
    _OFF[name] = (_top, w)
    _top += w

for _n in ("maskIL", "maskIU", "maskSU", "maskSL", "expoSU", "expoSL", "expoIU",
           "ones128", "ident"):
    _cols(_n, 128)
for _n in ("iota0", "iota1", "iotaC", "onescol", "epscol", "negMcol", "Mcol", "zerocol", "ln16col"):
    _cols(_n, 1)
_cols("zero64", 64)
CBW = _top


def build_const_blob():
    cb = np.zeros((128, CBW), np.float32)
    r = np.arange(128)
    lo = (r[:, None] >= r[None, :]).astype(np.float32)
    iu = (r[:, None] <= r[None, :]).astype(np.float32)
    su = (r[:, None] < r[None, :]).astype(np.float32)
    sl = (r[:, None] > r[None, :]).astype(np.float32)

    def put(name, arr):
        o, w = _OFF[name]
        cb[:, o:o + w] = arr

    put("maskIL", lo)
    put("maskIU", iu)
    put("maskSU", su)
    put("maskSL", sl)
    put("expoSU", (r[None, :] - 1 - r[:, None]) * su)
    put("expoSL", (r[:, None] - 1 - r[None, :]) * sl)
    put("expoIU", (r[None, :] - r[:, None]) * iu)
    put("ones128", np.ones((128, 128), np.float32))
    put("ident", np.eye(128, dtype=np.float32))
    put("iota0", r[:, None].astype(np.float32))
    put("iota1", (r[:, None] + 1).astype(np.float32))
    put("iotaC", (C - 1 - r[:, None]).astype(np.float32))
    put("onescol", np.ones((128, 1), np.float32))
    put("epscol", np.full((128, 1), LN_EPS, np.float32))
    put("negMcol", np.full((128, 1), -M_RND, np.float32))
    put("Mcol", np.full((128, 1), M_RND, np.float32))
    put("zerocol", np.zeros((128, 1), np.float32))
    put("ln16col", np.full((128, 1), math.log(16.0), np.float32))
    put("zero64", np.zeros((128, 64), np.float32))
    return cb


class SemT:
    def __init__(self, handle, name, dma=False):
        self.h = handle
        self.name = name
        self.n = 0
        self.dma = dma

    def bump(self):
        self.n += 16 if self.dma else 1
        return self.n


class Tracker:
    def __init__(self):
        self.steps = []

    def emit(self, eng, fn, r=(), w=(), sem=None, xwaits=(), cost=None, lat=None):
        self.steps.append(dict(eng=eng, fn=fn, r=tuple(r), w=tuple(w), sem=sem,
                               inc=False, xwaits=list(xwaits), cost=cost, lat=lat))
        return len(self.steps) - 1

    def schedule(self, costs=None):
        import heapq
        n = len(self.steps)
        cost = []
        base = {"pe": 260, "dv": 390, "ac": 640, "sp": 620, "gp": 900}
        lats = []
        for st in self.steps:
            c = st.get("cost")
            cost.append(c if c is not None else base.get(st["eng"], 400))
            lt = st.get("lat")
            if lt is None:
                lt = 800 if st["eng"] == "sp" else 0
            lats.append(lt)
        last_w = {}
        readers = {}
        succ = [[] for _ in range(n)]
        pred_cnt = [0] * n
        preds = [set() for _ in range(n)]
        for i, st in enumerate(self.steps):
            deps = set()
            for rr in st["r"]:
                j = last_w.get(rr)
                if j is not None:
                    deps.add(j)
            for rr in st["w"]:
                j = last_w.get(rr)
                if j is not None:
                    deps.add(j)
                for k in readers.get(rr, ()):
                    if k != i:
                        deps.add(k)
            for rr in st["r"]:
                readers.setdefault(rr, []).append(i)
            for rr in st["w"]:
                last_w[rr] = i
                readers[rr] = []
            preds[i] = deps
            pred_cnt[i] = len(deps)
            for j in deps:
                succ[j].append(i)
        cp = [0.0] * n
        for i in range(n - 1, -1, -1):
            m = 0.0
            for s_ in succ[i]:
                if cp[s_] > m:
                    m = cp[s_]
            cp[i] = cost[i] + lats[i] + m
        fin = [0.0] * n
        avail = [0.0] * n
        eng_clock = {}
        ready = []
        for i in range(n):
            if pred_cnt[i] == 0:
                ready.append(i)
        order = []
        remaining = list(pred_cnt)
        start = [0.0] * n
        import os
        policy = os.environ.get("TK_SCHED_POLICY", "est")
        while ready:
            if policy == "est":
                bi, bk = -1, None
                for idx, i in enumerate(ready):
                    est = eng_clock.get(self.steps[i]["eng"], 0.0)
                    if avail[i] > est:
                        est = avail[i]
                    key = (est, -cp[i])
                    if bk is None or key < bk:
                        bk, bi = key, idx
                i = ready.pop(bi)
            else:
                bi, bk = -1, None
                for idx, i in enumerate(ready):
                    key = -cp[i]
                    if bk is None or key < bk:
                        bk, bi = key, idx
                i = ready.pop(bi)
            st = self.steps[i]
            t0 = eng_clock.get(st["eng"], 0.0)
            if avail[i] > t0:
                t0 = avail[i]
            start[i] = t0
            fin[i] = t0 + cost[i] + lats[i]
            eng_clock[st["eng"]] = t0 + cost[i]
            order.append(i)
            for s_ in succ[i]:
                remaining[s_] -= 1
                if avail[s_] < fin[i]:
                    avail[s_] = fin[i]
                if remaining[s_] == 0:
                    ready.append(s_)
        assert len(order) == n
        if os.environ.get("TK_SCHED_DEBUG"):
            eng_busy = {}
            for i in range(n):
                e = self.steps[i]["eng"]
                eng_busy[e] = eng_busy.get(e, 0) + cost[i]
            print("SCHED makespan(model): %.1f us" % (max(fin) / 1e3))
            print("SCHED cp bound: %.1f us" % (max(cp) / 1e3))
            print("SCHED eng busy(model):", {k: round(v / 1e3) for k, v in eng_busy.items()})
            prev_end = {}
            attr = {}
            gaps = []
            for i in order:
                e = self.steps[i]["eng"]
                gap = start[i] - prev_end.get(e, 0.0)
                if gap > 1.0:
                    jb, fb = None, -1.0
                    for j in preds[i]:
                        if fin[j] > fb:
                            fb, jb = fin[j], j
                    be = self.steps[jb]["eng"] if jb is not None else "none"
                    res = ""
                    if jb is not None:
                        wr = set(self.steps[jb]["w"])
                        res = ",".join(sorted(wr & set(self.steps[i]["r"]))) or \
                              ",".join(sorted(wr)[:2])
                    key = (e, be, res)
                    attr[key] = attr.get(key, 0.0) + gap
                    gaps.append((gap, e, be, res, start[i]))
                prev_end[e] = fin[i]
            pe_gaps = sorted([g for g in gaps if g[1] == "pe"], reverse=True)[:20]
            for gp, e_, be, res, t_ in pe_gaps:
                print("SCHED pegap %.1f us at t=%.1f blocked-by %s [%s]" %
                      (gp / 1e3, t_ / 1e3, be, res))
            top = sorted(attr.items(), key=lambda kv: -kv[1])[:16]
            for (e, be, res), tot in top:
                print("SCHED idle %s blocked-by %s [%s]: %.1f us" % (e, be, res, tot / 1e3))
            tot_idle = {}
            nstep = {}
            endt = {}
            for i in order:
                e = self.steps[i]["eng"]
                nstep[e] = nstep.get(e, 0) + 1
                endt[e] = fin[i]
            prev_end = {}
            for i in order:
                e = self.steps[i]["eng"]
                tot_idle[e] = tot_idle.get(e, 0.0) + (start[i] - prev_end.get(e, 0.0))
                prev_end[e] = fin[i]
            print("SCHED per-eng: " + "  ".join(
                "%s n=%d idle=%.0fus end=%.0fus" % (e, nstep[e], tot_idle[e] / 1e3,
                                                    endt[e] / 1e3) for e in sorted(nstep)))
        self.steps = [self.steps[i] for i in order]

    def finalize(self, eng_sems):
        last_w = {}
        readers = {}
        for i, st in enumerate(self.steps):
            ei = st["eng"]
            deps = set()

            def need(j):
                ej = self.steps[j]["eng"]
                return ej != ei or ej == "sp"

            for rr in st["r"]:
                j = last_w.get(rr)
                if j is not None:
                    deps.add(j)
            for rr in st["w"]:
                j = last_w.get(rr)
                if j is not None and need(j):
                    deps.add(j)
                for k in readers.get(rr, ()):
                    if need(k):
                        deps.add(k)
            for rr in st["r"]:
                readers.setdefault(rr, []).append(i)
            for rr in st["w"]:
                last_w[rr] = i
                readers[rr] = []
            st["deps"] = deps
        for st in self.steps:
            for j in st["deps"]:
                self.steps[j]["inc"] = True
            if st["eng"] == "sp":
                st["inc"] = True
        for st in self.steps:
            if st["sem"] is None:
                st["sem"] = eng_sems[st["eng"]]
        for st in self.steps:
            st["val"] = st["sem"].bump() if st["inc"] else None
        for st in self.steps:
            wmap = {}
            for j in st["deps"]:
                sj = self.steps[j]
                k = id(sj["sem"])
                if k not in wmap or wmap[k][1] < sj["val"]:
                    wmap[k] = (sj["sem"], sj["val"])
            st["waits"] = list(wmap.values()) + st["xwaits"]

    def run_engine(self, name, eng):
        waited = {}
        issued = {}
        for st in self.steps:
            if st["eng"] != name:
                continue
            if name == "sp" and st["sem"] is not None and st["sem"].dma:
                s = st["sem"]
                n = issued.get(id(s), 0)
                if n > 0 and waited.get(id(s), -1) < 16 * n:
                    eng.wait_ge(s.h, 16 * n)
                    waited[id(s)] = 16 * n
                issued[id(s)] = n + 1
            for (s, v) in st["waits"]:
                if waited.get(id(s), -1) < v:
                    eng.wait_ge(s.h, v)
                    waited[id(s)] = v
            inst = st["fn"](eng)
            if st["inc"]:
                inst.then_inc(st["sem"].h, 16 if st["sem"].dma else 1)


class PackView:
    def __init__(self, ap, name):
        self._ap = ap
        self.name = name

    def __getitem__(self, idx):
        return self._ap[idx]


def build_nc():
    nc = bass.Bass("TRN2", target_bir_lowering=False)

    dx = nc.dram_tensor("xT", [D, T], F32, kind="ExternalInput").ap()
    dout = nc.dram_tensor("outT", [D, T], F32, kind="ExternalOutput").ap()
    dcb = nc.dram_tensor("cbd", [128, CBW], F32, kind="ExternalInput").ap()
    dw1T = nc.dram_tensor("w1T", [D, HID], F32, kind="ExternalInput").ap()
    dw2T = nc.dram_tensor("w2T", [HID, D], F32, kind="ExternalInput").ap()
    dsmall = {}
    for nme, shp in [("wqa", [D, H]), ("wka", [D, H]), ("wva", [D, H]),
                     ("woa", [H, D]), ("wkbT", [D, H]), ("wvbT", [D, H]),
                     ("wqbT", [D, H]), ("irw1T", [D, H]), ("pwT", [D, 32]),
                     ("nmw1T", [H, 2 * H]), ("nmw2T", [2 * H, H]),
                     ("irw2T", [H, 1]), ("wobT", [H, D])]:
        dsmall[nme] = nc.dram_tensor(nme, shp, F32, kind="ExternalInput").ap()
    dbias = {}
    for nme, shp in [("b1c", [128, 32]), ("b2c", [128, 8]), ("nmb1c", [128, 1]),
                     ("nmb2c", [H, 1]), ("irb1c", [H, 1]), ("irb2", [1, 1]),
                     ("normw", [H, 1]), ("alphaf", [1, 1]), ("alphas", [1, 1])]:
        dbias[nme] = nc.dram_tensor("d_" + nme, shp, F32, kind="ExternalInput").ap()
    dw1q = nc.dram_tensor("w1q_scr", [D, HID], BF16, kind="Internal").ap()
    dw2q = nc.dram_tensor("w2q_scr", [HID, D], BF16, kind="Internal").ap()

    def sb(name, shape, dtype):
        return nc.alloc_sbuf_tensor(name, shape, dtype)

    cbt = sb("cbt", [128, CBW], F32)
    xbuf = [sb(f"xb{i}", [128, 8 * P], F32) for i in range(3)]
    xr = [[sb(f"xr{s}{i}", [128, P], F32R) for i in range(2)] for s in range(2)]
    xsq_sh = [sb(f"xsq{i}", [128, P], F32R) for i in range(2)]
    xsq = [xsq_sh, xsq_sh]
    mrep = [sb(f"mrep{s}", [128, P], F32) for s in range(2)]
    rrep = [sb(f"rrep{s}", [128, P], F32) for s in range(2)]
    t1sh = sb("t1s0", [128, P], F32)
    t2sh = sb("t2s0", [128, P], F32)
    t1s_ = [t1sh, t1sh]
    t2s_ = [t2sh, t2sh]
    p1s = sb("p1s", [128, P], F32)
    p2s = sb("p2s", [128, P], F32)
    xqt = [[sb(f"xq{s}{i}", [128, P], BF16) for i in range(8)] for s in range(2)]
    hqt = [sb(f"hq{i}", [128, P], BF16) for i in range(32)]
    def hq_res(oc):
        return f"hq{oc}"
    slab = [sb(f"slab{i}", [128, 2048], BF16) for i in range(3)]
    qst = [sb(f"qst{i}", [128, 512], BF16) for i in range(2)]
    stg = [xbuf[2]]

    wqa_r = sb("wqa_r", [128, 8 * H], F32R)
    wka_r = sb("wka_r", [128, 8 * H], F32R)
    wva_r = sb("wva_r", [128, 8 * H], F32R)
    woa_r = sb("woa_r", [H, D], BF16)
    wkb_r = sb("wkb_r", [128, 8 * H], F32R)
    wvb_r = sb("wvb_r", [128, 8 * H], F32R)
    wqb_r = sb("wqb_r", [128, 8 * H], F32R)
    irw1_r = sb("irw1_r", [128, 8 * H], F32R)
    pw_r = sb("pw_r", [128, 8 * 32], F32R)
    nmw1_r = sb("nmw1_r", [H, 2 * H], F32R)
    nmw2_r = sb("nmw2_r", [2 * H, H], F32R)
    irw2_r = sb("irw2_r", [H, 8], F32R)
    irw2f = sb("irw2f", [H, 1], F32)
    wob_r = sb("wob_r", [H, D], BF16)
    ones_r = sb("ones_r", [128, 128], F32R)
    ident_r = sb("ident_r", [128, 128], F32R)

    cs_sum = sb("cs_sum", [H, 8], F32)
    cspw = sb("cspw", [32, 1], F32)

    b1i = sb("b1i", [128, 32], F32)
    b2qc = sb("b2qc", [128, 8], F32)
    nmb1c = sb("nmb1c", [128, 1], F32)
    nmb2c = sb("nmb2c", [H, 1], F32)
    irb1c = sb("irb1c", [H, 1], F32)
    irb2c = sb("irb2c", [128, 1], F32)
    nw_c = sb("nw_c", [H, 1], F32)

    lnA = [sb(f"lnA{m}", [128, 1], F32) for m in range(2)]
    acol = [sb(f"acol{m}", [128, 1], F32) for m in range(2)]
    bcol = [sb(f"bcol{m}", [128, 1], F32) for m in range(2)]
    g1c = [sb(f"g1c{m}", [128, 1], F32) for m in range(2)]
    g1n = [sb(f"g1n{m}", [128, 1], F32) for m in range(2)]
    g2c = [sb(f"g2c{m}", [128, 1], F32) for m in range(2)]
    g3c = [sb(f"g3c{m}", [128, 1], F32) for m in range(2)]
    aCc = [sb(f"aCc{m}", [128, 1], F32) for m in range(2)]
    DS_ = [sb(f"DS{m}", [128, 128], F32) for m in range(2)]
    DSB = [sb(f"DSB{m}", [128, 128], F32) for m in range(2)]
    DSBT = [sb(f"DSBT{m}", [128, 128], F32) for m in range(2)]
    DI_ = [sb(f"DI{m}", [128, 128], F32) for m in range(2)]

    F0m = [sb(f"F0m{m}", [H, H], F32R) for m in range(2)]
    Za = sb("Za", [H, H], F32R)
    zcol = sb("zcol", [H, 8], F32R)

    qTa = sb("qTa", [H, P], F32R)
    kTa = sb("kTa", [H, P], F32R)
    vTa = sb("vTa", [H, P], F32R)
    oTa = sb("oTa", [H, P], BF16)
    PTs = sb("PTs", [128, 128], F32R)
    QKs = sb("QKs", [128, 128], F32)
    deni = sb("deni", [128, 1], F32)
    dent = sb("dent", [128, 1], F32)
    ztmp = sb("ztmp", [H, 1], F32)
    Ktm_a = sb("Ktm_a", [128, H], F32R)
    Vtm_a = sb("Vtm_a", [128, H], F32R)
    Otm_a = sb("Otm_a", [128, H], F32R)

    kbT = sb("kbT", [H, P], F32R)
    qbT = sb("qbT", [H, P], F32R)
    vbT = sb("vbT", [H, P], F32R)
    nuT = sb("nuT", [H, P], F32)
    gcr = sb("gcr", [H, P], F32R)
    sc1r = sb("sc1r", [128, P], F32R)
    obT = sb("obT", [H, P], BF16)
    pwC = sb("pwC", [32, P], F32R)
    pwt = sb("pwt", [128, NSC * 3], F32)
    ucol = sb("ucol", [128, NSC], F32)
    urow = sb("urow", [1, 128], F32R)
    unrow = sb("unrow", [1, 128], F32R)
    uncol = sb("uncol", [128, 1], F32)
    Nfm = sb("Nfm", [H, P], F32)
    nsinit = sb("nsinit", [H, 1], F32)
    nsA = sb("nsA", [H, 128], F32)
    nsB = sb("nsB", [H, 128], F32)
    Ntm = sb("Ntm", [128, H], F32)
    nscr = sb("nscr", [128, 128], F32)

    Npw = [[sb(f"Npw{m}{i}", [128, 128], F32R) for i in range(2)] for m in range(2)]
    NpwT = [[sb(f"NpwT{m}{i}", [128, 128], F32R) for i in range(2)] for m in range(2)]
    Ut = [[sb(f"Ut{m}{i}", [128, H], F32R) for i in range(2)] for m in range(2)]
    Rm = [sb(f"Rm{m}", [128, H], F32R) for m in range(2)]
    WxT = [sb(f"WxT{m}", [128, 128], F32R) for m in range(2)]
    WiT = [sb(f"WiT{m}", [128, 128], F32R) for m in range(2)]
    Ktm = sb("Ktm", [128, H], F32R)
    Ktg = sb("Ktg", [128, H], F32R)
    Vtm = sb("Vtm", [128, H], F32)
    rdf = sb("rdf", [128, H], F32)
    rds = sb("rds", [128, H], F32)
    predt = sb("predt", [128, H], F32)
    dsq = sb("dsq", [128, H], F32)
    errc = sb("errc", [128, 1], F32)
    oatt = sb("oatt", [128, H], F32)
    onrm = sb("onrm", [128, H], F32R)
    msc = sb("msc", [128, 1], F32)

    pbt = [nc.alloc_psum_tensor(f"pb{i}", [128, 512], F32) for i in range(8)]
    pbr = [p.bitcast(F32R) for p in pbt]
    PB = [f"pb{i}" for i in range(8)]

    def cbv(name, rows=128):
        o, w = _OFF[name]
        return cbt[0:rows, o:o + w]

    TK = Tracker()
    sem_cms = []

    def mksem(name, dma=False):
        cm = nc.semaphore(name)
        h = cm.__enter__()
        sem_cms.append(cm)
        return SemT(h, name, dma)

    s_x = mksem("s_x", dma=True)
    s_sl = [mksem("s_sl0", dma=True), mksem("s_sl1", dma=True),
            mksem("s_sl2", dma=True)]
    s_sw = mksem("s_sw", dma=True)
    s_qs = [mksem("s_qs0", dma=True), mksem("s_qs1", dma=True),
            mksem("s_qs2", dma=True)]
    s_qt = [mksem("s_qt0", dma=True), mksem("s_qt1", dma=True)]
    s_out = mksem("s_out", dma=True)
    s_pe = mksem("s_pe")
    s_dv = mksem("s_dv")
    s_ac = mksem("s_ac")
    s_gp = mksem("s_gp")
    ENG_SEMS = {"pe": s_pe, "dv": s_dv, "ac": s_ac, "gp": s_gp}

    E = TK.emit

    E("sp", lambda e: e.dma_start(out=cbt[:], in_=dcb), w=["cbt"], sem=s_sw)
    xsrc = dx.rearrange("(ic p) t -> p ic t", p=128)
    odst = dout.rearrange("(ic p) t -> p ic t", p=128)

    def x_load(c, buf):
        E("sp", lambda e, c=c, buf=buf: e.dma_start(
            out=xbuf[buf][:].rearrange("p (ic t) -> p ic t", ic=8),
            in_=xsrc[:, :, c * P:(c + 1) * P]), lat=6000,
          w=[f"xb{buf}"], sem=s_x)

    x_load(0, 0)

    def wview(d):
        k = d.shape[1]
        return (d.rearrange("(ic p) k -> p ic k", p=128), k)

    small_jobs = [
        (dsmall["wqa"], wqa_r, 128, 8 * H, None, True),
        (dsmall["wka"], wka_r, 128, 8 * H, None, True),
        (dsmall["wva"], wva_r, 128, 8 * H, None, True),
        (dsmall["wkbT"], wkb_r, 128, 8 * H, None, True),
        (dsmall["wvbT"], wvb_r, 128, 8 * H, None, True),
        (dsmall["wqbT"], wqb_r, 128, 8 * H, None, True),
        (dsmall["irw1T"], irw1_r, 128, 8 * H, None, True),
        (dsmall["pwT"], pw_r, 128, 8 * 32, None, True),
        (dsmall["woa"], woa_r, H, D, None, False),
        (dsmall["nmw1T"], nmw1_r, H, 2 * H, None, False),
        (dsmall["nmw2T"], nmw2_r, 2 * H, H, None, False),
        (dsmall["irw2T"], irw2f, H, 1, None, False),
        (dsmall["wobT"], wob_r, H, D, "normw", False),
    ]
    E("sp", lambda e: e.dma_start(out=nw_c[:], in_=dbias["normw"]), w=["nw_c"], sem=s_sw)
    for i, (srcd, dst, rr, cc, fold, chunked) in enumerate(small_jobs):
        if chunked:
            sv, kk = wview(srcd)
            E("sp", lambda e, sv=sv, kk=kk:
              e.dma_start(out=stg[0][:].rearrange("p (ic k) -> p ic k", k=kk)[:, 0:8, :],
                          in_=sv),
              r=[], w=["xb2"], sem=s_sw)
        else:
            E("sp", lambda e, srcd=srcd, rr=rr, cc=cc:
              e.dma_start(out=stg[0][0:rr, 0:cc], in_=srcd),
              r=[], w=["xb2"], sem=s_sw)
        if fold == "normw":
            E("dv", lambda e, dst=dst, rr=rr, cc=cc:
              e.tensor_scalar(out=dst[0:rr, 0:cc], in0=stg[0][0:rr, 0:cc],
                              scalar1=nw_c[:], scalar2=None, op0=ALU.mult),
              r=["xb2", "nw_c"], w=[dst.name])
        else:
            E("dv", lambda e, dst=dst, rr=rr, cc=cc:
              e.tensor_copy(out=dst[0:rr, 0:cc], in_=stg[0][0:rr, 0:cc]),
              r=["xb2"], w=[dst.name])
    E("dv", lambda e: e.tensor_scalar(out=irw2_r[:], in0=cbv("ones128", rows=H)[:, 0:8],
                                      scalar1=irw2f[:], scalar2=None, op0=ALU.mult),
      r=["cbt", "irw2f"], w=["irw2_r"])
    E("dv", lambda e: e.tensor_copy(out=ones_r[:], in_=cbv("ones128")),
      r=["cbt"], w=["ones_r"])
    E("dv", lambda e: e.tensor_copy(out=ident_r[:], in_=cbv("ident")),
      r=["cbt"], w=["ident_r"])

    for nme, dst in [("b1c", b1i), ("b2c", b2qc), ("nmb1c", nmb1c), ("nmb2c", nmb2c),
                     ("irb1c", irb1c)]:
        E("sp", lambda e, nme=nme, dst=dst: e.dma_start(out=dst[:], in_=dbias[nme]),
          w=[dst.name], sem=s_sw)
    E("sp", lambda e: e.dma_start(out=irb2c[:], in_=dbias["irb2"].broadcast_to((128, 1)),
                                  allow_slow_non_contiguous=True),
      w=["irb2c"], sem=s_sw)
    E("dv", lambda e: e.tensor_scalar(out=b1i[:], in0=b1i[:], scalar1=64.0, scalar2=M_RND,
                                      op0=ALU.mult, op1=ALU.add), r=["b1i"], w=["b1i"])
    E("dv", lambda e: e.tensor_scalar(out=b1i[:], in0=b1i[:], scalar1=M_RND, scalar2=127.0,
                                      op0=ALU.subtract, op1=ALU.min), r=["b1i"], w=["b1i"])
    E("dv", lambda e: e.tensor_scalar(out=b1i[:], in0=b1i[:], scalar1=-128.0, scalar2=0.25,
                                      op0=ALU.max, op1=ALU.mult), r=["b1i"], w=["b1i"])
    E("dv", lambda e: e.tensor_scalar(out=b2qc[:], in0=b2qc[:], scalar1=64.0, scalar2=M_RND,
                                      op0=ALU.mult, op1=ALU.add), r=["b2qc"], w=["b2qc"])
    E("dv", lambda e: e.tensor_scalar(out=b2qc[:], in0=b2qc[:], scalar1=M_RND, scalar2=127.0,
                                      op0=ALU.subtract, op1=ALU.min), r=["b2qc"], w=["b2qc"])
    E("dv", lambda e: e.tensor_scalar(out=b2qc[:], in0=b2qc[:], scalar1=-128.0,
                                      scalar2=1.0 / 64.0, op0=ALU.max, op1=ALU.mult),
      r=["b2qc"], w=["b2qc"])

    cs_specs = [(wqa_r, H, 0), (wka_r, H, 1), (wva_r, H, 2), (wkb_r, H, 3),
                (wvb_r, H, 4), (wqb_r, H, 5), (irw1_r, H, 6)]
    for wt, od, j in cs_specs:
        for ic in range(8):
            E("pe", lambda e, wt=wt, od=od, j=j, ic=ic:
              e.matmul(pbt[6][0:od, 8 * j:8 * j + 8], wt[:, ic * od:(ic + 1) * od],
                       ones_r[:, 0:8], start=(ic == 0), stop=(ic == 7)),
              r=[wt.name, "ones_r"], w=[PB[6]])
    for ic in range(8):
        E("pe", lambda e, ic=ic:
          e.matmul(pbt[6][0:32, 56:64], pw_r[:, ic * 32:(ic + 1) * 32],
                   ones_r[:, 0:8], start=(ic == 0), stop=(ic == 7)),
          r=["pw_r", "ones_r"], w=[PB[6]])
    for j in range(7):
        E("dv", lambda e, j=j: e.tensor_copy(out=cs_sum[0:H, j:j + 1],
                                             in_=pbt[6][0:H, 8 * j:8 * j + 1]),
          r=[PB[6]], w=["cs_sum"])
    E("dv", lambda e: e.tensor_copy(out=cspw[:], in_=pbt[6][0:32, 56:57]),
      r=[PB[6]], w=["cspw"])

    def cs(j):
        return cs_sum[0:H, j:j + 1]

    for m in range(2):
        dal = dbias["alphaf"] if m == 0 else dbias["alphas"]
        E("sp", lambda e, dal=dal, m=m:
          e.dma_start(out=acol[m][:], in_=dal.broadcast_to((128, 1)),
                      allow_slow_non_contiguous=True),
          w=[acol[m].name], sem=s_sw)
        E("ac", lambda e, m=m: e.activation(out=lnA[m][:], in_=acol[m][:], func=AF.Ln),
          r=[acol[m].name], w=[lnA[m].name])
        E("dv", lambda e, m=m: e.tensor_scalar(out=bcol[m][:], in0=acol[m][:], scalar1=-1.0,
                                               scalar2=1.0, op0=ALU.mult, op1=ALU.add),
          r=[acol[m].name], w=[bcol[m].name])

        def gcol(dst, iota_name, m=m):
            E("dv", lambda e, dst=dst, iota_name=iota_name, m=m:
              e.tensor_scalar(out=dst[:], in0=cbv(iota_name), scalar1=lnA[m][:],
                              scalar2=None, op0=ALU.mult),
              r=["cbt", lnA[m].name], w=[dst.name])
            E("ac", lambda e, dst=dst: e.activation(out=dst[:], in_=dst[:], func=AF.Exp),
              r=[dst.name], w=[dst.name])

        gcol(g1c[m], "iota0")
        gcol(g2c[m], "iota1")
        gcol(g3c[m], "iotaC")
        E("dv", lambda e, m=m: e.tensor_scalar(out=aCc[m][:], in0=lnA[m][:], scalar1=float(C),
                                               scalar2=None, op0=ALU.mult),
          r=[lnA[m].name], w=[aCc[m].name])
        E("ac", lambda e, m=m: e.activation(out=aCc[m][:], in_=aCc[m][:], func=AF.Exp),
          r=[aCc[m].name], w=[aCc[m].name])
        E("dv", lambda e, m=m: e.tensor_tensor(out=g1n[m][:], in0=g1c[m][:], in1=bcol[m][:],
                                               op=ALU.mult),
          r=[g1c[m].name, bcol[m].name], w=[g1n[m].name])
        E("dv", lambda e, m=m: e.tensor_scalar(out=g1n[m][:], in0=g1n[m][:], scalar1=-1.0,
                                               scalar2=None, op0=ALU.mult),
          r=[g1n[m].name], w=[g1n[m].name])

        def dmat(dst, expo_name, mask_name, scale_beta, m=m):
            E("dv", lambda e, dst=dst, expo_name=expo_name, m=m:
              e.tensor_scalar(out=dst[:], in0=cbv(expo_name), scalar1=lnA[m][:],
                              scalar2=None, op0=ALU.mult),
              r=["cbt", lnA[m].name], w=[dst.name])
            E("ac", lambda e, dst=dst: e.activation(out=dst[:], in_=dst[:], func=AF.Exp),
              r=[dst.name], w=[dst.name])
            E("dv", lambda e, dst=dst, mask_name=mask_name:
              e.tensor_tensor(out=dst[:], in0=dst[:], in1=cbv(mask_name), op=ALU.mult),
              r=[dst.name, "cbt"], w=[dst.name])
            if scale_beta:
                E("dv", lambda e, dst=dst, m=m:
                  e.tensor_scalar(out=dst[:], in0=dst[:], scalar1=bcol[m][:], scalar2=None,
                                  op0=ALU.mult),
                  r=[dst.name, bcol[m].name], w=[dst.name])

        dmat(DS_[m], "expoSU", "maskSU", False)
        dmat(DSB[m], "expoSU", "maskSU", True)
        dmat(DSBT[m], "expoSL", "maskSL", True)
        dmat(DI_[m], "expoIU", "maskIU", False)

    for tns, rr_x, cc_x in [(F0m[0], H, H), (F0m[1], H, H), (Za, H, H), (zcol, H, 8)]:
        E("dv", lambda e, tns=tns, rr_x=rr_x, cc_x=cc_x:
          e.tensor_scalar(out=tns[0:rr_x, 0:cc_x], in0=cbv("zero64", rows=rr_x)[:, 0:cc_x],
                          scalar1=0.0, scalar2=None, op0=ALU.mult),
          r=["cbt"], w=[tns.name])
    E("dv", lambda e: e.tensor_scalar(out=nsinit[:], in0=cbv("zerocol", rows=H),
                                      scalar1=0.0, scalar2=None, op0=ALU.mult),
      r=["cbt"], w=["nsinit"])

    def quant_spill(dsrc, ddst, rows, cols, s0, store_res, sem_of):
        for s in range(rows // 128):
            b = 1 + (s + s0) % 2
            xbn = f"xb{b}"
            E("sp", lambda e, s=s, dsrc=dsrc, cols=cols, b=b:
              e.dma_start(out=xbuf[b][:, 0:cols], in_=dsrc[s * 128:(s + 1) * 128, 0:cols]),
              w=[xbn], sem=s_qs[(s + s0) % 3], lat=6000)
            E("ac", lambda e, cols=cols, b=b:
              e.activation(out=xbuf[b][:, 0:cols], in_=xbuf[b][:, 0:cols], func=AF.Relu,
                           scale=64.0, bias=cbv("Mcol")),
              r=[xbn, "cbt"], w=[xbn], cost=3600)
            E("dv", lambda e, cols=cols, b=b:
              e.tensor_scalar(out=xbuf[b][:, 0:cols], in0=xbuf[b][:, 0:cols], scalar1=M_RND,
                              scalar2=127.0, op0=ALU.subtract, op1=ALU.min),
              r=[xbn], w=[xbn], cost=2200)
            nq = cols // 512
            for q in range(nq):
                j = sem_of(s, q)
                E("dv", lambda e, q=q, j=j, b=b:
                  e.tensor_scalar(out=qst[j][:], in0=xbuf[b][:, q * 512:(q + 1) * 512],
                                  scalar1=-128.0, scalar2=None, op0=ALU.max),
                  r=[xbn], w=[f"qst{j}"], cost=330)
                E("sp", lambda e, s=s, ddst=ddst, q=q, j=j:
                  e.dma_start(out=ddst[s * 128:(s + 1) * 128, q * 512:(q + 1) * 512],
                              in_=qst[j][:]),
                  r=[f"qst{j}"], w=[store_res(s, q)], sem=s_qt[j], lat=800)

    quant_spill(dw1T, dw1q, D, HID, 0,
                lambda s, q: f"dw1q_q{q}", lambda s, q: q % 2)
    quant_spill(dw2T, dw2q, HID, D, 8,
                lambda s, q: f"dw2q_r{s // 2}", lambda s, q: (s // 2) % 2)
    x_load(1, 1)

    def emit_ln(buf, S, want_r16, proj_specs, sum_bank, sq_bank, passes=(0,)):
        xbn = f"xb{buf}"
        mr, rr_ = mrep[S], rrep[S]
        for pss in passes:
            specs = [sp_ for sp_ in proj_specs if sp_.get("pass", 0) == pss]
            for ic in range(8):
                pp = ic % 2
                xrt, xst = xr[S][pp], xsq[S][pp]
                E("dv", lambda e, ic=ic, xrt=xrt, buf=buf:
                  e.tensor_copy(out=xrt[:], in_=xbuf[buf][:, ic * P:(ic + 1) * P]),
                  r=[xbn], w=[xrt.name])
                if pss == 0:
                    E("ac", lambda e, ic=ic, xst=xst, buf=buf:
                      e.activation(out=xst[:], in_=xbuf[buf][:, ic * P:(ic + 1) * P],
                                   func=AF.Square),
                      r=[xbn], w=[xst.name])
                    E("pe", lambda e, ic=ic, xrt=xrt:
                      e.matmul(pbt[sum_bank][:], ones_r[:], xrt[:],
                               start=(ic == 0), stop=(ic == 7)),
                      r=["ones_r", xrt.name], w=[PB[sum_bank]])
                    E("pe", lambda e, ic=ic, xst=xst:
                      e.matmul(pbt[sq_bank][:], ones_r[:], xst[:],
                               start=(ic == 0), stop=(ic == 7)),
                      r=["ones_r", xst.name], w=[PB[sq_bank]])
                for spec in specs:
                    E("pe", lambda e, ic=ic, xrt=xrt, spec=spec:
                      e.matmul(spec["out"], spec["lhsT"](ic), xrt[:],
                               start=(ic == 0), stop=(ic == 7)),
                      r=[spec["lres"], xrt.name], w=[spec["bank"]])
            if pss == 0:
                E("dv", lambda e, mr=mr: e.tensor_scalar(out=mr[:], in0=pbt[sum_bank][:],
                                                         scalar1=1.0 / D, scalar2=None,
                                                         op0=ALU.mult),
                  r=[PB[sum_bank]], w=[mr.name])
                E("dv", lambda e, rr_=rr_: e.tensor_scalar(out=rr_[:], in0=pbt[sq_bank][:],
                                                           scalar1=1.0 / D, scalar2=None,
                                                           op0=ALU.mult),
                  r=[PB[sq_bank]], w=[rr_.name])
                E("dv", lambda e, mr=mr: e.tensor_tensor(out=p2s[:], in0=mr[:], in1=mr[:],
                                                         op=ALU.mult),
                  r=[mr.name], w=["p2s"])
                E("dv", lambda e, rr_=rr_: e.tensor_tensor(out=rr_[:], in0=rr_[:], in1=p2s[:],
                                                           op=ALU.subtract),
                  r=[rr_.name, "p2s"], w=[rr_.name])
                E("ac", lambda e, rr_=rr_: e.activation(out=rr_[:], in_=rr_[:], func=AF.Ln,
                                                        bias=cbv("epscol")),
                  r=[rr_.name, "cbt"], w=[rr_.name])
                if want_r16:
                    E("ac", lambda e, rr_=rr_: e.activation(out=rr_[:], in_=rr_[:],
                                                            func=AF.Exp, scale=-0.5,
                                                            bias=cbv("ln16col")),
                      r=[rr_.name, "cbt"], w=[rr_.name])
                else:
                    E("ac", lambda e, rr_=rr_: e.activation(out=rr_[:], in_=rr_[:],
                                                            func=AF.Exp, scale=-0.5),
                      r=[rr_.name], w=[rr_.name])

    def emit_proj_pass(buf, S, specs):
        xbn = f"xb{buf}"
        for ic in range(8):
            pp = ic % 2
            xrt = xr[S][pp]
            E("dv", lambda e, ic=ic, xrt=xrt, buf=buf:
              e.tensor_copy(out=xrt[:], in_=xbuf[buf][:, ic * P:(ic + 1) * P]),
              r=[xbn], w=[xrt.name])
            for spec in specs:
                E("pe", lambda e, ic=ic, xrt=xrt, spec=spec:
                  e.matmul(spec["out"], spec["lhsT"](ic), xrt[:],
                           start=(ic == 0), stop=(ic == 7)),
                  r=[spec["lres"], xrt.name], w=[spec["bank"]])

    def fold_correct(S, dst_ap, dst_res, bank, psum_ap, cscol, cs_res, rows):
        mr, rr_ = mrep[S], rrep[S]
        t2 = t2s_[S]
        E("dv", lambda e: e.tensor_scalar(out=t2[0:rows, :], in0=mr[0:rows, :],
                                          scalar1=cscol, scalar2=None, op0=ALU.mult),
          r=[mr.name, cs_res], w=[t2.name])
        E("dv", lambda e: e.tensor_tensor(out=t2[0:rows, :], in0=psum_ap,
                                          in1=t2[0:rows, :], op=ALU.subtract),
          r=[bank, t2.name], w=[t2.name])
        E("dv", lambda e: e.tensor_tensor(out=dst_ap, in0=t2[0:rows, :],
                                          in1=rr_[0:rows, :], op=ALU.mult),
          r=[t2.name, rr_.name], w=[dst_res])

    def phi_inplace(S, dst, dst_res):
        t1, t2 = t1s_[S], t2s_[S]
        E("dv", lambda e: e.tensor_scalar(out=t1[0:H, :], in0=t2[0:H, :], scalar1=0.0,
                                          scalar2=None, op0=ALU.min),
          r=[t2.name], w=[t1.name])
        E("ac", lambda e: e.activation(out=t1[0:H, :], in_=t1[0:H, :], func=AF.Exp),
          r=[t1.name], w=[t1.name])
        E("dv", lambda e, dst=dst: e.scalar_tensor_tensor(out=dst[:], in0=t2[0:H, :],
                                                          scalar=0.0, in1=t1[0:H, :],
                                                          op0=ALU.max, op1=ALU.add),
          r=[t1.name, t2.name], w=[dst_res])

    def l2norm_cols(S, raw_bank, raw_psum, cscol, cs_res, dst, dst_res, ss_bank):
        mr, rr_ = mrep[S], rrep[S]
        t1, t2 = t1s_[S], t2s_[S]
        E("dv", lambda e: e.tensor_scalar(out=t2[0:H, :], in0=mr[0:H, :],
                                          scalar1=cscol, scalar2=None, op0=ALU.mult),
          r=[mr.name, cs_res], w=[t2.name])
        E("dv", lambda e: e.tensor_tensor(out=t2[0:H, :], in0=raw_psum, in1=t2[0:H, :],
                                          op=ALU.subtract),
          r=[raw_bank, t2.name], w=[t2.name])
        E("dv", lambda e: e.tensor_tensor(out=t2[0:H, :], in0=t2[0:H, :],
                                          in1=rr_[0:H, :], op=ALU.mult),
          r=[t2.name, rr_.name], w=[t2.name])
        E("ac", lambda e: e.activation(out=sc1r[0:H, :], in_=t2[0:H, :], func=AF.Square),
          r=[t2.name], w=["sc1r"])
        E("pe", lambda e: e.matmul(pbt[ss_bank][:], ones_r[0:H, :], sc1r[0:H, :],
                                   start=True, stop=True),
          r=["ones_r", "sc1r"], w=[PB[ss_bank]])
        E("ac", lambda e: e.activation(out=t1[:], in_=pbt[ss_bank][:], func=AF.Ln),
          r=[PB[ss_bank]], w=[t1.name])
        E("ac", lambda e: e.activation(out=t1[:], in_=t1[:], func=AF.Exp, scale=0.5),
          r=[t1.name], w=[t1.name])
        E("dv", lambda e: e.tensor_scalar(out=t1[:], in0=t1[:], scalar1=1e-12,
                                          scalar2=None, op0=ALU.max),
          r=[t1.name], w=[t1.name])
        E("dv", lambda e: e.reciprocal(out=t1[:], in_=t1[:]), r=[t1.name], w=[t1.name])
        E("dv", lambda e, dst=dst: e.tensor_tensor(out=dst[:], in0=t2[0:H, :],
                                                   in1=t1[0:H, :], op=ALU.mult),
          r=[t1.name, t2.name], w=[dst_res])

    def emit_ffn_p1(buf, S):
        xbn = f"xb{buf}"
        mr, r16t = mrep[S], rrep[S]
        t1 = t1s_[S]
        emit_ln(buf, S, want_r16=True, proj_specs=[], sum_bank=0, sq_bank=1)
        for ic in range(8):
            E("dv", lambda e, ic=ic, buf=buf, mr=mr:
              e.tensor_tensor(out=t1[:], in0=xbuf[buf][:, ic * P:(ic + 1) * P],
                              in1=mr[:], op=ALU.subtract),
              r=[xbn, mr.name], w=[t1.name])
            E("dv", lambda e, r16t=r16t: e.tensor_tensor(out=t1[:], in0=t1[:], in1=r16t[:],
                                                         op=ALU.mult),
              r=[t1.name, r16t.name], w=[t1.name])
            E("dv", lambda e: e.tensor_scalar(out=t1[:], in0=t1[:], scalar1=M_RND,
                                              scalar2=M_RND, op0=ALU.add, op1=ALU.subtract),
              r=[t1.name], w=[t1.name])
            E("dv", lambda e, ic=ic, S=S: e.tensor_scalar(out=xqt[S][ic][:], in0=t1[:], scalar1=127.0,
                                                     scalar2=-128.0, op0=ALU.min, op1=ALU.max),
              r=[t1.name], w=[f"xq{S}{ic}"])

    def emit_ffn_p2(buf, S):
        xbn = f"xb{buf}"
        w1v = dw1q.rearrange("(ic p) o -> p ic o", p=128)
        for og in range(8):
            slh = [(2 * og) % 3, (2 * og + 1) % 3]
            for h in range(2):
                E("sp", lambda e, og=og, h=h, sl=slh[h]:
                  e.dma_start(out=slab[sl][:].rearrange("p (ic o) -> p ic o", ic=4),
                              in_=w1v[:, 4 * h:4 * h + 4, og * 512:(og + 1) * 512]),
                  w=[f"slab{slh[h]}"], r=[f"dw1q_q{og}"],
                  sem=s_sl[slh[h]], lat=1500)
            for o4 in range(4):
                oc = og * 4 + o4
                bank = 2 + oc % 2
                for ic in range(8):
                    sl = slh[ic // 4]
                    E("pe", lambda e, sl=sl, o4=o4, ic=ic, bank=bank:
                      e.matmul(pbt[bank][:],
                               slab[sl][:, (ic % 4) * 512 + o4 * 128:
                                           (ic % 4) * 512 + o4 * 128 + 128],
                               xqt[S][ic][:], start=(ic == 0), stop=(ic == 7)),
                      r=[f"slab{sl}", f"xq{S}{ic}"], w=[PB[bank]])
                E("ac", lambda e, oc=oc, bank=bank:
                  e.activation(out=p1s[:], in_=pbt[bank][:], func=AF.Relu,
                               scale=2.0 ** -6, bias=b1i[:, oc:oc + 1]),
                  r=[PB[bank], "b1i"], w=["p1s"])
                E("dv", lambda e: e.tensor_scalar(out=p2s[:], in0=p1s[:],
                                                  scalar1=M_RND, scalar2=M_RND,
                                                  op0=ALU.add, op1=ALU.subtract),
                  r=["p1s"], w=["p2s"], cost=400)
                E("dv", lambda e, oc=oc: e.tensor_scalar(out=hqt[oc][:], in0=p2s[:],
                                                         scalar1=127.0, scalar2=None,
                                                         op0=ALU.min),
                  r=["p2s"], w=[hq_res(oc)], cost=400)
        for og in range(8):
            slh = [(2 * og) % 3, (2 * og + 1) % 3]
            for h in range(2):
                E("sp", lambda e, og=og, h=h, sl=slh[h]:
                  e.dma_start(out=slab[sl][:].rearrange("p (o4 j) -> p o4 j", o4=2),
                              in_=dw2q[og * 512 + h * 256:og * 512 + h * 256 + 256,
                                       :].rearrange("(o4 p) j -> p o4 j", p=128)),
                  w=[f"slab{slh[h]}"], r=[f"dw2q_r{2 * og + h}"],
                  sem=s_sl[slh[h]], lat=1500)
            for o4 in range(4):
                oc = og * 4 + o4
                sl = slh[o4 // 2]
                for jc in range(8):
                    E("pe", lambda e, sl=sl, o4=o4, oc=oc, jc=jc:
                      e.matmul(pbt[jc][:],
                               slab[sl][:, (o4 % 2) * 1024 + jc * 128:
                                           (o4 % 2) * 1024 + jc * 128 + 128],
                               hqt[oc][:], start=(oc == 0), stop=(oc == 31)),
                      r=[f"slab{sl}", hq_res(oc)], w=[PB[jc]])
        for jc in range(8):
            E("dv", lambda e, jc=jc, buf=buf:
              e.tensor_scalar(out=xbuf[buf][:, jc * P:(jc + 1) * P],
                              in0=xbuf[buf][:, jc * P:(jc + 1) * P],
                              scalar1=b2qc[:, jc:jc + 1], scalar2=None, op0=ALU.add),
              r=[xbn, "b2qc"], w=[xbn])
            E("dv", lambda e, jc=jc, buf=buf:
              e.scalar_tensor_tensor(out=xbuf[buf][:, jc * P:(jc + 1) * P],
                                     in0=pbt[jc][:], scalar=2.0 ** -10,
                                     in1=xbuf[buf][:, jc * P:(jc + 1) * P],
                                     op0=ALU.mult, op1=ALU.add),
              r=[PB[jc], xbn], w=[xbn])

    def emit_attn_a_pre(buf):
        proj = [
            dict(bank=PB[2], out=pbt[2][0:H, 0:P], lres="wqa_r",
                 lhsT=lambda ic: wqa_r[:, ic * H:(ic + 1) * H]),
            dict(bank=PB[3], out=pbt[3][0:H, 0:P], lres="wka_r",
                 lhsT=lambda ic: wka_r[:, ic * H:(ic + 1) * H]),
            dict(bank=PB[4], out=pbt[4][0:H, 0:P], lres="wva_r",
                 lhsT=lambda ic: wva_r[:, ic * H:(ic + 1) * H]),
        ]
        emit_ln(buf, 0, want_r16=False, proj_specs=proj, sum_bank=0, sq_bank=1)
        fold_correct(0, t2s_[0][0:H, :], "t2s0", PB[2], pbt[2][0:H, 0:P], cs(0), "cs_sum", H)
        phi_inplace(0, qTa, "qTa")
        fold_correct(0, t2s_[0][0:H, :], "t2s0", PB[3], pbt[3][0:H, 0:P], cs(1), "cs_sum", H)
        phi_inplace(0, kTa, "kTa")
        fold_correct(0, vTa[:], "vTa", PB[4], pbt[4][0:H, 0:P], cs(2), "cs_sum", H)

    def emit_attn_a_sc(sc):
        sm = {"pe": 250, "dv": 260, "ac": 280}
        E = lambda eng, fn, r=(), w=(), **kw: TK.emit(eng, fn, r=r, w=w,
                                                      cost=kw.pop("cost", sm.get(eng)), **kw)
        s0 = sc * C
        qs = qTa[:, s0:s0 + C]
        ks = kTa[:, s0:s0 + C]
        vs = vTa[:, s0:s0 + C]
        E("pe", lambda e, ks=ks: e.transpose(pbr[2][:, 0:H], ks, ident_r[0:H, 0:H]),
          r=["kTa", "ident_r"], w=[PB[2]])
        E("dv", lambda e: e.tensor_copy(out=Ktm_a[:], in_=pbr[2][:, 0:H]),
          r=[PB[2]], w=["Ktm_a"])
        E("pe", lambda e, vs=vs: e.transpose(pbr[2][:, 64:128], vs, ident_r[0:H, 0:H]),
          r=["vTa", "ident_r"], w=[PB[2]])
        E("dv", lambda e: e.tensor_copy(out=Vtm_a[:], in_=pbr[2][:, 64:128]),
          r=[PB[2]], w=["Vtm_a"])
        E("pe", lambda e, vs=vs, qs=qs: e.matmul(pbt[2][:, 128:256], vs, qs,
                                                 start=True, stop=True),
          r=["vTa", "qTa"], w=[PB[2]])
        E("dv", lambda e: e.tensor_tensor(out=PTs[:], in0=pbt[2][:, 128:256],
                                          in1=cbv("maskIU"), op=ALU.mult),
          r=[PB[2], "cbt"], w=["PTs"])
        E("pe", lambda e, qs=qs, ks=ks: e.matmul(pbt[2][:, 256:384], qs, ks,
                                                 start=True, stop=True),
          r=["qTa", "kTa"], w=[PB[2]])
        E("dv", lambda e: e.tensor_tensor(out=QKs[:], in0=pbt[2][:, 256:384],
                                          in1=cbv("maskIL"), op=ALU.mult),
          r=[PB[2], "cbt"], w=["QKs"])
        E("dv", lambda e: e.reduce_sum(out=deni[:], in_=QKs[:], axis=AXX),
          r=["QKs"], w=["deni"])
        E("pe", lambda e, qs=qs: e.matmul(pbt[2][:, 384:392], qs, zcol[:],
                                          start=True, stop=True),
          r=["qTa", "zcol"], w=[PB[2]])
        E("dv", lambda e: e.tensor_tensor(out=dent[:], in0=deni[:],
                                          in1=pbt[2][:, 384:385], op=ALU.add),
          r=["deni", PB[2]], w=["dent"])
        E("dv", lambda e: e.tensor_scalar(out=dent[:], in0=dent[:], scalar1=1e-6,
                                          scalar2=None, op0=ALU.max),
          r=["dent"], w=["dent"])
        E("dv", lambda e: e.reciprocal(out=dent[:], in_=dent[:]), r=["dent"], w=["dent"])
        E("pe", lambda e: e.matmul(pbt[3][:, 0:H], PTs[:], Ktm_a[:], start=True, stop=False),
          r=["PTs", "Ktm_a"], w=[PB[3]])
        E("pe", lambda e, qs=qs: e.matmul(pbt[3][:, 0:H], qs, Za[:], start=False, stop=True),
          r=["qTa", "Za"], w=[PB[3]])
        E("dv", lambda e: e.tensor_scalar(out=Otm_a[:], in0=pbt[3][:, 0:H],
                                          scalar1=dent[:], scalar2=None, op0=ALU.mult),
          r=[PB[3], "dent"], w=["Otm_a"])
        E("pe", lambda e: e.transpose(pbr[3][0:H, 64:192], Otm_a[:], ident_r[:]),
          r=["Otm_a", "ident_r"], w=[PB[3]])
        E("dv", lambda e, s0=s0: e.tensor_copy(out=oTa[:, s0:s0 + C],
                                               in_=pbr[3][0:H, 64:192]),
          r=[PB[3]], w=["oTa"])
        E("pe", lambda e: e.matmul(pbt[3][0:H, 192:256], Vtm_a[:], Ktm_a[:],
                                   start=True, stop=True),
          r=["Vtm_a", "Ktm_a"], w=[PB[3]])
        E("dv", lambda e: e.tensor_tensor(out=Za[:], in0=Za[:], in1=pbt[3][0:H, 192:256],
                                          op=ALU.add),
          r=["Za", PB[3]], w=["Za"])
        E("dv", lambda e, ks=ks: e.reduce_sum(out=ztmp[:], in_=ks, axis=AXX),
          r=["kTa"], w=["ztmp"])
        E("dv", lambda e: e.tensor_scalar(out=zcol[:], in0=zcol[:], scalar1=ztmp[:],
                                          scalar2=None, op0=ALU.add),
          r=["zcol", "ztmp"], w=["zcol"])

    def emit_attn_a_out(buf):
        xbn = f"xb{buf}"
        for dc in range(8):
            bank = 2 + (dc % 2)
            E("pe", lambda e, dc=dc, bank=bank:
              e.matmul(pbt[bank][:], woa_r[:, dc * 128:(dc + 1) * 128], oTa[:],
                       start=True, stop=True),
              r=["woa_r", "oTa"], w=[PB[bank]])
            E("dv", lambda e, dc=dc, bank=bank, buf=buf:
              e.tensor_tensor(out=xbuf[buf][:, dc * P:(dc + 1) * P],
                              in0=xbuf[buf][:, dc * P:(dc + 1) * P], in1=pbt[bank][:],
                              op=ALU.add),
              r=[xbn, PB[bank]], w=[xbn])

    def emit_attn_b_pre(buf):
        proj = [
            dict(bank=PB[5], out=pbt[5][0:H, 0:P], lres="wkb_r",
                 lhsT=lambda ic: wkb_r[:, ic * H:(ic + 1) * H]),
            dict(bank=PB[6], out=pbt[6][0:H, 0:P], lres="wvb_r",
                 lhsT=lambda ic: wvb_r[:, ic * H:(ic + 1) * H]),
            dict(bank=PB[7], out=pbt[7][0:H, 0:P], lres="wqb_r",
                 lhsT=lambda ic: wqb_r[:, ic * H:(ic + 1) * H]),
        ]
        proj2 = [
            dict(bank=PB[5], out=pbt[5][0:H, 0:P], lres="irw1_r",
                 lhsT=lambda ic: irw1_r[:, ic * H:(ic + 1) * H], **{"pass": 1}),
            dict(bank=PB[6], out=pbt[6][0:32, 0:P], lres="pw_r",
                 lhsT=lambda ic: pw_r[:, ic * 32:(ic + 1) * 32], **{"pass": 1}),
        ]
        emit_ln(buf, 1, want_r16=False, proj_specs=proj, sum_bank=0, sq_bank=1)
        l2norm_cols(1, PB[5], pbt[5][0:H, 0:P], cs(3), "cs_sum", kbT, "kbT", ss_bank=4)
        fold_correct(1, vbT[:], "vbT", PB[6], pbt[6][0:H, 0:P], cs(4), "cs_sum", H)
        l2norm_cols(1, PB[7], pbt[7][0:H, 0:P], cs(5), "cs_sum", qbT, "qbT", ss_bank=4)
        emit_proj_pass(buf, 1, proj2)
        fold_correct(1, t2s_[1][0:H, :], "t2s0", PB[5], pbt[5][0:H, 0:P], cs(6), "cs_sum", H)
        E("ac", lambda e: e.activation(out=gcr[:], in_=t2s_[1][0:H, :], func=AF.Gelu,
                                       bias=irb1c[:]),
          r=["t2s0", "irb1c"], w=["gcr"])
        for sc in range(NSC):
            E("pe", lambda e, sc=sc: e.matmul(pbt[7][:, 8 * sc:8 * sc + 8],
                                              gcr[:, sc * C:(sc + 1) * C], irw2_r[:],
                                              start=True, stop=True),
              r=["gcr", "irw2_r"], w=[PB[7]])
        for sc in range(NSC):
            E("dv", lambda e, sc=sc: e.tensor_scalar(out=ucol[:, sc:sc + 1],
                                                     in0=pbt[7][:, 8 * sc:8 * sc + 1],
                                                     scalar1=irb2c[:], scalar2=0.0,
                                                     op0=ALU.add, op1=ALU.is_gt),
              r=[PB[7], "irb2c"], w=["ucol"])
        E("dv", lambda e: e.tensor_scalar(out=t2s_[1][0:32, :], in0=mrep[1][0:32, :],
                                          scalar1=cspw[:], scalar2=None, op0=ALU.mult),
          r=["mrep1", "cspw"], w=["t2s0"])
        E("dv", lambda e: e.tensor_tensor(out=t2s_[1][0:32, :], in0=pbt[6][0:32, 0:P],
                                          in1=t2s_[1][0:32, :], op=ALU.subtract),
          r=[PB[6], "t2s0"], w=["t2s0"])
        E("dv", lambda e: e.tensor_tensor(out=pwC[:], in0=t2s_[1][0:32, :],
                                          in1=rrep[1][0:32, :], op=ALU.mult),
          r=["t2s0", "rrep1"], w=["pwC"])
        E("pe", lambda e: e.matmul(pbt[5][:, 0:P], nmw1_r[:], vbT[:], start=True, stop=True),
          r=["nmw1_r", "vbT"], w=[PB[5]])
        E("ac", lambda e: e.activation(out=sc1r[:], in_=pbt[5][:, 0:P], func=AF.Gelu,
                                       bias=nmb1c[:]),
          r=[PB[5], "nmb1c"], w=["sc1r"])
        E("pe", lambda e: e.matmul(pbt[6][0:H, 0:P], nmw2_r[:], sc1r[:], start=True, stop=True),
          r=["nmw2_r", "sc1r"], w=[PB[6]])
        E("dv", lambda e: e.tensor_scalar(out=nuT[:], in0=pbt[6][0:H, 0:P],
                                          scalar1=nmb2c[:], scalar2=None, op0=ALU.add),
          r=[PB[6], "nmb2c"], w=["nuT"])
        for sc in range(NSC):
            E("pe", lambda e, sc=sc: e.transpose(pbr[7][:, 384 + 32 * sc:384 + 32 * (sc + 1)],
                                                 pwC[:, sc * C:(sc + 1) * C],
                                                 ident_r[0:32, 0:32]),
              r=["pwC", "ident_r"], w=[PB[7]])
        for sc in range(NSC):
            slc = pbt[7][:, 384 + 32 * sc:384 + 32 * sc + 3]
            E("dv", lambda e, slc=slc: e.reduce_max(out=msc[:], in_=slc, axis=AXX,
                                                    negate=True),
              r=[PB[7]], w=["msc"])
            E("ac", lambda e, sc=sc, slc=slc:
              e.activation(out=pwt[:, sc * 3:(sc + 1) * 3], in_=slc, func=AF.Exp,
                           bias=msc[:]),
              r=[PB[7], "msc"], w=["pwt"])
            E("dv", lambda e, sc=sc: e.reduce_sum(out=msc[:], in_=pwt[:, sc * 3:(sc + 1) * 3],
                                                  axis=AXX),
              r=["pwt"], w=["msc"])
            E("dv", lambda e: e.reciprocal(out=msc[:], in_=msc[:]), r=["msc"], w=["msc"])
            E("dv", lambda e, sc=sc: e.tensor_scalar(out=pwt[:, sc * 3:(sc + 1) * 3],
                                                     in0=pwt[:, sc * 3:(sc + 1) * 3],
                                                     scalar1=msc[:], scalar2=None,
                                                     op0=ALU.mult),
              r=["pwt", "msc"], w=["pwt"])

    def emit_attn_b_sc(sc):
        sm = {"pe": 250, "dv": 260, "ac": 280}
        E = lambda eng, fn, r=(), w=(), **kw: TK.emit(eng, fn, r=r, w=w,
                                                      cost=kw.pop("cost", sm.get(eng)), **kw)
        s0 = sc * C
        kss = kbT[:, s0:s0 + C]
        qss = qbT[:, s0:s0 + C]
        vss = vbT[:, s0:s0 + C]
        ucs = ucol[:, sc:sc + 1]
        E("pe", lambda e, kss=kss: e.transpose(pbr[4][:, 256:320], kss, ident_r[0:H, 0:H]),
          r=["kbT", "ident_r"], w=[PB[4]])
        E("dv", lambda e: e.tensor_copy(out=Ktm[:], in_=pbr[4][:, 256:320]),
          r=[PB[4]], w=["Ktm"])
        E("pe", lambda e, vss=vss: e.transpose(pbr[4][:, 320:384], vss, ident_r[0:H, 0:H]),
          r=["vbT", "ident_r"], w=[PB[4]])
        E("dv", lambda e: e.tensor_copy(out=Vtm[:], in_=pbr[4][:, 320:384]),
          r=[PB[4]], w=["Vtm"])
        E("pe", lambda e, ucs=ucs: e.transpose(pbt[4][0:1, 384:512], ucs, cbv("ident")),
          r=["ucol", "cbt"], w=[PB[4]])
        E("dv", lambda e: e.tensor_copy(out=urow[:], in_=pbt[4][0:1, 384:512]),
          r=[PB[4]], w=["urow"])
        E("pe", lambda e, kss=kss: e.matmul(pbt[4][:, 0:128], kss, kss, start=True, stop=True),
          r=["kbT"], w=[PB[4]])
        E("pe", lambda e, kss=kss, qss=qss: e.matmul(pbt[5][:, 0:128], kss, qss,
                                                     start=True, stop=True),
          r=["kbT", "qbT"], w=[PB[5]])
        E("pe", lambda e: e.matmul(pbt[4][:, 128:256], urow[:], urow[:], start=True, stop=True),
          r=["urow"], w=[PB[4]])
        E("dv", lambda e: e.tensor_tensor(out=Npw[0][0][:], in0=pbt[4][:, 0:128],
                                          in1=DSB[0][:], op=ALU.mult),
          r=[PB[4], "DSB0"], w=["Npw00"])
        E("dv", lambda e: e.tensor_tensor(out=NpwT[0][0][:], in0=pbt[4][:, 0:128],
                                          in1=DSBT[0][:], op=ALU.mult),
          r=[PB[4], "DSBT0"], w=["NpwT00"])
        E("dv", lambda e: e.tensor_tensor(out=nscr[:], in0=pbt[4][:, 0:128],
                                          in1=DSB[1][:], op=ALU.mult),
          r=[PB[4], "DSB1"], w=["nscr"])
        E("dv", lambda e: e.tensor_tensor(out=Npw[1][0][:], in0=nscr[:],
                                          in1=pbt[4][:, 128:256], op=ALU.mult),
          r=["nscr", PB[4]], w=["Npw10"])
        E("dv", lambda e: e.tensor_tensor(out=nscr[:], in0=pbt[4][:, 0:128],
                                          in1=DSBT[1][:], op=ALU.mult),
          r=[PB[4], "DSBT1"], w=["nscr"])
        E("dv", lambda e: e.tensor_tensor(out=NpwT[1][0][:], in0=nscr[:],
                                          in1=pbt[4][:, 128:256], op=ALU.mult),
          r=["nscr", PB[4]], w=["NpwT10"])
        E("dv", lambda e: e.tensor_tensor(out=WxT[0][:], in0=pbt[5][:, 0:128],
                                          in1=DS_[0][:], op=ALU.mult),
          r=[PB[5], "DS0"], w=["WxT0"])
        E("dv", lambda e: e.tensor_tensor(out=WiT[0][:], in0=pbt[5][:, 0:128],
                                          in1=DI_[0][:], op=ALU.mult),
          r=[PB[5], "DI0"], w=["WiT0"])
        E("dv", lambda e: e.tensor_tensor(out=WxT[1][:], in0=pbt[5][:, 0:128],
                                          in1=DS_[1][:], op=ALU.mult),
          r=[PB[5], "DS1"], w=["WxT1"])
        E("dv", lambda e, ucs=ucs: e.tensor_scalar(out=WxT[1][:], in0=WxT[1][:],
                                                   scalar1=ucs, scalar2=None, op0=ALU.mult),
          r=["WxT1", "ucol"], w=["WxT1"])
        E("dv", lambda e: e.tensor_tensor(out=WiT[1][:], in0=pbt[5][:, 0:128],
                                          in1=DI_[1][:], op=ALU.mult),
          r=[PB[5], "DI1"], w=["WiT1"])
        E("dv", lambda e, ucs=ucs: e.tensor_scalar(out=WiT[1][:], in0=WiT[1][:],
                                                   scalar1=ucs, scalar2=None, op0=ALU.mult),
          r=["WiT1", "ucol"], w=["WiT1"])
        for m in range(2):
            off = 128 + m * H
            E("pe", lambda e, kss=kss, m=m, off=off:
              e.matmul(pbt[5][:, off:off + H], kss, F0m[m][:], start=True, stop=True),
              r=["kbT", f"F0m{m}"], w=[PB[5]])
            if m == 0:
                E("dv", lambda e: e.tensor_scalar(out=rdf[:], in0=pbt[5][:, 128:128 + H],
                                                  scalar1=g1n[0][:], scalar2=None,
                                                  op0=ALU.mult),
                  r=[PB[5], "g1n0"], w=["rdf"])
                E("dv", lambda e: e.scalar_tensor_tensor(out=Rm[0][:], in0=Vtm[:],
                                                         scalar=bcol[0][:], in1=rdf[:],
                                                         op0=ALU.mult, op1=ALU.add),
                  r=["Vtm", "bcol0", "rdf"], w=["Rm0"])
            else:
                E("dv", lambda e: e.tensor_scalar(out=rds[:], in0=pbt[5][:, 192:192 + H],
                                                  scalar1=g1n[1][:], scalar2=ucs,
                                                  op0=ALU.mult, op1=ALU.mult),
                  r=[PB[5], "g1n1", "ucol"], w=["rds"])
                E("dv", lambda e: e.tensor_scalar(out=predt[:], in0=Vtm[:],
                                                  scalar1=bcol[1][:], scalar2=ucs,
                                                  op0=ALU.mult, op1=ALU.mult),
                  r=["Vtm", "bcol1", "ucol"], w=["predt"])
                E("dv", lambda e: e.tensor_tensor(out=Rm[1][:], in0=predt[:], in1=rds[:],
                                                  op=ALU.add),
                  r=["predt", "rds"], w=["Rm1"])
        for m in range(2):
            bk = 6 + m
            E("pe", lambda e, m=m, bk=bk: e.matmul(pbt[bk][:, 256:320], Npw[m][0][:],
                                                   Rm[m][:], start=True, stop=True),
              r=[f"Npw{m}0", f"Rm{m}"], w=[PB[bk]])
            E("dv", lambda e, m=m, bk=bk: e.tensor_tensor(out=Ut[m][0][:], in0=Rm[m][:],
                                                          in1=pbt[bk][:, 256:320],
                                                          op=ALU.subtract),
              r=[f"Rm{m}", PB[bk]], w=[f"Ut{m}0"])
        for j in range(6):
            jp = j % 2
            jn = (j + 1) % 2
            for m in range(2):
                bk = 6 + m
                E("pe", lambda e, m=m, jp=jp, bk=bk:
                  e.matmul(pbt[bk][:, 0:128], NpwT[m][jp][:], Npw[m][jp][:],
                           start=True, stop=True),
                  r=[f"NpwT{m}{jp}", f"Npw{m}{jp}"], w=[PB[bk]])
                E("dv", lambda e, m=m, jn=jn, bk=bk:
                  e.tensor_copy(out=Npw[m][jn][:], in_=pbr[bk][:, 0:128]),
                  r=[PB[bk]], w=[f"Npw{m}{jn}"])
                if j < 5:
                    E("pe", lambda e, m=m, jp=jp, bk=bk:
                      e.matmul(pbt[bk][:, 128:256], Npw[m][jp][:], NpwT[m][jp][:],
                               start=True, stop=True),
                      r=[f"Npw{m}{jp}", f"NpwT{m}{jp}"], w=[PB[bk]])
                    E("dv", lambda e, m=m, jn=jn, bk=bk:
                      e.tensor_copy(out=NpwT[m][jn][:], in_=pbr[bk][:, 128:256]),
                      r=[PB[bk]], w=[f"NpwT{m}{jn}"])
                E("pe", lambda e, m=m, jp=jp, jn=jn, bk=bk:
                  e.matmul(pbt[bk][:, 256:320], Npw[m][jn][:], Ut[m][jp][:],
                           start=True, stop=True),
                  r=[f"Npw{m}{jn}", f"Ut{m}{jp}"], w=[PB[bk]])
                E("dv", lambda e, m=m, jp=jp, jn=jn, bk=bk:
                  e.tensor_tensor(out=Ut[m][jn][:], in0=Ut[m][jp][:],
                                  in1=pbt[bk][:, 256:320], op=ALU.add),
                  r=[f"Ut{m}{jp}", PB[bk]], w=[f"Ut{m}{jn}"])
        for m in range(2):
            bk = 6 + m
            E("pe", lambda e, qss=qss, m=m: e.matmul(pbt[5][:, 256 + m * H:256 + (m + 1) * H],
                                                     qss, F0m[m][:], start=True, stop=True),
              r=["qbT", f"F0m{m}"], w=[PB[5]])
            E("pe", lambda e, m=m, bk=bk: e.matmul(pbt[bk][:, 320:384], WxT[m][:],
                                                   Ut[m][0][:], start=True, stop=True),
              r=[f"WxT{m}", f"Ut{m}0"], w=[PB[bk]])
            E("pe", lambda e, m=m, bk=bk: e.matmul(pbt[bk][:, 384:448], WiT[m][:],
                                                   Ut[m][0][:], start=True, stop=True),
              r=[f"WiT{m}", f"Ut{m}0"], w=[PB[bk]])
        E("dv", lambda e: e.tensor_scalar(out=rdf[:], in0=pbt[5][:, 256:256 + H],
                                          scalar1=g1c[0][:], scalar2=None, op0=ALU.mult),
          r=[PB[5], "g1c0"], w=["rdf"])
        E("dv", lambda e: e.tensor_tensor(out=rdf[:], in0=rdf[:], in1=pbt[6][:, 320:384],
                                          op=ALU.add),
          r=["rdf", PB[6]], w=["rdf"])
        E("dv", lambda e: e.tensor_scalar(out=rds[:], in0=pbt[5][:, 320:320 + H],
                                          scalar1=g1c[1][:], scalar2=None, op0=ALU.mult),
          r=[PB[5], "g1c1"], w=["rds"])
        E("dv", lambda e: e.tensor_tensor(out=rds[:], in0=rds[:], in1=pbt[7][:, 320:384],
                                          op=ALU.add),
          r=["rds", PB[7]], w=["rds"])
        E("dv", lambda e: e.tensor_tensor(out=predt[:], in0=rdf[:], in1=rds[:], op=ALU.add),
          r=["rdf", "rds"], w=["predt"])
        E("dv", lambda e: e.scalar_tensor_tensor(out=predt[:], in0=predt[:], scalar=-0.5,
                                                 in1=Vtm[:], op0=ALU.mult, op1=ALU.add),
          r=["predt", "Vtm"], w=["predt"])
        E("ac", lambda e: e.activation(out=dsq[:], in_=predt[:], func=AF.Square),
          r=["predt"], w=["dsq"])
        E("dv", lambda e: e.reduce_sum(out=errc[:], in_=dsq[:], axis=AXX),
          r=["dsq"], w=["errc"])
        E("dv", lambda e: e.tensor_scalar(out=uncol[:], in0=errc[:], scalar1=SUR_THR,
                                          scalar2=None, op0=ALU.is_gt),
          r=["errc"], w=["uncol"])
        E("dv", lambda e: e.tensor_scalar(out=rdf[:], in0=pbt[5][:, 256:256 + H],
                                          scalar1=g2c[0][:], scalar2=None, op0=ALU.mult),
          r=[PB[5], "g2c0"], w=["rdf"])
        E("dv", lambda e: e.tensor_tensor(out=rdf[:], in0=rdf[:], in1=pbt[6][:, 384:448],
                                          op=ALU.add),
          r=["rdf", PB[6]], w=["rdf"])
        E("dv", lambda e: e.tensor_scalar(out=rds[:], in0=pbt[5][:, 320:320 + H],
                                          scalar1=g2c[1][:], scalar2=None, op0=ALU.mult),
          r=[PB[5], "g2c1"], w=["rds"])
        E("dv", lambda e: e.tensor_tensor(out=rds[:], in0=rds[:], in1=pbt[7][:, 384:448],
                                          op=ALU.add),
          r=["rds", PB[7]], w=["rds"])
        E("pe", lambda e: e.transpose(pbt[7][0:1, 128:256], uncol[:], cbv("ident")),
          r=["uncol", "cbt"], w=[PB[7]])
        E("dv", lambda e: e.tensor_copy(out=unrow[:], in_=pbt[7][0:1, 128:256]),
          r=[PB[7]], w=["unrow"])
        E("pe", lambda e: e.matmul(pbt[6][0:H, 0:128], ones_r[0:1, 0:H], unrow[:],
                                   start=True, stop=True),
          r=["ones_r", "unrow"], w=[PB[6]])
        E("dv", lambda e: e.tensor_scalar(out=nsA[:], in0=pbt[6][0:H, 0:128],
                                          scalar1=-0.1, scalar2=1.0, op0=ALU.mult,
                                          op1=ALU.add),
          r=[PB[6]], w=["nsA"])
        E("dv", lambda e: e.tensor_scalar(out=nsB[:], in0=pbt[6][0:H, 0:128],
                                          scalar1=0.1, scalar2=None, op0=ALU.mult),
          r=[PB[6]], w=["nsB"])
        E("dv", lambda e, s0=s0: e.tensor_tensor(out=nsB[:], in0=nsB[:],
                                                 in1=nuT[:, s0:s0 + C], op=ALU.mult),
          r=["nsB", "nuT"], w=["nsB"])
        if sc == 0:
            E("dv", lambda e, s0=s0:
              e.tensor_tensor_scan(out=Nfm[:, s0:s0 + C], data0=nsA[:], data1=nsB[:],
                                   initial=nsinit[:], op0=ALU.mult, op1=ALU.add),
              r=["nsA", "nsB", "nsinit"], w=["Nfm"], cost=500)
        else:
            E("dv", lambda e, s0=s0:
              e.tensor_tensor_scan(out=Nfm[:, s0:s0 + C], data0=nsA[:], data1=nsB[:],
                                   initial=Nfm[:, s0 - 1:s0], op0=ALU.mult, op1=ALU.add),
              r=["nsA", "nsB", "Nfm"], w=["Nfm"], cost=500)
        E("pe", lambda e, s0=s0: e.transpose(pbt[6][:, 448:512], Nfm[:, s0:s0 + C],
                                             cbv("ident", rows=H)[0:H, 0:H]),
          r=["Nfm", "cbt"], w=[PB[6]])
        E("dv", lambda e: e.tensor_copy(out=Ntm[:], in_=pbt[6][:, 448:512]),
          r=[PB[6]], w=["Ntm"])
        E("dv", lambda e, sc=sc: e.tensor_scalar(out=oatt[:], in0=rdf[:],
                                                 scalar1=pwt[:, sc * 3:sc * 3 + 1],
                                                 scalar2=None, op0=ALU.mult),
          r=["rdf", "pwt"], w=["oatt"])
        E("dv", lambda e, sc=sc: e.scalar_tensor_tensor(out=oatt[:], in0=rds[:],
                                                        scalar=pwt[:, sc * 3 + 1:sc * 3 + 2],
                                                        in1=oatt[:], op0=ALU.mult,
                                                        op1=ALU.add),
          r=["rds", "pwt", "oatt"], w=["oatt"])
        E("dv", lambda e, sc=sc: e.scalar_tensor_tensor(out=oatt[:], in0=Ntm[:],
                                                        scalar=pwt[:, sc * 3 + 2:sc * 3 + 3],
                                                        in1=oatt[:], op0=ALU.mult,
                                                        op1=ALU.add),
          r=["Ntm", "pwt", "oatt"], w=["oatt"])
        E("ac", lambda e: e.activation(out=dsq[:], in_=oatt[:], func=AF.Square),
          r=["oatt"], w=["dsq"])
        E("dv", lambda e: e.reduce_sum(out=msc[:], in_=dsq[:], axis=AXX),
          r=["dsq"], w=["msc"])
        E("dv", lambda e: e.tensor_scalar(out=msc[:], in0=msc[:], scalar1=1.0 / H,
                                          scalar2=1e-6, op0=ALU.mult, op1=ALU.add),
          r=["msc"], w=["msc"])
        E("ac", lambda e: e.activation(out=msc[:], in_=msc[:], func=AF.Ln),
          r=["msc"], w=["msc"])
        E("ac", lambda e: e.activation(out=msc[:], in_=msc[:], func=AF.Exp, scale=-0.5),
          r=["msc"], w=["msc"])
        E("dv", lambda e: e.tensor_scalar(out=onrm[:], in0=oatt[:], scalar1=msc[:],
                                          scalar2=None, op0=ALU.mult),
          r=["oatt", "msc"], w=["onrm"])
        E("pe", lambda e: e.transpose(pbr[7][0:H, 0:128], onrm[:], ident_r[:]),
          r=["onrm", "ident_r"], w=[PB[7]])
        E("dv", lambda e, s0=s0: e.tensor_copy(out=obT[:, s0:s0 + C],
                                               in_=pbr[7][0:H, 0:128]),
          r=[PB[7]], w=["obT"])
        E("dv", lambda e, ucs=ucs: e.tensor_scalar(out=Ktg[:], in0=Ktm[:], scalar1=ucs,
                                                   scalar2=None, op0=ALU.mult),
          r=["Ktm", "ucol"], w=["Ktg"])
        for m in range(2):
            E("dv", lambda e, m=m: e.tensor_scalar(out=Ut[m][0][:], in0=Ut[m][0][:],
                                                   scalar1=g3c[m][:], scalar2=None,
                                                   op0=ALU.mult),
              r=[f"Ut{m}0", f"g3c{m}"], w=[f"Ut{m}0"])
            lhs = Ktm if m == 0 else Ktg
            lres = "Ktm" if m == 0 else "Ktg"
            E("pe", lambda e, m=m, lhs=lhs: e.matmul(pbt[5][0:H, 384 + m * H:384 + (m + 1) * H],
                                                     lhs[:], Ut[m][0][:],
                                                     start=True, stop=True),
              r=[lres, f"Ut{m}0"], w=[PB[5]])
            E("dv", lambda e, m=m: e.scalar_tensor_tensor(
                out=F0m[m][:], in0=F0m[m][:], scalar=aCc[m][0:H, :],
                in1=pbt[5][0:H, 384 + m * H:384 + (m + 1) * H],
                op0=ALU.mult, op1=ALU.add),
              r=[f"F0m{m}", PB[5], f"aCc{m}"], w=[f"F0m{m}"])

    def emit_attn_b_out(buf):
        xbn = f"xb{buf}"
        E("dv", lambda e: e.tensor_copy(out=nsinit[:], in_=Nfm[:, P - 1:P]),
          r=["Nfm"], w=["nsinit"])
        for dc in range(8):
            bank = 4 + (dc % 2)
            E("pe", lambda e, dc=dc, bank=bank:
              e.matmul(pbt[bank][:], wob_r[:, dc * 128:(dc + 1) * 128], obT[:],
                       start=True, stop=True),
              r=["wob_r", "obT"], w=[PB[bank]])
            E("dv", lambda e, dc=dc, bank=bank, buf=buf:
              e.tensor_tensor(out=xbuf[buf][:, dc * P:(dc + 1) * P],
                              in0=xbuf[buf][:, dc * P:(dc + 1) * P], in1=pbt[bank][:],
                              op=ALU.add),
              r=[xbn, PB[bank]], w=[xbn])

    if PIPELINE and SUBLAYERS >= 4:
        for i in range(NCHUNK + 1):
            bufA = i % 3
            bufB = (i - 1) % 3
            hasA = i < NCHUNK
            hasB = i > 0
            if i == 0:
                emit_attn_a_pre(bufA)
            if hasB:
                emit_attn_b_pre(bufB)
            for sc in range(NSC):
                if hasA:
                    emit_attn_a_sc(sc)
                if hasB:
                    emit_attn_b_sc(sc)
            if hasA:
                emit_attn_a_out(bufA)
            if hasB:
                emit_attn_b_out(bufB)
            if hasA:
                emit_ffn_p1(bufA, 0)
                emit_ffn_p2(bufA, 0)
            if hasB:
                emit_ffn_p1(bufB, 1)
            if hasA and i + 1 < NCHUNK:
                emit_attn_a_pre((i + 1) % 3)
            if hasB:
                emit_ffn_p2(bufB, 1)
                E("sp", lambda e, c=i - 1, bufB=bufB: e.dma_start(
                    out=odst[:, :, c * P:(c + 1) * P],
                    in_=xbuf[bufB][:].rearrange("p (ic t) -> p ic t", ic=8)),
                  r=[f"xb{bufB}"], w=[f"od{i - 1}"], sem=s_out)
            if i + 2 < NCHUNK:
                x_load(i + 2, (i + 2) % 3)
    else:
        for c_ in range(NCHUNK):
            buf = c_ % 3
            if SUBLAYERS >= 1:
                emit_attn_a_pre(buf)
                for sc in range(NSC):
                    emit_attn_a_sc(sc)
                emit_attn_a_out(buf)
            if SUBLAYERS >= 2:
                emit_ffn_p1(buf, 0)
                emit_ffn_p2(buf, 0)
            if SUBLAYERS >= 3:
                emit_attn_b_pre(buf)
                for sc in range(NSC):
                    emit_attn_b_sc(sc)
                emit_attn_b_out(buf)
            if SUBLAYERS >= 4:
                emit_ffn_p1(buf, 1)
                emit_ffn_p2(buf, 1)
            E("sp", lambda e, c_=c_, buf=buf: e.dma_start(
                out=odst[:, :, c_ * P:(c_ + 1) * P],
                in_=xbuf[buf][:].rearrange("p (ic t) -> p ic t", ic=8)),
              r=[f"xb{buf}"], w=[f"od{c_}"], sem=s_out)
            if c_ + 2 < NCHUNK:
                x_load(c_ + 2, (c_ + 2) % 3)

    TK.schedule()
    TK.finalize(ENG_SEMS)
    TK.steps.append(dict(eng="sp", fn=lambda e: e.nop(), r=(), w=(), sem=s_out,
                         inc=False, deps=set(), val=None,
                         waits=[(s_out, 16 * NCHUNK)], xwaits=[]))

    with nc.Block() as block:
        @block.sync
        def _(sync):
            TK.run_engine("sp", sync)

        @block.vector
        def _(vector):
            TK.run_engine("dv", vector)

        @block.scalar
        def _(scalar):
            TK.run_engine("ac", scalar)

        @block.tensor
        def _(tensor):
            TK.run_engine("pe", tensor)

        @block.gpsimd
        def _(gpsimd):
            TK.run_engine("gp", gpsimd)

    for cm in reversed(sem_cms):
        cm.__exit__(None, None, None)
    return nc


def kernel(**inputs):
    g = lambda k: np.asarray(inputs[k])
    x = g("L0_attn_a_x").astype(np.float32, copy=False)
    cbd = build_const_blob()
    pwT = np.zeros((D, 32), np.float32)
    pwT[:, 0:3] = g("L0_attn_b_pw").T

    common = dict(
        cbd=cbd,
        w1T=np.ascontiguousarray(g("L0_ffn_w1").T),
        w2T=np.ascontiguousarray(g("L0_ffn_w2").T),
        wqa=np.ascontiguousarray(g("L0_attn_a_W_q")),
        wka=np.ascontiguousarray(g("L0_attn_a_W_k")),
        wva=np.ascontiguousarray(g("L0_attn_a_W_v")),
        woa=np.ascontiguousarray(g("L0_attn_a_W_o")),
        wkbT=np.ascontiguousarray(g("L0_attn_b_Wk").T),
        wvbT=np.ascontiguousarray(g("L0_attn_b_Wv").T),
        wqbT=np.ascontiguousarray(g("L0_attn_b_Wq").T),
        irw1T=np.ascontiguousarray(g("L0_attn_b_ir_w1").T),
        pwT=pwT,
        nmw1T=np.ascontiguousarray(g("L0_attn_b_nm_w1").T),
        nmw2T=np.ascontiguousarray(g("L0_attn_b_nm_w2").T),
        irw2T=np.ascontiguousarray(g("L0_attn_b_ir_w2").T),
        wobT=np.ascontiguousarray(g("L0_attn_b_Wo").T),
        d_b1c=np.ascontiguousarray(g("L0_ffn_b1").reshape(32, 128).T),
        d_b2c=np.ascontiguousarray(g("L0_ffn_b2").reshape(8, 128).T),
        d_nmb1c=np.ascontiguousarray(g("L0_attn_b_nm_b1").reshape(128, 1)),
        d_nmb2c=np.ascontiguousarray(g("L0_attn_b_nm_b2").reshape(64, 1)),
        d_irb1c=np.ascontiguousarray(g("L0_attn_b_ir_b1").reshape(64, 1)),
        d_irb2=np.ascontiguousarray(g("L0_attn_b_ir_b2").reshape(1, 1)),
        d_normw=np.ascontiguousarray(g("L0_attn_b_norm_w").reshape(64, 1)),
        d_alphaf=np.ascontiguousarray(np.reshape(g("L0_attn_b_fast_alpha"), (1, 1))),
        d_alphas=np.ascontiguousarray(np.reshape(g("L0_attn_b_slow_alpha"), (1, 1))),
    )
    in_maps = []
    for b in range(8):
        m = dict(common)
        m["xT"] = np.ascontiguousarray(x[b].T)
        in_maps.append(m)

    nc = build_nc()
    res = run_bass_kernel_spmd(nc, in_maps, core_ids=list(range(8)))
    out = np.stack([np.ascontiguousarray(res.results[b]["outT"].T) for b in range(8)], axis=0)
    return out.astype(np.float32, copy=False)


# version 13
# speedup vs baseline: 1.0402x; 1.0402x over previous
import math

import numpy as np
import ml_dtypes
import concourse.bass as bass
import concourse.mybir as mybir
from concourse.bass_utils import run_bass_kernel_spmd

F32 = mybir.dt.float32
F32R = mybir.dt.float32r
BF16 = mybir.dt.bfloat16
AF = mybir.ActivationFunctionType
ALU = mybir.AluOpType
AXX = mybir.AxisListType.X

D = 1024
T = 4096
H = 64
HID = 4096
P = 512
NCHUNK = T // P
C = 128
NSC = P // C
M_RND = float(2.0 ** 23 + 2.0 ** 22)
LN_EPS = 1e-5
SUR_THR = float(math.log(0.7 / 0.3) * (1.0 + 1e-6))

SUBLAYERS = 4
PIPELINE = True

_OFF = {}
_top = 0
def _cols(name, w):
    global _top
    _OFF[name] = (_top, w)
    _top += w

for _n in ("maskIL", "maskIU", "maskSU", "maskSL", "expoSU", "expoSL", "expoIU",
           "ones128", "ident"):
    _cols(_n, 128)
for _n in ("iota0", "iota1", "iotaC", "onescol", "epscol", "negMcol", "Mcol", "zerocol", "ln16col"):
    _cols(_n, 1)
_cols("zero64", 64)
CBW = _top


def build_const_blob():
    cb = np.zeros((128, CBW), np.float32)
    r = np.arange(128)
    lo = (r[:, None] >= r[None, :]).astype(np.float32)
    iu = (r[:, None] <= r[None, :]).astype(np.float32)
    su = (r[:, None] < r[None, :]).astype(np.float32)
    sl = (r[:, None] > r[None, :]).astype(np.float32)

    def put(name, arr):
        o, w = _OFF[name]
        cb[:, o:o + w] = arr

    put("maskIL", lo)
    put("maskIU", iu)
    put("maskSU", su)
    put("maskSL", sl)
    put("expoSU", (r[None, :] - 1 - r[:, None]) * su)
    put("expoSL", (r[:, None] - 1 - r[None, :]) * sl)
    put("expoIU", (r[None, :] - r[:, None]) * iu)
    put("ones128", np.ones((128, 128), np.float32))
    put("ident", np.eye(128, dtype=np.float32))
    put("iota0", r[:, None].astype(np.float32))
    put("iota1", (r[:, None] + 1).astype(np.float32))
    put("iotaC", (C - 1 - r[:, None]).astype(np.float32))
    put("onescol", np.ones((128, 1), np.float32))
    put("epscol", np.full((128, 1), LN_EPS, np.float32))
    put("negMcol", np.full((128, 1), -M_RND, np.float32))
    put("Mcol", np.full((128, 1), M_RND, np.float32))
    put("zerocol", np.zeros((128, 1), np.float32))
    put("ln16col", np.full((128, 1), math.log(16.0), np.float32))
    put("zero64", np.zeros((128, 64), np.float32))
    return cb


class SemT:
    def __init__(self, handle, name, dma=False):
        self.h = handle
        self.name = name
        self.n = 0
        self.dma = dma

    def bump(self):
        self.n += 16 if self.dma else 1
        return self.n


class Tracker:
    def __init__(self):
        self.steps = []

    def emit(self, eng, fn, r=(), w=(), sem=None, xwaits=(), cost=None, lat=None):
        self.steps.append(dict(eng=eng, fn=fn, r=tuple(r), w=tuple(w), sem=sem,
                               inc=False, xwaits=list(xwaits), cost=cost, lat=lat))
        return len(self.steps) - 1

    def schedule(self, costs=None):
        import heapq
        n = len(self.steps)
        cost = []
        base = {"pe": 260, "dv": 390, "ac": 640, "sp": 620, "gp": 900}
        lats = []
        for st in self.steps:
            c = st.get("cost")
            cost.append(c if c is not None else base.get(st["eng"], 400))
            lt = st.get("lat")
            if lt is None:
                lt = 800 if st["eng"] == "sp" else 0
            lats.append(lt)
        last_w = {}
        readers = {}
        succ = [[] for _ in range(n)]
        pred_cnt = [0] * n
        preds = [set() for _ in range(n)]
        for i, st in enumerate(self.steps):
            deps = set()
            for rr in st["r"]:
                j = last_w.get(rr)
                if j is not None:
                    deps.add(j)
            for rr in st["w"]:
                j = last_w.get(rr)
                if j is not None:
                    deps.add(j)
                for k in readers.get(rr, ()):
                    if k != i:
                        deps.add(k)
            for rr in st["r"]:
                readers.setdefault(rr, []).append(i)
            for rr in st["w"]:
                last_w[rr] = i
                readers[rr] = []
            preds[i] = deps
            pred_cnt[i] = len(deps)
            for j in deps:
                succ[j].append(i)
        cp = [0.0] * n
        for i in range(n - 1, -1, -1):
            m = 0.0
            for s_ in succ[i]:
                if cp[s_] > m:
                    m = cp[s_]
            cp[i] = cost[i] + lats[i] + m
        fin = [0.0] * n
        avail = [0.0] * n
        eng_clock = {}
        ready = []
        for i in range(n):
            if pred_cnt[i] == 0:
                ready.append(i)
        order = []
        remaining = list(pred_cnt)
        start = [0.0] * n
        import os
        policy = os.environ.get("TK_SCHED_POLICY", "est")
        while ready:
            if policy == "est":
                bi, bk = -1, None
                for idx, i in enumerate(ready):
                    est = eng_clock.get(self.steps[i]["eng"], 0.0)
                    if avail[i] > est:
                        est = avail[i]
                    key = (est, -cp[i])
                    if bk is None or key < bk:
                        bk, bi = key, idx
                i = ready.pop(bi)
            else:
                bi, bk = -1, None
                for idx, i in enumerate(ready):
                    key = -cp[i]
                    if bk is None or key < bk:
                        bk, bi = key, idx
                i = ready.pop(bi)
            st = self.steps[i]
            t0 = eng_clock.get(st["eng"], 0.0)
            if avail[i] > t0:
                t0 = avail[i]
            start[i] = t0
            fin[i] = t0 + cost[i] + lats[i]
            eng_clock[st["eng"]] = t0 + cost[i]
            order.append(i)
            for s_ in succ[i]:
                remaining[s_] -= 1
                if avail[s_] < fin[i]:
                    avail[s_] = fin[i]
                if remaining[s_] == 0:
                    ready.append(s_)
        assert len(order) == n
        if os.environ.get("TK_SCHED_DEBUG"):
            eng_busy = {}
            for i in range(n):
                e = self.steps[i]["eng"]
                eng_busy[e] = eng_busy.get(e, 0) + cost[i]
            print("SCHED makespan(model): %.1f us" % (max(fin) / 1e3))
            print("SCHED cp bound: %.1f us" % (max(cp) / 1e3))
            print("SCHED eng busy(model):", {k: round(v / 1e3) for k, v in eng_busy.items()})
            prev_end = {}
            attr = {}
            gaps = []
            for i in order:
                e = self.steps[i]["eng"]
                gap = start[i] - prev_end.get(e, 0.0)
                if gap > 1.0:
                    jb, fb = None, -1.0
                    for j in preds[i]:
                        if fin[j] > fb:
                            fb, jb = fin[j], j
                    be = self.steps[jb]["eng"] if jb is not None else "none"
                    res = ""
                    if jb is not None:
                        wr = set(self.steps[jb]["w"])
                        res = ",".join(sorted(wr & set(self.steps[i]["r"]))) or \
                              ",".join(sorted(wr)[:2])
                    key = (e, be, res)
                    attr[key] = attr.get(key, 0.0) + gap
                    gaps.append((gap, e, be, res, start[i]))
                prev_end[e] = fin[i]
            pe_gaps = sorted([g for g in gaps if g[1] == "pe"], reverse=True)[:20]
            for gp, e_, be, res, t_ in pe_gaps:
                print("SCHED pegap %.1f us at t=%.1f blocked-by %s [%s]" %
                      (gp / 1e3, t_ / 1e3, be, res))
            top = sorted(attr.items(), key=lambda kv: -kv[1])[:16]
            for (e, be, res), tot in top:
                print("SCHED idle %s blocked-by %s [%s]: %.1f us" % (e, be, res, tot / 1e3))
            tot_idle = {}
            nstep = {}
            endt = {}
            for i in order:
                e = self.steps[i]["eng"]
                nstep[e] = nstep.get(e, 0) + 1
                endt[e] = fin[i]
            prev_end = {}
            for i in order:
                e = self.steps[i]["eng"]
                tot_idle[e] = tot_idle.get(e, 0.0) + (start[i] - prev_end.get(e, 0.0))
                prev_end[e] = fin[i]
            print("SCHED per-eng: " + "  ".join(
                "%s n=%d idle=%.0fus end=%.0fus" % (e, nstep[e], tot_idle[e] / 1e3,
                                                    endt[e] / 1e3) for e in sorted(nstep)))
        self.steps = [self.steps[i] for i in order]

    def finalize(self, eng_sems):
        last_w = {}
        readers = {}
        for i, st in enumerate(self.steps):
            ei = st["eng"]
            deps = set()

            def need(j):
                ej = self.steps[j]["eng"]
                return ej != ei or ej == "sp"

            for rr in st["r"]:
                j = last_w.get(rr)
                if j is not None:
                    deps.add(j)
            for rr in st["w"]:
                j = last_w.get(rr)
                if j is not None and need(j):
                    deps.add(j)
                for k in readers.get(rr, ()):
                    if need(k):
                        deps.add(k)
            for rr in st["r"]:
                readers.setdefault(rr, []).append(i)
            for rr in st["w"]:
                last_w[rr] = i
                readers[rr] = []
            st["deps"] = deps
        for st in self.steps:
            for j in st["deps"]:
                self.steps[j]["inc"] = True
            if st["eng"] == "sp":
                st["inc"] = True
        for st in self.steps:
            if st["sem"] is None:
                st["sem"] = eng_sems[st["eng"]]
        for st in self.steps:
            st["val"] = st["sem"].bump() if st["inc"] else None
        for st in self.steps:
            wmap = {}
            for j in st["deps"]:
                sj = self.steps[j]
                k = id(sj["sem"])
                if k not in wmap or wmap[k][1] < sj["val"]:
                    wmap[k] = (sj["sem"], sj["val"])
            st["waits"] = list(wmap.values()) + st["xwaits"]

    def run_engine(self, name, eng):
        waited = {}
        issued = {}
        for st in self.steps:
            if st["eng"] != name:
                continue
            if name == "sp" and st["sem"] is not None and st["sem"].dma:
                s = st["sem"]
                n = issued.get(id(s), 0)
                if n > 0 and waited.get(id(s), -1) < 16 * n:
                    eng.wait_ge(s.h, 16 * n)
                    waited[id(s)] = 16 * n
                issued[id(s)] = n + 1
            for (s, v) in st["waits"]:
                if waited.get(id(s), -1) < v:
                    eng.wait_ge(s.h, v)
                    waited[id(s)] = v
            inst = st["fn"](eng)
            if st["inc"]:
                inst.then_inc(st["sem"].h, 16 if st["sem"].dma else 1)


class PackView:
    def __init__(self, ap, name):
        self._ap = ap
        self.name = name

    def __getitem__(self, idx):
        return self._ap[idx]


def build_nc():
    nc = bass.Bass("TRN2", target_bir_lowering=False)

    dx = nc.dram_tensor("xT", [D, T], F32, kind="ExternalInput").ap()
    dout = nc.dram_tensor("outT", [D, T], F32, kind="ExternalOutput").ap()
    dcb = nc.dram_tensor("cbd", [128, CBW], F32, kind="ExternalInput").ap()
    dw1T = nc.dram_tensor("w1T", [D, HID], F32, kind="ExternalInput").ap()
    dw2T = nc.dram_tensor("w2T", [HID, D], F32, kind="ExternalInput").ap()
    dsmall = {}
    for nme, shp in [("wqa", [D, H]), ("wka", [D, H]), ("wva", [D, H]),
                     ("woa", [H, D]), ("wkbT", [D, H]), ("wvbT", [D, H]),
                     ("wqbT", [D, H]), ("irw1T", [D, H]), ("pwT", [D, 32]),
                     ("nmw1T", [H, 2 * H]), ("nmw2T", [2 * H, H]),
                     ("irw2T", [H, 1]), ("wobT", [H, D])]:
        dsmall[nme] = nc.dram_tensor(nme, shp, F32, kind="ExternalInput").ap()
    dbias = {}
    for nme, shp in [("b1c", [128, 32]), ("b2c", [128, 8]), ("nmb1c", [128, 1]),
                     ("nmb2c", [H, 1]), ("irb1c", [H, 1]), ("irb2", [1, 1]),
                     ("normw", [H, 1]), ("alphaf", [1, 1]), ("alphas", [1, 1])]:
        dbias[nme] = nc.dram_tensor("d_" + nme, shp, F32, kind="ExternalInput").ap()
    dw1q = nc.dram_tensor("w1q_scr", [D, HID], BF16, kind="Internal").ap()
    dw2q = nc.dram_tensor("w2q_scr", [HID, D], BF16, kind="Internal").ap()

    def sb(name, shape, dtype):
        return nc.alloc_sbuf_tensor(name, shape, dtype)

    cbt = sb("cbt", [128, CBW], F32)
    xbuf = [sb(f"xb{i}", [128, 8 * P], F32) for i in range(3)]
    xr = [[sb(f"xr{s}{i}", [128, P], F32R) for i in range(2)] for s in range(2)]
    xsq_sh = [sb(f"xsq{i}", [128, P], F32R) for i in range(2)]
    xsq = [xsq_sh, xsq_sh]
    mrep = [sb(f"mrep{s}", [128, P], F32) for s in range(2)]
    rrep = [sb(f"rrep{s}", [128, P], F32) for s in range(2)]
    t1sh = sb("t1s0", [128, P], F32)
    t2sh = sb("t2s0", [128, P], F32)
    t1s_ = [t1sh, t1sh]
    t2s_ = [t2sh, t2sh]
    p1s = sb("p1s", [128, P], F32)
    p2s = sb("p2s", [128, P], F32)
    xqt = [sb(f"xq{i}", [128, P], BF16) for i in range(8)]
    hqt = [sb(f"hq{i}", [128, P], BF16) for i in range(32)]
    def hq_res(oc):
        return f"hq{oc}"
    slab = [sb(f"slab{i}", [128, 2048], BF16) for i in range(4)]
    qst = [sb(f"qst{i}", [128, 512], BF16) for i in range(2)]
    stg = [xbuf[2]]

    wqa_r = sb("wqa_r", [128, 8 * H], F32R)
    wka_r = sb("wka_r", [128, 8 * H], F32R)
    wva_r = sb("wva_r", [128, 8 * H], F32R)
    woa_r = sb("woa_r", [H, D], BF16)
    wkb_r = sb("wkb_r", [128, 8 * H], F32R)
    wvb_r = sb("wvb_r", [128, 8 * H], F32R)
    wqb_r = sb("wqb_r", [128, 8 * H], F32R)
    irw1_r = sb("irw1_r", [128, 8 * H], F32R)
    pw_r = sb("pw_r", [128, 8 * 32], F32R)
    nmw1_r = sb("nmw1_r", [H, 2 * H], F32R)
    nmw2_r = sb("nmw2_r", [2 * H, H], F32R)
    irw2_r = sb("irw2_r", [H, 8], F32R)
    irw2f = sb("irw2f", [H, 1], F32)
    wob_r = sb("wob_r", [H, D], BF16)
    ones_r = sb("ones_r", [128, 128], F32R)
    ident_r = sb("ident_r", [128, 128], F32R)

    cs_sum = sb("cs_sum", [H, 8], F32)
    cspw = sb("cspw", [32, 1], F32)

    b1i = sb("b1i", [128, 32], F32)
    b2qc = sb("b2qc", [128, 8], F32)
    nmb1c = sb("nmb1c", [128, 1], F32)
    nmb2c = sb("nmb2c", [H, 1], F32)
    irb1c = sb("irb1c", [H, 1], F32)
    irb2c = sb("irb2c", [128, 1], F32)
    nw_c = sb("nw_c", [H, 1], F32)

    lnA = [sb(f"lnA{m}", [128, 1], F32) for m in range(2)]
    acol = [sb(f"acol{m}", [128, 1], F32) for m in range(2)]
    bcol = [sb(f"bcol{m}", [128, 1], F32) for m in range(2)]
    g1c = [sb(f"g1c{m}", [128, 1], F32) for m in range(2)]
    g1n = [sb(f"g1n{m}", [128, 1], F32) for m in range(2)]
    g2c = [sb(f"g2c{m}", [128, 1], F32) for m in range(2)]
    g3c = [sb(f"g3c{m}", [128, 1], F32) for m in range(2)]
    aCc = [sb(f"aCc{m}", [128, 1], F32) for m in range(2)]
    DS_ = [sb(f"DS{m}", [128, 128], F32) for m in range(2)]
    DSB = [sb(f"DSB{m}", [128, 128], F32) for m in range(2)]
    DSBT = [sb(f"DSBT{m}", [128, 128], F32) for m in range(2)]
    DI_ = [sb(f"DI{m}", [128, 128], F32) for m in range(2)]

    F0m = [sb(f"F0m{m}", [H, H], F32R) for m in range(2)]
    Za = sb("Za", [H, H], F32R)
    zcol = sb("zcol", [H, 8], F32R)

    qTa = sb("qTa", [H, P], F32R)
    kTa = sb("kTa", [H, P], F32R)
    vTa = sb("vTa", [H, P], F32R)
    oTa = sb("oTa", [H, P], BF16)
    PTs = sb("PTs", [128, 128], F32R)
    QKs = sb("QKs", [128, 128], F32)
    deni = sb("deni", [128, 1], F32)
    dent = sb("dent", [128, 1], F32)
    ztmp = sb("ztmp", [H, 1], F32)
    Ktm_a = sb("Ktm_a", [128, H], F32R)
    Vtm_a = sb("Vtm_a", [128, H], F32R)
    Otm_a = sb("Otm_a", [128, H], F32R)

    kbT = sb("kbT", [H, P], F32R)
    qbT = sb("qbT", [H, P], F32R)
    vbT = sb("vbT", [H, P], F32R)
    nuT = sb("nuT", [H, P], F32)
    gcr = sb("gcr", [H, P], F32R)
    sc1r = sb("sc1r", [128, P], F32R)
    obT = sb("obT", [H, P], BF16)
    pwC = sb("pwC", [32, P], F32R)
    pwt = sb("pwt", [128, NSC * 3], F32)
    ucol = sb("ucol", [128, NSC], F32)
    urow = sb("urow", [1, 128], F32R)
    unrow = sb("unrow", [1, 128], F32R)
    uncol = sb("uncol", [128, 1], F32)
    Nfm = sb("Nfm", [H, P], F32)
    nsinit = sb("nsinit", [H, 1], F32)
    nsA = sb("nsA", [H, 128], F32)
    nsB = sb("nsB", [H, 128], F32)
    Ntm = sb("Ntm", [128, H], F32)
    nscr = sb("nscr", [128, 128], F32)

    Npw = [[sb(f"Npw{m}{i}", [128, 128], F32R) for i in range(2)] for m in range(2)]
    NpwT = [[sb(f"NpwT{m}{i}", [128, 128], F32R) for i in range(2)] for m in range(2)]
    Ut = [[sb(f"Ut{m}{i}", [128, H], F32R) for i in range(2)] for m in range(2)]
    Rm = [sb(f"Rm{m}", [128, H], F32R) for m in range(2)]
    WxT = [sb(f"WxT{m}", [128, 128], F32R) for m in range(2)]
    WiT = [sb(f"WiT{m}", [128, 128], F32R) for m in range(2)]
    Ktm = sb("Ktm", [128, H], F32R)
    Ktg = sb("Ktg", [128, H], F32R)
    Vtm = sb("Vtm", [128, H], F32)
    rdf = sb("rdf", [128, H], F32)
    rds = sb("rds", [128, H], F32)
    predt = sb("predt", [128, H], F32)
    dsq = sb("dsq", [128, H], F32)
    errc = sb("errc", [128, 1], F32)
    oatt = sb("oatt", [128, H], F32)
    onrm = sb("onrm", [128, H], F32R)
    msc = sb("msc", [128, 1], F32)

    pbt = [nc.alloc_psum_tensor(f"pb{i}", [128, 512], F32) for i in range(8)]
    pbr = [p.bitcast(F32R) for p in pbt]
    PB = [f"pb{i}" for i in range(8)]

    def cbv(name, rows=128):
        o, w = _OFF[name]
        return cbt[0:rows, o:o + w]

    TK = Tracker()
    sem_cms = []

    def mksem(name, dma=False):
        cm = nc.semaphore(name)
        h = cm.__enter__()
        sem_cms.append(cm)
        return SemT(h, name, dma)

    s_x = mksem("s_x", dma=True)
    s_sl = [mksem("s_sl0", dma=True), mksem("s_sl1", dma=True),
            mksem("s_sl2", dma=True), mksem("s_sl3", dma=True)]
    s_sw = mksem("s_sw", dma=True)
    s_qs = [mksem("s_qs0", dma=True), mksem("s_qs1", dma=True),
            mksem("s_qs2", dma=True)]
    s_qt = [mksem("s_qt0", dma=True), mksem("s_qt1", dma=True)]
    s_out = mksem("s_out", dma=True)
    s_pe = mksem("s_pe")
    s_dv = mksem("s_dv")
    s_ac = mksem("s_ac")
    s_gp = mksem("s_gp")
    ENG_SEMS = {"pe": s_pe, "dv": s_dv, "ac": s_ac, "gp": s_gp}

    E = TK.emit

    E("sp", lambda e: e.dma_start(out=cbt[:], in_=dcb), w=["cbt"], sem=s_sw)
    xsrc = dx.rearrange("(ic p) t -> p ic t", p=128)
    odst = dout.rearrange("(ic p) t -> p ic t", p=128)

    def x_load(c, buf):
        E("sp", lambda e, c=c, buf=buf: e.dma_start(
            out=xbuf[buf][:].rearrange("p (ic t) -> p ic t", ic=8),
            in_=xsrc[:, :, c * P:(c + 1) * P]), lat=6000,
          w=[f"xb{buf}"], sem=s_x)

    x_load(0, 0)

    def wview(d):
        k = d.shape[1]
        return (d.rearrange("(ic p) k -> p ic k", p=128), k)

    small_jobs = [
        (dsmall["wqa"], wqa_r, 128, 8 * H, None, True),
        (dsmall["wka"], wka_r, 128, 8 * H, None, True),
        (dsmall["wva"], wva_r, 128, 8 * H, None, True),
        (dsmall["wkbT"], wkb_r, 128, 8 * H, None, True),
        (dsmall["wvbT"], wvb_r, 128, 8 * H, None, True),
        (dsmall["wqbT"], wqb_r, 128, 8 * H, None, True),
        (dsmall["irw1T"], irw1_r, 128, 8 * H, None, True),
        (dsmall["pwT"], pw_r, 128, 8 * 32, None, True),
        (dsmall["woa"], woa_r, H, D, None, False),
        (dsmall["nmw1T"], nmw1_r, H, 2 * H, None, False),
        (dsmall["nmw2T"], nmw2_r, 2 * H, H, None, False),
        (dsmall["irw2T"], irw2f, H, 1, None, False),
        (dsmall["wobT"], wob_r, H, D, "normw", False),
    ]
    E("sp", lambda e: e.dma_start(out=nw_c[:], in_=dbias["normw"]), w=["nw_c"], sem=s_sw)
    for i, (srcd, dst, rr, cc, fold, chunked) in enumerate(small_jobs):
        if chunked:
            sv, kk = wview(srcd)
            E("sp", lambda e, sv=sv, kk=kk:
              e.dma_start(out=stg[0][:].rearrange("p (ic k) -> p ic k", k=kk)[:, 0:8, :],
                          in_=sv),
              r=[], w=["xb2"], sem=s_sw)
        else:
            E("sp", lambda e, srcd=srcd, rr=rr, cc=cc:
              e.dma_start(out=stg[0][0:rr, 0:cc], in_=srcd),
              r=[], w=["xb2"], sem=s_sw)
        if fold == "normw":
            E("dv", lambda e, dst=dst, rr=rr, cc=cc:
              e.tensor_scalar(out=dst[0:rr, 0:cc], in0=stg[0][0:rr, 0:cc],
                              scalar1=nw_c[:], scalar2=None, op0=ALU.mult),
              r=["xb2", "nw_c"], w=[dst.name])
        else:
            E("dv", lambda e, dst=dst, rr=rr, cc=cc:
              e.tensor_copy(out=dst[0:rr, 0:cc], in_=stg[0][0:rr, 0:cc]),
              r=["xb2"], w=[dst.name])
    E("dv", lambda e: e.tensor_scalar(out=irw2_r[:], in0=cbv("ones128", rows=H)[:, 0:8],
                                      scalar1=irw2f[:], scalar2=None, op0=ALU.mult),
      r=["cbt", "irw2f"], w=["irw2_r"])
    E("dv", lambda e: e.tensor_copy(out=ones_r[:], in_=cbv("ones128")),
      r=["cbt"], w=["ones_r"])
    E("dv", lambda e: e.tensor_copy(out=ident_r[:], in_=cbv("ident")),
      r=["cbt"], w=["ident_r"])

    for nme, dst in [("b1c", b1i), ("b2c", b2qc), ("nmb1c", nmb1c), ("nmb2c", nmb2c),
                     ("irb1c", irb1c)]:
        E("sp", lambda e, nme=nme, dst=dst: e.dma_start(out=dst[:], in_=dbias[nme]),
          w=[dst.name], sem=s_sw)
    E("sp", lambda e: e.dma_start(out=irb2c[:], in_=dbias["irb2"].broadcast_to((128, 1)),
                                  allow_slow_non_contiguous=True),
      w=["irb2c"], sem=s_sw)
    E("dv", lambda e: e.tensor_scalar(out=b1i[:], in0=b1i[:], scalar1=64.0, scalar2=M_RND,
                                      op0=ALU.mult, op1=ALU.add), r=["b1i"], w=["b1i"])
    E("dv", lambda e: e.tensor_scalar(out=b1i[:], in0=b1i[:], scalar1=M_RND, scalar2=127.0,
                                      op0=ALU.subtract, op1=ALU.min), r=["b1i"], w=["b1i"])
    E("dv", lambda e: e.tensor_scalar(out=b1i[:], in0=b1i[:], scalar1=-128.0, scalar2=0.25,
                                      op0=ALU.max, op1=ALU.mult), r=["b1i"], w=["b1i"])
    E("dv", lambda e: e.tensor_scalar(out=b2qc[:], in0=b2qc[:], scalar1=64.0, scalar2=M_RND,
                                      op0=ALU.mult, op1=ALU.add), r=["b2qc"], w=["b2qc"])
    E("dv", lambda e: e.tensor_scalar(out=b2qc[:], in0=b2qc[:], scalar1=M_RND, scalar2=127.0,
                                      op0=ALU.subtract, op1=ALU.min), r=["b2qc"], w=["b2qc"])
    E("dv", lambda e: e.tensor_scalar(out=b2qc[:], in0=b2qc[:], scalar1=-128.0,
                                      scalar2=1.0 / 64.0, op0=ALU.max, op1=ALU.mult),
      r=["b2qc"], w=["b2qc"])

    cs_specs = [(wqa_r, H, 0), (wka_r, H, 1), (wva_r, H, 2), (wkb_r, H, 3),
                (wvb_r, H, 4), (wqb_r, H, 5), (irw1_r, H, 6)]
    for wt, od, j in cs_specs:
        for ic in range(8):
            E("pe", lambda e, wt=wt, od=od, j=j, ic=ic:
              e.matmul(pbt[6][0:od, 8 * j:8 * j + 8], wt[:, ic * od:(ic + 1) * od],
                       ones_r[:, 0:8], start=(ic == 0), stop=(ic == 7)),
              r=[wt.name, "ones_r"], w=[PB[6]])
    for ic in range(8):
        E("pe", lambda e, ic=ic:
          e.matmul(pbt[6][0:32, 56:64], pw_r[:, ic * 32:(ic + 1) * 32],
                   ones_r[:, 0:8], start=(ic == 0), stop=(ic == 7)),
          r=["pw_r", "ones_r"], w=[PB[6]])
    for j in range(7):
        E("dv", lambda e, j=j: e.tensor_copy(out=cs_sum[0:H, j:j + 1],
                                             in_=pbt[6][0:H, 8 * j:8 * j + 1]),
          r=[PB[6]], w=["cs_sum"])
    E("dv", lambda e: e.tensor_copy(out=cspw[:], in_=pbt[6][0:32, 56:57]),
      r=[PB[6]], w=["cspw"])

    def cs(j):
        return cs_sum[0:H, j:j + 1]

    for m in range(2):
        dal = dbias["alphaf"] if m == 0 else dbias["alphas"]
        E("sp", lambda e, dal=dal, m=m:
          e.dma_start(out=acol[m][:], in_=dal.broadcast_to((128, 1)),
                      allow_slow_non_contiguous=True),
          w=[acol[m].name], sem=s_sw)
        E("ac", lambda e, m=m: e.activation(out=lnA[m][:], in_=acol[m][:], func=AF.Ln),
          r=[acol[m].name], w=[lnA[m].name])
        E("dv", lambda e, m=m: e.tensor_scalar(out=bcol[m][:], in0=acol[m][:], scalar1=-1.0,
                                               scalar2=1.0, op0=ALU.mult, op1=ALU.add),
          r=[acol[m].name], w=[bcol[m].name])

        def gcol(dst, iota_name, m=m):
            E("dv", lambda e, dst=dst, iota_name=iota_name, m=m:
              e.tensor_scalar(out=dst[:], in0=cbv(iota_name), scalar1=lnA[m][:],
                              scalar2=None, op0=ALU.mult),
              r=["cbt", lnA[m].name], w=[dst.name])
            E("ac", lambda e, dst=dst: e.activation(out=dst[:], in_=dst[:], func=AF.Exp),
              r=[dst.name], w=[dst.name])

        gcol(g1c[m], "iota0")
        gcol(g2c[m], "iota1")
        gcol(g3c[m], "iotaC")
        E("dv", lambda e, m=m: e.tensor_scalar(out=aCc[m][:], in0=lnA[m][:], scalar1=float(C),
                                               scalar2=None, op0=ALU.mult),
          r=[lnA[m].name], w=[aCc[m].name])
        E("ac", lambda e, m=m: e.activation(out=aCc[m][:], in_=aCc[m][:], func=AF.Exp),
          r=[aCc[m].name], w=[aCc[m].name])
        E("dv", lambda e, m=m: e.tensor_tensor(out=g1n[m][:], in0=g1c[m][:], in1=bcol[m][:],
                                               op=ALU.mult),
          r=[g1c[m].name, bcol[m].name], w=[g1n[m].name])
        E("dv", lambda e, m=m: e.tensor_scalar(out=g1n[m][:], in0=g1n[m][:], scalar1=-1.0,
                                               scalar2=None, op0=ALU.mult),
          r=[g1n[m].name], w=[g1n[m].name])

        def dmat(dst, expo_name, mask_name, scale_beta, m=m):
            E("dv", lambda e, dst=dst, expo_name=expo_name, m=m:
              e.tensor_scalar(out=dst[:], in0=cbv(expo_name), scalar1=lnA[m][:],
                              scalar2=None, op0=ALU.mult),
              r=["cbt", lnA[m].name], w=[dst.name])
            E("ac", lambda e, dst=dst: e.activation(out=dst[:], in_=dst[:], func=AF.Exp),
              r=[dst.name], w=[dst.name])
            E("dv", lambda e, dst=dst, mask_name=mask_name:
              e.tensor_tensor(out=dst[:], in0=dst[:], in1=cbv(mask_name), op=ALU.mult),
              r=[dst.name, "cbt"], w=[dst.name])
            if scale_beta:
                E("dv", lambda e, dst=dst, m=m:
                  e.tensor_scalar(out=dst[:], in0=dst[:], scalar1=bcol[m][:], scalar2=None,
                                  op0=ALU.mult),
                  r=[dst.name, bcol[m].name], w=[dst.name])

        dmat(DS_[m], "expoSU", "maskSU", False)
        dmat(DSB[m], "expoSU", "maskSU", True)
        dmat(DSBT[m], "expoSL", "maskSL", True)
        dmat(DI_[m], "expoIU", "maskIU", False)

    for tns, rr_x, cc_x in [(F0m[0], H, H), (F0m[1], H, H), (Za, H, H), (zcol, H, 8)]:
        E("dv", lambda e, tns=tns, rr_x=rr_x, cc_x=cc_x:
          e.tensor_scalar(out=tns[0:rr_x, 0:cc_x], in0=cbv("zero64", rows=rr_x)[:, 0:cc_x],
                          scalar1=0.0, scalar2=None, op0=ALU.mult),
          r=["cbt"], w=[tns.name])
    E("dv", lambda e: e.tensor_scalar(out=nsinit[:], in0=cbv("zerocol", rows=H),
                                      scalar1=0.0, scalar2=None, op0=ALU.mult),
      r=["cbt"], w=["nsinit"])

    def quant_spill(dsrc, ddst, rows, cols, s0, store_res, sem_of):
        for s in range(rows // 128):
            b = 1 + (s + s0) % 2
            xbn = f"xb{b}"
            E("sp", lambda e, s=s, dsrc=dsrc, cols=cols, b=b:
              e.dma_start(out=xbuf[b][:, 0:cols], in_=dsrc[s * 128:(s + 1) * 128, 0:cols]),
              w=[xbn], sem=s_qs[(s + s0) % 3], lat=6000)
            E("ac", lambda e, cols=cols, b=b:
              e.activation(out=xbuf[b][:, 0:cols], in_=xbuf[b][:, 0:cols], func=AF.Relu,
                           scale=64.0, bias=cbv("Mcol")),
              r=[xbn, "cbt"], w=[xbn], cost=3600)
            E("dv", lambda e, cols=cols, b=b:
              e.tensor_scalar(out=xbuf[b][:, 0:cols], in0=xbuf[b][:, 0:cols], scalar1=M_RND,
                              scalar2=127.0, op0=ALU.subtract, op1=ALU.min),
              r=[xbn], w=[xbn], cost=2200)
            nq = cols // 512
            for q in range(nq):
                j = sem_of(s, q)
                E("dv", lambda e, q=q, j=j, b=b:
                  e.tensor_scalar(out=qst[j][:], in0=xbuf[b][:, q * 512:(q + 1) * 512],
                                  scalar1=-128.0, scalar2=None, op0=ALU.max),
                  r=[xbn], w=[f"qst{j}"], cost=330)
                E("sp", lambda e, s=s, ddst=ddst, q=q, j=j:
                  e.dma_start(out=ddst[s * 128:(s + 1) * 128, q * 512:(q + 1) * 512],
                              in_=qst[j][:]),
                  r=[f"qst{j}"], w=[store_res(s, q)], sem=s_qt[j], lat=800)

    quant_spill(dw1T, dw1q, D, HID, 0,
                lambda s, q: f"dw1q_q{q}", lambda s, q: q % 2)
    quant_spill(dw2T, dw2q, HID, D, 8,
                lambda s, q: f"dw2q_r{s // 2}", lambda s, q: (s // 2) % 2)
    x_load(1, 1)

    def emit_ln(buf, S, want_r16, proj_specs, sum_bank, sq_bank, passes=(0,)):
        xbn = f"xb{buf}"
        mr, rr_ = mrep[S], rrep[S]
        for pss in passes:
            specs = [sp_ for sp_ in proj_specs if sp_.get("pass", 0) == pss]
            for ic in range(8):
                pp = ic % 2
                xrt, xst = xr[S][pp], xsq[S][pp]
                E("ac", lambda e, ic=ic, xrt=xrt, buf=buf:
                  e.activation(out=xrt[:], in_=xbuf[buf][:, ic * P:(ic + 1) * P],
                               func=AF.Copy),
                  r=[xbn], w=[xrt.name])
                if pss == 0:
                    E("ac", lambda e, ic=ic, xst=xst, buf=buf:
                      e.activation(out=xst[:], in_=xbuf[buf][:, ic * P:(ic + 1) * P],
                                   func=AF.Square),
                      r=[xbn], w=[xst.name])
                    E("pe", lambda e, ic=ic, xrt=xrt:
                      e.matmul(pbt[sum_bank][:], ones_r[:], xrt[:],
                               start=(ic == 0), stop=(ic == 7)),
                      r=["ones_r", xrt.name], w=[PB[sum_bank]])
                    E("pe", lambda e, ic=ic, xst=xst:
                      e.matmul(pbt[sq_bank][:], ones_r[:], xst[:],
                               start=(ic == 0), stop=(ic == 7)),
                      r=["ones_r", xst.name], w=[PB[sq_bank]])
                for spec in specs:
                    E("pe", lambda e, ic=ic, xrt=xrt, spec=spec:
                      e.matmul(spec["out"], spec["lhsT"](ic), xrt[:],
                               start=(ic == 0), stop=(ic == 7)),
                      r=[spec["lres"], xrt.name], w=[spec["bank"]])
            if pss == 0:
                E("dv", lambda e, mr=mr: e.tensor_scalar(out=mr[:], in0=pbt[sum_bank][:],
                                                         scalar1=1.0 / D, scalar2=None,
                                                         op0=ALU.mult),
                  r=[PB[sum_bank]], w=[mr.name])
                E("dv", lambda e, rr_=rr_: e.tensor_scalar(out=rr_[:], in0=pbt[sq_bank][:],
                                                           scalar1=1.0 / D, scalar2=None,
                                                           op0=ALU.mult),
                  r=[PB[sq_bank]], w=[rr_.name])
                E("dv", lambda e, mr=mr: e.tensor_tensor(out=p2s[:], in0=mr[:], in1=mr[:],
                                                         op=ALU.mult),
                  r=[mr.name], w=["p2s"])
                E("dv", lambda e, rr_=rr_: e.tensor_tensor(out=rr_[:], in0=rr_[:], in1=p2s[:],
                                                           op=ALU.subtract),
                  r=[rr_.name, "p2s"], w=[rr_.name])
                E("ac", lambda e, rr_=rr_: e.activation(out=rr_[:], in_=rr_[:], func=AF.Ln,
                                                        bias=cbv("epscol")),
                  r=[rr_.name, "cbt"], w=[rr_.name])
                if want_r16:
                    E("ac", lambda e, rr_=rr_: e.activation(out=rr_[:], in_=rr_[:],
                                                            func=AF.Exp, scale=-0.5,
                                                            bias=cbv("ln16col")),
                      r=[rr_.name, "cbt"], w=[rr_.name])
                else:
                    E("ac", lambda e, rr_=rr_: e.activation(out=rr_[:], in_=rr_[:],
                                                            func=AF.Exp, scale=-0.5),
                      r=[rr_.name], w=[rr_.name])

    def emit_proj_pass(buf, S, specs):
        xbn = f"xb{buf}"
        for ic in range(8):
            pp = ic % 2
            xrt = xr[S][pp]
            E("dv", lambda e, ic=ic, xrt=xrt, buf=buf:
              e.tensor_copy(out=xrt[:], in_=xbuf[buf][:, ic * P:(ic + 1) * P]),
              r=[xbn], w=[xrt.name])
            for spec in specs:
                E("pe", lambda e, ic=ic, xrt=xrt, spec=spec:
                  e.matmul(spec["out"], spec["lhsT"](ic), xrt[:],
                           start=(ic == 0), stop=(ic == 7)),
                  r=[spec["lres"], xrt.name], w=[spec["bank"]])

    def fold_correct(S, dst_ap, dst_res, bank, psum_ap, cscol, cs_res, rows):
        mr, rr_ = mrep[S], rrep[S]
        t2 = t2s_[S]
        E("dv", lambda e: e.tensor_scalar(out=t2[0:rows, :], in0=mr[0:rows, :],
                                          scalar1=cscol, scalar2=None, op0=ALU.mult),
          r=[mr.name, cs_res], w=[t2.name])
        E("dv", lambda e: e.tensor_tensor(out=t2[0:rows, :], in0=psum_ap,
                                          in1=t2[0:rows, :], op=ALU.subtract),
          r=[bank, t2.name], w=[t2.name])
        E("dv", lambda e: e.tensor_tensor(out=dst_ap, in0=t2[0:rows, :],
                                          in1=rr_[0:rows, :], op=ALU.mult),
          r=[t2.name, rr_.name], w=[dst_res])

    def phi_inplace(S, dst, dst_res):
        t1, t2 = t1s_[S], t2s_[S]
        E("dv", lambda e: e.tensor_scalar(out=t1[0:H, :], in0=t2[0:H, :], scalar1=0.0,
                                          scalar2=None, op0=ALU.min),
          r=[t2.name], w=[t1.name])
        E("ac", lambda e: e.activation(out=t1[0:H, :], in_=t1[0:H, :], func=AF.Exp),
          r=[t1.name], w=[t1.name])
        E("dv", lambda e, dst=dst: e.scalar_tensor_tensor(out=dst[:], in0=t2[0:H, :],
                                                          scalar=0.0, in1=t1[0:H, :],
                                                          op0=ALU.max, op1=ALU.add),
          r=[t1.name, t2.name], w=[dst_res])

    def l2norm_cols(S, raw_bank, raw_psum, cscol, cs_res, dst, dst_res, ss_bank):
        mr, rr_ = mrep[S], rrep[S]
        t1, t2 = t1s_[S], t2s_[S]
        E("dv", lambda e: e.tensor_scalar(out=t2[0:H, :], in0=mr[0:H, :],
                                          scalar1=cscol, scalar2=None, op0=ALU.mult),
          r=[mr.name, cs_res], w=[t2.name])
        E("dv", lambda e: e.tensor_tensor(out=t2[0:H, :], in0=raw_psum, in1=t2[0:H, :],
                                          op=ALU.subtract),
          r=[raw_bank, t2.name], w=[t2.name])
        E("dv", lambda e: e.tensor_tensor(out=t2[0:H, :], in0=t2[0:H, :],
                                          in1=rr_[0:H, :], op=ALU.mult),
          r=[t2.name, rr_.name], w=[t2.name])
        E("ac", lambda e: e.activation(out=sc1r[0:H, :], in_=t2[0:H, :], func=AF.Square),
          r=[t2.name], w=["sc1r"])
        E("pe", lambda e: e.matmul(pbt[ss_bank][:], ones_r[0:H, :], sc1r[0:H, :],
                                   start=True, stop=True),
          r=["ones_r", "sc1r"], w=[PB[ss_bank]])
        E("ac", lambda e: e.activation(out=t1[:], in_=pbt[ss_bank][:], func=AF.Ln),
          r=[PB[ss_bank]], w=[t1.name])
        E("ac", lambda e: e.activation(out=t1[:], in_=t1[:], func=AF.Exp, scale=0.5),
          r=[t1.name], w=[t1.name])
        E("dv", lambda e: e.tensor_scalar(out=t1[:], in0=t1[:], scalar1=1e-12,
                                          scalar2=None, op0=ALU.max),
          r=[t1.name], w=[t1.name])
        E("dv", lambda e: e.reciprocal(out=t1[:], in_=t1[:]), r=[t1.name], w=[t1.name])
        E("dv", lambda e, dst=dst: e.tensor_tensor(out=dst[:], in0=t2[0:H, :],
                                                   in1=t1[0:H, :], op=ALU.mult),
          r=[t1.name, t2.name], w=[dst_res])

    def emit_ffn_p1(buf, S):
        xbn = f"xb{buf}"
        mr, r16t = mrep[S], rrep[S]
        t1 = t1s_[S]
        emit_ln(buf, S, want_r16=True, proj_specs=[], sum_bank=0, sq_bank=1)
        for ic in range(8):
            E("dv", lambda e, ic=ic, buf=buf, mr=mr:
              e.tensor_tensor(out=t1[:], in0=xbuf[buf][:, ic * P:(ic + 1) * P],
                              in1=mr[:], op=ALU.subtract),
              r=[xbn, mr.name], w=[t1.name])
            E("dv", lambda e, r16t=r16t: e.tensor_tensor(out=t1[:], in0=t1[:], in1=r16t[:],
                                                         op=ALU.mult),
              r=[t1.name, r16t.name], w=[t1.name])
            E("dv", lambda e: e.tensor_scalar(out=t1[:], in0=t1[:], scalar1=M_RND,
                                              scalar2=M_RND, op0=ALU.add, op1=ALU.subtract),
              r=[t1.name], w=[t1.name])
            E("dv", lambda e, ic=ic: e.tensor_scalar(out=xqt[ic][:], in0=t1[:], scalar1=127.0,
                                                     scalar2=-128.0, op0=ALU.min, op1=ALU.max),
              r=[t1.name], w=[f"xq{ic}"])

    def emit_ffn_p2(buf, S):
        xbn = f"xb{buf}"
        w1v = dw1q.rearrange("(ic p) o -> p ic o", p=128)
        for og in range(8):
            slh = [(2 * og) % 4, (2 * og + 1) % 4]
            for h in range(2):
                E("sp", lambda e, og=og, h=h, sl=slh[h]:
                  e.dma_start(out=slab[sl][:].rearrange("p (ic o) -> p ic o", ic=4),
                              in_=w1v[:, 4 * h:4 * h + 4, og * 512:(og + 1) * 512]),
                  w=[f"slab{slh[h]}"], r=[f"dw1q_q{og}"],
                  sem=s_sl[slh[h]], lat=1500)
            for o4 in range(4):
                oc = og * 4 + o4
                bank = 2 + oc % 2
                for ic in range(8):
                    sl = slh[ic // 4]
                    E("pe", lambda e, sl=sl, o4=o4, ic=ic, bank=bank:
                      e.matmul(pbt[bank][:],
                               slab[sl][:, (ic % 4) * 512 + o4 * 128:
                                           (ic % 4) * 512 + o4 * 128 + 128],
                               xqt[ic][:], start=(ic == 0), stop=(ic == 7)),
                      r=[f"slab{sl}", f"xq{ic}"], w=[PB[bank]])
                E("ac", lambda e, oc=oc, bank=bank:
                  e.activation(out=p1s[:], in_=pbt[bank][:], func=AF.Relu,
                               scale=2.0 ** -6, bias=b1i[:, oc:oc + 1]),
                  r=[PB[bank], "b1i"], w=["p1s"])
                E("ac", lambda e: e.activation(out=p2s[:], in_=p1s[:], func=AF.Relu,
                                               bias=cbv("Mcol")),
                  r=["p1s", "cbt"], w=["p2s"])
                E("dv", lambda e, oc=oc: e.tensor_scalar(out=hqt[oc][:], in0=p2s[:],
                                                         scalar1=M_RND, scalar2=127.0,
                                                         op0=ALU.subtract, op1=ALU.min),
                  r=["p2s"], w=[hq_res(oc)], cost=400)
        for og in range(8):
            slh = [(2 * og) % 4, (2 * og + 1) % 4]
            for h in range(2):
                E("sp", lambda e, og=og, h=h, sl=slh[h]:
                  e.dma_start(out=slab[sl][:].rearrange("p (o4 j) -> p o4 j", o4=2),
                              in_=dw2q[og * 512 + h * 256:og * 512 + h * 256 + 256,
                                       :].rearrange("(o4 p) j -> p o4 j", p=128)),
                  w=[f"slab{slh[h]}"], r=[f"dw2q_r{2 * og + h}"],
                  sem=s_sl[slh[h]], lat=1500)
            for o4 in range(4):
                oc = og * 4 + o4
                sl = slh[o4 // 2]
                for jc in range(8):
                    E("pe", lambda e, sl=sl, o4=o4, oc=oc, jc=jc:
                      e.matmul(pbt[jc][:],
                               slab[sl][:, (o4 % 2) * 1024 + jc * 128:
                                           (o4 % 2) * 1024 + jc * 128 + 128],
                               hqt[oc][:], start=(oc == 0), stop=(oc == 31)),
                      r=[f"slab{sl}", hq_res(oc)], w=[PB[jc]])
        for jc in range(8):
            E("dv", lambda e, jc=jc, buf=buf:
              e.tensor_scalar(out=xbuf[buf][:, jc * P:(jc + 1) * P],
                              in0=xbuf[buf][:, jc * P:(jc + 1) * P],
                              scalar1=b2qc[:, jc:jc + 1], scalar2=None, op0=ALU.add),
              r=[xbn, "b2qc"], w=[xbn])
            E("dv", lambda e, jc=jc, buf=buf:
              e.scalar_tensor_tensor(out=xbuf[buf][:, jc * P:(jc + 1) * P],
                                     in0=pbt[jc][:], scalar=2.0 ** -10,
                                     in1=xbuf[buf][:, jc * P:(jc + 1) * P],
                                     op0=ALU.mult, op1=ALU.add),
              r=[PB[jc], xbn], w=[xbn])

    def emit_attn_a_pre(buf):
        proj = [
            dict(bank=PB[2], out=pbt[2][0:H, 0:P], lres="wqa_r",
                 lhsT=lambda ic: wqa_r[:, ic * H:(ic + 1) * H]),
            dict(bank=PB[3], out=pbt[3][0:H, 0:P], lres="wka_r",
                 lhsT=lambda ic: wka_r[:, ic * H:(ic + 1) * H]),
            dict(bank=PB[4], out=pbt[4][0:H, 0:P], lres="wva_r",
                 lhsT=lambda ic: wva_r[:, ic * H:(ic + 1) * H]),
        ]
        emit_ln(buf, 0, want_r16=False, proj_specs=proj, sum_bank=0, sq_bank=1)
        fold_correct(0, t2s_[0][0:H, :], "t2s0", PB[2], pbt[2][0:H, 0:P], cs(0), "cs_sum", H)
        phi_inplace(0, qTa, "qTa")
        fold_correct(0, t2s_[0][0:H, :], "t2s0", PB[3], pbt[3][0:H, 0:P], cs(1), "cs_sum", H)
        phi_inplace(0, kTa, "kTa")
        fold_correct(0, vTa[:], "vTa", PB[4], pbt[4][0:H, 0:P], cs(2), "cs_sum", H)

    def emit_attn_a_sc(sc):
        sm = {"pe": 250, "dv": 260, "ac": 280}
        E = lambda eng, fn, r=(), w=(), **kw: TK.emit(eng, fn, r=r, w=w,
                                                      cost=kw.pop("cost", sm.get(eng)), **kw)
        s0 = sc * C
        qs = qTa[:, s0:s0 + C]
        ks = kTa[:, s0:s0 + C]
        vs = vTa[:, s0:s0 + C]
        E("pe", lambda e, ks=ks: e.transpose(pbr[2][:, 0:H], ks, ident_r[0:H, 0:H]),
          r=["kTa", "ident_r"], w=[PB[2]])
        E("dv", lambda e: e.tensor_copy(out=Ktm_a[:], in_=pbr[2][:, 0:H]),
          r=[PB[2]], w=["Ktm_a"])
        E("pe", lambda e, vs=vs: e.transpose(pbr[2][:, 64:128], vs, ident_r[0:H, 0:H]),
          r=["vTa", "ident_r"], w=[PB[2]])
        E("dv", lambda e: e.tensor_copy(out=Vtm_a[:], in_=pbr[2][:, 64:128]),
          r=[PB[2]], w=["Vtm_a"])
        E("pe", lambda e, vs=vs, qs=qs: e.matmul(pbt[2][:, 128:256], vs, qs,
                                                 start=True, stop=True),
          r=["vTa", "qTa"], w=[PB[2]])
        E("dv", lambda e: e.tensor_tensor(out=PTs[:], in0=pbt[2][:, 128:256],
                                          in1=cbv("maskIU"), op=ALU.mult),
          r=[PB[2], "cbt"], w=["PTs"])
        E("pe", lambda e, qs=qs, ks=ks: e.matmul(pbt[2][:, 256:384], qs, ks,
                                                 start=True, stop=True),
          r=["qTa", "kTa"], w=[PB[2]])
        E("dv", lambda e: e.tensor_tensor(out=QKs[:], in0=pbt[2][:, 256:384],
                                          in1=cbv("maskIL"), op=ALU.mult),
          r=[PB[2], "cbt"], w=["QKs"])
        E("dv", lambda e: e.reduce_sum(out=deni[:], in_=QKs[:], axis=AXX),
          r=["QKs"], w=["deni"])
        E("pe", lambda e, qs=qs: e.matmul(pbt[2][:, 384:392], qs, zcol[:],
                                          start=True, stop=True),
          r=["qTa", "zcol"], w=[PB[2]])
        E("dv", lambda e: e.tensor_tensor(out=dent[:], in0=deni[:],
                                          in1=pbt[2][:, 384:385], op=ALU.add),
          r=["deni", PB[2]], w=["dent"])
        E("dv", lambda e: e.tensor_scalar(out=dent[:], in0=dent[:], scalar1=1e-6,
                                          scalar2=None, op0=ALU.max),
          r=["dent"], w=["dent"])
        E("dv", lambda e: e.reciprocal(out=dent[:], in_=dent[:]), r=["dent"], w=["dent"])
        E("pe", lambda e: e.matmul(pbt[3][:, 0:H], PTs[:], Ktm_a[:], start=True, stop=False),
          r=["PTs", "Ktm_a"], w=[PB[3]])
        E("pe", lambda e, qs=qs: e.matmul(pbt[3][:, 0:H], qs, Za[:], start=False, stop=True),
          r=["qTa", "Za"], w=[PB[3]])
        E("dv", lambda e: e.tensor_scalar(out=Otm_a[:], in0=pbt[3][:, 0:H],
                                          scalar1=dent[:], scalar2=None, op0=ALU.mult),
          r=[PB[3], "dent"], w=["Otm_a"])
        E("pe", lambda e: e.transpose(pbr[3][0:H, 64:192], Otm_a[:], ident_r[:]),
          r=["Otm_a", "ident_r"], w=[PB[3]])
        E("dv", lambda e, s0=s0: e.tensor_copy(out=oTa[:, s0:s0 + C],
                                               in_=pbr[3][0:H, 64:192]),
          r=[PB[3]], w=["oTa"])
        E("pe", lambda e: e.matmul(pbt[3][0:H, 192:256], Vtm_a[:], Ktm_a[:],
                                   start=True, stop=True),
          r=["Vtm_a", "Ktm_a"], w=[PB[3]])
        E("dv", lambda e: e.tensor_tensor(out=Za[:], in0=Za[:], in1=pbt[3][0:H, 192:256],
                                          op=ALU.add),
          r=["Za", PB[3]], w=["Za"])
        E("dv", lambda e, ks=ks: e.reduce_sum(out=ztmp[:], in_=ks, axis=AXX),
          r=["kTa"], w=["ztmp"])
        E("dv", lambda e: e.tensor_scalar(out=zcol[:], in0=zcol[:], scalar1=ztmp[:],
                                          scalar2=None, op0=ALU.add),
          r=["zcol", "ztmp"], w=["zcol"])

    def emit_attn_a_out(buf):
        xbn = f"xb{buf}"
        for dc in range(8):
            bank = 2 + (dc % 2)
            E("pe", lambda e, dc=dc, bank=bank:
              e.matmul(pbt[bank][:], woa_r[:, dc * 128:(dc + 1) * 128], oTa[:],
                       start=True, stop=True),
              r=["woa_r", "oTa"], w=[PB[bank]])
            E("dv", lambda e, dc=dc, bank=bank, buf=buf:
              e.tensor_tensor(out=xbuf[buf][:, dc * P:(dc + 1) * P],
                              in0=xbuf[buf][:, dc * P:(dc + 1) * P], in1=pbt[bank][:],
                              op=ALU.add),
              r=[xbn, PB[bank]], w=[xbn])

    def emit_attn_b_pre(buf):
        proj = [
            dict(bank=PB[5], out=pbt[5][0:H, 0:P], lres="wkb_r",
                 lhsT=lambda ic: wkb_r[:, ic * H:(ic + 1) * H]),
            dict(bank=PB[6], out=pbt[6][0:H, 0:P], lres="wvb_r",
                 lhsT=lambda ic: wvb_r[:, ic * H:(ic + 1) * H]),
            dict(bank=PB[7], out=pbt[7][0:H, 0:P], lres="wqb_r",
                 lhsT=lambda ic: wqb_r[:, ic * H:(ic + 1) * H]),
        ]
        proj2 = [
            dict(bank=PB[5], out=pbt[5][0:H, 0:P], lres="irw1_r",
                 lhsT=lambda ic: irw1_r[:, ic * H:(ic + 1) * H], **{"pass": 1}),
            dict(bank=PB[6], out=pbt[6][0:32, 0:P], lres="pw_r",
                 lhsT=lambda ic: pw_r[:, ic * 32:(ic + 1) * 32], **{"pass": 1}),
        ]
        emit_ln(buf, 1, want_r16=False, proj_specs=proj, sum_bank=0, sq_bank=1)
        l2norm_cols(1, PB[5], pbt[5][0:H, 0:P], cs(3), "cs_sum", kbT, "kbT", ss_bank=4)
        fold_correct(1, vbT[:], "vbT", PB[6], pbt[6][0:H, 0:P], cs(4), "cs_sum", H)
        l2norm_cols(1, PB[7], pbt[7][0:H, 0:P], cs(5), "cs_sum", qbT, "qbT", ss_bank=4)
        emit_proj_pass(buf, 1, proj2)
        fold_correct(1, t2s_[1][0:H, :], "t2s0", PB[5], pbt[5][0:H, 0:P], cs(6), "cs_sum", H)
        E("ac", lambda e: e.activation(out=gcr[:], in_=t2s_[1][0:H, :], func=AF.Gelu,
                                       bias=irb1c[:]),
          r=["t2s0", "irb1c"], w=["gcr"])
        for sc in range(NSC):
            E("pe", lambda e, sc=sc: e.matmul(pbt[7][:, 8 * sc:8 * sc + 8],
                                              gcr[:, sc * C:(sc + 1) * C], irw2_r[:],
                                              start=True, stop=True),
              r=["gcr", "irw2_r"], w=[PB[7]])
        for sc in range(NSC):
            E("dv", lambda e, sc=sc: e.tensor_scalar(out=ucol[:, sc:sc + 1],
                                                     in0=pbt[7][:, 8 * sc:8 * sc + 1],
                                                     scalar1=irb2c[:], scalar2=0.0,
                                                     op0=ALU.add, op1=ALU.is_gt),
              r=[PB[7], "irb2c"], w=["ucol"])
        E("dv", lambda e: e.tensor_scalar(out=t2s_[1][0:32, :], in0=mrep[1][0:32, :],
                                          scalar1=cspw[:], scalar2=None, op0=ALU.mult),
          r=["mrep1", "cspw"], w=["t2s0"])
        E("dv", lambda e: e.tensor_tensor(out=t2s_[1][0:32, :], in0=pbt[6][0:32, 0:P],
                                          in1=t2s_[1][0:32, :], op=ALU.subtract),
          r=[PB[6], "t2s0"], w=["t2s0"])
        E("dv", lambda e: e.tensor_tensor(out=pwC[:], in0=t2s_[1][0:32, :],
                                          in1=rrep[1][0:32, :], op=ALU.mult),
          r=["t2s0", "rrep1"], w=["pwC"])
        E("pe", lambda e: e.matmul(pbt[5][:, 0:P], nmw1_r[:], vbT[:], start=True, stop=True),
          r=["nmw1_r", "vbT"], w=[PB[5]])
        E("ac", lambda e: e.activation(out=sc1r[:], in_=pbt[5][:, 0:P], func=AF.Gelu,
                                       bias=nmb1c[:]),
          r=[PB[5], "nmb1c"], w=["sc1r"])
        E("pe", lambda e: e.matmul(pbt[6][0:H, 0:P], nmw2_r[:], sc1r[:], start=True, stop=True),
          r=["nmw2_r", "sc1r"], w=[PB[6]])
        E("dv", lambda e: e.tensor_scalar(out=nuT[:], in0=pbt[6][0:H, 0:P],
                                          scalar1=nmb2c[:], scalar2=None, op0=ALU.add),
          r=[PB[6], "nmb2c"], w=["nuT"])
        for sc in range(NSC):
            E("pe", lambda e, sc=sc: e.transpose(pbr[7][:, 384 + 32 * sc:384 + 32 * (sc + 1)],
                                                 pwC[:, sc * C:(sc + 1) * C],
                                                 ident_r[0:32, 0:32]),
              r=["pwC", "ident_r"], w=[PB[7]])
        for sc in range(NSC):
            slc = pbt[7][:, 384 + 32 * sc:384 + 32 * sc + 3]
            E("dv", lambda e, slc=slc: e.reduce_max(out=msc[:], in_=slc, axis=AXX,
                                                    negate=True),
              r=[PB[7]], w=["msc"])
            E("ac", lambda e, sc=sc, slc=slc:
              e.activation(out=pwt[:, sc * 3:(sc + 1) * 3], in_=slc, func=AF.Exp,
                           bias=msc[:]),
              r=[PB[7], "msc"], w=["pwt"])
            E("dv", lambda e, sc=sc: e.reduce_sum(out=msc[:], in_=pwt[:, sc * 3:(sc + 1) * 3],
                                                  axis=AXX),
              r=["pwt"], w=["msc"])
            E("dv", lambda e: e.reciprocal(out=msc[:], in_=msc[:]), r=["msc"], w=["msc"])
            E("dv", lambda e, sc=sc: e.tensor_scalar(out=pwt[:, sc * 3:(sc + 1) * 3],
                                                     in0=pwt[:, sc * 3:(sc + 1) * 3],
                                                     scalar1=msc[:], scalar2=None,
                                                     op0=ALU.mult),
              r=["pwt", "msc"], w=["pwt"])

    def emit_attn_b_sc(sc):
        sm = {"pe": 250, "dv": 260, "ac": 280}
        E = lambda eng, fn, r=(), w=(), **kw: TK.emit(eng, fn, r=r, w=w,
                                                      cost=kw.pop("cost", sm.get(eng)), **kw)
        s0 = sc * C
        kss = kbT[:, s0:s0 + C]
        qss = qbT[:, s0:s0 + C]
        vss = vbT[:, s0:s0 + C]
        ucs = ucol[:, sc:sc + 1]
        E("pe", lambda e, kss=kss: e.transpose(pbr[4][:, 256:320], kss, ident_r[0:H, 0:H]),
          r=["kbT", "ident_r"], w=[PB[4]])
        E("dv", lambda e: e.tensor_copy(out=Ktm[:], in_=pbr[4][:, 256:320]),
          r=[PB[4]], w=["Ktm"])
        E("pe", lambda e, vss=vss: e.transpose(pbr[4][:, 320:384], vss, ident_r[0:H, 0:H]),
          r=["vbT", "ident_r"], w=[PB[4]])
        E("dv", lambda e: e.tensor_copy(out=Vtm[:], in_=pbr[4][:, 320:384]),
          r=[PB[4]], w=["Vtm"])
        E("pe", lambda e, ucs=ucs: e.transpose(pbt[4][0:1, 384:512], ucs, cbv("ident")),
          r=["ucol", "cbt"], w=[PB[4]])
        E("dv", lambda e: e.tensor_copy(out=urow[:], in_=pbt[4][0:1, 384:512]),
          r=[PB[4]], w=["urow"])
        E("pe", lambda e, kss=kss: e.matmul(pbt[4][:, 0:128], kss, kss, start=True, stop=True),
          r=["kbT"], w=[PB[4]])
        E("pe", lambda e, kss=kss, qss=qss: e.matmul(pbt[5][:, 0:128], kss, qss,
                                                     start=True, stop=True),
          r=["kbT", "qbT"], w=[PB[5]])
        E("pe", lambda e: e.matmul(pbt[4][:, 128:256], urow[:], urow[:], start=True, stop=True),
          r=["urow"], w=[PB[4]])
        E("dv", lambda e: e.tensor_tensor(out=Npw[0][0][:], in0=pbt[4][:, 0:128],
                                          in1=DSB[0][:], op=ALU.mult),
          r=[PB[4], "DSB0"], w=["Npw00"])
        E("dv", lambda e: e.tensor_tensor(out=NpwT[0][0][:], in0=pbt[4][:, 0:128],
                                          in1=DSBT[0][:], op=ALU.mult),
          r=[PB[4], "DSBT0"], w=["NpwT00"])
        E("dv", lambda e: e.tensor_tensor(out=nscr[:], in0=pbt[4][:, 0:128],
                                          in1=DSB[1][:], op=ALU.mult),
          r=[PB[4], "DSB1"], w=["nscr"])
        E("dv", lambda e: e.tensor_tensor(out=Npw[1][0][:], in0=nscr[:],
                                          in1=pbt[4][:, 128:256], op=ALU.mult),
          r=["nscr", PB[4]], w=["Npw10"])
        E("dv", lambda e: e.tensor_tensor(out=nscr[:], in0=pbt[4][:, 0:128],
                                          in1=DSBT[1][:], op=ALU.mult),
          r=[PB[4], "DSBT1"], w=["nscr"])
        E("dv", lambda e: e.tensor_tensor(out=NpwT[1][0][:], in0=nscr[:],
                                          in1=pbt[4][:, 128:256], op=ALU.mult),
          r=["nscr", PB[4]], w=["NpwT10"])
        E("dv", lambda e: e.tensor_tensor(out=WxT[0][:], in0=pbt[5][:, 0:128],
                                          in1=DS_[0][:], op=ALU.mult),
          r=[PB[5], "DS0"], w=["WxT0"])
        E("dv", lambda e: e.tensor_tensor(out=WiT[0][:], in0=pbt[5][:, 0:128],
                                          in1=DI_[0][:], op=ALU.mult),
          r=[PB[5], "DI0"], w=["WiT0"])
        E("dv", lambda e: e.tensor_tensor(out=WxT[1][:], in0=pbt[5][:, 0:128],
                                          in1=DS_[1][:], op=ALU.mult),
          r=[PB[5], "DS1"], w=["WxT1"])
        E("dv", lambda e, ucs=ucs: e.tensor_scalar(out=WxT[1][:], in0=WxT[1][:],
                                                   scalar1=ucs, scalar2=None, op0=ALU.mult),
          r=["WxT1", "ucol"], w=["WxT1"])
        E("dv", lambda e: e.tensor_tensor(out=WiT[1][:], in0=pbt[5][:, 0:128],
                                          in1=DI_[1][:], op=ALU.mult),
          r=[PB[5], "DI1"], w=["WiT1"])
        E("dv", lambda e, ucs=ucs: e.tensor_scalar(out=WiT[1][:], in0=WiT[1][:],
                                                   scalar1=ucs, scalar2=None, op0=ALU.mult),
          r=["WiT1", "ucol"], w=["WiT1"])
        for m in range(2):
            off = 128 + m * H
            E("pe", lambda e, kss=kss, m=m, off=off:
              e.matmul(pbt[5][:, off:off + H], kss, F0m[m][:], start=True, stop=True),
              r=["kbT", f"F0m{m}"], w=[PB[5]])
            if m == 0:
                E("dv", lambda e: e.tensor_scalar(out=rdf[:], in0=pbt[5][:, 128:128 + H],
                                                  scalar1=g1n[0][:], scalar2=None,
                                                  op0=ALU.mult),
                  r=[PB[5], "g1n0"], w=["rdf"])
                E("dv", lambda e: e.scalar_tensor_tensor(out=Rm[0][:], in0=Vtm[:],
                                                         scalar=bcol[0][:], in1=rdf[:],
                                                         op0=ALU.mult, op1=ALU.add),
                  r=["Vtm", "bcol0", "rdf"], w=["Rm0"])
            else:
                E("dv", lambda e: e.tensor_scalar(out=rds[:], in0=pbt[5][:, 192:192 + H],
                                                  scalar1=g1n[1][:], scalar2=ucs,
                                                  op0=ALU.mult, op1=ALU.mult),
                  r=[PB[5], "g1n1", "ucol"], w=["rds"])
                E("dv", lambda e: e.tensor_scalar(out=predt[:], in0=Vtm[:],
                                                  scalar1=bcol[1][:], scalar2=ucs,
                                                  op0=ALU.mult, op1=ALU.mult),
                  r=["Vtm", "bcol1", "ucol"], w=["predt"])
                E("dv", lambda e: e.tensor_tensor(out=Rm[1][:], in0=predt[:], in1=rds[:],
                                                  op=ALU.add),
                  r=["predt", "rds"], w=["Rm1"])
        for m in range(2):
            bk = 6 + m
            E("pe", lambda e, m=m, bk=bk: e.matmul(pbt[bk][:, 256:320], Npw[m][0][:],
                                                   Rm[m][:], start=True, stop=True),
              r=[f"Npw{m}0", f"Rm{m}"], w=[PB[bk]])
            E("dv", lambda e, m=m, bk=bk: e.tensor_tensor(out=Ut[m][0][:], in0=Rm[m][:],
                                                          in1=pbt[bk][:, 256:320],
                                                          op=ALU.subtract),
              r=[f"Rm{m}", PB[bk]], w=[f"Ut{m}0"])
        for j in range(6):
            jp = j % 2
            jn = (j + 1) % 2
            for m in range(2):
                bk = 6 + m
                E("pe", lambda e, m=m, jp=jp, bk=bk:
                  e.matmul(pbt[bk][:, 0:128], NpwT[m][jp][:], Npw[m][jp][:],
                           start=True, stop=True),
                  r=[f"NpwT{m}{jp}", f"Npw{m}{jp}"], w=[PB[bk]])
                E("dv", lambda e, m=m, jn=jn, bk=bk:
                  e.tensor_copy(out=Npw[m][jn][:], in_=pbr[bk][:, 0:128]),
                  r=[PB[bk]], w=[f"Npw{m}{jn}"])
                if j < 5:
                    E("pe", lambda e, m=m, jp=jp, bk=bk:
                      e.matmul(pbt[bk][:, 128:256], Npw[m][jp][:], NpwT[m][jp][:],
                               start=True, stop=True),
                      r=[f"Npw{m}{jp}", f"NpwT{m}{jp}"], w=[PB[bk]])
                    E("dv", lambda e, m=m, jn=jn, bk=bk:
                      e.tensor_copy(out=NpwT[m][jn][:], in_=pbr[bk][:, 128:256]),
                      r=[PB[bk]], w=[f"NpwT{m}{jn}"])
                E("pe", lambda e, m=m, jp=jp, jn=jn, bk=bk:
                  e.matmul(pbt[bk][:, 256:320], Npw[m][jn][:], Ut[m][jp][:],
                           start=True, stop=True),
                  r=[f"Npw{m}{jn}", f"Ut{m}{jp}"], w=[PB[bk]])
                E("dv", lambda e, m=m, jp=jp, jn=jn, bk=bk:
                  e.tensor_tensor(out=Ut[m][jn][:], in0=Ut[m][jp][:],
                                  in1=pbt[bk][:, 256:320], op=ALU.add),
                  r=[f"Ut{m}{jp}", PB[bk]], w=[f"Ut{m}{jn}"])
        for m in range(2):
            bk = 6 + m
            E("pe", lambda e, qss=qss, m=m: e.matmul(pbt[5][:, 256 + m * H:256 + (m + 1) * H],
                                                     qss, F0m[m][:], start=True, stop=True),
              r=["qbT", f"F0m{m}"], w=[PB[5]])
            E("pe", lambda e, m=m, bk=bk: e.matmul(pbt[bk][:, 320:384], WxT[m][:],
                                                   Ut[m][0][:], start=True, stop=True),
              r=[f"WxT{m}", f"Ut{m}0"], w=[PB[bk]])
            E("pe", lambda e, m=m, bk=bk: e.matmul(pbt[bk][:, 384:448], WiT[m][:],
                                                   Ut[m][0][:], start=True, stop=True),
              r=[f"WiT{m}", f"Ut{m}0"], w=[PB[bk]])
        E("dv", lambda e: e.tensor_scalar(out=rdf[:], in0=pbt[5][:, 256:256 + H],
                                          scalar1=g1c[0][:], scalar2=None, op0=ALU.mult),
          r=[PB[5], "g1c0"], w=["rdf"])
        E("dv", lambda e: e.tensor_tensor(out=rdf[:], in0=rdf[:], in1=pbt[6][:, 320:384],
                                          op=ALU.add),
          r=["rdf", PB[6]], w=["rdf"])
        E("dv", lambda e: e.tensor_scalar(out=rds[:], in0=pbt[5][:, 320:320 + H],
                                          scalar1=g1c[1][:], scalar2=None, op0=ALU.mult),
          r=[PB[5], "g1c1"], w=["rds"])
        E("dv", lambda e: e.tensor_tensor(out=rds[:], in0=rds[:], in1=pbt[7][:, 320:384],
                                          op=ALU.add),
          r=["rds", PB[7]], w=["rds"])
        E("dv", lambda e: e.tensor_tensor(out=predt[:], in0=rdf[:], in1=rds[:], op=ALU.add),
          r=["rdf", "rds"], w=["predt"])
        E("dv", lambda e: e.scalar_tensor_tensor(out=predt[:], in0=predt[:], scalar=-0.5,
                                                 in1=Vtm[:], op0=ALU.mult, op1=ALU.add),
          r=["predt", "Vtm"], w=["predt"])
        E("ac", lambda e: e.activation(out=dsq[:], in_=predt[:], func=AF.Square),
          r=["predt"], w=["dsq"])
        E("dv", lambda e: e.reduce_sum(out=errc[:], in_=dsq[:], axis=AXX),
          r=["dsq"], w=["errc"])
        E("dv", lambda e: e.tensor_scalar(out=uncol[:], in0=errc[:], scalar1=SUR_THR,
                                          scalar2=None, op0=ALU.is_gt),
          r=["errc"], w=["uncol"])
        E("dv", lambda e: e.tensor_scalar(out=rdf[:], in0=pbt[5][:, 256:256 + H],
                                          scalar1=g2c[0][:], scalar2=None, op0=ALU.mult),
          r=[PB[5], "g2c0"], w=["rdf"])
        E("dv", lambda e: e.tensor_tensor(out=rdf[:], in0=rdf[:], in1=pbt[6][:, 384:448],
                                          op=ALU.add),
          r=["rdf", PB[6]], w=["rdf"])
        E("dv", lambda e: e.tensor_scalar(out=rds[:], in0=pbt[5][:, 320:320 + H],
                                          scalar1=g2c[1][:], scalar2=None, op0=ALU.mult),
          r=[PB[5], "g2c1"], w=["rds"])
        E("dv", lambda e: e.tensor_tensor(out=rds[:], in0=rds[:], in1=pbt[7][:, 384:448],
                                          op=ALU.add),
          r=["rds", PB[7]], w=["rds"])
        E("pe", lambda e: e.transpose(pbt[7][0:1, 128:256], uncol[:], cbv("ident")),
          r=["uncol", "cbt"], w=[PB[7]])
        E("dv", lambda e: e.tensor_copy(out=unrow[:], in_=pbt[7][0:1, 128:256]),
          r=[PB[7]], w=["unrow"])
        E("pe", lambda e: e.matmul(pbt[6][0:H, 0:128], ones_r[0:1, 0:H], unrow[:],
                                   start=True, stop=True),
          r=["ones_r", "unrow"], w=[PB[6]])
        E("dv", lambda e: e.tensor_scalar(out=nsA[:], in0=pbt[6][0:H, 0:128],
                                          scalar1=-0.1, scalar2=1.0, op0=ALU.mult,
                                          op1=ALU.add),
          r=[PB[6]], w=["nsA"])
        E("dv", lambda e: e.tensor_scalar(out=nsB[:], in0=pbt[6][0:H, 0:128],
                                          scalar1=0.1, scalar2=None, op0=ALU.mult),
          r=[PB[6]], w=["nsB"])
        E("dv", lambda e, s0=s0: e.tensor_tensor(out=nsB[:], in0=nsB[:],
                                                 in1=nuT[:, s0:s0 + C], op=ALU.mult),
          r=["nsB", "nuT"], w=["nsB"])
        if sc == 0:
            E("dv", lambda e, s0=s0:
              e.tensor_tensor_scan(out=Nfm[:, s0:s0 + C], data0=nsA[:], data1=nsB[:],
                                   initial=nsinit[:], op0=ALU.mult, op1=ALU.add),
              r=["nsA", "nsB", "nsinit"], w=["Nfm"], cost=500)
        else:
            E("dv", lambda e, s0=s0:
              e.tensor_tensor_scan(out=Nfm[:, s0:s0 + C], data0=nsA[:], data1=nsB[:],
                                   initial=Nfm[:, s0 - 1:s0], op0=ALU.mult, op1=ALU.add),
              r=["nsA", "nsB", "Nfm"], w=["Nfm"], cost=500)
        E("pe", lambda e, s0=s0: e.transpose(pbt[6][:, 448:512], Nfm[:, s0:s0 + C],
                                             cbv("ident", rows=H)[0:H, 0:H]),
          r=["Nfm", "cbt"], w=[PB[6]])
        E("dv", lambda e: e.tensor_copy(out=Ntm[:], in_=pbt[6][:, 448:512]),
          r=[PB[6]], w=["Ntm"])
        E("dv", lambda e, sc=sc: e.tensor_scalar(out=oatt[:], in0=rdf[:],
                                                 scalar1=pwt[:, sc * 3:sc * 3 + 1],
                                                 scalar2=None, op0=ALU.mult),
          r=["rdf", "pwt"], w=["oatt"])
        E("dv", lambda e, sc=sc: e.scalar_tensor_tensor(out=oatt[:], in0=rds[:],
                                                        scalar=pwt[:, sc * 3 + 1:sc * 3 + 2],
                                                        in1=oatt[:], op0=ALU.mult,
                                                        op1=ALU.add),
          r=["rds", "pwt", "oatt"], w=["oatt"])
        E("dv", lambda e, sc=sc: e.scalar_tensor_tensor(out=oatt[:], in0=Ntm[:],
                                                        scalar=pwt[:, sc * 3 + 2:sc * 3 + 3],
                                                        in1=oatt[:], op0=ALU.mult,
                                                        op1=ALU.add),
          r=["Ntm", "pwt", "oatt"], w=["oatt"])
        E("ac", lambda e: e.activation(out=dsq[:], in_=oatt[:], func=AF.Square),
          r=["oatt"], w=["dsq"])
        E("dv", lambda e: e.reduce_sum(out=msc[:], in_=dsq[:], axis=AXX),
          r=["dsq"], w=["msc"])
        E("dv", lambda e: e.tensor_scalar(out=msc[:], in0=msc[:], scalar1=1.0 / H,
                                          scalar2=1e-6, op0=ALU.mult, op1=ALU.add),
          r=["msc"], w=["msc"])
        E("ac", lambda e: e.activation(out=msc[:], in_=msc[:], func=AF.Ln),
          r=["msc"], w=["msc"])
        E("ac", lambda e: e.activation(out=msc[:], in_=msc[:], func=AF.Exp, scale=-0.5),
          r=["msc"], w=["msc"])
        E("dv", lambda e: e.tensor_scalar(out=onrm[:], in0=oatt[:], scalar1=msc[:],
                                          scalar2=None, op0=ALU.mult),
          r=["oatt", "msc"], w=["onrm"])
        E("pe", lambda e: e.transpose(pbr[7][0:H, 0:128], onrm[:], ident_r[:]),
          r=["onrm", "ident_r"], w=[PB[7]])
        E("dv", lambda e, s0=s0: e.tensor_copy(out=obT[:, s0:s0 + C],
                                               in_=pbr[7][0:H, 0:128]),
          r=[PB[7]], w=["obT"])
        E("dv", lambda e, ucs=ucs: e.tensor_scalar(out=Ktg[:], in0=Ktm[:], scalar1=ucs,
                                                   scalar2=None, op0=ALU.mult),
          r=["Ktm", "ucol"], w=["Ktg"])
        for m in range(2):
            E("dv", lambda e, m=m: e.tensor_scalar(out=Ut[m][0][:], in0=Ut[m][0][:],
                                                   scalar1=g3c[m][:], scalar2=None,
                                                   op0=ALU.mult),
              r=[f"Ut{m}0", f"g3c{m}"], w=[f"Ut{m}0"])
            lhs = Ktm if m == 0 else Ktg
            lres = "Ktm" if m == 0 else "Ktg"
            E("pe", lambda e, m=m, lhs=lhs: e.matmul(pbt[5][0:H, 384 + m * H:384 + (m + 1) * H],
                                                     lhs[:], Ut[m][0][:],
                                                     start=True, stop=True),
              r=[lres, f"Ut{m}0"], w=[PB[5]])
            E("dv", lambda e, m=m: e.scalar_tensor_tensor(
                out=F0m[m][:], in0=F0m[m][:], scalar=aCc[m][0:H, :],
                in1=pbt[5][0:H, 384 + m * H:384 + (m + 1) * H],
                op0=ALU.mult, op1=ALU.add),
              r=[f"F0m{m}", PB[5], f"aCc{m}"], w=[f"F0m{m}"])

    def emit_attn_b_out(buf):
        xbn = f"xb{buf}"
        E("dv", lambda e: e.tensor_copy(out=nsinit[:], in_=Nfm[:, P - 1:P]),
          r=["Nfm"], w=["nsinit"])
        for dc in range(8):
            bank = 4 + (dc % 2)
            E("pe", lambda e, dc=dc, bank=bank:
              e.matmul(pbt[bank][:], wob_r[:, dc * 128:(dc + 1) * 128], obT[:],
                       start=True, stop=True),
              r=["wob_r", "obT"], w=[PB[bank]])
            E("dv", lambda e, dc=dc, bank=bank, buf=buf:
              e.tensor_tensor(out=xbuf[buf][:, dc * P:(dc + 1) * P],
                              in0=xbuf[buf][:, dc * P:(dc + 1) * P], in1=pbt[bank][:],
                              op=ALU.add),
              r=[xbn, PB[bank]], w=[xbn])

    if PIPELINE and SUBLAYERS >= 4:
        for i in range(NCHUNK + 1):
            bufA = i % 3
            bufB = (i - 1) % 3
            hasA = i < NCHUNK
            hasB = i > 0
            if i == 0:
                emit_attn_a_pre(bufA)
            if hasB:
                emit_attn_b_pre(bufB)
            for sc in range(NSC):
                if hasA:
                    emit_attn_a_sc(sc)
                if hasB:
                    emit_attn_b_sc(sc)
            if hasA:
                emit_attn_a_out(bufA)
            if hasB:
                emit_attn_b_out(bufB)
            if hasA:
                emit_ffn_p1(bufA, 0)
                emit_ffn_p2(bufA, 0)
            if hasB:
                emit_ffn_p1(bufB, 1)
            if hasA and i + 1 < NCHUNK:
                emit_attn_a_pre((i + 1) % 3)
            if hasB:
                emit_ffn_p2(bufB, 1)
                E("sp", lambda e, c=i - 1, bufB=bufB: e.dma_start(
                    out=odst[:, :, c * P:(c + 1) * P],
                    in_=xbuf[bufB][:].rearrange("p (ic t) -> p ic t", ic=8)),
                  r=[f"xb{bufB}"], w=[f"od{i - 1}"], sem=s_out)
            if i + 2 < NCHUNK:
                x_load(i + 2, (i + 2) % 3)
    else:
        for c_ in range(NCHUNK):
            buf = c_ % 3
            if SUBLAYERS >= 1:
                emit_attn_a_pre(buf)
                for sc in range(NSC):
                    emit_attn_a_sc(sc)
                emit_attn_a_out(buf)
            if SUBLAYERS >= 2:
                emit_ffn_p1(buf, 0)
                emit_ffn_p2(buf, 0)
            if SUBLAYERS >= 3:
                emit_attn_b_pre(buf)
                for sc in range(NSC):
                    emit_attn_b_sc(sc)
                emit_attn_b_out(buf)
            if SUBLAYERS >= 4:
                emit_ffn_p1(buf, 1)
                emit_ffn_p2(buf, 1)
            E("sp", lambda e, c_=c_, buf=buf: e.dma_start(
                out=odst[:, :, c_ * P:(c_ + 1) * P],
                in_=xbuf[buf][:].rearrange("p (ic t) -> p ic t", ic=8)),
              r=[f"xb{buf}"], w=[f"od{c_}"], sem=s_out)
            if c_ + 2 < NCHUNK:
                x_load(c_ + 2, (c_ + 2) % 3)

    TK.schedule()
    TK.finalize(ENG_SEMS)
    TK.steps.append(dict(eng="sp", fn=lambda e: e.nop(), r=(), w=(), sem=s_out,
                         inc=False, deps=set(), val=None,
                         waits=[(s_out, 16 * NCHUNK)], xwaits=[]))

    with nc.Block() as block:
        @block.sync
        def _(sync):
            TK.run_engine("sp", sync)

        @block.vector
        def _(vector):
            TK.run_engine("dv", vector)

        @block.scalar
        def _(scalar):
            TK.run_engine("ac", scalar)

        @block.tensor
        def _(tensor):
            TK.run_engine("pe", tensor)

        @block.gpsimd
        def _(gpsimd):
            TK.run_engine("gp", gpsimd)

    for cm in reversed(sem_cms):
        cm.__exit__(None, None, None)
    return nc


def kernel(**inputs):
    g = lambda k: np.asarray(inputs[k])
    x = g("L0_attn_a_x").astype(np.float32, copy=False)
    cbd = build_const_blob()
    pwT = np.zeros((D, 32), np.float32)
    pwT[:, 0:3] = g("L0_attn_b_pw").T

    common = dict(
        cbd=cbd,
        w1T=np.ascontiguousarray(g("L0_ffn_w1").T),
        w2T=np.ascontiguousarray(g("L0_ffn_w2").T),
        wqa=np.ascontiguousarray(g("L0_attn_a_W_q")),
        wka=np.ascontiguousarray(g("L0_attn_a_W_k")),
        wva=np.ascontiguousarray(g("L0_attn_a_W_v")),
        woa=np.ascontiguousarray(g("L0_attn_a_W_o")),
        wkbT=np.ascontiguousarray(g("L0_attn_b_Wk").T),
        wvbT=np.ascontiguousarray(g("L0_attn_b_Wv").T),
        wqbT=np.ascontiguousarray(g("L0_attn_b_Wq").T),
        irw1T=np.ascontiguousarray(g("L0_attn_b_ir_w1").T),
        pwT=pwT,
        nmw1T=np.ascontiguousarray(g("L0_attn_b_nm_w1").T),
        nmw2T=np.ascontiguousarray(g("L0_attn_b_nm_w2").T),
        irw2T=np.ascontiguousarray(g("L0_attn_b_ir_w2").T),
        wobT=np.ascontiguousarray(g("L0_attn_b_Wo").T),
        d_b1c=np.ascontiguousarray(g("L0_ffn_b1").reshape(32, 128).T),
        d_b2c=np.ascontiguousarray(g("L0_ffn_b2").reshape(8, 128).T),
        d_nmb1c=np.ascontiguousarray(g("L0_attn_b_nm_b1").reshape(128, 1)),
        d_nmb2c=np.ascontiguousarray(g("L0_attn_b_nm_b2").reshape(64, 1)),
        d_irb1c=np.ascontiguousarray(g("L0_attn_b_ir_b1").reshape(64, 1)),
        d_irb2=np.ascontiguousarray(g("L0_attn_b_ir_b2").reshape(1, 1)),
        d_normw=np.ascontiguousarray(g("L0_attn_b_norm_w").reshape(64, 1)),
        d_alphaf=np.ascontiguousarray(np.reshape(g("L0_attn_b_fast_alpha"), (1, 1))),
        d_alphas=np.ascontiguousarray(np.reshape(g("L0_attn_b_slow_alpha"), (1, 1))),
    )
    in_maps = []
    for b in range(8):
        m = dict(common)
        m["xT"] = np.ascontiguousarray(x[b].T)
        in_maps.append(m)

    nc = build_nc()
    res = run_bass_kernel_spmd(nc, in_maps, core_ids=list(range(8)))
    out = np.stack([np.ascontiguousarray(res.results[b]["outT"].T) for b in range(8)], axis=0)
    return out.astype(np.float32, copy=False)


# version 16
# speedup vs baseline: 1.0487x; 1.0082x over previous
import math

import numpy as np
import ml_dtypes
import concourse.bass as bass
import concourse.mybir as mybir
from concourse.bass_utils import run_bass_kernel_spmd

F32 = mybir.dt.float32
F32R = mybir.dt.float32r
BF16 = mybir.dt.bfloat16
AF = mybir.ActivationFunctionType
ALU = mybir.AluOpType
AXX = mybir.AxisListType.X

D = 1024
T = 4096
H = 64
HID = 4096
P = 512
NCHUNK = T // P
C = 128
NSC = P // C
M_RND = float(2.0 ** 23 + 2.0 ** 22)
LN_EPS = 1e-5
SUR_THR = float(math.log(0.7 / 0.3) * (1.0 + 1e-6))

SUBLAYERS = 4
PIPELINE = True

_OFF = {}
_top = 0
def _cols(name, w):
    global _top
    _OFF[name] = (_top, w)
    _top += w

for _n in ("maskIL", "maskIU", "maskSU", "maskSL", "expoSU", "expoSL", "expoIU",
           "ones128", "ident"):
    _cols(_n, 128)
for _n in ("iota0", "iota1", "iotaC", "onescol", "epscol", "negMcol", "Mcol", "zerocol", "ln16col"):
    _cols(_n, 1)
_cols("zero64", 64)
CBW = _top


def build_const_blob():
    cb = np.zeros((128, CBW), np.float32)
    r = np.arange(128)
    lo = (r[:, None] >= r[None, :]).astype(np.float32)
    iu = (r[:, None] <= r[None, :]).astype(np.float32)
    su = (r[:, None] < r[None, :]).astype(np.float32)
    sl = (r[:, None] > r[None, :]).astype(np.float32)

    def put(name, arr):
        o, w = _OFF[name]
        cb[:, o:o + w] = arr

    put("maskIL", lo)
    put("maskIU", iu)
    put("maskSU", su)
    put("maskSL", sl)
    put("expoSU", (r[None, :] - 1 - r[:, None]) * su)
    put("expoSL", (r[:, None] - 1 - r[None, :]) * sl)
    put("expoIU", (r[None, :] - r[:, None]) * iu)
    put("ones128", np.ones((128, 128), np.float32))
    put("ident", np.eye(128, dtype=np.float32))
    put("iota0", r[:, None].astype(np.float32))
    put("iota1", (r[:, None] + 1).astype(np.float32))
    put("iotaC", (C - 1 - r[:, None]).astype(np.float32))
    put("onescol", np.ones((128, 1), np.float32))
    put("epscol", np.full((128, 1), LN_EPS, np.float32))
    put("negMcol", np.full((128, 1), -M_RND, np.float32))
    put("Mcol", np.full((128, 1), M_RND, np.float32))
    put("zerocol", np.zeros((128, 1), np.float32))
    put("ln16col", np.full((128, 1), math.log(16.0), np.float32))
    put("zero64", np.zeros((128, 64), np.float32))
    return cb


class SemT:
    def __init__(self, handle, name, dma=False):
        self.h = handle
        self.name = name
        self.n = 0
        self.dma = dma

    def bump(self):
        self.n += 16 if self.dma else 1
        return self.n


class Tracker:
    def __init__(self):
        self.steps = []

    def emit(self, eng, fn, r=(), w=(), sem=None, xwaits=(), cost=None, lat=None):
        self.steps.append(dict(eng=eng, fn=fn, r=tuple(r), w=tuple(w), sem=sem,
                               inc=False, xwaits=list(xwaits), cost=cost, lat=lat))
        return len(self.steps) - 1

    def schedule(self, costs=None):
        import heapq
        n = len(self.steps)
        cost = []
        base = {"pe": 260, "dv": 390, "ac": 640, "sp": 620, "gp": 900}
        lats = []
        for st in self.steps:
            c = st.get("cost")
            cost.append(c if c is not None else base.get(st["eng"], 400))
            lt = st.get("lat")
            if lt is None:
                lt = 800 if st["eng"] == "sp" else 0
            lats.append(lt)
        last_w = {}
        readers = {}
        succ = [[] for _ in range(n)]
        pred_cnt = [0] * n
        preds = [set() for _ in range(n)]
        for i, st in enumerate(self.steps):
            deps = set()
            for rr in st["r"]:
                j = last_w.get(rr)
                if j is not None:
                    deps.add(j)
            for rr in st["w"]:
                j = last_w.get(rr)
                if j is not None:
                    deps.add(j)
                for k in readers.get(rr, ()):
                    if k != i:
                        deps.add(k)
            for rr in st["r"]:
                readers.setdefault(rr, []).append(i)
            for rr in st["w"]:
                last_w[rr] = i
                readers[rr] = []
            preds[i] = deps
            pred_cnt[i] = len(deps)
            for j in deps:
                succ[j].append(i)
        cp = [0.0] * n
        for i in range(n - 1, -1, -1):
            m = 0.0
            for s_ in succ[i]:
                if cp[s_] > m:
                    m = cp[s_]
            cp[i] = cost[i] + lats[i] + m
        fin = [0.0] * n
        avail = [0.0] * n
        eng_clock = {}
        ready = []
        for i in range(n):
            if pred_cnt[i] == 0:
                ready.append(i)
        order = []
        remaining = list(pred_cnt)
        start = [0.0] * n
        import os
        policy = os.environ.get("TK_SCHED_POLICY", "est")
        while ready:
            if policy == "est":
                bi, bk = -1, None
                for idx, i in enumerate(ready):
                    est = eng_clock.get(self.steps[i]["eng"], 0.0)
                    if avail[i] > est:
                        est = avail[i]
                    key = (est, -cp[i])
                    if bk is None or key < bk:
                        bk, bi = key, idx
                i = ready.pop(bi)
            else:
                bi, bk = -1, None
                for idx, i in enumerate(ready):
                    key = -cp[i]
                    if bk is None or key < bk:
                        bk, bi = key, idx
                i = ready.pop(bi)
            st = self.steps[i]
            t0 = eng_clock.get(st["eng"], 0.0)
            if avail[i] > t0:
                t0 = avail[i]
            start[i] = t0
            fin[i] = t0 + cost[i] + lats[i]
            eng_clock[st["eng"]] = t0 + cost[i]
            order.append(i)
            for s_ in succ[i]:
                remaining[s_] -= 1
                if avail[s_] < fin[i]:
                    avail[s_] = fin[i]
                if remaining[s_] == 0:
                    ready.append(s_)
        assert len(order) == n
        if os.environ.get("TK_SCHED_DEBUG"):
            eng_busy = {}
            for i in range(n):
                e = self.steps[i]["eng"]
                eng_busy[e] = eng_busy.get(e, 0) + cost[i]
            print("SCHED makespan(model): %.1f us" % (max(fin) / 1e3))
            print("SCHED cp bound: %.1f us" % (max(cp) / 1e3))
            print("SCHED eng busy(model):", {k: round(v / 1e3) for k, v in eng_busy.items()})
            prev_end = {}
            attr = {}
            gaps = []
            for i in order:
                e = self.steps[i]["eng"]
                gap = start[i] - prev_end.get(e, 0.0)
                if gap > 1.0:
                    jb, fb = None, -1.0
                    for j in preds[i]:
                        if fin[j] > fb:
                            fb, jb = fin[j], j
                    be = self.steps[jb]["eng"] if jb is not None else "none"
                    res = ""
                    if jb is not None:
                        wr = set(self.steps[jb]["w"])
                        res = ",".join(sorted(wr & set(self.steps[i]["r"]))) or \
                              ",".join(sorted(wr)[:2])
                    key = (e, be, res)
                    attr[key] = attr.get(key, 0.0) + gap
                    gaps.append((gap, e, be, res, start[i]))
                prev_end[e] = fin[i]
            pe_gaps = sorted([g for g in gaps if g[1] == "pe"], reverse=True)[:20]
            for gp, e_, be, res, t_ in pe_gaps:
                print("SCHED pegap %.1f us at t=%.1f blocked-by %s [%s]" %
                      (gp / 1e3, t_ / 1e3, be, res))
            top = sorted(attr.items(), key=lambda kv: -kv[1])[:16]
            for (e, be, res), tot in top:
                print("SCHED idle %s blocked-by %s [%s]: %.1f us" % (e, be, res, tot / 1e3))
            tot_idle = {}
            nstep = {}
            endt = {}
            for i in order:
                e = self.steps[i]["eng"]
                nstep[e] = nstep.get(e, 0) + 1
                endt[e] = fin[i]
            prev_end = {}
            for i in order:
                e = self.steps[i]["eng"]
                tot_idle[e] = tot_idle.get(e, 0.0) + (start[i] - prev_end.get(e, 0.0))
                prev_end[e] = fin[i]
            print("SCHED per-eng: " + "  ".join(
                "%s n=%d idle=%.0fus end=%.0fus" % (e, nstep[e], tot_idle[e] / 1e3,
                                                    endt[e] / 1e3) for e in sorted(nstep)))
        self.steps = [self.steps[i] for i in order]

    def finalize(self, eng_sems):
        last_w = {}
        readers = {}
        for i, st in enumerate(self.steps):
            ei = st["eng"]
            deps = set()

            def need(j):
                ej = self.steps[j]["eng"]
                return ej != ei or ej == "sp"

            for rr in st["r"]:
                j = last_w.get(rr)
                if j is not None:
                    deps.add(j)
            for rr in st["w"]:
                j = last_w.get(rr)
                if j is not None and need(j):
                    deps.add(j)
                for k in readers.get(rr, ()):
                    if need(k):
                        deps.add(k)
            for rr in st["r"]:
                readers.setdefault(rr, []).append(i)
            for rr in st["w"]:
                last_w[rr] = i
                readers[rr] = []
            st["deps"] = deps
        for st in self.steps:
            for j in st["deps"]:
                self.steps[j]["inc"] = True
            if st["eng"] == "sp":
                st["inc"] = True
        for st in self.steps:
            if st["sem"] is None:
                st["sem"] = eng_sems[st["eng"]]
        for st in self.steps:
            st["val"] = st["sem"].bump() if st["inc"] else None
        for st in self.steps:
            wmap = {}
            for j in st["deps"]:
                sj = self.steps[j]
                k = id(sj["sem"])
                if k not in wmap or wmap[k][1] < sj["val"]:
                    wmap[k] = (sj["sem"], sj["val"])
            st["waits"] = list(wmap.values()) + st["xwaits"]

    def run_engine(self, name, eng):
        waited = {}
        issued = {}
        for st in self.steps:
            if st["eng"] != name:
                continue
            if name == "sp" and st["sem"] is not None and st["sem"].dma:
                s = st["sem"]
                n = issued.get(id(s), 0)
                if n > 0 and waited.get(id(s), -1) < 16 * n:
                    eng.wait_ge(s.h, 16 * n)
                    waited[id(s)] = 16 * n
                issued[id(s)] = n + 1
            for (s, v) in st["waits"]:
                if waited.get(id(s), -1) < v:
                    eng.wait_ge(s.h, v)
                    waited[id(s)] = v
            inst = st["fn"](eng)
            if st["inc"]:
                inst.then_inc(st["sem"].h, 16 if st["sem"].dma else 1)


class PackView:
    def __init__(self, ap, name):
        self._ap = ap
        self.name = name

    def __getitem__(self, idx):
        return self._ap[idx]


def build_nc():
    nc = bass.Bass("TRN2", target_bir_lowering=False)

    dx = nc.dram_tensor("xT", [D, T], F32, kind="ExternalInput").ap()
    dout = nc.dram_tensor("outT", [D, T], F32, kind="ExternalOutput").ap()
    dcb = nc.dram_tensor("cbd", [128, CBW], F32, kind="ExternalInput").ap()
    dw1T = nc.dram_tensor("w1T", [D, HID], F32, kind="ExternalInput").ap()
    dw2T = nc.dram_tensor("w2T", [HID, D], F32, kind="ExternalInput").ap()
    dsmall = {}
    for nme, shp in [("wqa", [D, H]), ("wka", [D, H]), ("wva", [D, H]),
                     ("woa", [H, D]), ("wkbT", [D, H]), ("wvbT", [D, H]),
                     ("wqbT", [D, H]), ("irw1T", [D, H]), ("pwT", [D, 32]),
                     ("nmw1T", [H, 2 * H]), ("nmw2T", [2 * H, H]),
                     ("irw2T", [H, 1]), ("wobT", [H, D])]:
        dsmall[nme] = nc.dram_tensor(nme, shp, F32, kind="ExternalInput").ap()
    dbias = {}
    for nme, shp in [("b1c", [128, 32]), ("b2c", [128, 8]), ("nmb1c", [128, 1]),
                     ("nmb2c", [H, 1]), ("irb1c", [H, 1]), ("irb2", [1, 1]),
                     ("normw", [H, 1]), ("alphaf", [1, 1]), ("alphas", [1, 1])]:
        dbias[nme] = nc.dram_tensor("d_" + nme, shp, F32, kind="ExternalInput").ap()
    dw1q = nc.dram_tensor("w1q_scr", [D, HID], BF16, kind="Internal").ap()
    dw2q = nc.dram_tensor("w2q_scr", [HID, D], BF16, kind="Internal").ap()

    def sb(name, shape, dtype):
        return nc.alloc_sbuf_tensor(name, shape, dtype)

    cbt = sb("cbt", [128, CBW], F32)
    xbuf = [sb(f"xb{i}", [128, 8 * P], F32) for i in range(3)]
    xr = [[sb(f"xr{s}{i}", [128, P], F32R) for i in range(2)] for s in range(2)]
    xsq_sh = [sb(f"xsq{i}", [128, P], F32R) for i in range(2)]
    xsq = [xsq_sh, xsq_sh]
    mrep = [sb(f"mrep{s}", [128, P], F32) for s in range(2)]
    rrep = [sb(f"rrep{s}", [128, P], F32) for s in range(2)]
    t1sh = sb("t1s0", [128, P], F32)
    t2sh = sb("t2s0", [128, P], F32)
    t1s_ = [t1sh, t1sh]
    t2s_ = [t2sh, t2sh]
    p1s = sb("p1s", [128, P], F32)
    p2s = sb("p2s", [128, P], F32)
    xqt = [sb(f"xq{i}", [128, P], BF16) for i in range(8)]
    hqt = [sb(f"hq{i}", [128, P], BF16) for i in range(32)]
    def hq_res(oc):
        return f"hq{oc}"
    slab = [sb(f"slab{i}", [128, 2048], BF16) for i in range(4)]
    qst = [sb(f"qst{i}", [128, 512], BF16) for i in range(2)]
    stg = [xbuf[2]]

    wqa_r = sb("wqa_r", [128, 8 * H], F32R)
    wka_r = sb("wka_r", [128, 8 * H], F32R)
    wva_r = sb("wva_r", [128, 8 * H], F32R)
    woa_r = sb("woa_r", [H, D], BF16)
    wkb_r = sb("wkb_r", [128, 8 * H], F32R)
    wvb_r = sb("wvb_r", [128, 8 * H], F32R)
    wqb_r = sb("wqb_r", [128, 8 * H], F32R)
    irw1_r = sb("irw1_r", [128, 8 * H], F32R)
    pw_r = sb("pw_r", [128, 8 * 32], F32R)
    nmw1_r = sb("nmw1_r", [H, 2 * H], F32R)
    nmw2_r = sb("nmw2_r", [2 * H, H], F32R)
    irw2_r = sb("irw2_r", [H, 8], F32R)
    irw2f = sb("irw2f", [H, 1], F32)
    wob_r = sb("wob_r", [H, D], BF16)
    ones_r = sb("ones_r", [128, 128], F32R)
    ident_r = sb("ident_r", [128, 128], F32R)

    cs_sum = sb("cs_sum", [H, 8], F32)
    cspw = sb("cspw", [32, 1], F32)

    b1i = sb("b1i", [128, 32], F32)
    b2qc = sb("b2qc", [128, 8], F32)
    nmb1c = sb("nmb1c", [128, 1], F32)
    nmb2c = sb("nmb2c", [H, 1], F32)
    irb1c = sb("irb1c", [H, 1], F32)
    irb2c = sb("irb2c", [128, 1], F32)
    nw_c = sb("nw_c", [H, 1], F32)

    lnA = [sb(f"lnA{m}", [128, 1], F32) for m in range(2)]
    acol = [sb(f"acol{m}", [128, 1], F32) for m in range(2)]
    bcol = [sb(f"bcol{m}", [128, 1], F32) for m in range(2)]
    g1c = [sb(f"g1c{m}", [128, 1], F32) for m in range(2)]
    g1n = [sb(f"g1n{m}", [128, 1], F32) for m in range(2)]
    g2c = [sb(f"g2c{m}", [128, 1], F32) for m in range(2)]
    g3c = [sb(f"g3c{m}", [128, 1], F32) for m in range(2)]
    aCc = [sb(f"aCc{m}", [128, 1], F32) for m in range(2)]
    DS_ = [sb(f"DS{m}", [128, 128], F32) for m in range(2)]
    DSB = [sb(f"DSB{m}", [128, 128], F32) for m in range(2)]
    DSBT = [sb(f"DSBT{m}", [128, 128], F32) for m in range(2)]
    DI_ = [sb(f"DI{m}", [128, 128], F32) for m in range(2)]

    F0m = [sb(f"F0m{m}", [H, H], F32R) for m in range(2)]
    Za = sb("Za", [H, H], F32R)
    zcol = sb("zcol", [H, 8], F32R)

    qTa = sb("qTa", [H, P], F32R)
    kTa = sb("kTa", [H, P], F32R)
    vTa = sb("vTa", [H, P], F32R)
    oTa = sb("oTa", [H, P], BF16)
    PTs = sb("PTs", [128, 128], F32R)
    QKs = sb("QKs", [128, 128], F32)
    deni = sb("deni", [128, 1], F32)
    dent = sb("dent", [128, 1], F32)
    ztmp = sb("ztmp", [H, 1], F32)
    Ktm_a = sb("Ktm_a", [128, H], F32R)
    Vtm_a = sb("Vtm_a", [128, H], F32R)
    Otm_a = sb("Otm_a", [128, H], F32R)

    kbT = sb("kbT", [H, P], F32R)
    qbT = sb("qbT", [H, P], F32R)
    vbT = sb("vbT", [H, P], F32R)
    nuT = sb("nuT", [H, P], F32)
    gcr = sb("gcr", [H, P], F32R)
    sc1r = sb("sc1r", [128, P], F32R)
    obT = sb("obT", [H, P], BF16)
    pwC = sb("pwC", [32, P], F32R)
    pwt = sb("pwt", [128, NSC * 3], F32)
    ucol = sb("ucol", [128, NSC], F32)
    urow = sb("urow", [1, 128], F32R)
    unrow = sb("unrow", [1, 128], F32R)
    uncol = sb("uncol", [128, 1], F32)
    Nfm = sb("Nfm", [H, P], F32)
    nsinit = sb("nsinit", [H, 1], F32)
    nsA = sb("nsA", [H, 128], F32)
    nsB = sb("nsB", [H, 128], F32)
    Ntm = sb("Ntm", [128, H], F32)
    nscr = sb("nscr", [128, 128], F32)

    Npw = [[sb(f"Npw{m}{i}", [128, 128], F32R) for i in range(2)] for m in range(2)]
    NpwT = [[sb(f"NpwT{m}{i}", [128, 128], F32R) for i in range(2)] for m in range(2)]
    Ut = [[sb(f"Ut{m}{i}", [128, H], F32R) for i in range(2)] for m in range(2)]
    Rm = [sb(f"Rm{m}", [128, H], F32R) for m in range(2)]
    WxT = [sb(f"WxT{m}", [128, 128], F32R) for m in range(2)]
    WiT = [sb(f"WiT{m}", [128, 128], F32R) for m in range(2)]
    Ktm = sb("Ktm", [128, H], F32R)
    Ktg = sb("Ktg", [128, H], F32R)
    Vtm = sb("Vtm", [128, H], F32)
    rdf = sb("rdf", [128, H], F32)
    rds = sb("rds", [128, H], F32)
    predt = sb("predt", [128, H], F32)
    dsq = sb("dsq", [128, H], F32)
    errc = sb("errc", [128, 1], F32)
    oatt = sb("oatt", [128, H], F32)
    onrm = sb("onrm", [128, H], F32R)
    msc = sb("msc", [128, 1], F32)

    pbt = [nc.alloc_psum_tensor(f"pb{i}", [128, 512], F32) for i in range(8)]
    pbr = [p.bitcast(F32R) for p in pbt]
    PB = [f"pb{i}" for i in range(8)]

    def cbv(name, rows=128):
        o, w = _OFF[name]
        return cbt[0:rows, o:o + w]

    TK = Tracker()
    sem_cms = []

    def mksem(name, dma=False):
        cm = nc.semaphore(name)
        h = cm.__enter__()
        sem_cms.append(cm)
        return SemT(h, name, dma)

    s_x = mksem("s_x", dma=True)
    s_sl = [mksem("s_sl0", dma=True), mksem("s_sl1", dma=True),
            mksem("s_sl2", dma=True), mksem("s_sl3", dma=True)]
    s_sw = mksem("s_sw", dma=True)
    s_qs = [mksem("s_qs0", dma=True), mksem("s_qs1", dma=True),
            mksem("s_qs2", dma=True)]
    s_qt = [mksem("s_qt0", dma=True), mksem("s_qt1", dma=True)]
    s_out = mksem("s_out", dma=True)
    s_pe = mksem("s_pe")
    s_dv = mksem("s_dv")
    s_ac = mksem("s_ac")
    s_gp = mksem("s_gp")
    ENG_SEMS = {"pe": s_pe, "dv": s_dv, "ac": s_ac, "gp": s_gp}

    E = TK.emit

    E("sp", lambda e: e.dma_start(out=cbt[:], in_=dcb), w=["cbt"], sem=s_sw)
    xsrc = dx.rearrange("(ic p) t -> p ic t", p=128)
    odst = dout.rearrange("(ic p) t -> p ic t", p=128)

    def x_load(c, buf):
        E("sp", lambda e, c=c, buf=buf: e.dma_start(
            out=xbuf[buf][:].rearrange("p (ic t) -> p ic t", ic=8),
            in_=xsrc[:, :, c * P:(c + 1) * P]), lat=6000,
          w=[f"xb{buf}"], sem=s_x)

    x_load(0, 0)

    def wview(d):
        k = d.shape[1]
        return (d.rearrange("(ic p) k -> p ic k", p=128), k)

    small_jobs = [
        (dsmall["wqa"], wqa_r, 128, 8 * H, None, True),
        (dsmall["wka"], wka_r, 128, 8 * H, None, True),
        (dsmall["wva"], wva_r, 128, 8 * H, None, True),
        (dsmall["wkbT"], wkb_r, 128, 8 * H, None, True),
        (dsmall["wvbT"], wvb_r, 128, 8 * H, None, True),
        (dsmall["wqbT"], wqb_r, 128, 8 * H, None, True),
        (dsmall["irw1T"], irw1_r, 128, 8 * H, None, True),
        (dsmall["pwT"], pw_r, 128, 8 * 32, None, True),
        (dsmall["woa"], woa_r, H, D, None, False),
        (dsmall["nmw1T"], nmw1_r, H, 2 * H, None, False),
        (dsmall["nmw2T"], nmw2_r, 2 * H, H, None, False),
        (dsmall["irw2T"], irw2f, H, 1, None, False),
        (dsmall["wobT"], wob_r, H, D, "normw", False),
    ]
    E("sp", lambda e: e.dma_start(out=nw_c[:], in_=dbias["normw"]), w=["nw_c"], sem=s_sw)
    for i, (srcd, dst, rr, cc, fold, chunked) in enumerate(small_jobs):
        if chunked:
            sv, kk = wview(srcd)
            E("sp", lambda e, sv=sv, kk=kk:
              e.dma_start(out=stg[0][:].rearrange("p (ic k) -> p ic k", k=kk)[:, 0:8, :],
                          in_=sv),
              r=[], w=["xb2"], sem=s_sw)
        else:
            E("sp", lambda e, srcd=srcd, rr=rr, cc=cc:
              e.dma_start(out=stg[0][0:rr, 0:cc], in_=srcd),
              r=[], w=["xb2"], sem=s_sw)
        if fold == "normw":
            E("dv", lambda e, dst=dst, rr=rr, cc=cc:
              e.tensor_scalar(out=dst[0:rr, 0:cc], in0=stg[0][0:rr, 0:cc],
                              scalar1=nw_c[:], scalar2=None, op0=ALU.mult),
              r=["xb2", "nw_c"], w=[dst.name])
        else:
            E("dv", lambda e, dst=dst, rr=rr, cc=cc:
              e.tensor_copy(out=dst[0:rr, 0:cc], in_=stg[0][0:rr, 0:cc]),
              r=["xb2"], w=[dst.name])
    E("dv", lambda e: e.tensor_scalar(out=irw2_r[:], in0=cbv("ones128", rows=H)[:, 0:8],
                                      scalar1=irw2f[:], scalar2=None, op0=ALU.mult),
      r=["cbt", "irw2f"], w=["irw2_r"])
    E("dv", lambda e: e.tensor_copy(out=ones_r[:], in_=cbv("ones128")),
      r=["cbt"], w=["ones_r"])
    E("dv", lambda e: e.tensor_copy(out=ident_r[:], in_=cbv("ident")),
      r=["cbt"], w=["ident_r"])

    for nme, dst in [("b1c", b1i), ("b2c", b2qc), ("nmb1c", nmb1c), ("nmb2c", nmb2c),
                     ("irb1c", irb1c)]:
        E("sp", lambda e, nme=nme, dst=dst: e.dma_start(out=dst[:], in_=dbias[nme]),
          w=[dst.name], sem=s_sw)
    E("sp", lambda e: e.dma_start(out=irb2c[:], in_=dbias["irb2"].broadcast_to((128, 1)),
                                  allow_slow_non_contiguous=True),
      w=["irb2c"], sem=s_sw)
    E("dv", lambda e: e.tensor_scalar(out=b1i[:], in0=b1i[:], scalar1=64.0, scalar2=M_RND,
                                      op0=ALU.mult, op1=ALU.add), r=["b1i"], w=["b1i"])
    E("dv", lambda e: e.tensor_scalar(out=b1i[:], in0=b1i[:], scalar1=M_RND, scalar2=127.0,
                                      op0=ALU.subtract, op1=ALU.min), r=["b1i"], w=["b1i"])
    E("dv", lambda e: e.tensor_scalar(out=b1i[:], in0=b1i[:], scalar1=-128.0, scalar2=0.25,
                                      op0=ALU.max, op1=ALU.mult), r=["b1i"], w=["b1i"])
    E("dv", lambda e: e.tensor_scalar(out=b2qc[:], in0=b2qc[:], scalar1=64.0, scalar2=M_RND,
                                      op0=ALU.mult, op1=ALU.add), r=["b2qc"], w=["b2qc"])
    E("dv", lambda e: e.tensor_scalar(out=b2qc[:], in0=b2qc[:], scalar1=M_RND, scalar2=127.0,
                                      op0=ALU.subtract, op1=ALU.min), r=["b2qc"], w=["b2qc"])
    E("dv", lambda e: e.tensor_scalar(out=b2qc[:], in0=b2qc[:], scalar1=-128.0,
                                      scalar2=1.0 / 64.0, op0=ALU.max, op1=ALU.mult),
      r=["b2qc"], w=["b2qc"])

    cs_specs = [(wqa_r, H, 0), (wka_r, H, 1), (wva_r, H, 2), (wkb_r, H, 3),
                (wvb_r, H, 4), (wqb_r, H, 5), (irw1_r, H, 6)]
    for wt, od, j in cs_specs:
        for ic in range(8):
            E("pe", lambda e, wt=wt, od=od, j=j, ic=ic:
              e.matmul(pbt[6][0:od, 8 * j:8 * j + 8], wt[:, ic * od:(ic + 1) * od],
                       ones_r[:, 0:8], start=(ic == 0), stop=(ic == 7)),
              r=[wt.name, "ones_r"], w=[PB[6]])
    for ic in range(8):
        E("pe", lambda e, ic=ic:
          e.matmul(pbt[6][0:32, 56:64], pw_r[:, ic * 32:(ic + 1) * 32],
                   ones_r[:, 0:8], start=(ic == 0), stop=(ic == 7)),
          r=["pw_r", "ones_r"], w=[PB[6]])
    for j in range(7):
        E("dv", lambda e, j=j: e.tensor_copy(out=cs_sum[0:H, j:j + 1],
                                             in_=pbt[6][0:H, 8 * j:8 * j + 1]),
          r=[PB[6]], w=["cs_sum"])
    E("dv", lambda e: e.tensor_copy(out=cspw[:], in_=pbt[6][0:32, 56:57]),
      r=[PB[6]], w=["cspw"])

    def cs(j):
        return cs_sum[0:H, j:j + 1]

    for m in range(2):
        dal = dbias["alphaf"] if m == 0 else dbias["alphas"]
        E("sp", lambda e, dal=dal, m=m:
          e.dma_start(out=acol[m][:], in_=dal.broadcast_to((128, 1)),
                      allow_slow_non_contiguous=True),
          w=[acol[m].name], sem=s_sw)
        E("ac", lambda e, m=m: e.activation(out=lnA[m][:], in_=acol[m][:], func=AF.Ln),
          r=[acol[m].name], w=[lnA[m].name])
        E("dv", lambda e, m=m: e.tensor_scalar(out=bcol[m][:], in0=acol[m][:], scalar1=-1.0,
                                               scalar2=1.0, op0=ALU.mult, op1=ALU.add),
          r=[acol[m].name], w=[bcol[m].name])

        def gcol(dst, iota_name, m=m):
            E("dv", lambda e, dst=dst, iota_name=iota_name, m=m:
              e.tensor_scalar(out=dst[:], in0=cbv(iota_name), scalar1=lnA[m][:],
                              scalar2=None, op0=ALU.mult),
              r=["cbt", lnA[m].name], w=[dst.name])
            E("ac", lambda e, dst=dst: e.activation(out=dst[:], in_=dst[:], func=AF.Exp),
              r=[dst.name], w=[dst.name])

        gcol(g1c[m], "iota0")
        gcol(g2c[m], "iota1")
        gcol(g3c[m], "iotaC")
        E("dv", lambda e, m=m: e.tensor_scalar(out=aCc[m][:], in0=lnA[m][:], scalar1=float(C),
                                               scalar2=None, op0=ALU.mult),
          r=[lnA[m].name], w=[aCc[m].name])
        E("ac", lambda e, m=m: e.activation(out=aCc[m][:], in_=aCc[m][:], func=AF.Exp),
          r=[aCc[m].name], w=[aCc[m].name])
        E("dv", lambda e, m=m: e.tensor_tensor(out=g1n[m][:], in0=g1c[m][:], in1=bcol[m][:],
                                               op=ALU.mult),
          r=[g1c[m].name, bcol[m].name], w=[g1n[m].name])
        E("dv", lambda e, m=m: e.tensor_scalar(out=g1n[m][:], in0=g1n[m][:], scalar1=-1.0,
                                               scalar2=None, op0=ALU.mult),
          r=[g1n[m].name], w=[g1n[m].name])

        def dmat(dst, expo_name, mask_name, scale_beta, m=m):
            E("dv", lambda e, dst=dst, expo_name=expo_name, m=m:
              e.tensor_scalar(out=dst[:], in0=cbv(expo_name), scalar1=lnA[m][:],
                              scalar2=None, op0=ALU.mult),
              r=["cbt", lnA[m].name], w=[dst.name])
            E("ac", lambda e, dst=dst: e.activation(out=dst[:], in_=dst[:], func=AF.Exp),
              r=[dst.name], w=[dst.name])
            E("dv", lambda e, dst=dst, mask_name=mask_name:
              e.tensor_tensor(out=dst[:], in0=dst[:], in1=cbv(mask_name), op=ALU.mult),
              r=[dst.name, "cbt"], w=[dst.name])
            if scale_beta:
                E("dv", lambda e, dst=dst, m=m:
                  e.tensor_scalar(out=dst[:], in0=dst[:], scalar1=bcol[m][:], scalar2=None,
                                  op0=ALU.mult),
                  r=[dst.name, bcol[m].name], w=[dst.name])

        dmat(DS_[m], "expoSU", "maskSU", False)
        dmat(DSB[m], "expoSU", "maskSU", True)
        dmat(DSBT[m], "expoSL", "maskSL", True)
        dmat(DI_[m], "expoIU", "maskIU", False)

    for tns, rr_x, cc_x in [(F0m[0], H, H), (F0m[1], H, H), (Za, H, H), (zcol, H, 8)]:
        E("dv", lambda e, tns=tns, rr_x=rr_x, cc_x=cc_x:
          e.tensor_scalar(out=tns[0:rr_x, 0:cc_x], in0=cbv("zero64", rows=rr_x)[:, 0:cc_x],
                          scalar1=0.0, scalar2=None, op0=ALU.mult),
          r=["cbt"], w=[tns.name])
    E("dv", lambda e: e.tensor_scalar(out=nsinit[:], in0=cbv("zerocol", rows=H),
                                      scalar1=0.0, scalar2=None, op0=ALU.mult),
      r=["cbt"], w=["nsinit"])

    def quant_proc(b, cols, stores):
        xbn = f"xb{b}"
        E("ac", lambda e, cols=cols, b=b:
          e.activation(out=xbuf[b][:, 0:cols], in_=xbuf[b][:, 0:cols], func=AF.Relu,
                       scale=64.0, bias=cbv("Mcol")),
          r=[xbn, "cbt"], w=[xbn], cost=int(cols * 0.9))
        E("dv", lambda e, cols=cols, b=b:
          e.tensor_scalar(out=xbuf[b][:, 0:cols], in0=xbuf[b][:, 0:cols], scalar1=M_RND,
                          scalar2=127.0, op0=ALU.subtract, op1=ALU.min),
          r=[xbn], w=[xbn], cost=int(cols * 0.55))
        for q, (dst_ap, region, j) in enumerate(stores):
            E("dv", lambda e, q=q, j=j, b=b:
              e.tensor_scalar(out=qst[j][:], in0=xbuf[b][:, q * 512:(q + 1) * 512],
                              scalar1=-128.0, scalar2=None, op0=ALU.max),
              r=[xbn], w=[f"qst{j}"], cost=330)
            E("sp", lambda e, dst_ap=dst_ap, j=j:
              e.dma_start(out=dst_ap, in_=qst[j][:]),
              r=[f"qst{j}"], w=[region], sem=s_qt[j], lat=800)

    w1src = dw1T.rearrange("(s p) c -> p s c", p=128)
    for og in range(8):
        b = 1 + og % 2
        E("sp", lambda e, og=og, b=b:
          e.dma_start(out=xbuf[b][:].rearrange("p (s c) -> p s c", s=8),
                      in_=w1src[:, :, og * 512:(og + 1) * 512]),
          w=[f"xb{b}"], sem=s_qs[og % 3], lat=6000)
        quant_proc(b, HID, [
            (dw1q[s * 128:(s + 1) * 128, og * 512:(og + 1) * 512],
             f"dw1q_q{og}", s % 2) for s in range(8)])
    for s in range(32):
        b = 1 + s % 2
        E("sp", lambda e, s=s, b=b:
          e.dma_start(out=xbuf[b][:, 0:D], in_=dw2T[s * 128:(s + 1) * 128, 0:D]),
          w=[f"xb{b}"], sem=s_qs[s % 3], lat=2000)
        quant_proc(b, D, [
            (dw2q[s * 128:(s + 1) * 128, q * 512:(q + 1) * 512],
             f"dw2q_r{s // 2}", (s // 2) % 2) for q in range(2)])
    x_load(1, 1)

    def emit_ln(buf, S, want_r16, proj_specs, sum_bank, sq_bank, passes=(0,)):
        xbn = f"xb{buf}"
        mr, rr_ = mrep[S], rrep[S]
        for pss in passes:
            specs = [sp_ for sp_ in proj_specs if sp_.get("pass", 0) == pss]
            for ic in range(8):
                pp = ic % 2
                xrt, xst = xr[S][pp], xsq[S][pp]
                E("ac", lambda e, ic=ic, xrt=xrt, buf=buf:
                  e.activation(out=xrt[:], in_=xbuf[buf][:, ic * P:(ic + 1) * P],
                               func=AF.Copy),
                  r=[xbn], w=[xrt.name])
                if pss == 0:
                    E("ac", lambda e, ic=ic, xst=xst, buf=buf:
                      e.activation(out=xst[:], in_=xbuf[buf][:, ic * P:(ic + 1) * P],
                                   func=AF.Square),
                      r=[xbn], w=[xst.name])
                    E("pe", lambda e, ic=ic, xrt=xrt:
                      e.matmul(pbt[sum_bank][:], ones_r[:], xrt[:],
                               start=(ic == 0), stop=(ic == 7)),
                      r=["ones_r", xrt.name], w=[PB[sum_bank]])
                    E("pe", lambda e, ic=ic, xst=xst:
                      e.matmul(pbt[sq_bank][:], ones_r[:], xst[:],
                               start=(ic == 0), stop=(ic == 7)),
                      r=["ones_r", xst.name], w=[PB[sq_bank]])
                for spec in specs:
                    E("pe", lambda e, ic=ic, xrt=xrt, spec=spec:
                      e.matmul(spec["out"], spec["lhsT"](ic), xrt[:],
                               start=(ic == 0), stop=(ic == 7)),
                      r=[spec["lres"], xrt.name], w=[spec["bank"]])
            if pss == 0:
                E("dv", lambda e, mr=mr: e.tensor_scalar(out=mr[:], in0=pbt[sum_bank][:],
                                                         scalar1=1.0 / D, scalar2=None,
                                                         op0=ALU.mult),
                  r=[PB[sum_bank]], w=[mr.name])
                E("dv", lambda e, rr_=rr_: e.tensor_scalar(out=rr_[:], in0=pbt[sq_bank][:],
                                                           scalar1=1.0 / D, scalar2=None,
                                                           op0=ALU.mult),
                  r=[PB[sq_bank]], w=[rr_.name])
                E("dv", lambda e, mr=mr: e.tensor_tensor(out=p2s[:], in0=mr[:], in1=mr[:],
                                                         op=ALU.mult),
                  r=[mr.name], w=["p2s"])
                E("dv", lambda e, rr_=rr_: e.tensor_tensor(out=rr_[:], in0=rr_[:], in1=p2s[:],
                                                           op=ALU.subtract),
                  r=[rr_.name, "p2s"], w=[rr_.name])
                E("ac", lambda e, rr_=rr_: e.activation(out=rr_[:], in_=rr_[:], func=AF.Ln,
                                                        bias=cbv("epscol")),
                  r=[rr_.name, "cbt"], w=[rr_.name])
                if want_r16:
                    E("ac", lambda e, rr_=rr_: e.activation(out=rr_[:], in_=rr_[:],
                                                            func=AF.Exp, scale=-0.5,
                                                            bias=cbv("ln16col")),
                      r=[rr_.name, "cbt"], w=[rr_.name])
                else:
                    E("ac", lambda e, rr_=rr_: e.activation(out=rr_[:], in_=rr_[:],
                                                            func=AF.Exp, scale=-0.5),
                      r=[rr_.name], w=[rr_.name])

    def emit_proj_pass(buf, S, specs):
        xbn = f"xb{buf}"
        for ic in range(8):
            pp = ic % 2
            xrt = xr[S][pp]
            E("dv", lambda e, ic=ic, xrt=xrt, buf=buf:
              e.tensor_copy(out=xrt[:], in_=xbuf[buf][:, ic * P:(ic + 1) * P]),
              r=[xbn], w=[xrt.name])
            for spec in specs:
                E("pe", lambda e, ic=ic, xrt=xrt, spec=spec:
                  e.matmul(spec["out"], spec["lhsT"](ic), xrt[:],
                           start=(ic == 0), stop=(ic == 7)),
                  r=[spec["lres"], xrt.name], w=[spec["bank"]])

    def fold_correct(S, dst_ap, dst_res, bank, psum_ap, cscol, cs_res, rows):
        mr, rr_ = mrep[S], rrep[S]
        t2 = t2s_[S]
        E("dv", lambda e: e.tensor_scalar(out=t2[0:rows, :], in0=mr[0:rows, :],
                                          scalar1=cscol, scalar2=None, op0=ALU.mult),
          r=[mr.name, cs_res], w=[t2.name])
        E("dv", lambda e: e.tensor_tensor(out=t2[0:rows, :], in0=psum_ap,
                                          in1=t2[0:rows, :], op=ALU.subtract),
          r=[bank, t2.name], w=[t2.name])
        E("dv", lambda e: e.tensor_tensor(out=dst_ap, in0=t2[0:rows, :],
                                          in1=rr_[0:rows, :], op=ALU.mult),
          r=[t2.name, rr_.name], w=[dst_res])

    def phi_inplace(S, dst, dst_res):
        t1, t2 = t1s_[S], t2s_[S]
        E("dv", lambda e: e.tensor_scalar(out=t1[0:H, :], in0=t2[0:H, :], scalar1=0.0,
                                          scalar2=None, op0=ALU.min),
          r=[t2.name], w=[t1.name])
        E("ac", lambda e: e.activation(out=t1[0:H, :], in_=t1[0:H, :], func=AF.Exp),
          r=[t1.name], w=[t1.name])
        E("dv", lambda e, dst=dst: e.scalar_tensor_tensor(out=dst[:], in0=t2[0:H, :],
                                                          scalar=0.0, in1=t1[0:H, :],
                                                          op0=ALU.max, op1=ALU.add),
          r=[t1.name, t2.name], w=[dst_res])

    def l2norm_cols(S, raw_bank, raw_psum, cscol, cs_res, dst, dst_res, ss_bank):
        mr, rr_ = mrep[S], rrep[S]
        t1, t2 = t1s_[S], t2s_[S]
        E("dv", lambda e: e.tensor_scalar(out=t2[0:H, :], in0=mr[0:H, :],
                                          scalar1=cscol, scalar2=None, op0=ALU.mult),
          r=[mr.name, cs_res], w=[t2.name])
        E("dv", lambda e: e.tensor_tensor(out=t2[0:H, :], in0=raw_psum, in1=t2[0:H, :],
                                          op=ALU.subtract),
          r=[raw_bank, t2.name], w=[t2.name])
        E("dv", lambda e: e.tensor_tensor(out=t2[0:H, :], in0=t2[0:H, :],
                                          in1=rr_[0:H, :], op=ALU.mult),
          r=[t2.name, rr_.name], w=[t2.name])
        E("ac", lambda e: e.activation(out=sc1r[0:H, :], in_=t2[0:H, :], func=AF.Square),
          r=[t2.name], w=["sc1r"])
        E("pe", lambda e: e.matmul(pbt[ss_bank][:], ones_r[0:H, :], sc1r[0:H, :],
                                   start=True, stop=True),
          r=["ones_r", "sc1r"], w=[PB[ss_bank]])
        E("ac", lambda e: e.activation(out=t1[:], in_=pbt[ss_bank][:], func=AF.Ln),
          r=[PB[ss_bank]], w=[t1.name])
        E("ac", lambda e: e.activation(out=t1[:], in_=t1[:], func=AF.Exp, scale=0.5),
          r=[t1.name], w=[t1.name])
        E("dv", lambda e: e.tensor_scalar(out=t1[:], in0=t1[:], scalar1=1e-12,
                                          scalar2=None, op0=ALU.max),
          r=[t1.name], w=[t1.name])
        E("dv", lambda e: e.reciprocal(out=t1[:], in_=t1[:]), r=[t1.name], w=[t1.name])
        E("dv", lambda e, dst=dst: e.tensor_tensor(out=dst[:], in0=t2[0:H, :],
                                                   in1=t1[0:H, :], op=ALU.mult),
          r=[t1.name, t2.name], w=[dst_res])

    def emit_ffn_p1(buf, S):
        xbn = f"xb{buf}"
        mr, r16t = mrep[S], rrep[S]
        t1 = t1s_[S]
        emit_ln(buf, S, want_r16=True, proj_specs=[], sum_bank=0, sq_bank=1)
        for ic in range(8):
            E("dv", lambda e, ic=ic, buf=buf, mr=mr:
              e.tensor_tensor(out=t1[:], in0=xbuf[buf][:, ic * P:(ic + 1) * P],
                              in1=mr[:], op=ALU.subtract),
              r=[xbn, mr.name], w=[t1.name])
            E("dv", lambda e, r16t=r16t: e.tensor_tensor(out=t1[:], in0=t1[:], in1=r16t[:],
                                                         op=ALU.mult),
              r=[t1.name, r16t.name], w=[t1.name])
            E("dv", lambda e: e.tensor_scalar(out=t1[:], in0=t1[:], scalar1=M_RND,
                                              scalar2=M_RND, op0=ALU.add, op1=ALU.subtract),
              r=[t1.name], w=[t1.name])
            E("dv", lambda e, ic=ic: e.tensor_scalar(out=xqt[ic][:], in0=t1[:], scalar1=127.0,
                                                     scalar2=-128.0, op0=ALU.min, op1=ALU.max),
              r=[t1.name], w=[f"xq{ic}"])

    def emit_ffn_p2(buf, S):
        xbn = f"xb{buf}"
        w1v = dw1q.rearrange("(ic p) o -> p ic o", p=128)
        for og in range(8):
            slh = [(2 * og) % 4, (2 * og + 1) % 4]
            for h in range(2):
                E("sp", lambda e, og=og, h=h, sl=slh[h]:
                  e.dma_start(out=slab[sl][:].rearrange("p (ic o) -> p ic o", ic=4),
                              in_=w1v[:, 4 * h:4 * h + 4, og * 512:(og + 1) * 512]),
                  w=[f"slab{slh[h]}"], r=[f"dw1q_q{og}"],
                  sem=s_sl[slh[h]], lat=1500)
            for o4 in range(4):
                oc = og * 4 + o4
                bank = 2 + oc % 2
                for ic in range(8):
                    sl = slh[ic // 4]
                    E("pe", lambda e, sl=sl, o4=o4, ic=ic, bank=bank:
                      e.matmul(pbt[bank][:],
                               slab[sl][:, (ic % 4) * 512 + o4 * 128:
                                           (ic % 4) * 512 + o4 * 128 + 128],
                               xqt[ic][:], start=(ic == 0), stop=(ic == 7)),
                      r=[f"slab{sl}", f"xq{ic}"], w=[PB[bank]])
                E("ac", lambda e, oc=oc, bank=bank:
                  e.activation(out=p1s[:], in_=pbt[bank][:], func=AF.Relu,
                               scale=2.0 ** -6, bias=b1i[:, oc:oc + 1]),
                  r=[PB[bank], "b1i"], w=["p1s"])
                E("ac", lambda e: e.activation(out=p2s[:], in_=p1s[:], func=AF.Relu,
                                               bias=cbv("Mcol")),
                  r=["p1s", "cbt"], w=["p2s"])
                E("dv", lambda e, oc=oc: e.tensor_scalar(out=hqt[oc][:], in0=p2s[:],
                                                         scalar1=M_RND, scalar2=127.0,
                                                         op0=ALU.subtract, op1=ALU.min),
                  r=["p2s"], w=[hq_res(oc)], cost=400)
        for og in range(8):
            slh = [(2 * og) % 4, (2 * og + 1) % 4]
            for h in range(2):
                E("sp", lambda e, og=og, h=h, sl=slh[h]:
                  e.dma_start(out=slab[sl][:].rearrange("p (o4 j) -> p o4 j", o4=2),
                              in_=dw2q[og * 512 + h * 256:og * 512 + h * 256 + 256,
                                       :].rearrange("(o4 p) j -> p o4 j", p=128)),
                  w=[f"slab{slh[h]}"], r=[f"dw2q_r{2 * og + h}"],
                  sem=s_sl[slh[h]], lat=1500)
            for o4 in range(4):
                oc = og * 4 + o4
                sl = slh[o4 // 2]
                for jc in range(8):
                    E("pe", lambda e, sl=sl, o4=o4, oc=oc, jc=jc:
                      e.matmul(pbt[jc][:],
                               slab[sl][:, (o4 % 2) * 1024 + jc * 128:
                                           (o4 % 2) * 1024 + jc * 128 + 128],
                               hqt[oc][:], start=(oc == 0), stop=(oc == 31)),
                      r=[f"slab{sl}", hq_res(oc)], w=[PB[jc]])
        for jc in range(8):
            E("dv", lambda e, jc=jc, buf=buf:
              e.tensor_scalar(out=xbuf[buf][:, jc * P:(jc + 1) * P],
                              in0=xbuf[buf][:, jc * P:(jc + 1) * P],
                              scalar1=b2qc[:, jc:jc + 1], scalar2=None, op0=ALU.add),
              r=[xbn, "b2qc"], w=[xbn])
            E("dv", lambda e, jc=jc, buf=buf:
              e.scalar_tensor_tensor(out=xbuf[buf][:, jc * P:(jc + 1) * P],
                                     in0=pbt[jc][:], scalar=2.0 ** -10,
                                     in1=xbuf[buf][:, jc * P:(jc + 1) * P],
                                     op0=ALU.mult, op1=ALU.add),
              r=[PB[jc], xbn], w=[xbn])

    def emit_attn_a_pre(buf):
        proj = [
            dict(bank=PB[2], out=pbt[2][0:H, 0:P], lres="wqa_r",
                 lhsT=lambda ic: wqa_r[:, ic * H:(ic + 1) * H]),
            dict(bank=PB[3], out=pbt[3][0:H, 0:P], lres="wka_r",
                 lhsT=lambda ic: wka_r[:, ic * H:(ic + 1) * H]),
            dict(bank=PB[4], out=pbt[4][0:H, 0:P], lres="wva_r",
                 lhsT=lambda ic: wva_r[:, ic * H:(ic + 1) * H]),
        ]
        emit_ln(buf, 0, want_r16=False, proj_specs=proj, sum_bank=0, sq_bank=1)
        fold_correct(0, t2s_[0][0:H, :], "t2s0", PB[2], pbt[2][0:H, 0:P], cs(0), "cs_sum", H)
        phi_inplace(0, qTa, "qTa")
        fold_correct(0, t2s_[0][0:H, :], "t2s0", PB[3], pbt[3][0:H, 0:P], cs(1), "cs_sum", H)
        phi_inplace(0, kTa, "kTa")
        fold_correct(0, vTa[:], "vTa", PB[4], pbt[4][0:H, 0:P], cs(2), "cs_sum", H)

    def emit_attn_a_sc(sc):
        sm = {"pe": 250, "dv": 260, "ac": 280}
        E = lambda eng, fn, r=(), w=(), **kw: TK.emit(eng, fn, r=r, w=w,
                                                      cost=kw.pop("cost", sm.get(eng)), **kw)
        s0 = sc * C
        qs = qTa[:, s0:s0 + C]
        ks = kTa[:, s0:s0 + C]
        vs = vTa[:, s0:s0 + C]
        E("pe", lambda e, ks=ks: e.transpose(pbr[2][:, 0:H], ks, ident_r[0:H, 0:H]),
          r=["kTa", "ident_r"], w=[PB[2]])
        E("dv", lambda e: e.tensor_copy(out=Ktm_a[:], in_=pbr[2][:, 0:H]),
          r=[PB[2]], w=["Ktm_a"])
        E("pe", lambda e, vs=vs: e.transpose(pbr[2][:, 64:128], vs, ident_r[0:H, 0:H]),
          r=["vTa", "ident_r"], w=[PB[2]])
        E("dv", lambda e: e.tensor_copy(out=Vtm_a[:], in_=pbr[2][:, 64:128]),
          r=[PB[2]], w=["Vtm_a"])
        E("pe", lambda e, vs=vs, qs=qs: e.matmul(pbt[2][:, 128:256], vs, qs,
                                                 start=True, stop=True),
          r=["vTa", "qTa"], w=[PB[2]])
        E("dv", lambda e: e.tensor_tensor(out=PTs[:], in0=pbt[2][:, 128:256],
                                          in1=cbv("maskIU"), op=ALU.mult),
          r=[PB[2], "cbt"], w=["PTs"])
        E("pe", lambda e, qs=qs, ks=ks: e.matmul(pbt[2][:, 256:384], qs, ks,
                                                 start=True, stop=True),
          r=["qTa", "kTa"], w=[PB[2]])
        E("dv", lambda e: e.tensor_tensor(out=QKs[:], in0=pbt[2][:, 256:384],
                                          in1=cbv("maskIL"), op=ALU.mult),
          r=[PB[2], "cbt"], w=["QKs"])
        E("dv", lambda e: e.reduce_sum(out=deni[:], in_=QKs[:], axis=AXX),
          r=["QKs"], w=["deni"])
        E("pe", lambda e, qs=qs: e.matmul(pbt[2][:, 384:392], qs, zcol[:],
                                          start=True, stop=True),
          r=["qTa", "zcol"], w=[PB[2]])
        E("dv", lambda e: e.tensor_tensor(out=dent[:], in0=deni[:],
                                          in1=pbt[2][:, 384:385], op=ALU.add),
          r=["deni", PB[2]], w=["dent"])
        E("dv", lambda e: e.tensor_scalar(out=dent[:], in0=dent[:], scalar1=1e-6,
                                          scalar2=None, op0=ALU.max),
          r=["dent"], w=["dent"])
        E("dv", lambda e: e.reciprocal(out=dent[:], in_=dent[:]), r=["dent"], w=["dent"])
        E("pe", lambda e: e.matmul(pbt[3][:, 0:H], PTs[:], Ktm_a[:], start=True, stop=False),
          r=["PTs", "Ktm_a"], w=[PB[3]])
        E("pe", lambda e, qs=qs: e.matmul(pbt[3][:, 0:H], qs, Za[:], start=False, stop=True),
          r=["qTa", "Za"], w=[PB[3]])
        E("dv", lambda e: e.tensor_scalar(out=Otm_a[:], in0=pbt[3][:, 0:H],
                                          scalar1=dent[:], scalar2=None, op0=ALU.mult),
          r=[PB[3], "dent"], w=["Otm_a"])
        E("pe", lambda e: e.transpose(pbr[3][0:H, 64:192], Otm_a[:], ident_r[:]),
          r=["Otm_a", "ident_r"], w=[PB[3]])
        E("dv", lambda e, s0=s0: e.tensor_copy(out=oTa[:, s0:s0 + C],
                                               in_=pbr[3][0:H, 64:192]),
          r=[PB[3]], w=["oTa"])
        E("pe", lambda e: e.matmul(pbt[3][0:H, 192:256], Vtm_a[:], Ktm_a[:],
                                   start=True, stop=True),
          r=["Vtm_a", "Ktm_a"], w=[PB[3]])
        E("dv", lambda e: e.tensor_tensor(out=Za[:], in0=Za[:], in1=pbt[3][0:H, 192:256],
                                          op=ALU.add),
          r=["Za", PB[3]], w=["Za"])
        E("dv", lambda e, ks=ks: e.reduce_sum(out=ztmp[:], in_=ks, axis=AXX),
          r=["kTa"], w=["ztmp"])
        E("dv", lambda e: e.tensor_scalar(out=zcol[:], in0=zcol[:], scalar1=ztmp[:],
                                          scalar2=None, op0=ALU.add),
          r=["zcol", "ztmp"], w=["zcol"])

    def emit_attn_a_out(buf):
        xbn = f"xb{buf}"
        for dc in range(8):
            bank = 2 + (dc % 2)
            E("pe", lambda e, dc=dc, bank=bank:
              e.matmul(pbt[bank][:], woa_r[:, dc * 128:(dc + 1) * 128], oTa[:],
                       start=True, stop=True),
              r=["woa_r", "oTa"], w=[PB[bank]])
            E("dv", lambda e, dc=dc, bank=bank, buf=buf:
              e.tensor_tensor(out=xbuf[buf][:, dc * P:(dc + 1) * P],
                              in0=xbuf[buf][:, dc * P:(dc + 1) * P], in1=pbt[bank][:],
                              op=ALU.add),
              r=[xbn, PB[bank]], w=[xbn])

    def emit_attn_b_pre(buf):
        proj = [
            dict(bank=PB[5], out=pbt[5][0:H, 0:P], lres="wkb_r",
                 lhsT=lambda ic: wkb_r[:, ic * H:(ic + 1) * H]),
            dict(bank=PB[6], out=pbt[6][0:H, 0:P], lres="wvb_r",
                 lhsT=lambda ic: wvb_r[:, ic * H:(ic + 1) * H]),
            dict(bank=PB[7], out=pbt[7][0:H, 0:P], lres="wqb_r",
                 lhsT=lambda ic: wqb_r[:, ic * H:(ic + 1) * H]),
        ]
        proj2 = [
            dict(bank=PB[5], out=pbt[5][0:H, 0:P], lres="irw1_r",
                 lhsT=lambda ic: irw1_r[:, ic * H:(ic + 1) * H], **{"pass": 1}),
            dict(bank=PB[6], out=pbt[6][0:32, 0:P], lres="pw_r",
                 lhsT=lambda ic: pw_r[:, ic * 32:(ic + 1) * 32], **{"pass": 1}),
        ]
        emit_ln(buf, 1, want_r16=False, proj_specs=proj, sum_bank=0, sq_bank=1)
        l2norm_cols(1, PB[5], pbt[5][0:H, 0:P], cs(3), "cs_sum", kbT, "kbT", ss_bank=4)
        fold_correct(1, vbT[:], "vbT", PB[6], pbt[6][0:H, 0:P], cs(4), "cs_sum", H)
        l2norm_cols(1, PB[7], pbt[7][0:H, 0:P], cs(5), "cs_sum", qbT, "qbT", ss_bank=4)
        emit_proj_pass(buf, 1, proj2)
        fold_correct(1, t2s_[1][0:H, :], "t2s0", PB[5], pbt[5][0:H, 0:P], cs(6), "cs_sum", H)
        E("ac", lambda e: e.activation(out=gcr[:], in_=t2s_[1][0:H, :], func=AF.Gelu,
                                       bias=irb1c[:]),
          r=["t2s0", "irb1c"], w=["gcr"])
        for sc in range(NSC):
            E("pe", lambda e, sc=sc: e.matmul(pbt[7][:, 8 * sc:8 * sc + 8],
                                              gcr[:, sc * C:(sc + 1) * C], irw2_r[:],
                                              start=True, stop=True),
              r=["gcr", "irw2_r"], w=[PB[7]])
        for sc in range(NSC):
            E("dv", lambda e, sc=sc: e.tensor_scalar(out=ucol[:, sc:sc + 1],
                                                     in0=pbt[7][:, 8 * sc:8 * sc + 1],
                                                     scalar1=irb2c[:], scalar2=0.0,
                                                     op0=ALU.add, op1=ALU.is_gt),
              r=[PB[7], "irb2c"], w=["ucol"])
        E("dv", lambda e: e.tensor_scalar(out=t2s_[1][0:32, :], in0=mrep[1][0:32, :],
                                          scalar1=cspw[:], scalar2=None, op0=ALU.mult),
          r=["mrep1", "cspw"], w=["t2s0"])
        E("dv", lambda e: e.tensor_tensor(out=t2s_[1][0:32, :], in0=pbt[6][0:32, 0:P],
                                          in1=t2s_[1][0:32, :], op=ALU.subtract),
          r=[PB[6], "t2s0"], w=["t2s0"])
        E("dv", lambda e: e.tensor_tensor(out=pwC[:], in0=t2s_[1][0:32, :],
                                          in1=rrep[1][0:32, :], op=ALU.mult),
          r=["t2s0", "rrep1"], w=["pwC"])
        E("pe", lambda e: e.matmul(pbt[5][:, 0:P], nmw1_r[:], vbT[:], start=True, stop=True),
          r=["nmw1_r", "vbT"], w=[PB[5]])
        E("ac", lambda e: e.activation(out=sc1r[:], in_=pbt[5][:, 0:P], func=AF.Gelu,
                                       bias=nmb1c[:]),
          r=[PB[5], "nmb1c"], w=["sc1r"])
        E("pe", lambda e: e.matmul(pbt[6][0:H, 0:P], nmw2_r[:], sc1r[:], start=True, stop=True),
          r=["nmw2_r", "sc1r"], w=[PB[6]])
        E("dv", lambda e: e.tensor_scalar(out=nuT[:], in0=pbt[6][0:H, 0:P],
                                          scalar1=nmb2c[:], scalar2=None, op0=ALU.add),
          r=[PB[6], "nmb2c"], w=["nuT"])
        for sc in range(NSC):
            E("pe", lambda e, sc=sc: e.transpose(pbr[7][:, 384 + 32 * sc:384 + 32 * (sc + 1)],
                                                 pwC[:, sc * C:(sc + 1) * C],
                                                 ident_r[0:32, 0:32]),
              r=["pwC", "ident_r"], w=[PB[7]])
        for sc in range(NSC):
            slc = pbt[7][:, 384 + 32 * sc:384 + 32 * sc + 3]
            E("dv", lambda e, slc=slc: e.reduce_max(out=msc[:], in_=slc, axis=AXX,
                                                    negate=True),
              r=[PB[7]], w=["msc"])
            E("ac", lambda e, sc=sc, slc=slc:
              e.activation(out=pwt[:, sc * 3:(sc + 1) * 3], in_=slc, func=AF.Exp,
                           bias=msc[:]),
              r=[PB[7], "msc"], w=["pwt"])
            E("dv", lambda e, sc=sc: e.reduce_sum(out=msc[:], in_=pwt[:, sc * 3:(sc + 1) * 3],
                                                  axis=AXX),
              r=["pwt"], w=["msc"])
            E("dv", lambda e: e.reciprocal(out=msc[:], in_=msc[:]), r=["msc"], w=["msc"])
            E("dv", lambda e, sc=sc: e.tensor_scalar(out=pwt[:, sc * 3:(sc + 1) * 3],
                                                     in0=pwt[:, sc * 3:(sc + 1) * 3],
                                                     scalar1=msc[:], scalar2=None,
                                                     op0=ALU.mult),
              r=["pwt", "msc"], w=["pwt"])

    def emit_attn_b_sc(sc):
        sm = {"pe": 250, "dv": 260, "ac": 280}
        E = lambda eng, fn, r=(), w=(), **kw: TK.emit(eng, fn, r=r, w=w,
                                                      cost=kw.pop("cost", sm.get(eng)), **kw)
        s0 = sc * C
        kss = kbT[:, s0:s0 + C]
        qss = qbT[:, s0:s0 + C]
        vss = vbT[:, s0:s0 + C]
        ucs = ucol[:, sc:sc + 1]
        E("pe", lambda e, kss=kss: e.transpose(pbr[4][:, 256:320], kss, ident_r[0:H, 0:H]),
          r=["kbT", "ident_r"], w=[PB[4]])
        E("dv", lambda e: e.tensor_copy(out=Ktm[:], in_=pbr[4][:, 256:320]),
          r=[PB[4]], w=["Ktm"])
        E("pe", lambda e, vss=vss: e.transpose(pbr[4][:, 320:384], vss, ident_r[0:H, 0:H]),
          r=["vbT", "ident_r"], w=[PB[4]])
        E("dv", lambda e: e.tensor_copy(out=Vtm[:], in_=pbr[4][:, 320:384]),
          r=[PB[4]], w=["Vtm"])
        E("pe", lambda e, ucs=ucs: e.transpose(pbt[4][0:1, 384:512], ucs, cbv("ident")),
          r=["ucol", "cbt"], w=[PB[4]])
        E("dv", lambda e: e.tensor_copy(out=urow[:], in_=pbt[4][0:1, 384:512]),
          r=[PB[4]], w=["urow"])
        E("pe", lambda e, kss=kss: e.matmul(pbt[4][:, 0:128], kss, kss, start=True, stop=True),
          r=["kbT"], w=[PB[4]])
        E("pe", lambda e, kss=kss, qss=qss: e.matmul(pbt[5][:, 0:128], kss, qss,
                                                     start=True, stop=True),
          r=["kbT", "qbT"], w=[PB[5]])
        E("pe", lambda e: e.matmul(pbt[4][:, 128:256], urow[:], urow[:], start=True, stop=True),
          r=["urow"], w=[PB[4]])
        E("dv", lambda e: e.tensor_tensor(out=Npw[0][0][:], in0=pbt[4][:, 0:128],
                                          in1=DSB[0][:], op=ALU.mult),
          r=[PB[4], "DSB0"], w=["Npw00"])
        E("dv", lambda e: e.tensor_tensor(out=NpwT[0][0][:], in0=pbt[4][:, 0:128],
                                          in1=DSBT[0][:], op=ALU.mult),
          r=[PB[4], "DSBT0"], w=["NpwT00"])
        E("dv", lambda e: e.tensor_tensor(out=nscr[:], in0=pbt[4][:, 0:128],
                                          in1=DSB[1][:], op=ALU.mult),
          r=[PB[4], "DSB1"], w=["nscr"])
        E("dv", lambda e: e.tensor_tensor(out=Npw[1][0][:], in0=nscr[:],
                                          in1=pbt[4][:, 128:256], op=ALU.mult),
          r=["nscr", PB[4]], w=["Npw10"])
        E("dv", lambda e: e.tensor_tensor(out=nscr[:], in0=pbt[4][:, 0:128],
                                          in1=DSBT[1][:], op=ALU.mult),
          r=[PB[4], "DSBT1"], w=["nscr"])
        E("dv", lambda e: e.tensor_tensor(out=NpwT[1][0][:], in0=nscr[:],
                                          in1=pbt[4][:, 128:256], op=ALU.mult),
          r=["nscr", PB[4]], w=["NpwT10"])
        E("dv", lambda e: e.tensor_tensor(out=WxT[0][:], in0=pbt[5][:, 0:128],
                                          in1=DS_[0][:], op=ALU.mult),
          r=[PB[5], "DS0"], w=["WxT0"])
        E("dv", lambda e: e.tensor_tensor(out=WiT[0][:], in0=pbt[5][:, 0:128],
                                          in1=DI_[0][:], op=ALU.mult),
          r=[PB[5], "DI0"], w=["WiT0"])
        E("dv", lambda e: e.tensor_tensor(out=WxT[1][:], in0=pbt[5][:, 0:128],
                                          in1=DS_[1][:], op=ALU.mult),
          r=[PB[5], "DS1"], w=["WxT1"])
        E("dv", lambda e, ucs=ucs: e.tensor_scalar(out=WxT[1][:], in0=WxT[1][:],
                                                   scalar1=ucs, scalar2=None, op0=ALU.mult),
          r=["WxT1", "ucol"], w=["WxT1"])
        E("dv", lambda e: e.tensor_tensor(out=WiT[1][:], in0=pbt[5][:, 0:128],
                                          in1=DI_[1][:], op=ALU.mult),
          r=[PB[5], "DI1"], w=["WiT1"])
        E("dv", lambda e, ucs=ucs: e.tensor_scalar(out=WiT[1][:], in0=WiT[1][:],
                                                   scalar1=ucs, scalar2=None, op0=ALU.mult),
          r=["WiT1", "ucol"], w=["WiT1"])
        for m in range(2):
            off = 128 + m * H
            E("pe", lambda e, kss=kss, m=m, off=off:
              e.matmul(pbt[5][:, off:off + H], kss, F0m[m][:], start=True, stop=True),
              r=["kbT", f"F0m{m}"], w=[PB[5]])
            if m == 0:
                E("dv", lambda e: e.tensor_scalar(out=rdf[:], in0=pbt[5][:, 128:128 + H],
                                                  scalar1=g1n[0][:], scalar2=None,
                                                  op0=ALU.mult),
                  r=[PB[5], "g1n0"], w=["rdf"])
                E("dv", lambda e: e.scalar_tensor_tensor(out=Rm[0][:], in0=Vtm[:],
                                                         scalar=bcol[0][:], in1=rdf[:],
                                                         op0=ALU.mult, op1=ALU.add),
                  r=["Vtm", "bcol0", "rdf"], w=["Rm0"])
            else:
                E("dv", lambda e: e.tensor_scalar(out=rds[:], in0=pbt[5][:, 192:192 + H],
                                                  scalar1=g1n[1][:], scalar2=ucs,
                                                  op0=ALU.mult, op1=ALU.mult),
                  r=[PB[5], "g1n1", "ucol"], w=["rds"])
                E("dv", lambda e: e.tensor_scalar(out=predt[:], in0=Vtm[:],
                                                  scalar1=bcol[1][:], scalar2=ucs,
                                                  op0=ALU.mult, op1=ALU.mult),
                  r=["Vtm", "bcol1", "ucol"], w=["predt"])
                E("dv", lambda e: e.tensor_tensor(out=Rm[1][:], in0=predt[:], in1=rds[:],
                                                  op=ALU.add),
                  r=["predt", "rds"], w=["Rm1"])
        for m in range(2):
            bk = 6 + m
            E("pe", lambda e, m=m, bk=bk: e.matmul(pbt[bk][:, 256:320], Npw[m][0][:],
                                                   Rm[m][:], start=True, stop=True),
              r=[f"Npw{m}0", f"Rm{m}"], w=[PB[bk]])
            E("dv", lambda e, m=m, bk=bk: e.tensor_tensor(out=Ut[m][0][:], in0=Rm[m][:],
                                                          in1=pbt[bk][:, 256:320],
                                                          op=ALU.subtract),
              r=[f"Rm{m}", PB[bk]], w=[f"Ut{m}0"])
        for j in range(6):
            jp = j % 2
            jn = (j + 1) % 2
            for m in range(2):
                bk = 6 + m
                E("pe", lambda e, m=m, jp=jp, bk=bk:
                  e.matmul(pbt[bk][:, 0:128], NpwT[m][jp][:], Npw[m][jp][:],
                           start=True, stop=True),
                  r=[f"NpwT{m}{jp}", f"Npw{m}{jp}"], w=[PB[bk]])
                E("dv", lambda e, m=m, jn=jn, bk=bk:
                  e.tensor_copy(out=Npw[m][jn][:], in_=pbr[bk][:, 0:128]),
                  r=[PB[bk]], w=[f"Npw{m}{jn}"])
                if j < 5:
                    E("pe", lambda e, m=m, jp=jp, bk=bk:
                      e.matmul(pbt[bk][:, 128:256], Npw[m][jp][:], NpwT[m][jp][:],
                               start=True, stop=True),
                      r=[f"Npw{m}{jp}", f"NpwT{m}{jp}"], w=[PB[bk]])
                    E("dv", lambda e, m=m, jn=jn, bk=bk:
                      e.tensor_copy(out=NpwT[m][jn][:], in_=pbr[bk][:, 128:256]),
                      r=[PB[bk]], w=[f"NpwT{m}{jn}"])
                E("pe", lambda e, m=m, jp=jp, jn=jn, bk=bk:
                  e.matmul(pbt[bk][:, 256:320], Npw[m][jn][:], Ut[m][jp][:],
                           start=True, stop=True),
                  r=[f"Npw{m}{jn}", f"Ut{m}{jp}"], w=[PB[bk]])
                E("dv", lambda e, m=m, jp=jp, jn=jn, bk=bk:
                  e.tensor_tensor(out=Ut[m][jn][:], in0=Ut[m][jp][:],
                                  in1=pbt[bk][:, 256:320], op=ALU.add),
                  r=[f"Ut{m}{jp}", PB[bk]], w=[f"Ut{m}{jn}"])
        for m in range(2):
            bk = 6 + m
            E("pe", lambda e, qss=qss, m=m: e.matmul(pbt[5][:, 256 + m * H:256 + (m + 1) * H],
                                                     qss, F0m[m][:], start=True, stop=True),
              r=["qbT", f"F0m{m}"], w=[PB[5]])
            E("pe", lambda e, m=m, bk=bk: e.matmul(pbt[bk][:, 320:384], WxT[m][:],
                                                   Ut[m][0][:], start=True, stop=True),
              r=[f"WxT{m}", f"Ut{m}0"], w=[PB[bk]])
            E("pe", lambda e, m=m, bk=bk: e.matmul(pbt[bk][:, 384:448], WiT[m][:],
                                                   Ut[m][0][:], start=True, stop=True),
              r=[f"WiT{m}", f"Ut{m}0"], w=[PB[bk]])
        E("dv", lambda e: e.tensor_scalar(out=rdf[:], in0=pbt[5][:, 256:256 + H],
                                          scalar1=g1c[0][:], scalar2=None, op0=ALU.mult),
          r=[PB[5], "g1c0"], w=["rdf"])
        E("dv", lambda e: e.tensor_tensor(out=rdf[:], in0=rdf[:], in1=pbt[6][:, 320:384],
                                          op=ALU.add),
          r=["rdf", PB[6]], w=["rdf"])
        E("dv", lambda e: e.tensor_scalar(out=rds[:], in0=pbt[5][:, 320:320 + H],
                                          scalar1=g1c[1][:], scalar2=None, op0=ALU.mult),
          r=[PB[5], "g1c1"], w=["rds"])
        E("dv", lambda e: e.tensor_tensor(out=rds[:], in0=rds[:], in1=pbt[7][:, 320:384],
                                          op=ALU.add),
          r=["rds", PB[7]], w=["rds"])
        E("dv", lambda e: e.tensor_tensor(out=predt[:], in0=rdf[:], in1=rds[:], op=ALU.add),
          r=["rdf", "rds"], w=["predt"])
        E("dv", lambda e: e.scalar_tensor_tensor(out=predt[:], in0=predt[:], scalar=-0.5,
                                                 in1=Vtm[:], op0=ALU.mult, op1=ALU.add),
          r=["predt", "Vtm"], w=["predt"])
        E("ac", lambda e: e.activation(out=dsq[:], in_=predt[:], func=AF.Square),
          r=["predt"], w=["dsq"])
        E("dv", lambda e: e.reduce_sum(out=errc[:], in_=dsq[:], axis=AXX),
          r=["dsq"], w=["errc"])
        E("dv", lambda e: e.tensor_scalar(out=uncol[:], in0=errc[:], scalar1=SUR_THR,
                                          scalar2=None, op0=ALU.is_gt),
          r=["errc"], w=["uncol"])
        E("dv", lambda e: e.tensor_scalar(out=rdf[:], in0=pbt[5][:, 256:256 + H],
                                          scalar1=g2c[0][:], scalar2=None, op0=ALU.mult),
          r=[PB[5], "g2c0"], w=["rdf"])
        E("dv", lambda e: e.tensor_tensor(out=rdf[:], in0=rdf[:], in1=pbt[6][:, 384:448],
                                          op=ALU.add),
          r=["rdf", PB[6]], w=["rdf"])
        E("dv", lambda e: e.tensor_scalar(out=rds[:], in0=pbt[5][:, 320:320 + H],
                                          scalar1=g2c[1][:], scalar2=None, op0=ALU.mult),
          r=[PB[5], "g2c1"], w=["rds"])
        E("dv", lambda e: e.tensor_tensor(out=rds[:], in0=rds[:], in1=pbt[7][:, 384:448],
                                          op=ALU.add),
          r=["rds", PB[7]], w=["rds"])
        E("pe", lambda e: e.transpose(pbt[7][0:1, 128:256], uncol[:], cbv("ident")),
          r=["uncol", "cbt"], w=[PB[7]])
        E("dv", lambda e: e.tensor_copy(out=unrow[:], in_=pbt[7][0:1, 128:256]),
          r=[PB[7]], w=["unrow"])
        E("pe", lambda e: e.matmul(pbt[6][0:H, 0:128], ones_r[0:1, 0:H], unrow[:],
                                   start=True, stop=True),
          r=["ones_r", "unrow"], w=[PB[6]])
        E("dv", lambda e: e.tensor_scalar(out=nsA[:], in0=pbt[6][0:H, 0:128],
                                          scalar1=-0.1, scalar2=1.0, op0=ALU.mult,
                                          op1=ALU.add),
          r=[PB[6]], w=["nsA"])
        E("dv", lambda e: e.tensor_scalar(out=nsB[:], in0=pbt[6][0:H, 0:128],
                                          scalar1=0.1, scalar2=None, op0=ALU.mult),
          r=[PB[6]], w=["nsB"])
        E("dv", lambda e, s0=s0: e.tensor_tensor(out=nsB[:], in0=nsB[:],
                                                 in1=nuT[:, s0:s0 + C], op=ALU.mult),
          r=["nsB", "nuT"], w=["nsB"])
        if sc == 0:
            E("dv", lambda e, s0=s0:
              e.tensor_tensor_scan(out=Nfm[:, s0:s0 + C], data0=nsA[:], data1=nsB[:],
                                   initial=nsinit[:], op0=ALU.mult, op1=ALU.add),
              r=["nsA", "nsB", "nsinit"], w=["Nfm"], cost=500)
        else:
            E("dv", lambda e, s0=s0:
              e.tensor_tensor_scan(out=Nfm[:, s0:s0 + C], data0=nsA[:], data1=nsB[:],
                                   initial=Nfm[:, s0 - 1:s0], op0=ALU.mult, op1=ALU.add),
              r=["nsA", "nsB", "Nfm"], w=["Nfm"], cost=500)
        E("pe", lambda e, s0=s0: e.transpose(pbt[6][:, 448:512], Nfm[:, s0:s0 + C],
                                             cbv("ident", rows=H)[0:H, 0:H]),
          r=["Nfm", "cbt"], w=[PB[6]])
        E("dv", lambda e: e.tensor_copy(out=Ntm[:], in_=pbt[6][:, 448:512]),
          r=[PB[6]], w=["Ntm"])
        E("dv", lambda e, sc=sc: e.tensor_scalar(out=oatt[:], in0=rdf[:],
                                                 scalar1=pwt[:, sc * 3:sc * 3 + 1],
                                                 scalar2=None, op0=ALU.mult),
          r=["rdf", "pwt"], w=["oatt"])
        E("dv", lambda e, sc=sc: e.scalar_tensor_tensor(out=oatt[:], in0=rds[:],
                                                        scalar=pwt[:, sc * 3 + 1:sc * 3 + 2],
                                                        in1=oatt[:], op0=ALU.mult,
                                                        op1=ALU.add),
          r=["rds", "pwt", "oatt"], w=["oatt"])
        E("dv", lambda e, sc=sc: e.scalar_tensor_tensor(out=oatt[:], in0=Ntm[:],
                                                        scalar=pwt[:, sc * 3 + 2:sc * 3 + 3],
                                                        in1=oatt[:], op0=ALU.mult,
                                                        op1=ALU.add),
          r=["Ntm", "pwt", "oatt"], w=["oatt"])
        E("ac", lambda e: e.activation(out=dsq[:], in_=oatt[:], func=AF.Square),
          r=["oatt"], w=["dsq"])
        E("dv", lambda e: e.reduce_sum(out=msc[:], in_=dsq[:], axis=AXX),
          r=["dsq"], w=["msc"])
        E("dv", lambda e: e.tensor_scalar(out=msc[:], in0=msc[:], scalar1=1.0 / H,
                                          scalar2=1e-6, op0=ALU.mult, op1=ALU.add),
          r=["msc"], w=["msc"])
        E("ac", lambda e: e.activation(out=msc[:], in_=msc[:], func=AF.Ln),
          r=["msc"], w=["msc"])
        E("ac", lambda e: e.activation(out=msc[:], in_=msc[:], func=AF.Exp, scale=-0.5),
          r=["msc"], w=["msc"])
        E("dv", lambda e: e.tensor_scalar(out=onrm[:], in0=oatt[:], scalar1=msc[:],
                                          scalar2=None, op0=ALU.mult),
          r=["oatt", "msc"], w=["onrm"])
        E("pe", lambda e: e.transpose(pbr[7][0:H, 0:128], onrm[:], ident_r[:]),
          r=["onrm", "ident_r"], w=[PB[7]])
        E("dv", lambda e, s0=s0: e.tensor_copy(out=obT[:, s0:s0 + C],
                                               in_=pbr[7][0:H, 0:128]),
          r=[PB[7]], w=["obT"])
        E("dv", lambda e, ucs=ucs: e.tensor_scalar(out=Ktg[:], in0=Ktm[:], scalar1=ucs,
                                                   scalar2=None, op0=ALU.mult),
          r=["Ktm", "ucol"], w=["Ktg"])
        for m in range(2):
            E("dv", lambda e, m=m: e.tensor_scalar(out=Ut[m][0][:], in0=Ut[m][0][:],
                                                   scalar1=g3c[m][:], scalar2=None,
                                                   op0=ALU.mult),
              r=[f"Ut{m}0", f"g3c{m}"], w=[f"Ut{m}0"])
            lhs = Ktm if m == 0 else Ktg
            lres = "Ktm" if m == 0 else "Ktg"
            E("pe", lambda e, m=m, lhs=lhs: e.matmul(pbt[5][0:H, 384 + m * H:384 + (m + 1) * H],
                                                     lhs[:], Ut[m][0][:],
                                                     start=True, stop=True),
              r=[lres, f"Ut{m}0"], w=[PB[5]])
            E("dv", lambda e, m=m: e.scalar_tensor_tensor(
                out=F0m[m][:], in0=F0m[m][:], scalar=aCc[m][0:H, :],
                in1=pbt[5][0:H, 384 + m * H:384 + (m + 1) * H],
                op0=ALU.mult, op1=ALU.add),
              r=[f"F0m{m}", PB[5], f"aCc{m}"], w=[f"F0m{m}"])

    def emit_attn_b_out(buf):
        xbn = f"xb{buf}"
        E("dv", lambda e: e.tensor_copy(out=nsinit[:], in_=Nfm[:, P - 1:P]),
          r=["Nfm"], w=["nsinit"])
        for dc in range(8):
            bank = 4 + (dc % 2)
            E("pe", lambda e, dc=dc, bank=bank:
              e.matmul(pbt[bank][:], wob_r[:, dc * 128:(dc + 1) * 128], obT[:],
                       start=True, stop=True),
              r=["wob_r", "obT"], w=[PB[bank]])
            E("dv", lambda e, dc=dc, bank=bank, buf=buf:
              e.tensor_tensor(out=xbuf[buf][:, dc * P:(dc + 1) * P],
                              in0=xbuf[buf][:, dc * P:(dc + 1) * P], in1=pbt[bank][:],
                              op=ALU.add),
              r=[xbn, PB[bank]], w=[xbn])

    if PIPELINE and SUBLAYERS >= 4:
        for i in range(NCHUNK + 1):
            bufA = i % 3
            bufB = (i - 1) % 3
            hasA = i < NCHUNK
            hasB = i > 0
            if i == 0:
                emit_attn_a_pre(bufA)
            if hasB:
                emit_attn_b_pre(bufB)
            for sc in range(NSC):
                if hasA:
                    emit_attn_a_sc(sc)
                if hasB:
                    emit_attn_b_sc(sc)
            if hasA:
                emit_attn_a_out(bufA)
            if hasB:
                emit_attn_b_out(bufB)
            if hasA:
                emit_ffn_p1(bufA, 0)
                emit_ffn_p2(bufA, 0)
            if hasB:
                emit_ffn_p1(bufB, 1)
            if hasA and i + 1 < NCHUNK:
                emit_attn_a_pre((i + 1) % 3)
            if hasB:
                emit_ffn_p2(bufB, 1)
                E("sp", lambda e, c=i - 1, bufB=bufB: e.dma_start(
                    out=odst[:, :, c * P:(c + 1) * P],
                    in_=xbuf[bufB][:].rearrange("p (ic t) -> p ic t", ic=8)),
                  r=[f"xb{bufB}"], w=[f"od{i - 1}"], sem=s_out)
            if i + 2 < NCHUNK:
                x_load(i + 2, (i + 2) % 3)
    else:
        for c_ in range(NCHUNK):
            buf = c_ % 3
            if SUBLAYERS >= 1:
                emit_attn_a_pre(buf)
                for sc in range(NSC):
                    emit_attn_a_sc(sc)
                emit_attn_a_out(buf)
            if SUBLAYERS >= 2:
                emit_ffn_p1(buf, 0)
                emit_ffn_p2(buf, 0)
            if SUBLAYERS >= 3:
                emit_attn_b_pre(buf)
                for sc in range(NSC):
                    emit_attn_b_sc(sc)
                emit_attn_b_out(buf)
            if SUBLAYERS >= 4:
                emit_ffn_p1(buf, 1)
                emit_ffn_p2(buf, 1)
            E("sp", lambda e, c_=c_, buf=buf: e.dma_start(
                out=odst[:, :, c_ * P:(c_ + 1) * P],
                in_=xbuf[buf][:].rearrange("p (ic t) -> p ic t", ic=8)),
              r=[f"xb{buf}"], w=[f"od{c_}"], sem=s_out)
            if c_ + 2 < NCHUNK:
                x_load(c_ + 2, (c_ + 2) % 3)

    TK.schedule()
    TK.finalize(ENG_SEMS)
    TK.steps.append(dict(eng="sp", fn=lambda e: e.nop(), r=(), w=(), sem=s_out,
                         inc=False, deps=set(), val=None,
                         waits=[(s_out, 16 * NCHUNK)], xwaits=[]))

    with nc.Block() as block:
        @block.sync
        def _(sync):
            TK.run_engine("sp", sync)

        @block.vector
        def _(vector):
            TK.run_engine("dv", vector)

        @block.scalar
        def _(scalar):
            TK.run_engine("ac", scalar)

        @block.tensor
        def _(tensor):
            TK.run_engine("pe", tensor)

        @block.gpsimd
        def _(gpsimd):
            TK.run_engine("gp", gpsimd)

    for cm in reversed(sem_cms):
        cm.__exit__(None, None, None)
    return nc


def kernel(**inputs):
    g = lambda k: np.asarray(inputs[k])
    x = g("L0_attn_a_x").astype(np.float32, copy=False)
    cbd = build_const_blob()
    pwT = np.zeros((D, 32), np.float32)
    pwT[:, 0:3] = g("L0_attn_b_pw").T

    common = dict(
        cbd=cbd,
        w1T=np.ascontiguousarray(g("L0_ffn_w1").T),
        w2T=np.ascontiguousarray(g("L0_ffn_w2").T),
        wqa=np.ascontiguousarray(g("L0_attn_a_W_q")),
        wka=np.ascontiguousarray(g("L0_attn_a_W_k")),
        wva=np.ascontiguousarray(g("L0_attn_a_W_v")),
        woa=np.ascontiguousarray(g("L0_attn_a_W_o")),
        wkbT=np.ascontiguousarray(g("L0_attn_b_Wk").T),
        wvbT=np.ascontiguousarray(g("L0_attn_b_Wv").T),
        wqbT=np.ascontiguousarray(g("L0_attn_b_Wq").T),
        irw1T=np.ascontiguousarray(g("L0_attn_b_ir_w1").T),
        pwT=pwT,
        nmw1T=np.ascontiguousarray(g("L0_attn_b_nm_w1").T),
        nmw2T=np.ascontiguousarray(g("L0_attn_b_nm_w2").T),
        irw2T=np.ascontiguousarray(g("L0_attn_b_ir_w2").T),
        wobT=np.ascontiguousarray(g("L0_attn_b_Wo").T),
        d_b1c=np.ascontiguousarray(g("L0_ffn_b1").reshape(32, 128).T),
        d_b2c=np.ascontiguousarray(g("L0_ffn_b2").reshape(8, 128).T),
        d_nmb1c=np.ascontiguousarray(g("L0_attn_b_nm_b1").reshape(128, 1)),
        d_nmb2c=np.ascontiguousarray(g("L0_attn_b_nm_b2").reshape(64, 1)),
        d_irb1c=np.ascontiguousarray(g("L0_attn_b_ir_b1").reshape(64, 1)),
        d_irb2=np.ascontiguousarray(g("L0_attn_b_ir_b2").reshape(1, 1)),
        d_normw=np.ascontiguousarray(g("L0_attn_b_norm_w").reshape(64, 1)),
        d_alphaf=np.ascontiguousarray(np.reshape(g("L0_attn_b_fast_alpha"), (1, 1))),
        d_alphas=np.ascontiguousarray(np.reshape(g("L0_attn_b_slow_alpha"), (1, 1))),
    )
    in_maps = []
    for b in range(8):
        m = dict(common)
        m["xT"] = np.ascontiguousarray(x[b].T)
        in_maps.append(m)

    nc = build_nc()
    res = run_bass_kernel_spmd(nc, in_maps, core_ids=list(range(8)))
    out = np.stack([np.ascontiguousarray(res.results[b]["outT"].T) for b in range(8)], axis=0)
    return out.astype(np.float32, copy=False)


# version 17
# speedup vs baseline: 1.0511x; 1.0023x over previous
import math

import numpy as np
import ml_dtypes
import concourse.bass as bass
import concourse.mybir as mybir
from concourse.bass_utils import run_bass_kernel_spmd

F32 = mybir.dt.float32
F32R = mybir.dt.float32r
BF16 = mybir.dt.bfloat16
AF = mybir.ActivationFunctionType
ALU = mybir.AluOpType
AXX = mybir.AxisListType.X

D = 1024
T = 4096
H = 64
HID = 4096
P = 512
NCHUNK = T // P
C = 128
NSC = P // C
M_RND = float(2.0 ** 23 + 2.0 ** 22)
LN_EPS = 1e-5
SUR_THR = float(math.log(0.7 / 0.3) * (1.0 + 1e-6))

SUBLAYERS = 4
PIPELINE = True

_OFF = {}
_top = 0
def _cols(name, w):
    global _top
    _OFF[name] = (_top, w)
    _top += w

for _n in ("maskIL", "maskIU", "maskSU", "maskSL", "expoSU", "expoSL", "expoIU",
           "ones128", "ident"):
    _cols(_n, 128)
for _n in ("iota0", "iota1", "iotaC", "onescol", "epscol", "negMcol", "Mcol", "zerocol", "ln16col"):
    _cols(_n, 1)
_cols("zero64", 64)
CBW = _top


def build_const_blob():
    cb = np.zeros((128, CBW), np.float32)
    r = np.arange(128)
    lo = (r[:, None] >= r[None, :]).astype(np.float32)
    iu = (r[:, None] <= r[None, :]).astype(np.float32)
    su = (r[:, None] < r[None, :]).astype(np.float32)
    sl = (r[:, None] > r[None, :]).astype(np.float32)

    def put(name, arr):
        o, w = _OFF[name]
        cb[:, o:o + w] = arr

    put("maskIL", lo)
    put("maskIU", iu)
    put("maskSU", su)
    put("maskSL", sl)
    put("expoSU", (r[None, :] - 1 - r[:, None]) * su)
    put("expoSL", (r[:, None] - 1 - r[None, :]) * sl)
    put("expoIU", (r[None, :] - r[:, None]) * iu)
    put("ones128", np.ones((128, 128), np.float32))
    put("ident", np.eye(128, dtype=np.float32))
    put("iota0", r[:, None].astype(np.float32))
    put("iota1", (r[:, None] + 1).astype(np.float32))
    put("iotaC", (C - 1 - r[:, None]).astype(np.float32))
    put("onescol", np.ones((128, 1), np.float32))
    put("epscol", np.full((128, 1), LN_EPS, np.float32))
    put("negMcol", np.full((128, 1), -M_RND, np.float32))
    put("Mcol", np.full((128, 1), M_RND, np.float32))
    put("zerocol", np.zeros((128, 1), np.float32))
    put("ln16col", np.full((128, 1), math.log(16.0), np.float32))
    put("zero64", np.zeros((128, 64), np.float32))
    return cb


class SemT:
    def __init__(self, handle, name, dma=False):
        self.h = handle
        self.name = name
        self.n = 0
        self.dma = dma

    def bump(self):
        self.n += 16 if self.dma else 1
        return self.n


class Tracker:
    def __init__(self):
        self.steps = []

    def emit(self, eng, fn, r=(), w=(), sem=None, xwaits=(), cost=None, lat=None):
        self.steps.append(dict(eng=eng, fn=fn, r=tuple(r), w=tuple(w), sem=sem,
                               inc=False, xwaits=list(xwaits), cost=cost, lat=lat))
        return len(self.steps) - 1

    def schedule(self, costs=None):
        import heapq
        n = len(self.steps)
        cost = []
        base = {"pe": 260, "dv": 390, "ac": 640, "sp": 620, "gp": 900}
        lats = []
        for st in self.steps:
            c = st.get("cost")
            cost.append(c if c is not None else base.get(st["eng"], 400))
            lt = st.get("lat")
            if lt is None:
                lt = 800 if st["eng"] == "sp" else 0
            lats.append(lt)
        last_w = {}
        readers = {}
        succ = [[] for _ in range(n)]
        pred_cnt = [0] * n
        preds = [set() for _ in range(n)]
        for i, st in enumerate(self.steps):
            deps = set()
            for rr in st["r"]:
                j = last_w.get(rr)
                if j is not None:
                    deps.add(j)
            for rr in st["w"]:
                j = last_w.get(rr)
                if j is not None:
                    deps.add(j)
                for k in readers.get(rr, ()):
                    if k != i:
                        deps.add(k)
            for rr in st["r"]:
                readers.setdefault(rr, []).append(i)
            for rr in st["w"]:
                last_w[rr] = i
                readers[rr] = []
            preds[i] = deps
            pred_cnt[i] = len(deps)
            for j in deps:
                succ[j].append(i)
        cp = [0.0] * n
        for i in range(n - 1, -1, -1):
            m = 0.0
            for s_ in succ[i]:
                if cp[s_] > m:
                    m = cp[s_]
            cp[i] = cost[i] + lats[i] + m
        fin = [0.0] * n
        avail = [0.0] * n
        eng_clock = {}
        ready = []
        for i in range(n):
            if pred_cnt[i] == 0:
                ready.append(i)
        order = []
        remaining = list(pred_cnt)
        start = [0.0] * n
        import os
        policy = os.environ.get("TK_SCHED_POLICY", "est")
        while ready:
            if policy == "est":
                bi, bk = -1, None
                for idx, i in enumerate(ready):
                    est = eng_clock.get(self.steps[i]["eng"], 0.0)
                    if avail[i] > est:
                        est = avail[i]
                    key = (est, -cp[i])
                    if bk is None or key < bk:
                        bk, bi = key, idx
                i = ready.pop(bi)
            else:
                bi, bk = -1, None
                for idx, i in enumerate(ready):
                    key = -cp[i]
                    if bk is None or key < bk:
                        bk, bi = key, idx
                i = ready.pop(bi)
            st = self.steps[i]
            t0 = eng_clock.get(st["eng"], 0.0)
            if avail[i] > t0:
                t0 = avail[i]
            start[i] = t0
            fin[i] = t0 + cost[i] + lats[i]
            eng_clock[st["eng"]] = t0 + cost[i]
            order.append(i)
            for s_ in succ[i]:
                remaining[s_] -= 1
                if avail[s_] < fin[i]:
                    avail[s_] = fin[i]
                if remaining[s_] == 0:
                    ready.append(s_)
        assert len(order) == n
        if os.environ.get("TK_SCHED_DEBUG"):
            eng_busy = {}
            for i in range(n):
                e = self.steps[i]["eng"]
                eng_busy[e] = eng_busy.get(e, 0) + cost[i]
            print("SCHED makespan(model): %.1f us" % (max(fin) / 1e3))
            print("SCHED cp bound: %.1f us" % (max(cp) / 1e3))
            print("SCHED eng busy(model):", {k: round(v / 1e3) for k, v in eng_busy.items()})
            prev_end = {}
            attr = {}
            gaps = []
            for i in order:
                e = self.steps[i]["eng"]
                gap = start[i] - prev_end.get(e, 0.0)
                if gap > 1.0:
                    jb, fb = None, -1.0
                    for j in preds[i]:
                        if fin[j] > fb:
                            fb, jb = fin[j], j
                    be = self.steps[jb]["eng"] if jb is not None else "none"
                    res = ""
                    if jb is not None:
                        wr = set(self.steps[jb]["w"])
                        res = ",".join(sorted(wr & set(self.steps[i]["r"]))) or \
                              ",".join(sorted(wr)[:2])
                    key = (e, be, res)
                    attr[key] = attr.get(key, 0.0) + gap
                    gaps.append((gap, e, be, res, start[i]))
                prev_end[e] = fin[i]
            pe_gaps = sorted([g for g in gaps if g[1] == "pe"], reverse=True)[:20]
            for gp, e_, be, res, t_ in pe_gaps:
                print("SCHED pegap %.1f us at t=%.1f blocked-by %s [%s]" %
                      (gp / 1e3, t_ / 1e3, be, res))
            top = sorted(attr.items(), key=lambda kv: -kv[1])[:16]
            for (e, be, res), tot in top:
                print("SCHED idle %s blocked-by %s [%s]: %.1f us" % (e, be, res, tot / 1e3))
            tot_idle = {}
            nstep = {}
            endt = {}
            for i in order:
                e = self.steps[i]["eng"]
                nstep[e] = nstep.get(e, 0) + 1
                endt[e] = fin[i]
            prev_end = {}
            for i in order:
                e = self.steps[i]["eng"]
                tot_idle[e] = tot_idle.get(e, 0.0) + (start[i] - prev_end.get(e, 0.0))
                prev_end[e] = fin[i]
            print("SCHED per-eng: " + "  ".join(
                "%s n=%d idle=%.0fus end=%.0fus" % (e, nstep[e], tot_idle[e] / 1e3,
                                                    endt[e] / 1e3) for e in sorted(nstep)))
        self.steps = [self.steps[i] for i in order]

    def finalize(self, eng_sems):
        last_w = {}
        readers = {}
        for i, st in enumerate(self.steps):
            ei = st["eng"]
            deps = set()

            def need(j):
                ej = self.steps[j]["eng"]
                return ej != ei or ej == "sp"

            for rr in st["r"]:
                j = last_w.get(rr)
                if j is not None:
                    deps.add(j)
            for rr in st["w"]:
                j = last_w.get(rr)
                if j is not None and need(j):
                    deps.add(j)
                for k in readers.get(rr, ()):
                    if need(k):
                        deps.add(k)
            for rr in st["r"]:
                readers.setdefault(rr, []).append(i)
            for rr in st["w"]:
                last_w[rr] = i
                readers[rr] = []
            st["deps"] = deps
        for st in self.steps:
            for j in st["deps"]:
                self.steps[j]["inc"] = True
            if st["eng"] == "sp":
                st["inc"] = True
        for st in self.steps:
            if st["sem"] is None:
                st["sem"] = eng_sems[st["eng"]]
        for st in self.steps:
            st["val"] = st["sem"].bump() if st["inc"] else None
        for st in self.steps:
            wmap = {}
            for j in st["deps"]:
                sj = self.steps[j]
                k = id(sj["sem"])
                if k not in wmap or wmap[k][1] < sj["val"]:
                    wmap[k] = (sj["sem"], sj["val"])
            st["waits"] = list(wmap.values()) + st["xwaits"]

    def run_engine(self, name, eng):
        waited = {}
        issued = {}
        for st in self.steps:
            if st["eng"] != name:
                continue
            if name == "sp" and st["sem"] is not None and st["sem"].dma:
                s = st["sem"]
                n = issued.get(id(s), 0)
                if n > 0 and waited.get(id(s), -1) < 16 * n:
                    eng.wait_ge(s.h, 16 * n)
                    waited[id(s)] = 16 * n
                issued[id(s)] = n + 1
            for (s, v) in st["waits"]:
                if waited.get(id(s), -1) < v:
                    eng.wait_ge(s.h, v)
                    waited[id(s)] = v
            inst = st["fn"](eng)
            if st["inc"]:
                inst.then_inc(st["sem"].h, 16 if st["sem"].dma else 1)


class PackView:
    def __init__(self, ap, name):
        self._ap = ap
        self.name = name

    def __getitem__(self, idx):
        return self._ap[idx]


def build_nc():
    nc = bass.Bass("TRN2", target_bir_lowering=False)

    dx = nc.dram_tensor("xT", [D, T], F32, kind="ExternalInput").ap()
    dout = nc.dram_tensor("outT", [D, T], F32, kind="ExternalOutput").ap()
    dcb = nc.dram_tensor("cbd", [128, CBW], F32, kind="ExternalInput").ap()
    dw1T = nc.dram_tensor("w1T", [D, HID], F32, kind="ExternalInput").ap()
    dw2T = nc.dram_tensor("w2T", [HID, D], F32, kind="ExternalInput").ap()
    dsmall = {}
    for nme, shp in [("wqa", [128, 8 * H]), ("wka", [128, 8 * H]), ("wva", [128, 8 * H]),
                     ("woa", [H, D]), ("wkbT", [128, 8 * H]), ("wvbT", [128, 8 * H]),
                     ("wqbT", [128, 8 * H]), ("irw1T", [128, 8 * H]), ("pwT", [128, 8 * 32]),
                     ("nmw1T", [H, 2 * H]), ("nmw2T", [2 * H, H]),
                     ("irw2T", [H, 1]), ("wobT", [H, D])]:
        dsmall[nme] = nc.dram_tensor(nme, shp, F32, kind="ExternalInput").ap()
    dbias = {}
    for nme, shp in [("b1c", [128, 32]), ("b2c", [128, 8]), ("nmb1c", [128, 1]),
                     ("nmb2c", [H, 1]), ("irb1c", [H, 1]), ("irb2", [1, 1]),
                     ("normw", [H, 1]), ("alphaf", [1, 1]), ("alphas", [1, 1])]:
        dbias[nme] = nc.dram_tensor("d_" + nme, shp, F32, kind="ExternalInput").ap()
    dw1q = nc.dram_tensor("w1q_scr", [128, 32768], BF16, kind="Internal").ap()
    dw2q = nc.dram_tensor("w2q_scr", [128, 32768], BF16, kind="Internal").ap()

    def sb(name, shape, dtype):
        return nc.alloc_sbuf_tensor(name, shape, dtype)

    cbt = sb("cbt", [128, CBW], F32)
    xbuf = [sb(f"xb{i}", [128, 8 * P], F32) for i in range(3)]
    xr = [[sb(f"xr{s}{i}", [128, P], F32R) for i in range(2)] for s in range(2)]
    xsq_sh = [sb(f"xsq{i}", [128, P], F32R) for i in range(2)]
    xsq = [xsq_sh, xsq_sh]
    mrep = [sb(f"mrep{s}", [128, P], F32) for s in range(2)]
    rrep = [sb(f"rrep{s}", [128, P], F32) for s in range(2)]
    t1sh = sb("t1s0", [128, P], F32)
    t2sh = sb("t2s0", [128, P], F32)
    t1s_ = [t1sh, t1sh]
    t2s_ = [t2sh, t2sh]
    p1s = sb("p1s", [128, P], F32)
    p2s = sb("p2s", [128, P], F32)
    xqt = [sb(f"xq{i}", [128, P], BF16) for i in range(8)]
    hqt = [sb(f"hq{i}", [128, P], BF16) for i in range(32)]
    def hq_res(oc):
        return f"hq{oc}"
    slab = [sb(f"slab{i}", [128, 2048], BF16) for i in range(4)]
    qst = [sb(f"qst{i}", [128, 512], BF16) for i in range(2)]
    stg = [xbuf[2]]

    wqa_r = sb("wqa_r", [128, 8 * H], F32R)
    wka_r = sb("wka_r", [128, 8 * H], F32R)
    wva_r = sb("wva_r", [128, 8 * H], F32R)
    woa_r = sb("woa_r", [H, D], BF16)
    wkb_r = sb("wkb_r", [128, 8 * H], F32R)
    wvb_r = sb("wvb_r", [128, 8 * H], F32R)
    wqb_r = sb("wqb_r", [128, 8 * H], F32R)
    irw1_r = sb("irw1_r", [128, 8 * H], F32R)
    pw_r = sb("pw_r", [128, 8 * 32], F32R)
    nmw1_r = sb("nmw1_r", [H, 2 * H], F32R)
    nmw2_r = sb("nmw2_r", [2 * H, H], F32R)
    irw2_r = sb("irw2_r", [H, 8], F32R)
    irw2f = sb("irw2f", [H, 1], F32)
    wob_r = sb("wob_r", [H, D], BF16)
    ones_r = sb("ones_r", [128, 128], F32R)
    ident_r = sb("ident_r", [128, 128], F32R)

    cs_sum = sb("cs_sum", [H, 8], F32)
    cspw = sb("cspw", [32, 1], F32)

    b1i = sb("b1i", [128, 32], F32)
    b2qc = sb("b2qc", [128, 8], F32)
    nmb1c = sb("nmb1c", [128, 1], F32)
    nmb2c = sb("nmb2c", [H, 1], F32)
    irb1c = sb("irb1c", [H, 1], F32)
    irb2c = sb("irb2c", [128, 1], F32)
    nw_c = sb("nw_c", [H, 1], F32)

    lnA = [sb(f"lnA{m}", [128, 1], F32) for m in range(2)]
    acol = [sb(f"acol{m}", [128, 1], F32) for m in range(2)]
    bcol = [sb(f"bcol{m}", [128, 1], F32) for m in range(2)]
    g1c = [sb(f"g1c{m}", [128, 1], F32) for m in range(2)]
    g1n = [sb(f"g1n{m}", [128, 1], F32) for m in range(2)]
    g2c = [sb(f"g2c{m}", [128, 1], F32) for m in range(2)]
    g3c = [sb(f"g3c{m}", [128, 1], F32) for m in range(2)]
    aCc = [sb(f"aCc{m}", [128, 1], F32) for m in range(2)]
    DS_ = [sb(f"DS{m}", [128, 128], F32) for m in range(2)]
    DSB = [sb(f"DSB{m}", [128, 128], F32) for m in range(2)]
    DSBT = [sb(f"DSBT{m}", [128, 128], F32) for m in range(2)]
    DI_ = [sb(f"DI{m}", [128, 128], F32) for m in range(2)]

    F0m = [sb(f"F0m{m}", [H, H], F32R) for m in range(2)]
    Za = sb("Za", [H, H], F32R)
    zcol = sb("zcol", [H, 8], F32R)

    qTa = sb("qTa", [H, P], F32R)
    kTa = sb("kTa", [H, P], F32R)
    vTa = sb("vTa", [H, P], F32R)
    oTa = sb("oTa", [H, P], BF16)
    PTs = sb("PTs", [128, 128], F32R)
    QKs = sb("QKs", [128, 128], F32)
    deni = sb("deni", [128, 1], F32)
    dent = sb("dent", [128, 1], F32)
    ztmp = sb("ztmp", [H, 1], F32)
    Ktm_a = sb("Ktm_a", [128, H], F32R)
    Vtm_a = sb("Vtm_a", [128, H], F32R)
    Otm_a = sb("Otm_a", [128, H], F32R)

    kbT = sb("kbT", [H, P], F32R)
    qbT = sb("qbT", [H, P], F32R)
    vbT = sb("vbT", [H, P], F32R)
    nuT = sb("nuT", [H, P], F32)
    gcr = sb("gcr", [H, P], F32R)
    sc1r = sb("sc1r", [128, P], F32R)
    obT = sb("obT", [H, P], BF16)
    pwC = sb("pwC", [32, P], F32R)
    pwt = sb("pwt", [128, NSC * 3], F32)
    ucol = sb("ucol", [128, NSC], F32)
    urow = sb("urow", [1, 128], F32R)
    unrow = sb("unrow", [1, 128], F32R)
    uncol = sb("uncol", [128, 1], F32)
    Nfm = sb("Nfm", [H, P], F32)
    nsinit = sb("nsinit", [H, 1], F32)
    nsA = sb("nsA", [H, 128], F32)
    nsB = sb("nsB", [H, 128], F32)
    Ntm = sb("Ntm", [128, H], F32)
    nscr = sb("nscr", [128, 128], F32)

    Npw = [[sb(f"Npw{m}{i}", [128, 128], F32R) for i in range(2)] for m in range(2)]
    NpwT = [[sb(f"NpwT{m}{i}", [128, 128], F32R) for i in range(2)] for m in range(2)]
    Ut = [[sb(f"Ut{m}{i}", [128, H], F32R) for i in range(2)] for m in range(2)]
    Rm = [sb(f"Rm{m}", [128, H], F32R) for m in range(2)]
    WxT = [sb(f"WxT{m}", [128, 128], F32R) for m in range(2)]
    WiT = [sb(f"WiT{m}", [128, 128], F32R) for m in range(2)]
    Ktm = sb("Ktm", [128, H], F32R)
    Ktg = sb("Ktg", [128, H], F32R)
    Vtm = sb("Vtm", [128, H], F32)
    rdf = sb("rdf", [128, H], F32)
    rds = sb("rds", [128, H], F32)
    predt = sb("predt", [128, H], F32)
    dsq = sb("dsq", [128, H], F32)
    errc = sb("errc", [128, 1], F32)
    oatt = sb("oatt", [128, H], F32)
    onrm = sb("onrm", [128, H], F32R)
    msc = sb("msc", [128, 1], F32)

    pbt = [nc.alloc_psum_tensor(f"pb{i}", [128, 512], F32) for i in range(8)]
    pbr = [p.bitcast(F32R) for p in pbt]
    PB = [f"pb{i}" for i in range(8)]

    def cbv(name, rows=128):
        o, w = _OFF[name]
        return cbt[0:rows, o:o + w]

    TK = Tracker()
    sem_cms = []

    def mksem(name, dma=False):
        cm = nc.semaphore(name)
        h = cm.__enter__()
        sem_cms.append(cm)
        return SemT(h, name, dma)

    s_x = mksem("s_x", dma=True)
    s_sl = [mksem("s_sl0", dma=True), mksem("s_sl1", dma=True),
            mksem("s_sl2", dma=True), mksem("s_sl3", dma=True)]
    s_sw = mksem("s_sw", dma=True)
    s_qs = [mksem("s_qs0", dma=True), mksem("s_qs1", dma=True),
            mksem("s_qs2", dma=True)]
    s_qt = [mksem("s_qt0", dma=True), mksem("s_qt1", dma=True)]
    s_out = mksem("s_out", dma=True)
    s_pe = mksem("s_pe")
    s_dv = mksem("s_dv")
    s_ac = mksem("s_ac")
    s_gp = mksem("s_gp")
    ENG_SEMS = {"pe": s_pe, "dv": s_dv, "ac": s_ac, "gp": s_gp}

    E = TK.emit

    E("sp", lambda e: e.dma_start(out=cbt[:], in_=dcb), w=["cbt"], sem=s_sw)
    def x_load(c, buf):
        E("sp", lambda e, c=c, buf=buf: e.dma_start(
            out=xbuf[buf][:],
            in_=dx[c * 128:(c + 1) * 128, :]), lat=2500,
          w=[f"xb{buf}"], sem=s_x)

    x_load(0, 0)

    small_jobs = [
        (dsmall["wqa"], wqa_r, 128, 8 * H, None, True),
        (dsmall["wka"], wka_r, 128, 8 * H, None, True),
        (dsmall["wva"], wva_r, 128, 8 * H, None, True),
        (dsmall["wkbT"], wkb_r, 128, 8 * H, None, True),
        (dsmall["wvbT"], wvb_r, 128, 8 * H, None, True),
        (dsmall["wqbT"], wqb_r, 128, 8 * H, None, True),
        (dsmall["irw1T"], irw1_r, 128, 8 * H, None, True),
        (dsmall["pwT"], pw_r, 128, 8 * 32, None, True),
        (dsmall["woa"], woa_r, H, D, None, False),
        (dsmall["nmw1T"], nmw1_r, H, 2 * H, None, False),
        (dsmall["nmw2T"], nmw2_r, 2 * H, H, None, False),
        (dsmall["irw2T"], irw2f, H, 1, None, False),
        (dsmall["wobT"], wob_r, H, D, "normw", False),
    ]
    E("sp", lambda e: e.dma_start(out=nw_c[:], in_=dbias["normw"]), w=["nw_c"], sem=s_sw)
    for i, (srcd, dst, rr, cc, fold, chunked) in enumerate(small_jobs):
        E("sp", lambda e, srcd=srcd, rr=rr, cc=cc:
          e.dma_start(out=stg[0][0:rr, 0:cc], in_=srcd),
          r=[], w=["xb2"], sem=s_sw)
        if fold == "normw":
            E("dv", lambda e, dst=dst, rr=rr, cc=cc:
              e.tensor_scalar(out=dst[0:rr, 0:cc], in0=stg[0][0:rr, 0:cc],
                              scalar1=nw_c[:], scalar2=None, op0=ALU.mult),
              r=["xb2", "nw_c"], w=[dst.name])
        else:
            E("dv", lambda e, dst=dst, rr=rr, cc=cc:
              e.tensor_copy(out=dst[0:rr, 0:cc], in_=stg[0][0:rr, 0:cc]),
              r=["xb2"], w=[dst.name])
    E("dv", lambda e: e.tensor_scalar(out=irw2_r[:], in0=cbv("ones128", rows=H)[:, 0:8],
                                      scalar1=irw2f[:], scalar2=None, op0=ALU.mult),
      r=["cbt", "irw2f"], w=["irw2_r"])
    E("dv", lambda e: e.tensor_copy(out=ones_r[:], in_=cbv("ones128")),
      r=["cbt"], w=["ones_r"])
    E("dv", lambda e: e.tensor_copy(out=ident_r[:], in_=cbv("ident")),
      r=["cbt"], w=["ident_r"])

    for nme, dst in [("b1c", b1i), ("b2c", b2qc), ("nmb1c", nmb1c), ("nmb2c", nmb2c),
                     ("irb1c", irb1c)]:
        E("sp", lambda e, nme=nme, dst=dst: e.dma_start(out=dst[:], in_=dbias[nme]),
          w=[dst.name], sem=s_sw)
    E("sp", lambda e: e.dma_start(out=irb2c[:], in_=dbias["irb2"].broadcast_to((128, 1)),
                                  allow_slow_non_contiguous=True),
      w=["irb2c"], sem=s_sw)
    E("dv", lambda e: e.tensor_scalar(out=b1i[:], in0=b1i[:], scalar1=64.0, scalar2=M_RND,
                                      op0=ALU.mult, op1=ALU.add), r=["b1i"], w=["b1i"])
    E("dv", lambda e: e.tensor_scalar(out=b1i[:], in0=b1i[:], scalar1=M_RND, scalar2=127.0,
                                      op0=ALU.subtract, op1=ALU.min), r=["b1i"], w=["b1i"])
    E("dv", lambda e: e.tensor_scalar(out=b1i[:], in0=b1i[:], scalar1=-128.0, scalar2=0.25,
                                      op0=ALU.max, op1=ALU.mult), r=["b1i"], w=["b1i"])
    E("dv", lambda e: e.tensor_scalar(out=b2qc[:], in0=b2qc[:], scalar1=64.0, scalar2=M_RND,
                                      op0=ALU.mult, op1=ALU.add), r=["b2qc"], w=["b2qc"])
    E("dv", lambda e: e.tensor_scalar(out=b2qc[:], in0=b2qc[:], scalar1=M_RND, scalar2=127.0,
                                      op0=ALU.subtract, op1=ALU.min), r=["b2qc"], w=["b2qc"])
    E("dv", lambda e: e.tensor_scalar(out=b2qc[:], in0=b2qc[:], scalar1=-128.0,
                                      scalar2=1.0 / 64.0, op0=ALU.max, op1=ALU.mult),
      r=["b2qc"], w=["b2qc"])

    cs_specs = [(wqa_r, H, 0), (wka_r, H, 1), (wva_r, H, 2), (wkb_r, H, 3),
                (wvb_r, H, 4), (wqb_r, H, 5), (irw1_r, H, 6)]
    for wt, od, j in cs_specs:
        for ic in range(8):
            E("pe", lambda e, wt=wt, od=od, j=j, ic=ic:
              e.matmul(pbt[6][0:od, 8 * j:8 * j + 8], wt[:, ic * od:(ic + 1) * od],
                       ones_r[:, 0:8], start=(ic == 0), stop=(ic == 7)),
              r=[wt.name, "ones_r"], w=[PB[6]])
    for ic in range(8):
        E("pe", lambda e, ic=ic:
          e.matmul(pbt[6][0:32, 56:64], pw_r[:, ic * 32:(ic + 1) * 32],
                   ones_r[:, 0:8], start=(ic == 0), stop=(ic == 7)),
          r=["pw_r", "ones_r"], w=[PB[6]])
    for j in range(7):
        E("dv", lambda e, j=j: e.tensor_copy(out=cs_sum[0:H, j:j + 1],
                                             in_=pbt[6][0:H, 8 * j:8 * j + 1]),
          r=[PB[6]], w=["cs_sum"])
    E("dv", lambda e: e.tensor_copy(out=cspw[:], in_=pbt[6][0:32, 56:57]),
      r=[PB[6]], w=["cspw"])

    def cs(j):
        return cs_sum[0:H, j:j + 1]

    for m in range(2):
        dal = dbias["alphaf"] if m == 0 else dbias["alphas"]
        E("sp", lambda e, dal=dal, m=m:
          e.dma_start(out=acol[m][:], in_=dal.broadcast_to((128, 1)),
                      allow_slow_non_contiguous=True),
          w=[acol[m].name], sem=s_sw)
        E("ac", lambda e, m=m: e.activation(out=lnA[m][:], in_=acol[m][:], func=AF.Ln),
          r=[acol[m].name], w=[lnA[m].name])
        E("dv", lambda e, m=m: e.tensor_scalar(out=bcol[m][:], in0=acol[m][:], scalar1=-1.0,
                                               scalar2=1.0, op0=ALU.mult, op1=ALU.add),
          r=[acol[m].name], w=[bcol[m].name])

        def gcol(dst, iota_name, m=m):
            E("dv", lambda e, dst=dst, iota_name=iota_name, m=m:
              e.tensor_scalar(out=dst[:], in0=cbv(iota_name), scalar1=lnA[m][:],
                              scalar2=None, op0=ALU.mult),
              r=["cbt", lnA[m].name], w=[dst.name])
            E("ac", lambda e, dst=dst: e.activation(out=dst[:], in_=dst[:], func=AF.Exp),
              r=[dst.name], w=[dst.name])

        gcol(g1c[m], "iota0")
        gcol(g2c[m], "iota1")
        gcol(g3c[m], "iotaC")
        E("dv", lambda e, m=m: e.tensor_scalar(out=aCc[m][:], in0=lnA[m][:], scalar1=float(C),
                                               scalar2=None, op0=ALU.mult),
          r=[lnA[m].name], w=[aCc[m].name])
        E("ac", lambda e, m=m: e.activation(out=aCc[m][:], in_=aCc[m][:], func=AF.Exp),
          r=[aCc[m].name], w=[aCc[m].name])
        E("dv", lambda e, m=m: e.tensor_tensor(out=g1n[m][:], in0=g1c[m][:], in1=bcol[m][:],
                                               op=ALU.mult),
          r=[g1c[m].name, bcol[m].name], w=[g1n[m].name])
        E("dv", lambda e, m=m: e.tensor_scalar(out=g1n[m][:], in0=g1n[m][:], scalar1=-1.0,
                                               scalar2=None, op0=ALU.mult),
          r=[g1n[m].name], w=[g1n[m].name])

        def dmat(dst, expo_name, mask_name, scale_beta, m=m):
            E("dv", lambda e, dst=dst, expo_name=expo_name, m=m:
              e.tensor_scalar(out=dst[:], in0=cbv(expo_name), scalar1=lnA[m][:],
                              scalar2=None, op0=ALU.mult),
              r=["cbt", lnA[m].name], w=[dst.name])
            E("ac", lambda e, dst=dst: e.activation(out=dst[:], in_=dst[:], func=AF.Exp),
              r=[dst.name], w=[dst.name])
            E("dv", lambda e, dst=dst, mask_name=mask_name:
              e.tensor_tensor(out=dst[:], in0=dst[:], in1=cbv(mask_name), op=ALU.mult),
              r=[dst.name, "cbt"], w=[dst.name])
            if scale_beta:
                E("dv", lambda e, dst=dst, m=m:
                  e.tensor_scalar(out=dst[:], in0=dst[:], scalar1=bcol[m][:], scalar2=None,
                                  op0=ALU.mult),
                  r=[dst.name, bcol[m].name], w=[dst.name])

        dmat(DS_[m], "expoSU", "maskSU", False)
        dmat(DSB[m], "expoSU", "maskSU", True)
        dmat(DSBT[m], "expoSL", "maskSL", True)
        dmat(DI_[m], "expoIU", "maskIU", False)

    for tns, rr_x, cc_x in [(F0m[0], H, H), (F0m[1], H, H), (Za, H, H), (zcol, H, 8)]:
        E("dv", lambda e, tns=tns, rr_x=rr_x, cc_x=cc_x:
          e.tensor_scalar(out=tns[0:rr_x, 0:cc_x], in0=cbv("zero64", rows=rr_x)[:, 0:cc_x],
                          scalar1=0.0, scalar2=None, op0=ALU.mult),
          r=["cbt"], w=[tns.name])
    E("dv", lambda e: e.tensor_scalar(out=nsinit[:], in0=cbv("zerocol", rows=H),
                                      scalar1=0.0, scalar2=None, op0=ALU.mult),
      r=["cbt"], w=["nsinit"])

    def quant_proc(b, cols, stores):
        xbn = f"xb{b}"
        E("ac", lambda e, cols=cols, b=b:
          e.activation(out=xbuf[b][:, 0:cols], in_=xbuf[b][:, 0:cols], func=AF.Relu,
                       scale=64.0, bias=cbv("Mcol")),
          r=[xbn, "cbt"], w=[xbn], cost=int(cols * 0.9))
        E("dv", lambda e, cols=cols, b=b:
          e.tensor_scalar(out=xbuf[b][:, 0:cols], in0=xbuf[b][:, 0:cols], scalar1=M_RND,
                          scalar2=127.0, op0=ALU.subtract, op1=ALU.min),
          r=[xbn], w=[xbn], cost=int(cols * 0.55))
        for q, (dst_ap, region, j) in enumerate(stores):
            E("dv", lambda e, q=q, j=j, b=b:
              e.tensor_scalar(out=qst[j][:], in0=xbuf[b][:, q * 512:(q + 1) * 512],
                              scalar1=-128.0, scalar2=None, op0=ALU.max),
              r=[xbn], w=[f"qst{j}"], cost=330)
            E("sp", lambda e, dst_ap=dst_ap, j=j:
              e.dma_start(out=dst_ap, in_=qst[j][:]),
              r=[f"qst{j}"], w=[region], sem=s_qt[j], lat=800)

    for og in range(8):
        b = 1 + og % 2
        E("sp", lambda e, og=og, b=b:
          e.dma_start(out=xbuf[b][:], in_=dw1T[og * 128:(og + 1) * 128, :]),
          w=[f"xb{b}"], sem=s_qs[og % 3], lat=2500)
        quant_proc(b, HID, [
            (dw1q[:, og * 4096 + s * 512:og * 4096 + (s + 1) * 512],
             f"dw1q_q{og}", s % 2) for s in range(8)])
    for s in range(32):
        b = 1 + s % 2
        E("sp", lambda e, s=s, b=b:
          e.dma_start(out=xbuf[b][:, 0:D], in_=dw2T[s * 128:(s + 1) * 128, 0:D]),
          w=[f"xb{b}"], sem=s_qs[s % 3], lat=2000)
        quant_proc(b, D, [
            (dw2q[:, s * 1024 + q * 512:s * 1024 + (q + 1) * 512],
             f"dw2q_r{s // 2}", (s // 2) % 2) for q in range(2)])
    x_load(1, 1)

    def emit_ln(buf, S, want_r16, proj_specs, sum_bank, sq_bank, passes=(0,)):
        xbn = f"xb{buf}"
        mr, rr_ = mrep[S], rrep[S]
        for pss in passes:
            specs = [sp_ for sp_ in proj_specs if sp_.get("pass", 0) == pss]
            for ic in range(8):
                pp = ic % 2
                xrt, xst = xr[S][pp], xsq[S][pp]
                E("ac", lambda e, ic=ic, xrt=xrt, buf=buf:
                  e.activation(out=xrt[:], in_=xbuf[buf][:, ic * P:(ic + 1) * P],
                               func=AF.Copy),
                  r=[xbn], w=[xrt.name])
                if pss == 0:
                    E("ac", lambda e, ic=ic, xst=xst, buf=buf:
                      e.activation(out=xst[:], in_=xbuf[buf][:, ic * P:(ic + 1) * P],
                                   func=AF.Square),
                      r=[xbn], w=[xst.name])
                    E("pe", lambda e, ic=ic, xrt=xrt:
                      e.matmul(pbt[sum_bank][:], ones_r[:], xrt[:],
                               start=(ic == 0), stop=(ic == 7)),
                      r=["ones_r", xrt.name], w=[PB[sum_bank]])
                    E("pe", lambda e, ic=ic, xst=xst:
                      e.matmul(pbt[sq_bank][:], ones_r[:], xst[:],
                               start=(ic == 0), stop=(ic == 7)),
                      r=["ones_r", xst.name], w=[PB[sq_bank]])
                for spec in specs:
                    E("pe", lambda e, ic=ic, xrt=xrt, spec=spec:
                      e.matmul(spec["out"], spec["lhsT"](ic), xrt[:],
                               start=(ic == 0), stop=(ic == 7)),
                      r=[spec["lres"], xrt.name], w=[spec["bank"]])
            if pss == 0:
                E("dv", lambda e, mr=mr: e.tensor_scalar(out=mr[:], in0=pbt[sum_bank][:],
                                                         scalar1=1.0 / D, scalar2=None,
                                                         op0=ALU.mult),
                  r=[PB[sum_bank]], w=[mr.name])
                E("dv", lambda e, rr_=rr_: e.tensor_scalar(out=rr_[:], in0=pbt[sq_bank][:],
                                                           scalar1=1.0 / D, scalar2=None,
                                                           op0=ALU.mult),
                  r=[PB[sq_bank]], w=[rr_.name])
                E("dv", lambda e, mr=mr: e.tensor_tensor(out=p2s[:], in0=mr[:], in1=mr[:],
                                                         op=ALU.mult),
                  r=[mr.name], w=["p2s"])
                E("dv", lambda e, rr_=rr_: e.tensor_tensor(out=rr_[:], in0=rr_[:], in1=p2s[:],
                                                           op=ALU.subtract),
                  r=[rr_.name, "p2s"], w=[rr_.name])
                E("ac", lambda e, rr_=rr_: e.activation(out=rr_[:], in_=rr_[:], func=AF.Ln,
                                                        bias=cbv("epscol")),
                  r=[rr_.name, "cbt"], w=[rr_.name])
                if want_r16:
                    E("ac", lambda e, rr_=rr_: e.activation(out=rr_[:], in_=rr_[:],
                                                            func=AF.Exp, scale=-0.5,
                                                            bias=cbv("ln16col")),
                      r=[rr_.name, "cbt"], w=[rr_.name])
                else:
                    E("ac", lambda e, rr_=rr_: e.activation(out=rr_[:], in_=rr_[:],
                                                            func=AF.Exp, scale=-0.5),
                      r=[rr_.name], w=[rr_.name])

    def emit_proj_pass(buf, S, specs):
        xbn = f"xb{buf}"
        for ic in range(8):
            pp = ic % 2
            xrt = xr[S][pp]
            E("dv", lambda e, ic=ic, xrt=xrt, buf=buf:
              e.tensor_copy(out=xrt[:], in_=xbuf[buf][:, ic * P:(ic + 1) * P]),
              r=[xbn], w=[xrt.name])
            for spec in specs:
                E("pe", lambda e, ic=ic, xrt=xrt, spec=spec:
                  e.matmul(spec["out"], spec["lhsT"](ic), xrt[:],
                           start=(ic == 0), stop=(ic == 7)),
                  r=[spec["lres"], xrt.name], w=[spec["bank"]])

    def fold_correct(S, dst_ap, dst_res, bank, psum_ap, cscol, cs_res, rows):
        mr, rr_ = mrep[S], rrep[S]
        t2 = t2s_[S]
        E("dv", lambda e: e.tensor_scalar(out=t2[0:rows, :], in0=mr[0:rows, :],
                                          scalar1=cscol, scalar2=None, op0=ALU.mult),
          r=[mr.name, cs_res], w=[t2.name])
        E("dv", lambda e: e.tensor_tensor(out=t2[0:rows, :], in0=psum_ap,
                                          in1=t2[0:rows, :], op=ALU.subtract),
          r=[bank, t2.name], w=[t2.name])
        E("dv", lambda e: e.tensor_tensor(out=dst_ap, in0=t2[0:rows, :],
                                          in1=rr_[0:rows, :], op=ALU.mult),
          r=[t2.name, rr_.name], w=[dst_res])

    def phi_inplace(S, dst, dst_res):
        t1, t2 = t1s_[S], t2s_[S]
        E("dv", lambda e: e.tensor_scalar(out=t1[0:H, :], in0=t2[0:H, :], scalar1=0.0,
                                          scalar2=None, op0=ALU.min),
          r=[t2.name], w=[t1.name])
        E("ac", lambda e: e.activation(out=t1[0:H, :], in_=t1[0:H, :], func=AF.Exp),
          r=[t1.name], w=[t1.name])
        E("dv", lambda e, dst=dst: e.scalar_tensor_tensor(out=dst[:], in0=t2[0:H, :],
                                                          scalar=0.0, in1=t1[0:H, :],
                                                          op0=ALU.max, op1=ALU.add),
          r=[t1.name, t2.name], w=[dst_res])

    def l2norm_cols(S, raw_bank, raw_psum, cscol, cs_res, dst, dst_res, ss_bank):
        mr, rr_ = mrep[S], rrep[S]
        t1, t2 = t1s_[S], t2s_[S]
        E("dv", lambda e: e.tensor_scalar(out=t2[0:H, :], in0=mr[0:H, :],
                                          scalar1=cscol, scalar2=None, op0=ALU.mult),
          r=[mr.name, cs_res], w=[t2.name])
        E("dv", lambda e: e.tensor_tensor(out=t2[0:H, :], in0=raw_psum, in1=t2[0:H, :],
                                          op=ALU.subtract),
          r=[raw_bank, t2.name], w=[t2.name])
        E("dv", lambda e: e.tensor_tensor(out=t2[0:H, :], in0=t2[0:H, :],
                                          in1=rr_[0:H, :], op=ALU.mult),
          r=[t2.name, rr_.name], w=[t2.name])
        E("ac", lambda e: e.activation(out=sc1r[0:H, :], in_=t2[0:H, :], func=AF.Square),
          r=[t2.name], w=["sc1r"])
        E("pe", lambda e: e.matmul(pbt[ss_bank][:], ones_r[0:H, :], sc1r[0:H, :],
                                   start=True, stop=True),
          r=["ones_r", "sc1r"], w=[PB[ss_bank]])
        E("ac", lambda e: e.activation(out=t1[:], in_=pbt[ss_bank][:], func=AF.Ln),
          r=[PB[ss_bank]], w=[t1.name])
        E("ac", lambda e: e.activation(out=t1[:], in_=t1[:], func=AF.Exp, scale=0.5),
          r=[t1.name], w=[t1.name])
        E("dv", lambda e: e.tensor_scalar(out=t1[:], in0=t1[:], scalar1=1e-12,
                                          scalar2=None, op0=ALU.max),
          r=[t1.name], w=[t1.name])
        E("dv", lambda e: e.reciprocal(out=t1[:], in_=t1[:]), r=[t1.name], w=[t1.name])
        E("dv", lambda e, dst=dst: e.tensor_tensor(out=dst[:], in0=t2[0:H, :],
                                                   in1=t1[0:H, :], op=ALU.mult),
          r=[t1.name, t2.name], w=[dst_res])

    def emit_ffn_p1(buf, S):
        xbn = f"xb{buf}"
        mr, r16t = mrep[S], rrep[S]
        t1 = t1s_[S]
        emit_ln(buf, S, want_r16=True, proj_specs=[], sum_bank=0, sq_bank=1)
        for ic in range(8):
            E("dv", lambda e, ic=ic, buf=buf, mr=mr:
              e.tensor_tensor(out=t1[:], in0=xbuf[buf][:, ic * P:(ic + 1) * P],
                              in1=mr[:], op=ALU.subtract),
              r=[xbn, mr.name], w=[t1.name])
            E("dv", lambda e, r16t=r16t: e.tensor_tensor(out=t1[:], in0=t1[:], in1=r16t[:],
                                                         op=ALU.mult),
              r=[t1.name, r16t.name], w=[t1.name])
            E("dv", lambda e: e.tensor_scalar(out=t1[:], in0=t1[:], scalar1=M_RND,
                                              scalar2=M_RND, op0=ALU.add, op1=ALU.subtract),
              r=[t1.name], w=[t1.name])
            E("dv", lambda e, ic=ic: e.tensor_scalar(out=xqt[ic][:], in0=t1[:], scalar1=127.0,
                                                     scalar2=-128.0, op0=ALU.min, op1=ALU.max),
              r=[t1.name], w=[f"xq{ic}"])

    def emit_ffn_p2(buf, S):
        xbn = f"xb{buf}"
        for og in range(8):
            slh = [(2 * og) % 4, (2 * og + 1) % 4]
            for h in range(2):
                E("sp", lambda e, og=og, h=h, sl=slh[h]:
                  e.dma_start(out=slab[sl][:],
                              in_=dw1q[:, og * 4096 + h * 2048:og * 4096 + h * 2048 + 2048]),
                  w=[f"slab{slh[h]}"], r=[f"dw1q_q{og}"],
                  sem=s_sl[slh[h]], lat=1000)
            for o4 in range(4):
                oc = og * 4 + o4
                bank = 2 + oc % 2
                for ic in range(8):
                    sl = slh[ic // 4]
                    E("pe", lambda e, sl=sl, o4=o4, ic=ic, bank=bank:
                      e.matmul(pbt[bank][:],
                               slab[sl][:, (ic % 4) * 512 + o4 * 128:
                                           (ic % 4) * 512 + o4 * 128 + 128],
                               xqt[ic][:], start=(ic == 0), stop=(ic == 7)),
                      r=[f"slab{sl}", f"xq{ic}"], w=[PB[bank]])
                E("ac", lambda e, oc=oc, bank=bank:
                  e.activation(out=p1s[:], in_=pbt[bank][:], func=AF.Relu,
                               scale=2.0 ** -6, bias=b1i[:, oc:oc + 1]),
                  r=[PB[bank], "b1i"], w=["p1s"])
                E("ac", lambda e: e.activation(out=p2s[:], in_=p1s[:], func=AF.Relu,
                                               bias=cbv("Mcol")),
                  r=["p1s", "cbt"], w=["p2s"])
                E("dv", lambda e, oc=oc: e.tensor_scalar(out=hqt[oc][:], in0=p2s[:],
                                                         scalar1=M_RND, scalar2=127.0,
                                                         op0=ALU.subtract, op1=ALU.min),
                  r=["p2s"], w=[hq_res(oc)], cost=400)
        for og in range(8):
            slh = [(2 * og) % 4, (2 * og + 1) % 4]
            for h in range(2):
                E("sp", lambda e, og=og, h=h, sl=slh[h]:
                  e.dma_start(out=slab[sl][:],
                              in_=dw2q[:, (og * 4 + h * 2) * 1024:
                                          (og * 4 + h * 2) * 1024 + 2048]),
                  w=[f"slab{slh[h]}"], r=[f"dw2q_r{2 * og + h}"],
                  sem=s_sl[slh[h]], lat=1000)
            for o4 in range(4):
                oc = og * 4 + o4
                sl = slh[o4 // 2]
                for jc in range(8):
                    E("pe", lambda e, sl=sl, o4=o4, oc=oc, jc=jc:
                      e.matmul(pbt[jc][:],
                               slab[sl][:, (o4 % 2) * 1024 + jc * 128:
                                           (o4 % 2) * 1024 + jc * 128 + 128],
                               hqt[oc][:], start=(oc == 0), stop=(oc == 31)),
                      r=[f"slab{sl}", hq_res(oc)], w=[PB[jc]])
        for jc in range(8):
            E("dv", lambda e, jc=jc, buf=buf:
              e.tensor_scalar(out=xbuf[buf][:, jc * P:(jc + 1) * P],
                              in0=xbuf[buf][:, jc * P:(jc + 1) * P],
                              scalar1=b2qc[:, jc:jc + 1], scalar2=None, op0=ALU.add),
              r=[xbn, "b2qc"], w=[xbn])
            E("dv", lambda e, jc=jc, buf=buf:
              e.scalar_tensor_tensor(out=xbuf[buf][:, jc * P:(jc + 1) * P],
                                     in0=pbt[jc][:], scalar=2.0 ** -10,
                                     in1=xbuf[buf][:, jc * P:(jc + 1) * P],
                                     op0=ALU.mult, op1=ALU.add),
              r=[PB[jc], xbn], w=[xbn])

    def emit_attn_a_pre(buf):
        proj = [
            dict(bank=PB[2], out=pbt[2][0:H, 0:P], lres="wqa_r",
                 lhsT=lambda ic: wqa_r[:, ic * H:(ic + 1) * H]),
            dict(bank=PB[3], out=pbt[3][0:H, 0:P], lres="wka_r",
                 lhsT=lambda ic: wka_r[:, ic * H:(ic + 1) * H]),
            dict(bank=PB[4], out=pbt[4][0:H, 0:P], lres="wva_r",
                 lhsT=lambda ic: wva_r[:, ic * H:(ic + 1) * H]),
        ]
        emit_ln(buf, 0, want_r16=False, proj_specs=proj, sum_bank=0, sq_bank=1)
        fold_correct(0, t2s_[0][0:H, :], "t2s0", PB[2], pbt[2][0:H, 0:P], cs(0), "cs_sum", H)
        phi_inplace(0, qTa, "qTa")
        fold_correct(0, t2s_[0][0:H, :], "t2s0", PB[3], pbt[3][0:H, 0:P], cs(1), "cs_sum", H)
        phi_inplace(0, kTa, "kTa")
        fold_correct(0, vTa[:], "vTa", PB[4], pbt[4][0:H, 0:P], cs(2), "cs_sum", H)

    def emit_attn_a_sc(sc):
        sm = {"pe": 250, "dv": 260, "ac": 280}
        E = lambda eng, fn, r=(), w=(), **kw: TK.emit(eng, fn, r=r, w=w,
                                                      cost=kw.pop("cost", sm.get(eng)), **kw)
        s0 = sc * C
        qs = qTa[:, s0:s0 + C]
        ks = kTa[:, s0:s0 + C]
        vs = vTa[:, s0:s0 + C]
        E("pe", lambda e, ks=ks: e.transpose(pbr[2][:, 0:H], ks, ident_r[0:H, 0:H]),
          r=["kTa", "ident_r"], w=[PB[2]])
        E("dv", lambda e: e.tensor_copy(out=Ktm_a[:], in_=pbr[2][:, 0:H]),
          r=[PB[2]], w=["Ktm_a"])
        E("pe", lambda e, vs=vs: e.transpose(pbr[2][:, 64:128], vs, ident_r[0:H, 0:H]),
          r=["vTa", "ident_r"], w=[PB[2]])
        E("dv", lambda e: e.tensor_copy(out=Vtm_a[:], in_=pbr[2][:, 64:128]),
          r=[PB[2]], w=["Vtm_a"])
        E("pe", lambda e, vs=vs, qs=qs: e.matmul(pbt[2][:, 128:256], vs, qs,
                                                 start=True, stop=True),
          r=["vTa", "qTa"], w=[PB[2]])
        E("dv", lambda e: e.tensor_tensor(out=PTs[:], in0=pbt[2][:, 128:256],
                                          in1=cbv("maskIU"), op=ALU.mult),
          r=[PB[2], "cbt"], w=["PTs"])
        E("pe", lambda e, qs=qs, ks=ks: e.matmul(pbt[2][:, 256:384], qs, ks,
                                                 start=True, stop=True),
          r=["qTa", "kTa"], w=[PB[2]])
        E("dv", lambda e: e.tensor_tensor(out=QKs[:], in0=pbt[2][:, 256:384],
                                          in1=cbv("maskIL"), op=ALU.mult),
          r=[PB[2], "cbt"], w=["QKs"])
        E("dv", lambda e: e.reduce_sum(out=deni[:], in_=QKs[:], axis=AXX),
          r=["QKs"], w=["deni"])
        E("pe", lambda e, qs=qs: e.matmul(pbt[2][:, 384:392], qs, zcol[:],
                                          start=True, stop=True),
          r=["qTa", "zcol"], w=[PB[2]])
        E("dv", lambda e: e.tensor_tensor(out=dent[:], in0=deni[:],
                                          in1=pbt[2][:, 384:385], op=ALU.add),
          r=["deni", PB[2]], w=["dent"])
        E("dv", lambda e: e.tensor_scalar(out=dent[:], in0=dent[:], scalar1=1e-6,
                                          scalar2=None, op0=ALU.max),
          r=["dent"], w=["dent"])
        E("dv", lambda e: e.reciprocal(out=dent[:], in_=dent[:]), r=["dent"], w=["dent"])
        E("pe", lambda e: e.matmul(pbt[3][:, 0:H], PTs[:], Ktm_a[:], start=True, stop=False),
          r=["PTs", "Ktm_a"], w=[PB[3]])
        E("pe", lambda e, qs=qs: e.matmul(pbt[3][:, 0:H], qs, Za[:], start=False, stop=True),
          r=["qTa", "Za"], w=[PB[3]])
        E("dv", lambda e: e.tensor_scalar(out=Otm_a[:], in0=pbt[3][:, 0:H],
                                          scalar1=dent[:], scalar2=None, op0=ALU.mult),
          r=[PB[3], "dent"], w=["Otm_a"])
        E("pe", lambda e: e.transpose(pbr[3][0:H, 64:192], Otm_a[:], ident_r[:]),
          r=["Otm_a", "ident_r"], w=[PB[3]])
        E("dv", lambda e, s0=s0: e.tensor_copy(out=oTa[:, s0:s0 + C],
                                               in_=pbr[3][0:H, 64:192]),
          r=[PB[3]], w=["oTa"])
        E("pe", lambda e: e.matmul(pbt[3][0:H, 192:256], Vtm_a[:], Ktm_a[:],
                                   start=True, stop=True),
          r=["Vtm_a", "Ktm_a"], w=[PB[3]])
        E("dv", lambda e: e.tensor_tensor(out=Za[:], in0=Za[:], in1=pbt[3][0:H, 192:256],
                                          op=ALU.add),
          r=["Za", PB[3]], w=["Za"])
        E("dv", lambda e, ks=ks: e.reduce_sum(out=ztmp[:], in_=ks, axis=AXX),
          r=["kTa"], w=["ztmp"])
        E("dv", lambda e: e.tensor_scalar(out=zcol[:], in0=zcol[:], scalar1=ztmp[:],
                                          scalar2=None, op0=ALU.add),
          r=["zcol", "ztmp"], w=["zcol"])

    def emit_attn_a_out(buf):
        xbn = f"xb{buf}"
        for dc in range(8):
            bank = 2 + (dc % 2)
            E("pe", lambda e, dc=dc, bank=bank:
              e.matmul(pbt[bank][:], woa_r[:, dc * 128:(dc + 1) * 128], oTa[:],
                       start=True, stop=True),
              r=["woa_r", "oTa"], w=[PB[bank]])
            E("dv", lambda e, dc=dc, bank=bank, buf=buf:
              e.tensor_tensor(out=xbuf[buf][:, dc * P:(dc + 1) * P],
                              in0=xbuf[buf][:, dc * P:(dc + 1) * P], in1=pbt[bank][:],
                              op=ALU.add),
              r=[xbn, PB[bank]], w=[xbn])

    def emit_attn_b_pre(buf):
        proj = [
            dict(bank=PB[5], out=pbt[5][0:H, 0:P], lres="wkb_r",
                 lhsT=lambda ic: wkb_r[:, ic * H:(ic + 1) * H]),
            dict(bank=PB[6], out=pbt[6][0:H, 0:P], lres="wvb_r",
                 lhsT=lambda ic: wvb_r[:, ic * H:(ic + 1) * H]),
            dict(bank=PB[7], out=pbt[7][0:H, 0:P], lres="wqb_r",
                 lhsT=lambda ic: wqb_r[:, ic * H:(ic + 1) * H]),
        ]
        proj2 = [
            dict(bank=PB[5], out=pbt[5][0:H, 0:P], lres="irw1_r",
                 lhsT=lambda ic: irw1_r[:, ic * H:(ic + 1) * H], **{"pass": 1}),
            dict(bank=PB[6], out=pbt[6][0:32, 0:P], lres="pw_r",
                 lhsT=lambda ic: pw_r[:, ic * 32:(ic + 1) * 32], **{"pass": 1}),
        ]
        emit_ln(buf, 1, want_r16=False, proj_specs=proj, sum_bank=0, sq_bank=1)
        l2norm_cols(1, PB[5], pbt[5][0:H, 0:P], cs(3), "cs_sum", kbT, "kbT", ss_bank=4)
        fold_correct(1, vbT[:], "vbT", PB[6], pbt[6][0:H, 0:P], cs(4), "cs_sum", H)
        l2norm_cols(1, PB[7], pbt[7][0:H, 0:P], cs(5), "cs_sum", qbT, "qbT", ss_bank=4)
        emit_proj_pass(buf, 1, proj2)
        fold_correct(1, t2s_[1][0:H, :], "t2s0", PB[5], pbt[5][0:H, 0:P], cs(6), "cs_sum", H)
        E("ac", lambda e: e.activation(out=gcr[:], in_=t2s_[1][0:H, :], func=AF.Gelu,
                                       bias=irb1c[:]),
          r=["t2s0", "irb1c"], w=["gcr"])
        for sc in range(NSC):
            E("pe", lambda e, sc=sc: e.matmul(pbt[7][:, 8 * sc:8 * sc + 8],
                                              gcr[:, sc * C:(sc + 1) * C], irw2_r[:],
                                              start=True, stop=True),
              r=["gcr", "irw2_r"], w=[PB[7]])
        for sc in range(NSC):
            E("dv", lambda e, sc=sc: e.tensor_scalar(out=ucol[:, sc:sc + 1],
                                                     in0=pbt[7][:, 8 * sc:8 * sc + 1],
                                                     scalar1=irb2c[:], scalar2=0.0,
                                                     op0=ALU.add, op1=ALU.is_gt),
              r=[PB[7], "irb2c"], w=["ucol"])
        E("dv", lambda e: e.tensor_scalar(out=t2s_[1][0:32, :], in0=mrep[1][0:32, :],
                                          scalar1=cspw[:], scalar2=None, op0=ALU.mult),
          r=["mrep1", "cspw"], w=["t2s0"])
        E("dv", lambda e: e.tensor_tensor(out=t2s_[1][0:32, :], in0=pbt[6][0:32, 0:P],
                                          in1=t2s_[1][0:32, :], op=ALU.subtract),
          r=[PB[6], "t2s0"], w=["t2s0"])
        E("dv", lambda e: e.tensor_tensor(out=pwC[:], in0=t2s_[1][0:32, :],
                                          in1=rrep[1][0:32, :], op=ALU.mult),
          r=["t2s0", "rrep1"], w=["pwC"])
        E("pe", lambda e: e.matmul(pbt[5][:, 0:P], nmw1_r[:], vbT[:], start=True, stop=True),
          r=["nmw1_r", "vbT"], w=[PB[5]])
        E("ac", lambda e: e.activation(out=sc1r[:], in_=pbt[5][:, 0:P], func=AF.Gelu,
                                       bias=nmb1c[:]),
          r=[PB[5], "nmb1c"], w=["sc1r"])
        E("pe", lambda e: e.matmul(pbt[6][0:H, 0:P], nmw2_r[:], sc1r[:], start=True, stop=True),
          r=["nmw2_r", "sc1r"], w=[PB[6]])
        E("dv", lambda e: e.tensor_scalar(out=nuT[:], in0=pbt[6][0:H, 0:P],
                                          scalar1=nmb2c[:], scalar2=None, op0=ALU.add),
          r=[PB[6], "nmb2c"], w=["nuT"])
        for sc in range(NSC):
            E("pe", lambda e, sc=sc: e.transpose(pbr[7][:, 384 + 32 * sc:384 + 32 * (sc + 1)],
                                                 pwC[:, sc * C:(sc + 1) * C],
                                                 ident_r[0:32, 0:32]),
              r=["pwC", "ident_r"], w=[PB[7]])
        for sc in range(NSC):
            slc = pbt[7][:, 384 + 32 * sc:384 + 32 * sc + 3]
            E("dv", lambda e, slc=slc: e.reduce_max(out=msc[:], in_=slc, axis=AXX,
                                                    negate=True),
              r=[PB[7]], w=["msc"])
            E("ac", lambda e, sc=sc, slc=slc:
              e.activation(out=pwt[:, sc * 3:(sc + 1) * 3], in_=slc, func=AF.Exp,
                           bias=msc[:]),
              r=[PB[7], "msc"], w=["pwt"])
            E("dv", lambda e, sc=sc: e.reduce_sum(out=msc[:], in_=pwt[:, sc * 3:(sc + 1) * 3],
                                                  axis=AXX),
              r=["pwt"], w=["msc"])
            E("dv", lambda e: e.reciprocal(out=msc[:], in_=msc[:]), r=["msc"], w=["msc"])
            E("dv", lambda e, sc=sc: e.tensor_scalar(out=pwt[:, sc * 3:(sc + 1) * 3],
                                                     in0=pwt[:, sc * 3:(sc + 1) * 3],
                                                     scalar1=msc[:], scalar2=None,
                                                     op0=ALU.mult),
              r=["pwt", "msc"], w=["pwt"])

    def emit_attn_b_sc(sc):
        sm = {"pe": 250, "dv": 260, "ac": 280}
        E = lambda eng, fn, r=(), w=(), **kw: TK.emit(eng, fn, r=r, w=w,
                                                      cost=kw.pop("cost", sm.get(eng)), **kw)
        s0 = sc * C
        kss = kbT[:, s0:s0 + C]
        qss = qbT[:, s0:s0 + C]
        vss = vbT[:, s0:s0 + C]
        ucs = ucol[:, sc:sc + 1]
        E("pe", lambda e, kss=kss: e.transpose(pbr[4][:, 256:320], kss, ident_r[0:H, 0:H]),
          r=["kbT", "ident_r"], w=[PB[4]])
        E("dv", lambda e: e.tensor_copy(out=Ktm[:], in_=pbr[4][:, 256:320]),
          r=[PB[4]], w=["Ktm"])
        E("pe", lambda e, vss=vss: e.transpose(pbr[4][:, 320:384], vss, ident_r[0:H, 0:H]),
          r=["vbT", "ident_r"], w=[PB[4]])
        E("dv", lambda e: e.tensor_copy(out=Vtm[:], in_=pbr[4][:, 320:384]),
          r=[PB[4]], w=["Vtm"])
        E("pe", lambda e, ucs=ucs: e.transpose(pbt[4][0:1, 384:512], ucs, cbv("ident")),
          r=["ucol", "cbt"], w=[PB[4]])
        E("dv", lambda e: e.tensor_copy(out=urow[:], in_=pbt[4][0:1, 384:512]),
          r=[PB[4]], w=["urow"])
        E("pe", lambda e, kss=kss: e.matmul(pbt[4][:, 0:128], kss, kss, start=True, stop=True),
          r=["kbT"], w=[PB[4]])
        E("pe", lambda e, kss=kss, qss=qss: e.matmul(pbt[5][:, 0:128], kss, qss,
                                                     start=True, stop=True),
          r=["kbT", "qbT"], w=[PB[5]])
        E("pe", lambda e: e.matmul(pbt[4][:, 128:256], urow[:], urow[:], start=True, stop=True),
          r=["urow"], w=[PB[4]])
        E("dv", lambda e: e.tensor_tensor(out=Npw[0][0][:], in0=pbt[4][:, 0:128],
                                          in1=DSB[0][:], op=ALU.mult),
          r=[PB[4], "DSB0"], w=["Npw00"])
        E("dv", lambda e: e.tensor_tensor(out=NpwT[0][0][:], in0=pbt[4][:, 0:128],
                                          in1=DSBT[0][:], op=ALU.mult),
          r=[PB[4], "DSBT0"], w=["NpwT00"])
        E("dv", lambda e: e.tensor_tensor(out=nscr[:], in0=pbt[4][:, 0:128],
                                          in1=DSB[1][:], op=ALU.mult),
          r=[PB[4], "DSB1"], w=["nscr"])
        E("dv", lambda e: e.tensor_tensor(out=Npw[1][0][:], in0=nscr[:],
                                          in1=pbt[4][:, 128:256], op=ALU.mult),
          r=["nscr", PB[4]], w=["Npw10"])
        E("dv", lambda e: e.tensor_tensor(out=nscr[:], in0=pbt[4][:, 0:128],
                                          in1=DSBT[1][:], op=ALU.mult),
          r=[PB[4], "DSBT1"], w=["nscr"])
        E("dv", lambda e: e.tensor_tensor(out=NpwT[1][0][:], in0=nscr[:],
                                          in1=pbt[4][:, 128:256], op=ALU.mult),
          r=["nscr", PB[4]], w=["NpwT10"])
        E("dv", lambda e: e.tensor_tensor(out=WxT[0][:], in0=pbt[5][:, 0:128],
                                          in1=DS_[0][:], op=ALU.mult),
          r=[PB[5], "DS0"], w=["WxT0"])
        E("dv", lambda e: e.tensor_tensor(out=WiT[0][:], in0=pbt[5][:, 0:128],
                                          in1=DI_[0][:], op=ALU.mult),
          r=[PB[5], "DI0"], w=["WiT0"])
        E("dv", lambda e: e.tensor_tensor(out=WxT[1][:], in0=pbt[5][:, 0:128],
                                          in1=DS_[1][:], op=ALU.mult),
          r=[PB[5], "DS1"], w=["WxT1"])
        E("dv", lambda e, ucs=ucs: e.tensor_scalar(out=WxT[1][:], in0=WxT[1][:],
                                                   scalar1=ucs, scalar2=None, op0=ALU.mult),
          r=["WxT1", "ucol"], w=["WxT1"])
        E("dv", lambda e: e.tensor_tensor(out=WiT[1][:], in0=pbt[5][:, 0:128],
                                          in1=DI_[1][:], op=ALU.mult),
          r=[PB[5], "DI1"], w=["WiT1"])
        E("dv", lambda e, ucs=ucs: e.tensor_scalar(out=WiT[1][:], in0=WiT[1][:],
                                                   scalar1=ucs, scalar2=None, op0=ALU.mult),
          r=["WiT1", "ucol"], w=["WiT1"])
        for m in range(2):
            off = 128 + m * H
            E("pe", lambda e, kss=kss, m=m, off=off:
              e.matmul(pbt[5][:, off:off + H], kss, F0m[m][:], start=True, stop=True),
              r=["kbT", f"F0m{m}"], w=[PB[5]])
            if m == 0:
                E("dv", lambda e: e.tensor_scalar(out=rdf[:], in0=pbt[5][:, 128:128 + H],
                                                  scalar1=g1n[0][:], scalar2=None,
                                                  op0=ALU.mult),
                  r=[PB[5], "g1n0"], w=["rdf"])
                E("dv", lambda e: e.scalar_tensor_tensor(out=Rm[0][:], in0=Vtm[:],
                                                         scalar=bcol[0][:], in1=rdf[:],
                                                         op0=ALU.mult, op1=ALU.add),
                  r=["Vtm", "bcol0", "rdf"], w=["Rm0"])
            else:
                E("dv", lambda e: e.tensor_scalar(out=rds[:], in0=pbt[5][:, 192:192 + H],
                                                  scalar1=g1n[1][:], scalar2=ucs,
                                                  op0=ALU.mult, op1=ALU.mult),
                  r=[PB[5], "g1n1", "ucol"], w=["rds"])
                E("dv", lambda e: e.tensor_scalar(out=predt[:], in0=Vtm[:],
                                                  scalar1=bcol[1][:], scalar2=ucs,
                                                  op0=ALU.mult, op1=ALU.mult),
                  r=["Vtm", "bcol1", "ucol"], w=["predt"])
                E("dv", lambda e: e.tensor_tensor(out=Rm[1][:], in0=predt[:], in1=rds[:],
                                                  op=ALU.add),
                  r=["predt", "rds"], w=["Rm1"])
        for m in range(2):
            bk = 6 + m
            E("pe", lambda e, m=m, bk=bk: e.matmul(pbt[bk][:, 256:320], Npw[m][0][:],
                                                   Rm[m][:], start=True, stop=True),
              r=[f"Npw{m}0", f"Rm{m}"], w=[PB[bk]])
            E("dv", lambda e, m=m, bk=bk: e.tensor_tensor(out=Ut[m][0][:], in0=Rm[m][:],
                                                          in1=pbt[bk][:, 256:320],
                                                          op=ALU.subtract),
              r=[f"Rm{m}", PB[bk]], w=[f"Ut{m}0"])
        for j in range(6):
            jp = j % 2
            jn = (j + 1) % 2
            for m in range(2):
                bk = 6 + m
                E("pe", lambda e, m=m, jp=jp, bk=bk:
                  e.matmul(pbt[bk][:, 0:128], NpwT[m][jp][:], Npw[m][jp][:],
                           start=True, stop=True),
                  r=[f"NpwT{m}{jp}", f"Npw{m}{jp}"], w=[PB[bk]])
                E("dv", lambda e, m=m, jn=jn, bk=bk:
                  e.tensor_copy(out=Npw[m][jn][:], in_=pbr[bk][:, 0:128]),
                  r=[PB[bk]], w=[f"Npw{m}{jn}"])
                if j < 5:
                    E("pe", lambda e, m=m, jp=jp, bk=bk:
                      e.matmul(pbt[bk][:, 128:256], Npw[m][jp][:], NpwT[m][jp][:],
                               start=True, stop=True),
                      r=[f"Npw{m}{jp}", f"NpwT{m}{jp}"], w=[PB[bk]])
                    E("dv", lambda e, m=m, jn=jn, bk=bk:
                      e.tensor_copy(out=NpwT[m][jn][:], in_=pbr[bk][:, 128:256]),
                      r=[PB[bk]], w=[f"NpwT{m}{jn}"])
                E("pe", lambda e, m=m, jp=jp, jn=jn, bk=bk:
                  e.matmul(pbt[bk][:, 256:320], Npw[m][jn][:], Ut[m][jp][:],
                           start=True, stop=True),
                  r=[f"Npw{m}{jn}", f"Ut{m}{jp}"], w=[PB[bk]])
                E("dv", lambda e, m=m, jp=jp, jn=jn, bk=bk:
                  e.tensor_tensor(out=Ut[m][jn][:], in0=Ut[m][jp][:],
                                  in1=pbt[bk][:, 256:320], op=ALU.add),
                  r=[f"Ut{m}{jp}", PB[bk]], w=[f"Ut{m}{jn}"])
        for m in range(2):
            bk = 6 + m
            E("pe", lambda e, qss=qss, m=m: e.matmul(pbt[5][:, 256 + m * H:256 + (m + 1) * H],
                                                     qss, F0m[m][:], start=True, stop=True),
              r=["qbT", f"F0m{m}"], w=[PB[5]])
            E("pe", lambda e, m=m, bk=bk: e.matmul(pbt[bk][:, 320:384], WxT[m][:],
                                                   Ut[m][0][:], start=True, stop=True),
              r=[f"WxT{m}", f"Ut{m}0"], w=[PB[bk]])
            E("pe", lambda e, m=m, bk=bk: e.matmul(pbt[bk][:, 384:448], WiT[m][:],
                                                   Ut[m][0][:], start=True, stop=True),
              r=[f"WiT{m}", f"Ut{m}0"], w=[PB[bk]])
        E("dv", lambda e: e.tensor_scalar(out=rdf[:], in0=pbt[5][:, 256:256 + H],
                                          scalar1=g1c[0][:], scalar2=None, op0=ALU.mult),
          r=[PB[5], "g1c0"], w=["rdf"])
        E("dv", lambda e: e.tensor_tensor(out=rdf[:], in0=rdf[:], in1=pbt[6][:, 320:384],
                                          op=ALU.add),
          r=["rdf", PB[6]], w=["rdf"])
        E("dv", lambda e: e.tensor_scalar(out=rds[:], in0=pbt[5][:, 320:320 + H],
                                          scalar1=g1c[1][:], scalar2=None, op0=ALU.mult),
          r=[PB[5], "g1c1"], w=["rds"])
        E("dv", lambda e: e.tensor_tensor(out=rds[:], in0=rds[:], in1=pbt[7][:, 320:384],
                                          op=ALU.add),
          r=["rds", PB[7]], w=["rds"])
        E("dv", lambda e: e.tensor_tensor(out=predt[:], in0=rdf[:], in1=rds[:], op=ALU.add),
          r=["rdf", "rds"], w=["predt"])
        E("dv", lambda e: e.scalar_tensor_tensor(out=predt[:], in0=predt[:], scalar=-0.5,
                                                 in1=Vtm[:], op0=ALU.mult, op1=ALU.add),
          r=["predt", "Vtm"], w=["predt"])
        E("ac", lambda e: e.activation(out=dsq[:], in_=predt[:], func=AF.Square),
          r=["predt"], w=["dsq"])
        E("dv", lambda e: e.reduce_sum(out=errc[:], in_=dsq[:], axis=AXX),
          r=["dsq"], w=["errc"])
        E("dv", lambda e: e.tensor_scalar(out=uncol[:], in0=errc[:], scalar1=SUR_THR,
                                          scalar2=None, op0=ALU.is_gt),
          r=["errc"], w=["uncol"])
        E("dv", lambda e: e.tensor_scalar(out=rdf[:], in0=pbt[5][:, 256:256 + H],
                                          scalar1=g2c[0][:], scalar2=None, op0=ALU.mult),
          r=[PB[5], "g2c0"], w=["rdf"])
        E("dv", lambda e: e.tensor_tensor(out=rdf[:], in0=rdf[:], in1=pbt[6][:, 384:448],
                                          op=ALU.add),
          r=["rdf", PB[6]], w=["rdf"])
        E("dv", lambda e: e.tensor_scalar(out=rds[:], in0=pbt[5][:, 320:320 + H],
                                          scalar1=g2c[1][:], scalar2=None, op0=ALU.mult),
          r=[PB[5], "g2c1"], w=["rds"])
        E("dv", lambda e: e.tensor_tensor(out=rds[:], in0=rds[:], in1=pbt[7][:, 384:448],
                                          op=ALU.add),
          r=["rds", PB[7]], w=["rds"])
        E("pe", lambda e: e.transpose(pbt[7][0:1, 128:256], uncol[:], cbv("ident")),
          r=["uncol", "cbt"], w=[PB[7]])
        E("dv", lambda e: e.tensor_copy(out=unrow[:], in_=pbt[7][0:1, 128:256]),
          r=[PB[7]], w=["unrow"])
        E("pe", lambda e: e.matmul(pbt[6][0:H, 0:128], ones_r[0:1, 0:H], unrow[:],
                                   start=True, stop=True),
          r=["ones_r", "unrow"], w=[PB[6]])
        E("dv", lambda e: e.tensor_scalar(out=nsA[:], in0=pbt[6][0:H, 0:128],
                                          scalar1=-0.1, scalar2=1.0, op0=ALU.mult,
                                          op1=ALU.add),
          r=[PB[6]], w=["nsA"])
        E("dv", lambda e: e.tensor_scalar(out=nsB[:], in0=pbt[6][0:H, 0:128],
                                          scalar1=0.1, scalar2=None, op0=ALU.mult),
          r=[PB[6]], w=["nsB"])
        E("dv", lambda e, s0=s0: e.tensor_tensor(out=nsB[:], in0=nsB[:],
                                                 in1=nuT[:, s0:s0 + C], op=ALU.mult),
          r=["nsB", "nuT"], w=["nsB"])
        if sc == 0:
            E("dv", lambda e, s0=s0:
              e.tensor_tensor_scan(out=Nfm[:, s0:s0 + C], data0=nsA[:], data1=nsB[:],
                                   initial=nsinit[:], op0=ALU.mult, op1=ALU.add),
              r=["nsA", "nsB", "nsinit"], w=["Nfm"], cost=500)
        else:
            E("dv", lambda e, s0=s0:
              e.tensor_tensor_scan(out=Nfm[:, s0:s0 + C], data0=nsA[:], data1=nsB[:],
                                   initial=Nfm[:, s0 - 1:s0], op0=ALU.mult, op1=ALU.add),
              r=["nsA", "nsB", "Nfm"], w=["Nfm"], cost=500)
        E("pe", lambda e, s0=s0: e.transpose(pbt[6][:, 448:512], Nfm[:, s0:s0 + C],
                                             cbv("ident", rows=H)[0:H, 0:H]),
          r=["Nfm", "cbt"], w=[PB[6]])
        E("dv", lambda e: e.tensor_copy(out=Ntm[:], in_=pbt[6][:, 448:512]),
          r=[PB[6]], w=["Ntm"])
        E("dv", lambda e, sc=sc: e.tensor_scalar(out=oatt[:], in0=rdf[:],
                                                 scalar1=pwt[:, sc * 3:sc * 3 + 1],
                                                 scalar2=None, op0=ALU.mult),
          r=["rdf", "pwt"], w=["oatt"])
        E("dv", lambda e, sc=sc: e.scalar_tensor_tensor(out=oatt[:], in0=rds[:],
                                                        scalar=pwt[:, sc * 3 + 1:sc * 3 + 2],
                                                        in1=oatt[:], op0=ALU.mult,
                                                        op1=ALU.add),
          r=["rds", "pwt", "oatt"], w=["oatt"])
        E("dv", lambda e, sc=sc: e.scalar_tensor_tensor(out=oatt[:], in0=Ntm[:],
                                                        scalar=pwt[:, sc * 3 + 2:sc * 3 + 3],
                                                        in1=oatt[:], op0=ALU.mult,
                                                        op1=ALU.add),
          r=["Ntm", "pwt", "oatt"], w=["oatt"])
        E("ac", lambda e: e.activation(out=dsq[:], in_=oatt[:], func=AF.Square),
          r=["oatt"], w=["dsq"])
        E("dv", lambda e: e.reduce_sum(out=msc[:], in_=dsq[:], axis=AXX),
          r=["dsq"], w=["msc"])
        E("dv", lambda e: e.tensor_scalar(out=msc[:], in0=msc[:], scalar1=1.0 / H,
                                          scalar2=1e-6, op0=ALU.mult, op1=ALU.add),
          r=["msc"], w=["msc"])
        E("ac", lambda e: e.activation(out=msc[:], in_=msc[:], func=AF.Ln),
          r=["msc"], w=["msc"])
        E("ac", lambda e: e.activation(out=msc[:], in_=msc[:], func=AF.Exp, scale=-0.5),
          r=["msc"], w=["msc"])
        E("dv", lambda e: e.tensor_scalar(out=onrm[:], in0=oatt[:], scalar1=msc[:],
                                          scalar2=None, op0=ALU.mult),
          r=["oatt", "msc"], w=["onrm"])
        E("pe", lambda e: e.transpose(pbr[7][0:H, 0:128], onrm[:], ident_r[:]),
          r=["onrm", "ident_r"], w=[PB[7]])
        E("dv", lambda e, s0=s0: e.tensor_copy(out=obT[:, s0:s0 + C],
                                               in_=pbr[7][0:H, 0:128]),
          r=[PB[7]], w=["obT"])
        E("dv", lambda e, ucs=ucs: e.tensor_scalar(out=Ktg[:], in0=Ktm[:], scalar1=ucs,
                                                   scalar2=None, op0=ALU.mult),
          r=["Ktm", "ucol"], w=["Ktg"])
        for m in range(2):
            E("dv", lambda e, m=m: e.tensor_scalar(out=Ut[m][0][:], in0=Ut[m][0][:],
                                                   scalar1=g3c[m][:], scalar2=None,
                                                   op0=ALU.mult),
              r=[f"Ut{m}0", f"g3c{m}"], w=[f"Ut{m}0"])
            lhs = Ktm if m == 0 else Ktg
            lres = "Ktm" if m == 0 else "Ktg"
            E("pe", lambda e, m=m, lhs=lhs: e.matmul(pbt[5][0:H, 384 + m * H:384 + (m + 1) * H],
                                                     lhs[:], Ut[m][0][:],
                                                     start=True, stop=True),
              r=[lres, f"Ut{m}0"], w=[PB[5]])
            E("dv", lambda e, m=m: e.scalar_tensor_tensor(
                out=F0m[m][:], in0=F0m[m][:], scalar=aCc[m][0:H, :],
                in1=pbt[5][0:H, 384 + m * H:384 + (m + 1) * H],
                op0=ALU.mult, op1=ALU.add),
              r=[f"F0m{m}", PB[5], f"aCc{m}"], w=[f"F0m{m}"])

    def emit_attn_b_out(buf):
        xbn = f"xb{buf}"
        E("dv", lambda e: e.tensor_copy(out=nsinit[:], in_=Nfm[:, P - 1:P]),
          r=["Nfm"], w=["nsinit"])
        for dc in range(8):
            bank = 4 + (dc % 2)
            E("pe", lambda e, dc=dc, bank=bank:
              e.matmul(pbt[bank][:], wob_r[:, dc * 128:(dc + 1) * 128], obT[:],
                       start=True, stop=True),
              r=["wob_r", "obT"], w=[PB[bank]])
            E("dv", lambda e, dc=dc, bank=bank, buf=buf:
              e.tensor_tensor(out=xbuf[buf][:, dc * P:(dc + 1) * P],
                              in0=xbuf[buf][:, dc * P:(dc + 1) * P], in1=pbt[bank][:],
                              op=ALU.add),
              r=[xbn, PB[bank]], w=[xbn])

    if PIPELINE and SUBLAYERS >= 4:
        for i in range(NCHUNK + 1):
            bufA = i % 3
            bufB = (i - 1) % 3
            hasA = i < NCHUNK
            hasB = i > 0
            if i == 0:
                emit_attn_a_pre(bufA)
            if hasB:
                emit_attn_b_pre(bufB)
            for sc in range(NSC):
                if hasA:
                    emit_attn_a_sc(sc)
                if hasB:
                    emit_attn_b_sc(sc)
            if hasA:
                emit_attn_a_out(bufA)
            if hasB:
                emit_attn_b_out(bufB)
            if hasA:
                emit_ffn_p1(bufA, 0)
                emit_ffn_p2(bufA, 0)
            if hasB:
                emit_ffn_p1(bufB, 1)
            if hasA and i + 1 < NCHUNK:
                emit_attn_a_pre((i + 1) % 3)
            if hasB:
                emit_ffn_p2(bufB, 1)
                E("sp", lambda e, c=i - 1, bufB=bufB: e.dma_start(
                    out=dout[c * 128:(c + 1) * 128, :],
                    in_=xbuf[bufB][:]),
                  r=[f"xb{bufB}"], w=[f"od{i - 1}"], sem=s_out)
            if i + 2 < NCHUNK:
                x_load(i + 2, (i + 2) % 3)
    else:
        for c_ in range(NCHUNK):
            buf = c_ % 3
            if SUBLAYERS >= 1:
                emit_attn_a_pre(buf)
                for sc in range(NSC):
                    emit_attn_a_sc(sc)
                emit_attn_a_out(buf)
            if SUBLAYERS >= 2:
                emit_ffn_p1(buf, 0)
                emit_ffn_p2(buf, 0)
            if SUBLAYERS >= 3:
                emit_attn_b_pre(buf)
                for sc in range(NSC):
                    emit_attn_b_sc(sc)
                emit_attn_b_out(buf)
            if SUBLAYERS >= 4:
                emit_ffn_p1(buf, 1)
                emit_ffn_p2(buf, 1)
            E("sp", lambda e, c_=c_, buf=buf: e.dma_start(
                out=dout[c_ * 128:(c_ + 1) * 128, :],
                in_=xbuf[buf][:]),
              r=[f"xb{buf}"], w=[f"od{c_}"], sem=s_out)
            if c_ + 2 < NCHUNK:
                x_load(c_ + 2, (c_ + 2) % 3)

    TK.schedule()
    TK.finalize(ENG_SEMS)
    TK.steps.append(dict(eng="sp", fn=lambda e: e.nop(), r=(), w=(), sem=s_out,
                         inc=False, deps=set(), val=None,
                         waits=[(s_out, 16 * NCHUNK)], xwaits=[]))

    with nc.Block() as block:
        @block.sync
        def _(sync):
            TK.run_engine("sp", sync)

        @block.vector
        def _(vector):
            TK.run_engine("dv", vector)

        @block.scalar
        def _(scalar):
            TK.run_engine("ac", scalar)

        @block.tensor
        def _(tensor):
            TK.run_engine("pe", tensor)

        @block.gpsimd
        def _(gpsimd):
            TK.run_engine("gp", gpsimd)

    for cm in reversed(sem_cms):
        cm.__exit__(None, None, None)
    return nc


def kernel(**inputs):
    g = lambda k: np.asarray(inputs[k])
    x = g("L0_attn_a_x").astype(np.float32, copy=False)
    cbd = build_const_blob()
    pwT = np.zeros((D, 32), np.float32)
    pwT[:, 0:3] = g("L0_attn_b_pw").T

    def pgather(a):
        k = a.shape[1]
        return np.ascontiguousarray(
            a.reshape(8, 128, k).transpose(1, 0, 2).reshape(128, 8 * k))

    def ogmajor(a):
        return np.ascontiguousarray(
            a.reshape(8, 128, 8, 512).transpose(2, 1, 0, 3).reshape(1024, 4096))

    common = dict(
        cbd=cbd,
        w1T=ogmajor(np.ascontiguousarray(g("L0_ffn_w1").T)),
        w2T=np.ascontiguousarray(g("L0_ffn_w2").T),
        wqa=pgather(g("L0_attn_a_W_q")),
        wka=pgather(g("L0_attn_a_W_k")),
        wva=pgather(g("L0_attn_a_W_v")),
        woa=np.ascontiguousarray(g("L0_attn_a_W_o")),
        wkbT=pgather(np.ascontiguousarray(g("L0_attn_b_Wk").T)),
        wvbT=pgather(np.ascontiguousarray(g("L0_attn_b_Wv").T)),
        wqbT=pgather(np.ascontiguousarray(g("L0_attn_b_Wq").T)),
        irw1T=pgather(np.ascontiguousarray(g("L0_attn_b_ir_w1").T)),
        pwT=pgather(pwT),
        nmw1T=np.ascontiguousarray(g("L0_attn_b_nm_w1").T),
        nmw2T=np.ascontiguousarray(g("L0_attn_b_nm_w2").T),
        irw2T=np.ascontiguousarray(g("L0_attn_b_ir_w2").T),
        wobT=np.ascontiguousarray(g("L0_attn_b_Wo").T),
        d_b1c=np.ascontiguousarray(g("L0_ffn_b1").reshape(32, 128).T),
        d_b2c=np.ascontiguousarray(g("L0_ffn_b2").reshape(8, 128).T),
        d_nmb1c=np.ascontiguousarray(g("L0_attn_b_nm_b1").reshape(128, 1)),
        d_nmb2c=np.ascontiguousarray(g("L0_attn_b_nm_b2").reshape(64, 1)),
        d_irb1c=np.ascontiguousarray(g("L0_attn_b_ir_b1").reshape(64, 1)),
        d_irb2=np.ascontiguousarray(g("L0_attn_b_ir_b2").reshape(1, 1)),
        d_normw=np.ascontiguousarray(g("L0_attn_b_norm_w").reshape(64, 1)),
        d_alphaf=np.ascontiguousarray(np.reshape(g("L0_attn_b_fast_alpha"), (1, 1))),
        d_alphas=np.ascontiguousarray(np.reshape(g("L0_attn_b_slow_alpha"), (1, 1))),
    )
    def cmajor(a):
        return np.ascontiguousarray(
            a.reshape(8, 128, 8, 512).transpose(2, 1, 0, 3).reshape(1024, 4096))

    in_maps = []
    for b in range(8):
        m = dict(common)
        m["xT"] = cmajor(np.ascontiguousarray(x[b].T))
        in_maps.append(m)

    nc = build_nc()
    res = run_bass_kernel_spmd(nc, in_maps, core_ids=list(range(8)))
    out = np.stack(
        [np.ascontiguousarray(cmajor(res.results[b]["outT"]).T) for b in range(8)],
        axis=0)
    return out.astype(np.float32, copy=False)


# version 19 (final)
# speedup vs baseline: 1.0613x; 1.0097x over previous
import math

import numpy as np
import ml_dtypes
import concourse.bass as bass
import concourse.mybir as mybir
from concourse.bass_utils import run_bass_kernel_spmd

F32 = mybir.dt.float32
F32R = mybir.dt.float32r
BF16 = mybir.dt.bfloat16
AF = mybir.ActivationFunctionType
ALU = mybir.AluOpType
AXX = mybir.AxisListType.X

D = 1024
T = 4096
H = 64
HID = 4096
P = 512
NCHUNK = T // P
C = 128
NSC = P // C
M_RND = float(2.0 ** 23 + 2.0 ** 22)
LN_EPS = 1e-5
SUR_THR = float(math.log(0.7 / 0.3) * (1.0 + 1e-6))

SUBLAYERS = 4
PIPELINE = True

_OFF = {}
_top = 0
def _cols(name, w):
    global _top
    _OFF[name] = (_top, w)
    _top += w

for _n in ("maskIL", "maskIU", "maskSU", "maskSL", "expoSU", "expoSL", "expoIU",
           "ones128", "ident"):
    _cols(_n, 128)
for _n in ("iota0", "iota1", "iotaC", "onescol", "epscol", "negMcol", "Mcol", "zerocol", "ln16col"):
    _cols(_n, 1)
_cols("zero64", 64)
CBW = _top


def build_const_blob():
    cb = np.zeros((128, CBW), np.float32)
    r = np.arange(128)
    lo = (r[:, None] >= r[None, :]).astype(np.float32)
    iu = (r[:, None] <= r[None, :]).astype(np.float32)
    su = (r[:, None] < r[None, :]).astype(np.float32)
    sl = (r[:, None] > r[None, :]).astype(np.float32)

    def put(name, arr):
        o, w = _OFF[name]
        cb[:, o:o + w] = arr

    put("maskIL", lo)
    put("maskIU", iu)
    put("maskSU", su)
    put("maskSL", sl)
    put("expoSU", (r[None, :] - 1 - r[:, None]) * su)
    put("expoSL", (r[:, None] - 1 - r[None, :]) * sl)
    put("expoIU", (r[None, :] - r[:, None]) * iu)
    put("ones128", np.ones((128, 128), np.float32))
    put("ident", np.eye(128, dtype=np.float32))
    put("iota0", r[:, None].astype(np.float32))
    put("iota1", (r[:, None] + 1).astype(np.float32))
    put("iotaC", (C - 1 - r[:, None]).astype(np.float32))
    put("onescol", np.ones((128, 1), np.float32))
    put("epscol", np.full((128, 1), LN_EPS, np.float32))
    put("negMcol", np.full((128, 1), -M_RND, np.float32))
    put("Mcol", np.full((128, 1), M_RND, np.float32))
    put("zerocol", np.zeros((128, 1), np.float32))
    put("ln16col", np.full((128, 1), math.log(16.0), np.float32))
    put("zero64", np.zeros((128, 64), np.float32))
    return cb


class SemT:
    def __init__(self, handle, name, dma=False):
        self.h = handle
        self.name = name
        self.n = 0
        self.dma = dma

    def bump(self):
        self.n += 16 if self.dma else 1
        return self.n


class Tracker:
    def __init__(self):
        self.steps = []

    def emit(self, eng, fn, r=(), w=(), sem=None, xwaits=(), cost=None, lat=None):
        self.steps.append(dict(eng=eng, fn=fn, r=tuple(r), w=tuple(w), sem=sem,
                               inc=False, xwaits=list(xwaits), cost=cost, lat=lat))
        return len(self.steps) - 1

    def schedule(self, costs=None):
        import heapq
        n = len(self.steps)
        cost = []
        base = {"pe": 260, "dv": 390, "ac": 640, "sp": 620, "gp": 900}
        lats = []
        for st in self.steps:
            c = st.get("cost")
            cost.append(c if c is not None else base.get(st["eng"], 400))
            lt = st.get("lat")
            if lt is None:
                lt = 800 if st["eng"] == "sp" else 0
            lats.append(lt)
        last_w = {}
        readers = {}
        succ = [[] for _ in range(n)]
        pred_cnt = [0] * n
        preds = [set() for _ in range(n)]
        for i, st in enumerate(self.steps):
            deps = set()
            for rr in st["r"]:
                j = last_w.get(rr)
                if j is not None:
                    deps.add(j)
            for rr in st["w"]:
                j = last_w.get(rr)
                if j is not None:
                    deps.add(j)
                for k in readers.get(rr, ()):
                    if k != i:
                        deps.add(k)
            for rr in st["r"]:
                readers.setdefault(rr, []).append(i)
            for rr in st["w"]:
                last_w[rr] = i
                readers[rr] = []
            preds[i] = deps
            pred_cnt[i] = len(deps)
            for j in deps:
                succ[j].append(i)
        cp = [0.0] * n
        for i in range(n - 1, -1, -1):
            m = 0.0
            for s_ in succ[i]:
                if cp[s_] > m:
                    m = cp[s_]
            cp[i] = cost[i] + lats[i] + m
        fin = [0.0] * n
        avail = [0.0] * n
        eng_clock = {}
        ready = []
        for i in range(n):
            if pred_cnt[i] == 0:
                ready.append(i)
        order = []
        remaining = list(pred_cnt)
        start = [0.0] * n
        import os
        policy = os.environ.get("TK_SCHED_POLICY", "est")
        while ready:
            if policy == "est":
                bi, bk = -1, None
                for idx, i in enumerate(ready):
                    est = eng_clock.get(self.steps[i]["eng"], 0.0)
                    if avail[i] > est:
                        est = avail[i]
                    key = (est, -cp[i])
                    if bk is None or key < bk:
                        bk, bi = key, idx
                i = ready.pop(bi)
            else:
                bi, bk = -1, None
                for idx, i in enumerate(ready):
                    key = -cp[i]
                    if bk is None or key < bk:
                        bk, bi = key, idx
                i = ready.pop(bi)
            st = self.steps[i]
            t0 = eng_clock.get(st["eng"], 0.0)
            if avail[i] > t0:
                t0 = avail[i]
            start[i] = t0
            fin[i] = t0 + cost[i] + lats[i]
            eng_clock[st["eng"]] = t0 + cost[i]
            order.append(i)
            for s_ in succ[i]:
                remaining[s_] -= 1
                if avail[s_] < fin[i]:
                    avail[s_] = fin[i]
                if remaining[s_] == 0:
                    ready.append(s_)
        assert len(order) == n
        if os.environ.get("TK_SCHED_DEBUG"):
            eng_busy = {}
            for i in range(n):
                e = self.steps[i]["eng"]
                eng_busy[e] = eng_busy.get(e, 0) + cost[i]
            print("SCHED makespan(model): %.1f us" % (max(fin) / 1e3))
            print("SCHED cp bound: %.1f us" % (max(cp) / 1e3))
            print("SCHED eng busy(model):", {k: round(v / 1e3) for k, v in eng_busy.items()})
            prev_end = {}
            attr = {}
            gaps = []
            for i in order:
                e = self.steps[i]["eng"]
                gap = start[i] - prev_end.get(e, 0.0)
                if gap > 1.0:
                    jb, fb = None, -1.0
                    for j in preds[i]:
                        if fin[j] > fb:
                            fb, jb = fin[j], j
                    be = self.steps[jb]["eng"] if jb is not None else "none"
                    res = ""
                    if jb is not None:
                        wr = set(self.steps[jb]["w"])
                        res = ",".join(sorted(wr & set(self.steps[i]["r"]))) or \
                              ",".join(sorted(wr)[:2])
                    key = (e, be, res)
                    attr[key] = attr.get(key, 0.0) + gap
                    gaps.append((gap, e, be, res, start[i]))
                prev_end[e] = fin[i]
            pe_gaps = sorted([g for g in gaps if g[1] == "pe"], reverse=True)[:20]
            for gp, e_, be, res, t_ in pe_gaps:
                print("SCHED pegap %.1f us at t=%.1f blocked-by %s [%s]" %
                      (gp / 1e3, t_ / 1e3, be, res))
            top = sorted(attr.items(), key=lambda kv: -kv[1])[:16]
            for (e, be, res), tot in top:
                print("SCHED idle %s blocked-by %s [%s]: %.1f us" % (e, be, res, tot / 1e3))
            tot_idle = {}
            nstep = {}
            endt = {}
            for i in order:
                e = self.steps[i]["eng"]
                nstep[e] = nstep.get(e, 0) + 1
                endt[e] = fin[i]
            prev_end = {}
            for i in order:
                e = self.steps[i]["eng"]
                tot_idle[e] = tot_idle.get(e, 0.0) + (start[i] - prev_end.get(e, 0.0))
                prev_end[e] = fin[i]
            print("SCHED per-eng: " + "  ".join(
                "%s n=%d idle=%.0fus end=%.0fus" % (e, nstep[e], tot_idle[e] / 1e3,
                                                    endt[e] / 1e3) for e in sorted(nstep)))
        self.steps = [self.steps[i] for i in order]

    def finalize(self, eng_sems):
        last_w = {}
        readers = {}
        for i, st in enumerate(self.steps):
            ei = st["eng"]
            deps = set()

            def need(j):
                ej = self.steps[j]["eng"]
                return ej != ei or ej == "sp"

            for rr in st["r"]:
                j = last_w.get(rr)
                if j is not None:
                    deps.add(j)
            for rr in st["w"]:
                j = last_w.get(rr)
                if j is not None and need(j):
                    deps.add(j)
                for k in readers.get(rr, ()):
                    if need(k):
                        deps.add(k)
            for rr in st["r"]:
                readers.setdefault(rr, []).append(i)
            for rr in st["w"]:
                last_w[rr] = i
                readers[rr] = []
            st["deps"] = deps
        for st in self.steps:
            for j in st["deps"]:
                self.steps[j]["inc"] = True
            if st["eng"] == "sp":
                st["inc"] = True
        for st in self.steps:
            if st["sem"] is None:
                st["sem"] = eng_sems[st["eng"]]
        for st in self.steps:
            st["val"] = st["sem"].bump() if st["inc"] else None
        for st in self.steps:
            wmap = {}
            for j in st["deps"]:
                sj = self.steps[j]
                k = id(sj["sem"])
                if k not in wmap or wmap[k][1] < sj["val"]:
                    wmap[k] = (sj["sem"], sj["val"])
            st["waits"] = list(wmap.values()) + st["xwaits"]

    def run_engine(self, name, eng):
        waited = {}
        issued = {}
        for st in self.steps:
            if st["eng"] != name:
                continue
            if name == "sp" and st["sem"] is not None and st["sem"].dma:
                s = st["sem"]
                n = issued.get(id(s), 0)
                if n > 0 and waited.get(id(s), -1) < 16 * n:
                    eng.wait_ge(s.h, 16 * n)
                    waited[id(s)] = 16 * n
                issued[id(s)] = n + 1
            for (s, v) in st["waits"]:
                if waited.get(id(s), -1) < v:
                    eng.wait_ge(s.h, v)
                    waited[id(s)] = v
            inst = st["fn"](eng)
            if st["inc"]:
                inst.then_inc(st["sem"].h, 16 if st["sem"].dma else 1)


class PackView:
    def __init__(self, ap, name):
        self._ap = ap
        self.name = name

    def __getitem__(self, idx):
        return self._ap[idx]


def build_nc():
    nc = bass.Bass("TRN2", target_bir_lowering=False)

    dx = nc.dram_tensor("xT", [D, T], F32, kind="ExternalInput").ap()
    dout = nc.dram_tensor("outT", [D, T], F32, kind="ExternalOutput").ap()
    dcb = nc.dram_tensor("cbd", [128, CBW], F32, kind="ExternalInput").ap()
    dw1T = nc.dram_tensor("w1T", [D, HID], F32, kind="ExternalInput").ap()
    dw2T = nc.dram_tensor("w2T", [HID, D], F32, kind="ExternalInput").ap()
    dsmall = {}
    for nme, shp in [("wqa", [128, 8 * H]), ("wka", [128, 8 * H]), ("wva", [128, 8 * H]),
                     ("woa", [H, D]), ("wkbT", [128, 8 * H]), ("wvbT", [128, 8 * H]),
                     ("wqbT", [128, 8 * H]), ("irw1T", [128, 8 * H]), ("pwT", [128, 8 * 32]),
                     ("nmw1T", [H, 2 * H]), ("nmw2T", [2 * H, H]),
                     ("irw2T", [H, 1]), ("wobT", [H, D])]:
        dsmall[nme] = nc.dram_tensor(nme, shp, F32, kind="ExternalInput").ap()
    dbias = {}
    for nme, shp in [("b1c", [128, 32]), ("b2c", [128, 8]), ("nmb1c", [128, 1]),
                     ("nmb2c", [H, 1]), ("irb1c", [H, 1]), ("irb2", [1, 1]),
                     ("normw", [H, 1]), ("alphaf", [1, 1]), ("alphas", [1, 1])]:
        dbias[nme] = nc.dram_tensor("d_" + nme, shp, F32, kind="ExternalInput").ap()
    dw1q = nc.dram_tensor("w1q_scr", [128, 32768], BF16, kind="Internal").ap()
    dw2q = nc.dram_tensor("w2q_scr", [128, 32768], BF16, kind="Internal").ap()

    def sb(name, shape, dtype):
        return nc.alloc_sbuf_tensor(name, shape, dtype)

    cbt = sb("cbt", [128, CBW], F32)
    xbuf = [sb(f"xb{i}", [128, 8 * P], F32) for i in range(3)]
    xr = [[sb(f"xr{s}{i}", [128, P], F32R) for i in range(2)] for s in range(2)]
    xsq_sh = [sb(f"xsq{i}", [128, P], F32R) for i in range(2)]
    xsq = [xsq_sh, xsq_sh]
    mrep = [sb(f"mrep{s}", [128, P], F32) for s in range(2)]
    rrep = [sb(f"rrep{s}", [128, P], F32) for s in range(2)]
    t1sh = sb("t1s0", [128, P], F32)
    t2sh = sb("t2s0", [128, P], F32)
    t1s_ = [t1sh, t1sh]
    t2s_ = [t2sh, t2sh]
    p1s = sb("p1s", [128, P], F32)
    p2s = sb("p2s", [128, P], F32)
    xqt = [sb(f"xq{i}", [128, P], BF16) for i in range(8)]
    hqt = [sb(f"hq{i}", [128, P], BF16) for i in range(32)]
    def hq_res(oc):
        return f"hq{oc}"
    slab = [sb(f"slab{i}", [128, 2048], BF16) for i in range(4)]
    qst = [sb(f"qst{i}", [128, 512], BF16) for i in range(2)]
    stg = [xbuf[2]]

    wqa_r = sb("wqa_r", [128, 8 * H], F32R)
    wka_r = sb("wka_r", [128, 8 * H], F32R)
    wva_r = sb("wva_r", [128, 8 * H], F32R)
    woa_r = sb("woa_r", [H, D], BF16)
    wkb_r = sb("wkb_r", [128, 8 * H], F32R)
    wvb_r = sb("wvb_r", [128, 8 * H], F32R)
    wqb_r = sb("wqb_r", [128, 8 * H], F32R)
    irw1_r = sb("irw1_r", [128, 8 * H], F32R)
    pw_r = sb("pw_r", [128, 8 * 32], F32R)
    nmw1_r = sb("nmw1_r", [H, 2 * H], F32R)
    nmw2_r = sb("nmw2_r", [2 * H, H], F32R)
    irw2_r = sb("irw2_r", [H, 8], F32R)
    irw2f = sb("irw2f", [H, 1], F32)
    wob_r = sb("wob_r", [H, D], BF16)
    ones_r = sb("ones_r", [128, 128], F32R)
    ident_r = sb("ident_r", [128, 128], F32R)

    cs_sum = sb("cs_sum", [H, 8], F32)
    cspw = sb("cspw", [32, 1], F32)

    b1i = sb("b1i", [128, 32], F32)
    b2qc = sb("b2qc", [128, 8], F32)
    nmb1c = sb("nmb1c", [128, 1], F32)
    nmb2c = sb("nmb2c", [H, 1], F32)
    irb1c = sb("irb1c", [H, 1], F32)
    irb2c = sb("irb2c", [128, 1], F32)
    nw_c = sb("nw_c", [H, 1], F32)

    lnA = [sb(f"lnA{m}", [128, 1], F32) for m in range(2)]
    acol = [sb(f"acol{m}", [128, 1], F32) for m in range(2)]
    bcol = [sb(f"bcol{m}", [128, 1], F32) for m in range(2)]
    g1c = [sb(f"g1c{m}", [128, 1], F32) for m in range(2)]
    g1n = [sb(f"g1n{m}", [128, 1], F32) for m in range(2)]
    g2c = [sb(f"g2c{m}", [128, 1], F32) for m in range(2)]
    g3c = [sb(f"g3c{m}", [128, 1], F32) for m in range(2)]
    aCc = [sb(f"aCc{m}", [128, 1], F32) for m in range(2)]
    DS_ = [sb(f"DS{m}", [128, 128], F32) for m in range(2)]
    DSB = [sb(f"DSB{m}", [128, 128], F32) for m in range(2)]
    DSBT = [sb(f"DSBT{m}", [128, 128], F32) for m in range(2)]
    DI_ = [sb(f"DI{m}", [128, 128], F32) for m in range(2)]

    F0m = [sb(f"F0m{m}", [H, H], F32R) for m in range(2)]
    Za = sb("Za", [H, H], F32R)
    zcol = sb("zcol", [H, 8], F32R)

    qTa = sb("qTa", [H, P], F32R)
    kTa = sb("kTa", [H, P], F32R)
    vTa = sb("vTa", [H, P], F32R)
    oTa = sb("oTa", [H, P], BF16)
    PTs = sb("PTs", [128, 128], F32R)
    QKs = sb("QKs", [128, 128], F32)
    deni = sb("deni", [128, 1], F32)
    dent = sb("dent", [128, 1], F32)
    ztmp = sb("ztmp", [H, 1], F32)
    Ktm_a = sb("Ktm_a", [128, H], F32R)
    Vtm_a = sb("Vtm_a", [128, H], F32R)
    Otm_a = sb("Otm_a", [128, H], F32R)

    kbT = sb("kbT", [H, P], F32R)
    qbT = sb("qbT", [H, P], F32R)
    vbT = sb("vbT", [H, P], F32R)
    nuT = sb("nuT", [H, P], F32)
    gcr = sb("gcr", [H, P], F32R)
    sc1r = sb("sc1r", [128, P], F32R)
    obT = sb("obT", [H, P], BF16)
    pwC = sb("pwC", [32, P], F32R)
    pwt = sb("pwt", [128, NSC * 3], F32)
    ucol = sb("ucol", [128, NSC], F32)
    urow = sb("urow", [1, 128], F32R)
    unrow = sb("unrow", [1, 128], F32R)
    uncol = sb("uncol", [128, 1], F32)
    Nfm = sb("Nfm", [H, P], F32)
    nsinit = sb("nsinit", [H, 1], F32)
    nsA = sb("nsA", [H, 128], F32)
    nsB = sb("nsB", [H, 128], F32)
    Ntm = sb("Ntm", [128, H], F32)
    nscr = sb("nscr", [128, 128], F32)

    Npw = [[sb(f"Npw{m}{i}", [128, 128], F32R) for i in range(2)] for m in range(2)]
    NpwT = [[sb(f"NpwT{m}{i}", [128, 128], F32R) for i in range(2)] for m in range(2)]
    Ut = [[sb(f"Ut{m}{i}", [128, H], F32R) for i in range(2)] for m in range(2)]
    Rm = [sb(f"Rm{m}", [128, H], F32R) for m in range(2)]
    WxT = [sb(f"WxT{m}", [128, 128], F32R) for m in range(2)]
    WiT = [sb(f"WiT{m}", [128, 128], F32R) for m in range(2)]
    Ktm = sb("Ktm", [128, H], F32R)
    Ktg = sb("Ktg", [128, H], F32R)
    Vtm = sb("Vtm", [128, H], F32)
    rdf = sb("rdf", [128, H], F32)
    rds = sb("rds", [128, H], F32)
    predt = sb("predt", [128, H], F32)
    dsq = sb("dsq", [128, H], F32)
    errc = sb("errc", [128, 1], F32)
    oatt = sb("oatt", [128, H], F32)
    onrm = sb("onrm", [128, H], F32R)
    msc = sb("msc", [128, 1], F32)

    pbt = [nc.alloc_psum_tensor(f"pb{i}", [128, 512], F32) for i in range(8)]
    pbr = [p.bitcast(F32R) for p in pbt]
    PB = [f"pb{i}" for i in range(8)]

    def cbv(name, rows=128):
        o, w = _OFF[name]
        return cbt[0:rows, o:o + w]

    TK = Tracker()
    sem_cms = []

    def mksem(name, dma=False):
        cm = nc.semaphore(name)
        h = cm.__enter__()
        sem_cms.append(cm)
        return SemT(h, name, dma)

    s_x = mksem("s_x", dma=True)
    s_sl = [mksem("s_sl0", dma=True), mksem("s_sl1", dma=True),
            mksem("s_sl2", dma=True), mksem("s_sl3", dma=True)]
    s_s2 = [mksem("s_s2a", dma=True), mksem("s_s2b", dma=True),
            mksem("s_s2c", dma=True), mksem("s_s2d", dma=True)]
    s_sw = mksem("s_sw", dma=True)
    s_qs = [mksem("s_qs0", dma=True), mksem("s_qs1", dma=True),
            mksem("s_qs2", dma=True)]
    s_qt = [mksem("s_qt0", dma=True), mksem("s_qt1", dma=True)]
    s_out = mksem("s_out", dma=True)
    s_pe = mksem("s_pe")
    s_dv = mksem("s_dv")
    s_ac = mksem("s_ac")
    s_gp = mksem("s_gp")
    ENG_SEMS = {"pe": s_pe, "dv": s_dv, "ac": s_ac, "gp": s_gp}

    E = TK.emit

    E("sp", lambda e: e.dma_start(out=cbt[:], in_=dcb), w=["cbt"], sem=s_sw)
    def x_load(c, buf):
        E("sp", lambda e, c=c, buf=buf: e.dma_start(
            out=xbuf[buf][:],
            in_=dx[c * 128:(c + 1) * 128, :]), lat=2500,
          w=[f"xb{buf}"], sem=s_x)

    x_load(0, 0)

    small_jobs = [
        (dsmall["wqa"], wqa_r, 128, 8 * H, None, True),
        (dsmall["wka"], wka_r, 128, 8 * H, None, True),
        (dsmall["wva"], wva_r, 128, 8 * H, None, True),
        (dsmall["wkbT"], wkb_r, 128, 8 * H, None, True),
        (dsmall["wvbT"], wvb_r, 128, 8 * H, None, True),
        (dsmall["wqbT"], wqb_r, 128, 8 * H, None, True),
        (dsmall["irw1T"], irw1_r, 128, 8 * H, None, True),
        (dsmall["pwT"], pw_r, 128, 8 * 32, None, True),
        (dsmall["woa"], woa_r, H, D, None, False),
        (dsmall["nmw1T"], nmw1_r, H, 2 * H, None, False),
        (dsmall["nmw2T"], nmw2_r, 2 * H, H, None, False),
        (dsmall["irw2T"], irw2f, H, 1, None, False),
        (dsmall["wobT"], wob_r, H, D, "normw", False),
    ]
    E("sp", lambda e: e.dma_start(out=nw_c[:], in_=dbias["normw"]), w=["nw_c"], sem=s_sw)
    for i, (srcd, dst, rr, cc, fold, chunked) in enumerate(small_jobs):
        E("sp", lambda e, srcd=srcd, rr=rr, cc=cc:
          e.dma_start(out=stg[0][0:rr, 0:cc], in_=srcd),
          r=[], w=["xb2"], sem=s_sw)
        if fold == "normw":
            E("dv", lambda e, dst=dst, rr=rr, cc=cc:
              e.tensor_scalar(out=dst[0:rr, 0:cc], in0=stg[0][0:rr, 0:cc],
                              scalar1=nw_c[:], scalar2=None, op0=ALU.mult),
              r=["xb2", "nw_c"], w=[dst.name])
        else:
            E("dv", lambda e, dst=dst, rr=rr, cc=cc:
              e.tensor_copy(out=dst[0:rr, 0:cc], in_=stg[0][0:rr, 0:cc]),
              r=["xb2"], w=[dst.name])
    E("dv", lambda e: e.tensor_scalar(out=irw2_r[:], in0=cbv("ones128", rows=H)[:, 0:8],
                                      scalar1=irw2f[:], scalar2=None, op0=ALU.mult),
      r=["cbt", "irw2f"], w=["irw2_r"])
    E("dv", lambda e: e.tensor_copy(out=ones_r[:], in_=cbv("ones128")),
      r=["cbt"], w=["ones_r"])
    E("dv", lambda e: e.tensor_copy(out=ident_r[:], in_=cbv("ident")),
      r=["cbt"], w=["ident_r"])

    for nme, dst in [("b1c", b1i), ("b2c", b2qc), ("nmb1c", nmb1c), ("nmb2c", nmb2c),
                     ("irb1c", irb1c)]:
        E("sp", lambda e, nme=nme, dst=dst: e.dma_start(out=dst[:], in_=dbias[nme]),
          w=[dst.name], sem=s_sw)
    E("sp", lambda e: e.dma_start(out=irb2c[:], in_=dbias["irb2"].broadcast_to((128, 1)),
                                  allow_slow_non_contiguous=True),
      w=["irb2c"], sem=s_sw)
    E("dv", lambda e: e.tensor_scalar(out=b1i[:], in0=b1i[:], scalar1=64.0, scalar2=M_RND,
                                      op0=ALU.mult, op1=ALU.add), r=["b1i"], w=["b1i"])
    E("dv", lambda e: e.tensor_scalar(out=b1i[:], in0=b1i[:], scalar1=M_RND, scalar2=127.0,
                                      op0=ALU.subtract, op1=ALU.min), r=["b1i"], w=["b1i"])
    E("dv", lambda e: e.tensor_scalar(out=b1i[:], in0=b1i[:], scalar1=-128.0, scalar2=0.25,
                                      op0=ALU.max, op1=ALU.mult), r=["b1i"], w=["b1i"])
    E("dv", lambda e: e.tensor_scalar(out=b2qc[:], in0=b2qc[:], scalar1=64.0, scalar2=M_RND,
                                      op0=ALU.mult, op1=ALU.add), r=["b2qc"], w=["b2qc"])
    E("dv", lambda e: e.tensor_scalar(out=b2qc[:], in0=b2qc[:], scalar1=M_RND, scalar2=127.0,
                                      op0=ALU.subtract, op1=ALU.min), r=["b2qc"], w=["b2qc"])
    E("dv", lambda e: e.tensor_scalar(out=b2qc[:], in0=b2qc[:], scalar1=-128.0,
                                      scalar2=1.0 / 64.0, op0=ALU.max, op1=ALU.mult),
      r=["b2qc"], w=["b2qc"])

    cs_specs = [(wqa_r, H, 0), (wka_r, H, 1), (wva_r, H, 2), (wkb_r, H, 3),
                (wvb_r, H, 4), (wqb_r, H, 5), (irw1_r, H, 6)]
    for wt, od, j in cs_specs:
        for ic in range(8):
            E("pe", lambda e, wt=wt, od=od, j=j, ic=ic:
              e.matmul(pbt[6][0:od, 8 * j:8 * j + 8], wt[:, ic * od:(ic + 1) * od],
                       ones_r[:, 0:8], start=(ic == 0), stop=(ic == 7)),
              r=[wt.name, "ones_r"], w=[PB[6]])
    for ic in range(8):
        E("pe", lambda e, ic=ic:
          e.matmul(pbt[6][0:32, 56:64], pw_r[:, ic * 32:(ic + 1) * 32],
                   ones_r[:, 0:8], start=(ic == 0), stop=(ic == 7)),
          r=["pw_r", "ones_r"], w=[PB[6]])
    for j in range(7):
        E("dv", lambda e, j=j: e.tensor_copy(out=cs_sum[0:H, j:j + 1],
                                             in_=pbt[6][0:H, 8 * j:8 * j + 1]),
          r=[PB[6]], w=["cs_sum"])
    E("dv", lambda e: e.tensor_copy(out=cspw[:], in_=pbt[6][0:32, 56:57]),
      r=[PB[6]], w=["cspw"])

    def cs(j):
        return cs_sum[0:H, j:j + 1]

    for m in range(2):
        dal = dbias["alphaf"] if m == 0 else dbias["alphas"]
        E("sp", lambda e, dal=dal, m=m:
          e.dma_start(out=acol[m][:], in_=dal.broadcast_to((128, 1)),
                      allow_slow_non_contiguous=True),
          w=[acol[m].name], sem=s_sw)
        E("ac", lambda e, m=m: e.activation(out=lnA[m][:], in_=acol[m][:], func=AF.Ln),
          r=[acol[m].name], w=[lnA[m].name])
        E("dv", lambda e, m=m: e.tensor_scalar(out=bcol[m][:], in0=acol[m][:], scalar1=-1.0,
                                               scalar2=1.0, op0=ALU.mult, op1=ALU.add),
          r=[acol[m].name], w=[bcol[m].name])

        def gcol(dst, iota_name, m=m):
            E("dv", lambda e, dst=dst, iota_name=iota_name, m=m:
              e.tensor_scalar(out=dst[:], in0=cbv(iota_name), scalar1=lnA[m][:],
                              scalar2=None, op0=ALU.mult),
              r=["cbt", lnA[m].name], w=[dst.name])
            E("ac", lambda e, dst=dst: e.activation(out=dst[:], in_=dst[:], func=AF.Exp),
              r=[dst.name], w=[dst.name])

        gcol(g1c[m], "iota0")
        gcol(g2c[m], "iota1")
        gcol(g3c[m], "iotaC")
        E("dv", lambda e, m=m: e.tensor_scalar(out=aCc[m][:], in0=lnA[m][:], scalar1=float(C),
                                               scalar2=None, op0=ALU.mult),
          r=[lnA[m].name], w=[aCc[m].name])
        E("ac", lambda e, m=m: e.activation(out=aCc[m][:], in_=aCc[m][:], func=AF.Exp),
          r=[aCc[m].name], w=[aCc[m].name])
        E("dv", lambda e, m=m: e.tensor_tensor(out=g1n[m][:], in0=g1c[m][:], in1=bcol[m][:],
                                               op=ALU.mult),
          r=[g1c[m].name, bcol[m].name], w=[g1n[m].name])
        E("dv", lambda e, m=m: e.tensor_scalar(out=g1n[m][:], in0=g1n[m][:], scalar1=-1.0,
                                               scalar2=None, op0=ALU.mult),
          r=[g1n[m].name], w=[g1n[m].name])

        def dmat(dst, expo_name, mask_name, scale_beta, m=m):
            E("dv", lambda e, dst=dst, expo_name=expo_name, m=m:
              e.tensor_scalar(out=dst[:], in0=cbv(expo_name), scalar1=lnA[m][:],
                              scalar2=None, op0=ALU.mult),
              r=["cbt", lnA[m].name], w=[dst.name])
            E("ac", lambda e, dst=dst: e.activation(out=dst[:], in_=dst[:], func=AF.Exp),
              r=[dst.name], w=[dst.name])
            E("dv", lambda e, dst=dst, mask_name=mask_name:
              e.tensor_tensor(out=dst[:], in0=dst[:], in1=cbv(mask_name), op=ALU.mult),
              r=[dst.name, "cbt"], w=[dst.name])
            if scale_beta:
                E("dv", lambda e, dst=dst, m=m:
                  e.tensor_scalar(out=dst[:], in0=dst[:], scalar1=bcol[m][:], scalar2=None,
                                  op0=ALU.mult),
                  r=[dst.name, bcol[m].name], w=[dst.name])

        dmat(DS_[m], "expoSU", "maskSU", False)
        dmat(DSB[m], "expoSU", "maskSU", True)
        dmat(DSBT[m], "expoSL", "maskSL", True)
        dmat(DI_[m], "expoIU", "maskIU", False)

    for tns, rr_x, cc_x in [(F0m[0], H, H), (F0m[1], H, H), (Za, H, H), (zcol, H, 8)]:
        E("dv", lambda e, tns=tns, rr_x=rr_x, cc_x=cc_x:
          e.tensor_scalar(out=tns[0:rr_x, 0:cc_x], in0=cbv("zero64", rows=rr_x)[:, 0:cc_x],
                          scalar1=0.0, scalar2=None, op0=ALU.mult),
          r=["cbt"], w=[tns.name])
    E("dv", lambda e: e.tensor_scalar(out=nsinit[:], in0=cbv("zerocol", rows=H),
                                      scalar1=0.0, scalar2=None, op0=ALU.mult),
      r=["cbt"], w=["nsinit"])

    def quant_proc(b, cols, stores):
        xbn = f"xb{b}"
        E("ac", lambda e, cols=cols, b=b:
          e.activation(out=xbuf[b][:, 0:cols], in_=xbuf[b][:, 0:cols], func=AF.Relu,
                       scale=64.0, bias=cbv("Mcol")),
          r=[xbn, "cbt"], w=[xbn], cost=int(cols * 0.9))
        E("dv", lambda e, cols=cols, b=b:
          e.tensor_scalar(out=xbuf[b][:, 0:cols], in0=xbuf[b][:, 0:cols], scalar1=M_RND,
                          scalar2=127.0, op0=ALU.subtract, op1=ALU.min),
          r=[xbn], w=[xbn], cost=int(cols * 0.55))
        for q, (dst_ap, region, j) in enumerate(stores):
            E("dv", lambda e, q=q, j=j, b=b:
              e.tensor_scalar(out=qst[j][:], in0=xbuf[b][:, q * 512:(q + 1) * 512],
                              scalar1=-128.0, scalar2=None, op0=ALU.max),
              r=[xbn], w=[f"qst{j}"], cost=330)
            E("sp", lambda e, dst_ap=dst_ap, j=j:
              e.dma_start(out=dst_ap, in_=qst[j][:]),
              r=[f"qst{j}"], w=[region], sem=s_qt[j], lat=800)

    for og in range(8):
        b = 1 + og % 2
        E("sp", lambda e, og=og, b=b:
          e.dma_start(out=xbuf[b][:], in_=dw1T[og * 128:(og + 1) * 128, :]),
          w=[f"xb{b}"], sem=s_qs[og % 3], lat=2500)
        quant_proc(b, HID, [
            (dw1q[:, og * 4096 + s * 512:og * 4096 + (s + 1) * 512],
             f"dw1q_q{og}", s % 2) for s in range(8)])
    for s in range(32):
        b = 1 + s % 2
        E("sp", lambda e, s=s, b=b:
          e.dma_start(out=xbuf[b][:, 0:D], in_=dw2T[s * 128:(s + 1) * 128, 0:D]),
          w=[f"xb{b}"], sem=s_qs[s % 3], lat=2000)
        quant_proc(b, D, [
            (dw2q[:, s * 1024 + q * 512:s * 1024 + (q + 1) * 512],
             f"dw2q_r{s // 2}", (s // 2) % 2) for q in range(2)])
    x_load(1, 1)

    def emit_ln(buf, S, want_r16, proj_specs, sum_bank, sq_bank, passes=(0,)):
        xbn = f"xb{buf}"
        mr, rr_ = mrep[S], rrep[S]
        for pss in passes:
            specs = [sp_ for sp_ in proj_specs if sp_.get("pass", 0) == pss]
            for ic in range(8):
                pp = ic % 2
                xrt, xst = xr[S][pp], xsq[S][pp]
                E("ac", lambda e, ic=ic, xrt=xrt, buf=buf:
                  e.activation(out=xrt[:], in_=xbuf[buf][:, ic * P:(ic + 1) * P],
                               func=AF.Copy),
                  r=[xbn], w=[xrt.name])
                if pss == 0:
                    E("ac", lambda e, ic=ic, xst=xst, buf=buf:
                      e.activation(out=xst[:], in_=xbuf[buf][:, ic * P:(ic + 1) * P],
                                   func=AF.Square),
                      r=[xbn], w=[xst.name])
                    E("pe", lambda e, ic=ic, xrt=xrt:
                      e.matmul(pbt[sum_bank][:], ones_r[:], xrt[:],
                               start=(ic == 0), stop=(ic == 7)),
                      r=["ones_r", xrt.name], w=[PB[sum_bank]])
                    E("pe", lambda e, ic=ic, xst=xst:
                      e.matmul(pbt[sq_bank][:], ones_r[:], xst[:],
                               start=(ic == 0), stop=(ic == 7)),
                      r=["ones_r", xst.name], w=[PB[sq_bank]])
                for spec in specs:
                    E("pe", lambda e, ic=ic, xrt=xrt, spec=spec:
                      e.matmul(spec["out"], spec["lhsT"](ic), xrt[:],
                               start=(ic == 0), stop=(ic == 7)),
                      r=[spec["lres"], xrt.name], w=[spec["bank"]])
            if pss == 0:
                E("dv", lambda e, mr=mr: e.tensor_scalar(out=mr[:], in0=pbt[sum_bank][:],
                                                         scalar1=1.0 / D, scalar2=None,
                                                         op0=ALU.mult),
                  r=[PB[sum_bank]], w=[mr.name])
                E("dv", lambda e, rr_=rr_: e.tensor_scalar(out=rr_[:], in0=pbt[sq_bank][:],
                                                           scalar1=1.0 / D, scalar2=None,
                                                           op0=ALU.mult),
                  r=[PB[sq_bank]], w=[rr_.name])
                E("dv", lambda e, mr=mr: e.tensor_tensor(out=p2s[:], in0=mr[:], in1=mr[:],
                                                         op=ALU.mult),
                  r=[mr.name], w=["p2s"])
                E("dv", lambda e, rr_=rr_: e.tensor_tensor(out=rr_[:], in0=rr_[:], in1=p2s[:],
                                                           op=ALU.subtract),
                  r=[rr_.name, "p2s"], w=[rr_.name])
                E("ac", lambda e, rr_=rr_: e.activation(out=rr_[:], in_=rr_[:], func=AF.Ln,
                                                        bias=cbv("epscol")),
                  r=[rr_.name, "cbt"], w=[rr_.name])
                if want_r16:
                    E("ac", lambda e, rr_=rr_: e.activation(out=rr_[:], in_=rr_[:],
                                                            func=AF.Exp, scale=-0.5,
                                                            bias=cbv("ln16col")),
                      r=[rr_.name, "cbt"], w=[rr_.name])
                else:
                    E("ac", lambda e, rr_=rr_: e.activation(out=rr_[:], in_=rr_[:],
                                                            func=AF.Exp, scale=-0.5),
                      r=[rr_.name], w=[rr_.name])

    def emit_proj_pass(buf, S, specs):
        xbn = f"xb{buf}"
        for ic in range(8):
            pp = ic % 2
            xrt = xr[S][pp]
            E("dv", lambda e, ic=ic, xrt=xrt, buf=buf:
              e.tensor_copy(out=xrt[:], in_=xbuf[buf][:, ic * P:(ic + 1) * P]),
              r=[xbn], w=[xrt.name])
            for spec in specs:
                E("pe", lambda e, ic=ic, xrt=xrt, spec=spec:
                  e.matmul(spec["out"], spec["lhsT"](ic), xrt[:],
                           start=(ic == 0), stop=(ic == 7)),
                  r=[spec["lres"], xrt.name], w=[spec["bank"]])

    def fold_correct(S, dst_ap, dst_res, bank, psum_ap, cscol, cs_res, rows):
        mr, rr_ = mrep[S], rrep[S]
        t2 = t2s_[S]
        E("dv", lambda e: e.tensor_scalar(out=t2[0:rows, :], in0=mr[0:rows, :],
                                          scalar1=cscol, scalar2=None, op0=ALU.mult),
          r=[mr.name, cs_res], w=[t2.name])
        E("dv", lambda e: e.tensor_tensor(out=t2[0:rows, :], in0=psum_ap,
                                          in1=t2[0:rows, :], op=ALU.subtract),
          r=[bank, t2.name], w=[t2.name])
        E("dv", lambda e: e.tensor_tensor(out=dst_ap, in0=t2[0:rows, :],
                                          in1=rr_[0:rows, :], op=ALU.mult),
          r=[t2.name, rr_.name], w=[dst_res])

    def phi_inplace(S, dst, dst_res):
        t1, t2 = t1s_[S], t2s_[S]
        E("dv", lambda e: e.tensor_scalar(out=t1[0:H, :], in0=t2[0:H, :], scalar1=0.0,
                                          scalar2=None, op0=ALU.min),
          r=[t2.name], w=[t1.name])
        E("ac", lambda e: e.activation(out=t1[0:H, :], in_=t1[0:H, :], func=AF.Exp),
          r=[t1.name], w=[t1.name])
        E("dv", lambda e, dst=dst: e.scalar_tensor_tensor(out=dst[:], in0=t2[0:H, :],
                                                          scalar=0.0, in1=t1[0:H, :],
                                                          op0=ALU.max, op1=ALU.add),
          r=[t1.name, t2.name], w=[dst_res])

    def l2norm_cols(S, raw_bank, raw_psum, cscol, cs_res, dst, dst_res, ss_bank):
        mr, rr_ = mrep[S], rrep[S]
        t1, t2 = t1s_[S], t2s_[S]
        E("dv", lambda e: e.tensor_scalar(out=t2[0:H, :], in0=mr[0:H, :],
                                          scalar1=cscol, scalar2=None, op0=ALU.mult),
          r=[mr.name, cs_res], w=[t2.name])
        E("dv", lambda e: e.tensor_tensor(out=t2[0:H, :], in0=raw_psum, in1=t2[0:H, :],
                                          op=ALU.subtract),
          r=[raw_bank, t2.name], w=[t2.name])
        E("dv", lambda e: e.tensor_tensor(out=t2[0:H, :], in0=t2[0:H, :],
                                          in1=rr_[0:H, :], op=ALU.mult),
          r=[t2.name, rr_.name], w=[t2.name])
        E("ac", lambda e: e.activation(out=sc1r[0:H, :], in_=t2[0:H, :], func=AF.Square),
          r=[t2.name], w=["sc1r"])
        E("pe", lambda e: e.matmul(pbt[ss_bank][:], ones_r[0:H, :], sc1r[0:H, :],
                                   start=True, stop=True),
          r=["ones_r", "sc1r"], w=[PB[ss_bank]])
        E("ac", lambda e: e.activation(out=t1[:], in_=pbt[ss_bank][:], func=AF.Ln),
          r=[PB[ss_bank]], w=[t1.name])
        E("ac", lambda e: e.activation(out=t1[:], in_=t1[:], func=AF.Exp, scale=0.5),
          r=[t1.name], w=[t1.name])
        E("dv", lambda e: e.tensor_scalar(out=t1[:], in0=t1[:], scalar1=1e-12,
                                          scalar2=None, op0=ALU.max),
          r=[t1.name], w=[t1.name])
        E("dv", lambda e: e.reciprocal(out=t1[:], in_=t1[:]), r=[t1.name], w=[t1.name])
        E("dv", lambda e, dst=dst: e.tensor_tensor(out=dst[:], in0=t2[0:H, :],
                                                   in1=t1[0:H, :], op=ALU.mult),
          r=[t1.name, t2.name], w=[dst_res])

    def emit_ffn_p1(buf, S):
        xbn = f"xb{buf}"
        mr, r16t = mrep[S], rrep[S]
        t1 = t1s_[S]
        emit_ln(buf, S, want_r16=True, proj_specs=[], sum_bank=0, sq_bank=1)
        for ic in range(8):
            E("dv", lambda e, ic=ic, buf=buf, mr=mr:
              e.tensor_tensor(out=t1[:], in0=xbuf[buf][:, ic * P:(ic + 1) * P],
                              in1=mr[:], op=ALU.subtract),
              r=[xbn, mr.name], w=[t1.name])
            E("dv", lambda e, r16t=r16t: e.tensor_tensor(out=t1[:], in0=t1[:], in1=r16t[:],
                                                         op=ALU.mult),
              r=[t1.name, r16t.name], w=[t1.name])
            E("dv", lambda e: e.tensor_scalar(out=t1[:], in0=t1[:], scalar1=M_RND,
                                              scalar2=M_RND, op0=ALU.add, op1=ALU.subtract),
              r=[t1.name], w=[t1.name])
            E("dv", lambda e, ic=ic: e.tensor_scalar(out=xqt[ic][:], in0=t1[:], scalar1=127.0,
                                                     scalar2=-128.0, op0=ALU.min, op1=ALU.max),
              r=[t1.name], w=[f"xq{ic}"])

    def emit_ffn_p2(buf, S):
        xbn = f"xb{buf}"
        for og in range(8):
            slh = [(2 * og) % 4, (2 * og + 1) % 4]
            for h in range(2):
                sl = slh[h]
                for hf in range(2):
                    sem = s_sl[sl] if hf == 0 else s_s2[sl]
                    E("sp", lambda e, og=og, h=h, sl=sl, hf=hf:
                      e.dma_start(out=slab[sl][:, hf * 1024:(hf + 1) * 1024],
                                  in_=dw1q[:, og * 4096 + h * 2048 + hf * 1024:
                                              og * 4096 + h * 2048 + (hf + 1) * 1024]),
                      w=[f"slab{sl}h{hf}"], r=[f"dw1q_q{og}"],
                      sem=sem, lat=800)
            for o4 in range(4):
                oc = og * 4 + o4
                bank = 2 + oc % 2
                for ic in range(8):
                    sl = slh[ic // 4]
                    hf = (ic % 4) // 2
                    E("pe", lambda e, sl=sl, o4=o4, ic=ic, bank=bank:
                      e.matmul(pbt[bank][:],
                               slab[sl][:, (ic % 4) * 512 + o4 * 128:
                                           (ic % 4) * 512 + o4 * 128 + 128],
                               xqt[ic][:], start=(ic == 0), stop=(ic == 7)),
                      r=[f"slab{sl}h{hf}", f"xq{ic}"], w=[PB[bank]])
                E("ac", lambda e, oc=oc, bank=bank:
                  e.activation(out=p1s[:], in_=pbt[bank][:], func=AF.Relu,
                               scale=2.0 ** -6, bias=b1i[:, oc:oc + 1]),
                  r=[PB[bank], "b1i"], w=["p1s"])
                E("ac", lambda e: e.activation(out=p2s[:], in_=p1s[:], func=AF.Relu,
                                               bias=cbv("Mcol")),
                  r=["p1s", "cbt"], w=["p2s"])
                E("dv", lambda e, oc=oc: e.tensor_scalar(out=hqt[oc][:], in0=p2s[:],
                                                         scalar1=M_RND, scalar2=127.0,
                                                         op0=ALU.subtract, op1=ALU.min),
                  r=["p2s"], w=[hq_res(oc)], cost=400)
        for og in range(8):
            slh = [(2 * og) % 4, (2 * og + 1) % 4]
            for h in range(2):
                sl = slh[h]
                for hf in range(2):
                    sem = s_sl[sl] if hf == 0 else s_s2[sl]
                    E("sp", lambda e, og=og, h=h, sl=sl, hf=hf:
                      e.dma_start(out=slab[sl][:, hf * 1024:(hf + 1) * 1024],
                                  in_=dw2q[:, (og * 4 + h * 2) * 1024 + hf * 1024:
                                              (og * 4 + h * 2) * 1024 + (hf + 1) * 1024]),
                      w=[f"slab{sl}h{hf}"], r=[f"dw2q_r{2 * og + h}"],
                      sem=sem, lat=800)
            for o4 in range(4):
                oc = og * 4 + o4
                sl = slh[o4 // 2]
                hf = o4 % 2
                for jc in range(8):
                    E("pe", lambda e, sl=sl, o4=o4, oc=oc, jc=jc:
                      e.matmul(pbt[jc][:],
                               slab[sl][:, (o4 % 2) * 1024 + jc * 128:
                                           (o4 % 2) * 1024 + jc * 128 + 128],
                               hqt[oc][:], start=(oc == 0), stop=(oc == 31)),
                      r=[f"slab{sl}h{hf}", hq_res(oc)], w=[PB[jc]])
        for jc in range(8):
            E("dv", lambda e, jc=jc, buf=buf:
              e.tensor_scalar(out=xbuf[buf][:, jc * P:(jc + 1) * P],
                              in0=xbuf[buf][:, jc * P:(jc + 1) * P],
                              scalar1=b2qc[:, jc:jc + 1], scalar2=None, op0=ALU.add),
              r=[xbn, "b2qc"], w=[xbn])
            E("dv", lambda e, jc=jc, buf=buf:
              e.scalar_tensor_tensor(out=xbuf[buf][:, jc * P:(jc + 1) * P],
                                     in0=pbt[jc][:], scalar=2.0 ** -10,
                                     in1=xbuf[buf][:, jc * P:(jc + 1) * P],
                                     op0=ALU.mult, op1=ALU.add),
              r=[PB[jc], xbn], w=[xbn])

    def emit_attn_a_pre(buf):
        proj = [
            dict(bank=PB[2], out=pbt[2][0:H, 0:P], lres="wqa_r",
                 lhsT=lambda ic: wqa_r[:, ic * H:(ic + 1) * H]),
            dict(bank=PB[3], out=pbt[3][0:H, 0:P], lres="wka_r",
                 lhsT=lambda ic: wka_r[:, ic * H:(ic + 1) * H]),
            dict(bank=PB[4], out=pbt[4][0:H, 0:P], lres="wva_r",
                 lhsT=lambda ic: wva_r[:, ic * H:(ic + 1) * H]),
        ]
        emit_ln(buf, 0, want_r16=False, proj_specs=proj, sum_bank=0, sq_bank=1)
        fold_correct(0, t2s_[0][0:H, :], "t2s0", PB[2], pbt[2][0:H, 0:P], cs(0), "cs_sum", H)
        phi_inplace(0, qTa, "qTa")
        fold_correct(0, t2s_[0][0:H, :], "t2s0", PB[3], pbt[3][0:H, 0:P], cs(1), "cs_sum", H)
        phi_inplace(0, kTa, "kTa")
        fold_correct(0, vTa[:], "vTa", PB[4], pbt[4][0:H, 0:P], cs(2), "cs_sum", H)

    def emit_attn_a_sc(sc):
        sm = {"pe": 250, "dv": 260, "ac": 280}
        E = lambda eng, fn, r=(), w=(), **kw: TK.emit(eng, fn, r=r, w=w,
                                                      cost=kw.pop("cost", sm.get(eng)), **kw)
        s0 = sc * C
        qs = qTa[:, s0:s0 + C]
        ks = kTa[:, s0:s0 + C]
        vs = vTa[:, s0:s0 + C]
        E("pe", lambda e, ks=ks: e.transpose(pbr[2][:, 0:H], ks, ident_r[0:H, 0:H]),
          r=["kTa", "ident_r"], w=[PB[2]])
        E("dv", lambda e: e.tensor_copy(out=Ktm_a[:], in_=pbr[2][:, 0:H]),
          r=[PB[2]], w=["Ktm_a"])
        E("pe", lambda e, vs=vs: e.transpose(pbr[2][:, 64:128], vs, ident_r[0:H, 0:H]),
          r=["vTa", "ident_r"], w=[PB[2]])
        E("dv", lambda e: e.tensor_copy(out=Vtm_a[:], in_=pbr[2][:, 64:128]),
          r=[PB[2]], w=["Vtm_a"])
        E("pe", lambda e, vs=vs, qs=qs: e.matmul(pbt[2][:, 128:256], vs, qs,
                                                 start=True, stop=True),
          r=["vTa", "qTa"], w=[PB[2]])
        E("dv", lambda e: e.tensor_tensor(out=PTs[:], in0=pbt[2][:, 128:256],
                                          in1=cbv("maskIU"), op=ALU.mult),
          r=[PB[2], "cbt"], w=["PTs"])
        E("pe", lambda e, qs=qs, ks=ks: e.matmul(pbt[2][:, 256:384], qs, ks,
                                                 start=True, stop=True),
          r=["qTa", "kTa"], w=[PB[2]])
        E("dv", lambda e: e.tensor_tensor(out=QKs[:], in0=pbt[2][:, 256:384],
                                          in1=cbv("maskIL"), op=ALU.mult),
          r=[PB[2], "cbt"], w=["QKs"])
        E("dv", lambda e: e.reduce_sum(out=deni[:], in_=QKs[:], axis=AXX),
          r=["QKs"], w=["deni"])
        E("pe", lambda e, qs=qs: e.matmul(pbt[2][:, 384:392], qs, zcol[:],
                                          start=True, stop=True),
          r=["qTa", "zcol"], w=[PB[2]])
        E("dv", lambda e: e.tensor_tensor(out=dent[:], in0=deni[:],
                                          in1=pbt[2][:, 384:385], op=ALU.add),
          r=["deni", PB[2]], w=["dent"])
        E("dv", lambda e: e.tensor_scalar(out=dent[:], in0=dent[:], scalar1=1e-6,
                                          scalar2=None, op0=ALU.max),
          r=["dent"], w=["dent"])
        E("dv", lambda e: e.reciprocal(out=dent[:], in_=dent[:]), r=["dent"], w=["dent"])
        E("pe", lambda e: e.matmul(pbt[3][:, 0:H], PTs[:], Ktm_a[:], start=True, stop=False),
          r=["PTs", "Ktm_a"], w=[PB[3]])
        E("pe", lambda e, qs=qs: e.matmul(pbt[3][:, 0:H], qs, Za[:], start=False, stop=True),
          r=["qTa", "Za"], w=[PB[3]])
        E("dv", lambda e: e.tensor_scalar(out=Otm_a[:], in0=pbt[3][:, 0:H],
                                          scalar1=dent[:], scalar2=None, op0=ALU.mult),
          r=[PB[3], "dent"], w=["Otm_a"])
        E("pe", lambda e: e.transpose(pbr[3][0:H, 64:192], Otm_a[:], ident_r[:]),
          r=["Otm_a", "ident_r"], w=[PB[3]])
        E("dv", lambda e, s0=s0: e.tensor_copy(out=oTa[:, s0:s0 + C],
                                               in_=pbr[3][0:H, 64:192]),
          r=[PB[3]], w=["oTa"])
        E("pe", lambda e: e.matmul(pbt[3][0:H, 192:256], Vtm_a[:], Ktm_a[:],
                                   start=True, stop=True),
          r=["Vtm_a", "Ktm_a"], w=[PB[3]])
        E("dv", lambda e: e.tensor_tensor(out=Za[:], in0=Za[:], in1=pbt[3][0:H, 192:256],
                                          op=ALU.add),
          r=["Za", PB[3]], w=["Za"])
        E("dv", lambda e, ks=ks: e.reduce_sum(out=ztmp[:], in_=ks, axis=AXX),
          r=["kTa"], w=["ztmp"])
        E("dv", lambda e: e.tensor_scalar(out=zcol[:], in0=zcol[:], scalar1=ztmp[:],
                                          scalar2=None, op0=ALU.add),
          r=["zcol", "ztmp"], w=["zcol"])

    def emit_attn_a_out(buf):
        xbn = f"xb{buf}"
        for dc in range(8):
            bank = 2 + (dc % 2)
            E("pe", lambda e, dc=dc, bank=bank:
              e.matmul(pbt[bank][:], woa_r[:, dc * 128:(dc + 1) * 128], oTa[:],
                       start=True, stop=True),
              r=["woa_r", "oTa"], w=[PB[bank]])
            E("dv", lambda e, dc=dc, bank=bank, buf=buf:
              e.tensor_tensor(out=xbuf[buf][:, dc * P:(dc + 1) * P],
                              in0=xbuf[buf][:, dc * P:(dc + 1) * P], in1=pbt[bank][:],
                              op=ALU.add),
              r=[xbn, PB[bank]], w=[xbn])

    def emit_attn_b_pre(buf):
        proj = [
            dict(bank=PB[5], out=pbt[5][0:H, 0:P], lres="wkb_r",
                 lhsT=lambda ic: wkb_r[:, ic * H:(ic + 1) * H]),
            dict(bank=PB[6], out=pbt[6][0:H, 0:P], lres="wvb_r",
                 lhsT=lambda ic: wvb_r[:, ic * H:(ic + 1) * H]),
            dict(bank=PB[7], out=pbt[7][0:H, 0:P], lres="wqb_r",
                 lhsT=lambda ic: wqb_r[:, ic * H:(ic + 1) * H]),
        ]
        proj2 = [
            dict(bank=PB[5], out=pbt[5][0:H, 0:P], lres="irw1_r",
                 lhsT=lambda ic: irw1_r[:, ic * H:(ic + 1) * H], **{"pass": 1}),
            dict(bank=PB[6], out=pbt[6][0:32, 0:P], lres="pw_r",
                 lhsT=lambda ic: pw_r[:, ic * 32:(ic + 1) * 32], **{"pass": 1}),
        ]
        emit_ln(buf, 1, want_r16=False, proj_specs=proj, sum_bank=0, sq_bank=1)
        l2norm_cols(1, PB[5], pbt[5][0:H, 0:P], cs(3), "cs_sum", kbT, "kbT", ss_bank=4)
        fold_correct(1, vbT[:], "vbT", PB[6], pbt[6][0:H, 0:P], cs(4), "cs_sum", H)
        l2norm_cols(1, PB[7], pbt[7][0:H, 0:P], cs(5), "cs_sum", qbT, "qbT", ss_bank=4)
        emit_proj_pass(buf, 1, proj2)
        fold_correct(1, t2s_[1][0:H, :], "t2s0", PB[5], pbt[5][0:H, 0:P], cs(6), "cs_sum", H)
        E("ac", lambda e: e.activation(out=gcr[:], in_=t2s_[1][0:H, :], func=AF.Gelu,
                                       bias=irb1c[:]),
          r=["t2s0", "irb1c"], w=["gcr"])
        for sc in range(NSC):
            E("pe", lambda e, sc=sc: e.matmul(pbt[7][:, 8 * sc:8 * sc + 8],
                                              gcr[:, sc * C:(sc + 1) * C], irw2_r[:],
                                              start=True, stop=True),
              r=["gcr", "irw2_r"], w=[PB[7]])
        for sc in range(NSC):
            E("dv", lambda e, sc=sc: e.tensor_scalar(out=ucol[:, sc:sc + 1],
                                                     in0=pbt[7][:, 8 * sc:8 * sc + 1],
                                                     scalar1=irb2c[:], scalar2=0.0,
                                                     op0=ALU.add, op1=ALU.is_gt),
              r=[PB[7], "irb2c"], w=["ucol"])
        E("dv", lambda e: e.tensor_scalar(out=t2s_[1][0:32, :], in0=mrep[1][0:32, :],
                                          scalar1=cspw[:], scalar2=None, op0=ALU.mult),
          r=["mrep1", "cspw"], w=["t2s0"])
        E("dv", lambda e: e.tensor_tensor(out=t2s_[1][0:32, :], in0=pbt[6][0:32, 0:P],
                                          in1=t2s_[1][0:32, :], op=ALU.subtract),
          r=[PB[6], "t2s0"], w=["t2s0"])
        E("dv", lambda e: e.tensor_tensor(out=pwC[:], in0=t2s_[1][0:32, :],
                                          in1=rrep[1][0:32, :], op=ALU.mult),
          r=["t2s0", "rrep1"], w=["pwC"])
        E("pe", lambda e: e.matmul(pbt[5][:, 0:P], nmw1_r[:], vbT[:], start=True, stop=True),
          r=["nmw1_r", "vbT"], w=[PB[5]])
        E("ac", lambda e: e.activation(out=sc1r[:], in_=pbt[5][:, 0:P], func=AF.Gelu,
                                       bias=nmb1c[:]),
          r=[PB[5], "nmb1c"], w=["sc1r"])
        E("pe", lambda e: e.matmul(pbt[6][0:H, 0:P], nmw2_r[:], sc1r[:], start=True, stop=True),
          r=["nmw2_r", "sc1r"], w=[PB[6]])
        E("dv", lambda e: e.tensor_scalar(out=nuT[:], in0=pbt[6][0:H, 0:P],
                                          scalar1=nmb2c[:], scalar2=None, op0=ALU.add),
          r=[PB[6], "nmb2c"], w=["nuT"])
        for sc in range(NSC):
            E("pe", lambda e, sc=sc: e.transpose(pbr[7][:, 384 + 32 * sc:384 + 32 * (sc + 1)],
                                                 pwC[:, sc * C:(sc + 1) * C],
                                                 ident_r[0:32, 0:32]),
              r=["pwC", "ident_r"], w=[PB[7]])
        for sc in range(NSC):
            slc = pbt[7][:, 384 + 32 * sc:384 + 32 * sc + 3]
            E("dv", lambda e, slc=slc: e.reduce_max(out=msc[:], in_=slc, axis=AXX,
                                                    negate=True),
              r=[PB[7]], w=["msc"])
            E("ac", lambda e, sc=sc, slc=slc:
              e.activation(out=pwt[:, sc * 3:(sc + 1) * 3], in_=slc, func=AF.Exp,
                           bias=msc[:]),
              r=[PB[7], "msc"], w=["pwt"])
            E("dv", lambda e, sc=sc: e.reduce_sum(out=msc[:], in_=pwt[:, sc * 3:(sc + 1) * 3],
                                                  axis=AXX),
              r=["pwt"], w=["msc"])
            E("dv", lambda e: e.reciprocal(out=msc[:], in_=msc[:]), r=["msc"], w=["msc"])
            E("dv", lambda e, sc=sc: e.tensor_scalar(out=pwt[:, sc * 3:(sc + 1) * 3],
                                                     in0=pwt[:, sc * 3:(sc + 1) * 3],
                                                     scalar1=msc[:], scalar2=None,
                                                     op0=ALU.mult),
              r=["pwt", "msc"], w=["pwt"])

    def emit_attn_b_sc(sc):
        sm = {"pe": 250, "dv": 260, "ac": 280}
        E = lambda eng, fn, r=(), w=(), **kw: TK.emit(eng, fn, r=r, w=w,
                                                      cost=kw.pop("cost", sm.get(eng)), **kw)
        s0 = sc * C
        kss = kbT[:, s0:s0 + C]
        qss = qbT[:, s0:s0 + C]
        vss = vbT[:, s0:s0 + C]
        ucs = ucol[:, sc:sc + 1]
        E("pe", lambda e, kss=kss: e.transpose(pbr[4][:, 256:320], kss, ident_r[0:H, 0:H]),
          r=["kbT", "ident_r"], w=[PB[4]])
        E("dv", lambda e: e.tensor_copy(out=Ktm[:], in_=pbr[4][:, 256:320]),
          r=[PB[4]], w=["Ktm"])
        E("pe", lambda e, vss=vss: e.transpose(pbr[4][:, 320:384], vss, ident_r[0:H, 0:H]),
          r=["vbT", "ident_r"], w=[PB[4]])
        E("dv", lambda e: e.tensor_copy(out=Vtm[:], in_=pbr[4][:, 320:384]),
          r=[PB[4]], w=["Vtm"])
        E("pe", lambda e, ucs=ucs: e.transpose(pbt[4][0:1, 384:512], ucs, cbv("ident")),
          r=["ucol", "cbt"], w=[PB[4]])
        E("dv", lambda e: e.tensor_copy(out=urow[:], in_=pbt[4][0:1, 384:512]),
          r=[PB[4]], w=["urow"])
        E("pe", lambda e, kss=kss: e.matmul(pbt[4][:, 0:128], kss, kss, start=True, stop=True),
          r=["kbT"], w=[PB[4]])
        E("pe", lambda e, kss=kss, qss=qss: e.matmul(pbt[5][:, 0:128], kss, qss,
                                                     start=True, stop=True),
          r=["kbT", "qbT"], w=[PB[5]])
        E("pe", lambda e: e.matmul(pbt[4][:, 128:256], urow[:], urow[:], start=True, stop=True),
          r=["urow"], w=[PB[4]])
        E("dv", lambda e: e.tensor_tensor(out=Npw[0][0][:], in0=pbt[4][:, 0:128],
                                          in1=DSB[0][:], op=ALU.mult),
          r=[PB[4], "DSB0"], w=["Npw00"])
        E("dv", lambda e: e.tensor_tensor(out=NpwT[0][0][:], in0=pbt[4][:, 0:128],
                                          in1=DSBT[0][:], op=ALU.mult),
          r=[PB[4], "DSBT0"], w=["NpwT00"])
        E("dv", lambda e: e.tensor_tensor(out=nscr[:], in0=pbt[4][:, 0:128],
                                          in1=DSB[1][:], op=ALU.mult),
          r=[PB[4], "DSB1"], w=["nscr"])
        E("dv", lambda e: e.tensor_tensor(out=Npw[1][0][:], in0=nscr[:],
                                          in1=pbt[4][:, 128:256], op=ALU.mult),
          r=["nscr", PB[4]], w=["Npw10"])
        E("dv", lambda e: e.tensor_tensor(out=nscr[:], in0=pbt[4][:, 0:128],
                                          in1=DSBT[1][:], op=ALU.mult),
          r=[PB[4], "DSBT1"], w=["nscr"])
        E("dv", lambda e: e.tensor_tensor(out=NpwT[1][0][:], in0=nscr[:],
                                          in1=pbt[4][:, 128:256], op=ALU.mult),
          r=["nscr", PB[4]], w=["NpwT10"])
        E("dv", lambda e: e.tensor_tensor(out=WxT[0][:], in0=pbt[5][:, 0:128],
                                          in1=DS_[0][:], op=ALU.mult),
          r=[PB[5], "DS0"], w=["WxT0"])
        E("dv", lambda e: e.tensor_tensor(out=WiT[0][:], in0=pbt[5][:, 0:128],
                                          in1=DI_[0][:], op=ALU.mult),
          r=[PB[5], "DI0"], w=["WiT0"])
        E("dv", lambda e: e.tensor_tensor(out=WxT[1][:], in0=pbt[5][:, 0:128],
                                          in1=DS_[1][:], op=ALU.mult),
          r=[PB[5], "DS1"], w=["WxT1"])
        E("dv", lambda e, ucs=ucs: e.tensor_scalar(out=WxT[1][:], in0=WxT[1][:],
                                                   scalar1=ucs, scalar2=None, op0=ALU.mult),
          r=["WxT1", "ucol"], w=["WxT1"])
        E("dv", lambda e: e.tensor_tensor(out=WiT[1][:], in0=pbt[5][:, 0:128],
                                          in1=DI_[1][:], op=ALU.mult),
          r=[PB[5], "DI1"], w=["WiT1"])
        E("dv", lambda e, ucs=ucs: e.tensor_scalar(out=WiT[1][:], in0=WiT[1][:],
                                                   scalar1=ucs, scalar2=None, op0=ALU.mult),
          r=["WiT1", "ucol"], w=["WiT1"])
        for m in range(2):
            off = 128 + m * H
            E("pe", lambda e, kss=kss, m=m, off=off:
              e.matmul(pbt[5][:, off:off + H], kss, F0m[m][:], start=True, stop=True),
              r=["kbT", f"F0m{m}"], w=[PB[5]])
            if m == 0:
                E("dv", lambda e: e.tensor_scalar(out=rdf[:], in0=pbt[5][:, 128:128 + H],
                                                  scalar1=g1n[0][:], scalar2=None,
                                                  op0=ALU.mult),
                  r=[PB[5], "g1n0"], w=["rdf"])
                E("dv", lambda e: e.scalar_tensor_tensor(out=Rm[0][:], in0=Vtm[:],
                                                         scalar=bcol[0][:], in1=rdf[:],
                                                         op0=ALU.mult, op1=ALU.add),
                  r=["Vtm", "bcol0", "rdf"], w=["Rm0"])
            else:
                E("dv", lambda e: e.tensor_scalar(out=rds[:], in0=pbt[5][:, 192:192 + H],
                                                  scalar1=g1n[1][:], scalar2=ucs,
                                                  op0=ALU.mult, op1=ALU.mult),
                  r=[PB[5], "g1n1", "ucol"], w=["rds"])
                E("dv", lambda e: e.tensor_scalar(out=predt[:], in0=Vtm[:],
                                                  scalar1=bcol[1][:], scalar2=ucs,
                                                  op0=ALU.mult, op1=ALU.mult),
                  r=["Vtm", "bcol1", "ucol"], w=["predt"])
                E("dv", lambda e: e.tensor_tensor(out=Rm[1][:], in0=predt[:], in1=rds[:],
                                                  op=ALU.add),
                  r=["predt", "rds"], w=["Rm1"])
        for m in range(2):
            bk = 6 + m
            E("pe", lambda e, m=m, bk=bk: e.matmul(pbt[bk][:, 256:320], Npw[m][0][:],
                                                   Rm[m][:], start=True, stop=True),
              r=[f"Npw{m}0", f"Rm{m}"], w=[PB[bk]])
            E("dv", lambda e, m=m, bk=bk: e.tensor_tensor(out=Ut[m][0][:], in0=Rm[m][:],
                                                          in1=pbt[bk][:, 256:320],
                                                          op=ALU.subtract),
              r=[f"Rm{m}", PB[bk]], w=[f"Ut{m}0"])
        for j in range(6):
            jp = j % 2
            jn = (j + 1) % 2
            for m in range(2):
                bk = 6 + m
                E("pe", lambda e, m=m, jp=jp, bk=bk:
                  e.matmul(pbt[bk][:, 0:128], NpwT[m][jp][:], Npw[m][jp][:],
                           start=True, stop=True),
                  r=[f"NpwT{m}{jp}", f"Npw{m}{jp}"], w=[PB[bk]])
                E("dv", lambda e, m=m, jn=jn, bk=bk:
                  e.tensor_copy(out=Npw[m][jn][:], in_=pbr[bk][:, 0:128]),
                  r=[PB[bk]], w=[f"Npw{m}{jn}"])
                if j < 5:
                    E("pe", lambda e, m=m, jp=jp, bk=bk:
                      e.matmul(pbt[bk][:, 128:256], Npw[m][jp][:], NpwT[m][jp][:],
                               start=True, stop=True),
                      r=[f"Npw{m}{jp}", f"NpwT{m}{jp}"], w=[PB[bk]])
                    E("dv", lambda e, m=m, jn=jn, bk=bk:
                      e.tensor_copy(out=NpwT[m][jn][:], in_=pbr[bk][:, 128:256]),
                      r=[PB[bk]], w=[f"NpwT{m}{jn}"])
                E("pe", lambda e, m=m, jp=jp, jn=jn, bk=bk:
                  e.matmul(pbt[bk][:, 256:320], Npw[m][jn][:], Ut[m][jp][:],
                           start=True, stop=True),
                  r=[f"Npw{m}{jn}", f"Ut{m}{jp}"], w=[PB[bk]])
                E("dv", lambda e, m=m, jp=jp, jn=jn, bk=bk:
                  e.tensor_tensor(out=Ut[m][jn][:], in0=Ut[m][jp][:],
                                  in1=pbt[bk][:, 256:320], op=ALU.add),
                  r=[f"Ut{m}{jp}", PB[bk]], w=[f"Ut{m}{jn}"])
        for m in range(2):
            bk = 6 + m
            E("pe", lambda e, qss=qss, m=m: e.matmul(pbt[5][:, 256 + m * H:256 + (m + 1) * H],
                                                     qss, F0m[m][:], start=True, stop=True),
              r=["qbT", f"F0m{m}"], w=[PB[5]])
            E("pe", lambda e, m=m, bk=bk: e.matmul(pbt[bk][:, 320:384], WxT[m][:],
                                                   Ut[m][0][:], start=True, stop=True),
              r=[f"WxT{m}", f"Ut{m}0"], w=[PB[bk]])
            E("pe", lambda e, m=m, bk=bk: e.matmul(pbt[bk][:, 384:448], WiT[m][:],
                                                   Ut[m][0][:], start=True, stop=True),
              r=[f"WiT{m}", f"Ut{m}0"], w=[PB[bk]])
        E("dv", lambda e: e.tensor_scalar(out=rdf[:], in0=pbt[5][:, 256:256 + H],
                                          scalar1=g1c[0][:], scalar2=None, op0=ALU.mult),
          r=[PB[5], "g1c0"], w=["rdf"])
        E("dv", lambda e: e.tensor_tensor(out=rdf[:], in0=rdf[:], in1=pbt[6][:, 320:384],
                                          op=ALU.add),
          r=["rdf", PB[6]], w=["rdf"])
        E("dv", lambda e: e.tensor_scalar(out=rds[:], in0=pbt[5][:, 320:320 + H],
                                          scalar1=g1c[1][:], scalar2=None, op0=ALU.mult),
          r=[PB[5], "g1c1"], w=["rds"])
        E("dv", lambda e: e.tensor_tensor(out=rds[:], in0=rds[:], in1=pbt[7][:, 320:384],
                                          op=ALU.add),
          r=["rds", PB[7]], w=["rds"])
        E("dv", lambda e: e.tensor_tensor(out=predt[:], in0=rdf[:], in1=rds[:], op=ALU.add),
          r=["rdf", "rds"], w=["predt"])
        E("dv", lambda e: e.scalar_tensor_tensor(out=predt[:], in0=predt[:], scalar=-0.5,
                                                 in1=Vtm[:], op0=ALU.mult, op1=ALU.add),
          r=["predt", "Vtm"], w=["predt"])
        E("ac", lambda e: e.activation(out=dsq[:], in_=predt[:], func=AF.Square),
          r=["predt"], w=["dsq"])
        E("dv", lambda e: e.reduce_sum(out=errc[:], in_=dsq[:], axis=AXX),
          r=["dsq"], w=["errc"])
        E("dv", lambda e: e.tensor_scalar(out=uncol[:], in0=errc[:], scalar1=SUR_THR,
                                          scalar2=None, op0=ALU.is_gt),
          r=["errc"], w=["uncol"])
        E("dv", lambda e: e.tensor_scalar(out=rdf[:], in0=pbt[5][:, 256:256 + H],
                                          scalar1=g2c[0][:], scalar2=None, op0=ALU.mult),
          r=[PB[5], "g2c0"], w=["rdf"])
        E("dv", lambda e: e.tensor_tensor(out=rdf[:], in0=rdf[:], in1=pbt[6][:, 384:448],
                                          op=ALU.add),
          r=["rdf", PB[6]], w=["rdf"])
        E("dv", lambda e: e.tensor_scalar(out=rds[:], in0=pbt[5][:, 320:320 + H],
                                          scalar1=g2c[1][:], scalar2=None, op0=ALU.mult),
          r=[PB[5], "g2c1"], w=["rds"])
        E("dv", lambda e: e.tensor_tensor(out=rds[:], in0=rds[:], in1=pbt[7][:, 384:448],
                                          op=ALU.add),
          r=["rds", PB[7]], w=["rds"])
        E("pe", lambda e: e.transpose(pbt[7][0:1, 128:256], uncol[:], cbv("ident")),
          r=["uncol", "cbt"], w=[PB[7]])
        E("dv", lambda e: e.tensor_copy(out=unrow[:], in_=pbt[7][0:1, 128:256]),
          r=[PB[7]], w=["unrow"])
        E("pe", lambda e: e.matmul(pbt[6][0:H, 0:128], ones_r[0:1, 0:H], unrow[:],
                                   start=True, stop=True),
          r=["ones_r", "unrow"], w=[PB[6]])
        E("dv", lambda e: e.tensor_scalar(out=nsA[:], in0=pbt[6][0:H, 0:128],
                                          scalar1=-0.1, scalar2=1.0, op0=ALU.mult,
                                          op1=ALU.add),
          r=[PB[6]], w=["nsA"])
        E("dv", lambda e: e.tensor_scalar(out=nsB[:], in0=pbt[6][0:H, 0:128],
                                          scalar1=0.1, scalar2=None, op0=ALU.mult),
          r=[PB[6]], w=["nsB"])
        E("dv", lambda e, s0=s0: e.tensor_tensor(out=nsB[:], in0=nsB[:],
                                                 in1=nuT[:, s0:s0 + C], op=ALU.mult),
          r=["nsB", "nuT"], w=["nsB"])
        if sc == 0:
            E("dv", lambda e, s0=s0:
              e.tensor_tensor_scan(out=Nfm[:, s0:s0 + C], data0=nsA[:], data1=nsB[:],
                                   initial=nsinit[:], op0=ALU.mult, op1=ALU.add),
              r=["nsA", "nsB", "nsinit"], w=["Nfm"], cost=500)
        else:
            E("dv", lambda e, s0=s0:
              e.tensor_tensor_scan(out=Nfm[:, s0:s0 + C], data0=nsA[:], data1=nsB[:],
                                   initial=Nfm[:, s0 - 1:s0], op0=ALU.mult, op1=ALU.add),
              r=["nsA", "nsB", "Nfm"], w=["Nfm"], cost=500)
        E("pe", lambda e, s0=s0: e.transpose(pbt[6][:, 448:512], Nfm[:, s0:s0 + C],
                                             cbv("ident", rows=H)[0:H, 0:H]),
          r=["Nfm", "cbt"], w=[PB[6]])
        E("dv", lambda e: e.tensor_copy(out=Ntm[:], in_=pbt[6][:, 448:512]),
          r=[PB[6]], w=["Ntm"])
        E("dv", lambda e, sc=sc: e.tensor_scalar(out=oatt[:], in0=rdf[:],
                                                 scalar1=pwt[:, sc * 3:sc * 3 + 1],
                                                 scalar2=None, op0=ALU.mult),
          r=["rdf", "pwt"], w=["oatt"])
        E("dv", lambda e, sc=sc: e.scalar_tensor_tensor(out=oatt[:], in0=rds[:],
                                                        scalar=pwt[:, sc * 3 + 1:sc * 3 + 2],
                                                        in1=oatt[:], op0=ALU.mult,
                                                        op1=ALU.add),
          r=["rds", "pwt", "oatt"], w=["oatt"])
        E("dv", lambda e, sc=sc: e.scalar_tensor_tensor(out=oatt[:], in0=Ntm[:],
                                                        scalar=pwt[:, sc * 3 + 2:sc * 3 + 3],
                                                        in1=oatt[:], op0=ALU.mult,
                                                        op1=ALU.add),
          r=["Ntm", "pwt", "oatt"], w=["oatt"])
        E("ac", lambda e: e.activation(out=dsq[:], in_=oatt[:], func=AF.Square),
          r=["oatt"], w=["dsq"])
        E("dv", lambda e: e.reduce_sum(out=msc[:], in_=dsq[:], axis=AXX),
          r=["dsq"], w=["msc"])
        E("dv", lambda e: e.tensor_scalar(out=msc[:], in0=msc[:], scalar1=1.0 / H,
                                          scalar2=1e-6, op0=ALU.mult, op1=ALU.add),
          r=["msc"], w=["msc"])
        E("ac", lambda e: e.activation(out=msc[:], in_=msc[:], func=AF.Ln),
          r=["msc"], w=["msc"])
        E("ac", lambda e: e.activation(out=msc[:], in_=msc[:], func=AF.Exp, scale=-0.5),
          r=["msc"], w=["msc"])
        E("dv", lambda e: e.tensor_scalar(out=onrm[:], in0=oatt[:], scalar1=msc[:],
                                          scalar2=None, op0=ALU.mult),
          r=["oatt", "msc"], w=["onrm"])
        E("pe", lambda e: e.transpose(pbr[7][0:H, 0:128], onrm[:], ident_r[:]),
          r=["onrm", "ident_r"], w=[PB[7]])
        E("dv", lambda e, s0=s0: e.tensor_copy(out=obT[:, s0:s0 + C],
                                               in_=pbr[7][0:H, 0:128]),
          r=[PB[7]], w=["obT"])
        E("dv", lambda e, ucs=ucs: e.tensor_scalar(out=Ktg[:], in0=Ktm[:], scalar1=ucs,
                                                   scalar2=None, op0=ALU.mult),
          r=["Ktm", "ucol"], w=["Ktg"])
        for m in range(2):
            E("dv", lambda e, m=m: e.tensor_scalar(out=Ut[m][0][:], in0=Ut[m][0][:],
                                                   scalar1=g3c[m][:], scalar2=None,
                                                   op0=ALU.mult),
              r=[f"Ut{m}0", f"g3c{m}"], w=[f"Ut{m}0"])
            lhs = Ktm if m == 0 else Ktg
            lres = "Ktm" if m == 0 else "Ktg"
            E("pe", lambda e, m=m, lhs=lhs: e.matmul(pbt[5][0:H, 384 + m * H:384 + (m + 1) * H],
                                                     lhs[:], Ut[m][0][:],
                                                     start=True, stop=True),
              r=[lres, f"Ut{m}0"], w=[PB[5]])
            E("dv", lambda e, m=m: e.scalar_tensor_tensor(
                out=F0m[m][:], in0=F0m[m][:], scalar=aCc[m][0:H, :],
                in1=pbt[5][0:H, 384 + m * H:384 + (m + 1) * H],
                op0=ALU.mult, op1=ALU.add),
              r=[f"F0m{m}", PB[5], f"aCc{m}"], w=[f"F0m{m}"])

    def emit_attn_b_out(buf):
        xbn = f"xb{buf}"
        E("dv", lambda e: e.tensor_copy(out=nsinit[:], in_=Nfm[:, P - 1:P]),
          r=["Nfm"], w=["nsinit"])
        for dc in range(8):
            bank = 4 + (dc % 2)
            E("pe", lambda e, dc=dc, bank=bank:
              e.matmul(pbt[bank][:], wob_r[:, dc * 128:(dc + 1) * 128], obT[:],
                       start=True, stop=True),
              r=["wob_r", "obT"], w=[PB[bank]])
            E("dv", lambda e, dc=dc, bank=bank, buf=buf:
              e.tensor_tensor(out=xbuf[buf][:, dc * P:(dc + 1) * P],
                              in0=xbuf[buf][:, dc * P:(dc + 1) * P], in1=pbt[bank][:],
                              op=ALU.add),
              r=[xbn, PB[bank]], w=[xbn])

    if PIPELINE and SUBLAYERS >= 4:
        for i in range(NCHUNK + 1):
            bufA = i % 3
            bufB = (i - 1) % 3
            hasA = i < NCHUNK
            hasB = i > 0
            if i == 0:
                emit_attn_a_pre(bufA)
            if hasB:
                emit_attn_b_pre(bufB)
            for sc in range(NSC):
                if hasA:
                    emit_attn_a_sc(sc)
                if hasB:
                    emit_attn_b_sc(sc)
            if hasA:
                emit_attn_a_out(bufA)
            if hasB:
                emit_attn_b_out(bufB)
            if hasA:
                emit_ffn_p1(bufA, 0)
                emit_ffn_p2(bufA, 0)
            if hasB:
                emit_ffn_p1(bufB, 1)
            if hasA and i + 1 < NCHUNK:
                emit_attn_a_pre((i + 1) % 3)
            if hasB:
                emit_ffn_p2(bufB, 1)
                E("sp", lambda e, c=i - 1, bufB=bufB: e.dma_start(
                    out=dout[c * 128:(c + 1) * 128, :],
                    in_=xbuf[bufB][:]),
                  r=[f"xb{bufB}"], w=[f"od{i - 1}"], sem=s_out)
            if i + 2 < NCHUNK:
                x_load(i + 2, (i + 2) % 3)
    else:
        for c_ in range(NCHUNK):
            buf = c_ % 3
            if SUBLAYERS >= 1:
                emit_attn_a_pre(buf)
                for sc in range(NSC):
                    emit_attn_a_sc(sc)
                emit_attn_a_out(buf)
            if SUBLAYERS >= 2:
                emit_ffn_p1(buf, 0)
                emit_ffn_p2(buf, 0)
            if SUBLAYERS >= 3:
                emit_attn_b_pre(buf)
                for sc in range(NSC):
                    emit_attn_b_sc(sc)
                emit_attn_b_out(buf)
            if SUBLAYERS >= 4:
                emit_ffn_p1(buf, 1)
                emit_ffn_p2(buf, 1)
            E("sp", lambda e, c_=c_, buf=buf: e.dma_start(
                out=dout[c_ * 128:(c_ + 1) * 128, :],
                in_=xbuf[buf][:]),
              r=[f"xb{buf}"], w=[f"od{c_}"], sem=s_out)
            if c_ + 2 < NCHUNK:
                x_load(c_ + 2, (c_ + 2) % 3)

    TK.schedule()
    TK.finalize(ENG_SEMS)
    TK.steps.append(dict(eng="sp", fn=lambda e: e.nop(), r=(), w=(), sem=s_out,
                         inc=False, deps=set(), val=None,
                         waits=[(s_out, 16 * NCHUNK)], xwaits=[]))

    with nc.Block() as block:
        @block.sync
        def _(sync):
            TK.run_engine("sp", sync)

        @block.vector
        def _(vector):
            TK.run_engine("dv", vector)

        @block.scalar
        def _(scalar):
            TK.run_engine("ac", scalar)

        @block.tensor
        def _(tensor):
            TK.run_engine("pe", tensor)

        @block.gpsimd
        def _(gpsimd):
            TK.run_engine("gp", gpsimd)

    for cm in reversed(sem_cms):
        cm.__exit__(None, None, None)
    return nc


def kernel(**inputs):
    g = lambda k: np.asarray(inputs[k])
    x = g("L0_attn_a_x").astype(np.float32, copy=False)
    cbd = build_const_blob()
    pwT = np.zeros((D, 32), np.float32)
    pwT[:, 0:3] = g("L0_attn_b_pw").T

    def pgather(a):
        k = a.shape[1]
        return np.ascontiguousarray(
            a.reshape(8, 128, k).transpose(1, 0, 2).reshape(128, 8 * k))

    def ogmajor(a):
        return np.ascontiguousarray(
            a.reshape(8, 128, 8, 512).transpose(2, 1, 0, 3).reshape(1024, 4096))

    common = dict(
        cbd=cbd,
        w1T=ogmajor(np.ascontiguousarray(g("L0_ffn_w1").T)),
        w2T=np.ascontiguousarray(g("L0_ffn_w2").T),
        wqa=pgather(g("L0_attn_a_W_q")),
        wka=pgather(g("L0_attn_a_W_k")),
        wva=pgather(g("L0_attn_a_W_v")),
        woa=np.ascontiguousarray(g("L0_attn_a_W_o")),
        wkbT=pgather(np.ascontiguousarray(g("L0_attn_b_Wk").T)),
        wvbT=pgather(np.ascontiguousarray(g("L0_attn_b_Wv").T)),
        wqbT=pgather(np.ascontiguousarray(g("L0_attn_b_Wq").T)),
        irw1T=pgather(np.ascontiguousarray(g("L0_attn_b_ir_w1").T)),
        pwT=pgather(pwT),
        nmw1T=np.ascontiguousarray(g("L0_attn_b_nm_w1").T),
        nmw2T=np.ascontiguousarray(g("L0_attn_b_nm_w2").T),
        irw2T=np.ascontiguousarray(g("L0_attn_b_ir_w2").T),
        wobT=np.ascontiguousarray(g("L0_attn_b_Wo").T),
        d_b1c=np.ascontiguousarray(g("L0_ffn_b1").reshape(32, 128).T),
        d_b2c=np.ascontiguousarray(g("L0_ffn_b2").reshape(8, 128).T),
        d_nmb1c=np.ascontiguousarray(g("L0_attn_b_nm_b1").reshape(128, 1)),
        d_nmb2c=np.ascontiguousarray(g("L0_attn_b_nm_b2").reshape(64, 1)),
        d_irb1c=np.ascontiguousarray(g("L0_attn_b_ir_b1").reshape(64, 1)),
        d_irb2=np.ascontiguousarray(g("L0_attn_b_ir_b2").reshape(1, 1)),
        d_normw=np.ascontiguousarray(g("L0_attn_b_norm_w").reshape(64, 1)),
        d_alphaf=np.ascontiguousarray(np.reshape(g("L0_attn_b_fast_alpha"), (1, 1))),
        d_alphas=np.ascontiguousarray(np.reshape(g("L0_attn_b_slow_alpha"), (1, 1))),
    )
    def cmajor(a):
        return np.ascontiguousarray(
            a.reshape(8, 128, 8, 512).transpose(2, 1, 0, 3).reshape(1024, 4096))

    in_maps = []
    for b in range(8):
        m = dict(common)
        m["xT"] = cmajor(np.ascontiguousarray(x[b].T))
        in_maps.append(m)

    nc = build_nc()
    res = run_bass_kernel_spmd(nc, in_maps, core_ids=list(range(8)))
    out = np.stack(
        [np.ascontiguousarray(cmajor(res.results[b]["outT"]).T) for b in range(8)],
        axis=0)
    return out.astype(np.float32, copy=False)
